# version 03
# baseline (speedup 1.0000x reference)
import numpy as np
from collections import defaultdict
from contextlib import ExitStack

import concourse.bass as bass
import concourse.mybir as mybir
from concourse.bass_core_rust import IndirectOffsetOnAxis
from concourse.bass_utils import run_bass_kernel_spmd

F32 = mybir.dt.float32
BF16 = mybir.dt.bfloat16
I32 = mybir.dt.int32
AF = mybir.ActivationFunctionType
ALU = mybir.AluOpType
AX = mybir.AxisListType

NCORES = 8
T = 8192
D = 2048
TLOC = T // NCORES
NT = TLOC // 128
ND = D // 128
QC = 512
NQC = T // QC

PLAN = [[0, 1, 2, 3]]
DEBUG_L0F = False
L1F_NEXP = 8
L1F_SHARED = True
L1F_SH2 = True
L1F_NOSDB = False
L1F_NIT = None
LAST_RESULTS = None


class Prog:
    ENGINES = ("sync", "gpsimd", "scalar", "vector", "tensor")

    def __init__(self, nc, ctx):
        self.nc = nc
        self.ctx = ctx
        self.q = {e: [] for e in self.ENGINES}
        self.cnt = defaultdict(int)
        self.sems = {}

    def sem(self, name):
        if name not in self.sems:
            self.sems[name] = self.ctx.enter_context(self.nc.semaphore(name))
        return self.sems[name]

    def op(self, eng, build, waits=(), inc=None):
        waits = tuple((self.sem(s), int(v)) for (s, v) in waits if v)
        newv = None
        if inc is not None:
            sname, n = inc
            self.cnt[sname] += n
            newv = self.cnt[sname]
            semh = self.sem(sname)

            def closure(e, build=build, waits=waits, semh=semh, n=n):
                for (sh, v) in waits:
                    e.wait_ge(sh, v)
                build(e).then_inc(semh, n)
        else:
            def closure(e, build=build, waits=waits):
                for (sh, v) in waits:
                    e.wait_ge(sh, v)
                build(e)
        self.q[eng].append(closure)
        return newv

    def val(self, name):
        return self.cnt[name]

    def run_queues(self):
        with self.nc.Block() as block:
            block.sync(lambda e: [f(e) for f in self.q["sync"]])
            block.gpsimd(lambda e: [f(e) for f in self.q["gpsimd"]])
            block.scalar(lambda e: [f(e) for f in self.q["scalar"]])
            block.vector(lambda e: [f(e) for f in self.q["vector"]])
            block.tensor(lambda e: [f(e) for f in self.q["tensor"]])


def _retile_dmajor(W_ed):
    e_out, d_in = W_ed.shape
    nct, ndt = e_out // 128, d_in // 128
    arr = W_ed.reshape(nct, 128, ndt, 128)
    arr = arr.transpose(0, 3, 2, 1)
    return np.ascontiguousarray(arr).reshape(nct, 128, d_in)


def _rows_block(W_de):
    d_in, e_out = W_de.shape
    return np.ascontiguousarray(W_de.reshape(d_in // 128, 128, e_out))


def _cm_vec(v):
    return np.ascontiguousarray(np.asarray(v).reshape(-1, 128).T)


def shard_tokens(x_full):
    return [np.ascontiguousarray(x_full[c * TLOC:(c + 1) * TLOC])
            for c in range(NCORES)]


def unshard_tokens(shards):
    return np.concatenate(shards, axis=0)


class B:
    def __init__(self, blocks):
        self.blocks = blocks
        self.nc = bass.Bass("TRN2", target_bir_lowering=False, debug=False)
        self.ctx = ExitStack()
        self.P = Prog(self.nc, self.ctx)
        self.shared_names = []
        self.percore_names = []

    def din(self, name, shape, percore=False, dtype=F32):
        (self.percore_names if percore else self.shared_names).append(name)
        return self.nc.dram_tensor(name, list(shape), dtype, kind="ExternalInput").ap()

    def sbuf(self, name, shape, dtype):
        return self.ctx.enter_context(self.nc.sbuf_tensor(name, list(shape), dtype))

    def psum(self, name, shape, dtype=F32):
        return self.ctx.enter_context(self.nc.psum_tensor(name, list(shape), dtype))

    def col(self, name, width=1):
        if name not in self.cols:
            assert self._small_off + width <= 256, "smalls overflow"
            self.cols[name] = self.smalls[:, self._small_off:self._small_off + width]
            self._small_off += width
        return self.cols[name]

    def reset_cols(self):
        self._small_off = 0
        self.cols = {}

    def build(self):
        nc = self.nc
        self.x_in = self.din("x_in", [TLOC, D], percore=True)
        self.y_out = nc.dram_tensor("y_out", [TLOC, D], F32, kind="ExternalOutput").ap()

        self.res = self.sbuf("res", [128, NT * D], F32)
        self.xlnT = self.sbuf("xlnT", [128, ND * TLOC], BF16)
        self.bigA = self.sbuf("bigA", [128, ND * TLOC], BF16)
        self.bigB = self.sbuf("bigB", [128, ND * TLOC], BF16)
        self.wbuf = [self.sbuf("wbuf0", [128, 6144], BF16),
                     self.sbuf("wbuf1", [128, 6144], BF16)]
        self.scrp = self.sbuf("scrp", [128, 4 * 1024], F32)
        self.tg = self.sbuf("tg", [128, 1024], F32)
        self.identb = self.sbuf("identb", [128, 128], BF16)
        self.identf = self.sbuf("identf", [128, 128], F32)
        self.ones = self.sbuf("onesb", [128, 512], F32)
        self.onesbf = self.sbuf("onesbf", [128, 8], BF16)
        self.epscol = self.sbuf("epscol", [128, 1], F32)
        self.smalls = self.sbuf("smalls", [128, 256], F32)
        self.pb = [self.psum(f"pb{i}", [128, 512], F32) for i in range(6)]
        self.ptr = [self.psum(f"ptr{i}", [128, 1024], BF16) for i in range(2)]
        self.scr = [self.scrp[:, i * 1024:(i + 1) * 1024] for i in range(4)]
        self.reset_cols()

        self.setup()
        if 0 in self.blocks:
            self.l0_attn()
        if 1 in self.blocks:
            self.l0_ffn()
        if 2 in self.blocks:
            self.l1_attn()
        if 3 in self.blocks:
            self.l1_ffn()
        self.finish()
        self.P.run_queues()
        nc.finalize()
        return self

    def setup(self):
        P = self.P
        for i in range(NT):
            P.op("sync", lambda e, i=i: e.dma_start(
                out=self.res[:, i * D:(i + 1) * D],
                in_=self.x_in[i * 128:(i + 1) * 128, :]), inc=("xl", 16))
        self.ev_res_loaded = P.val("xl")

        ident_h = self.din("ident_h", [128, 128])
        P.op("gpsimd", lambda e: e.dma_start(out=self.identb[:, :], in_=ident_h),
             inc=("wl", 16))
        P.op("gpsimd", lambda e: e.dma_start(out=self.identf[:, :], in_=ident_h),
             inc=("wl", 16))
        P.op("gpsimd", lambda e: e.memset(self.ones[:, :], 1.0), inc=("ge", 1))
        P.op("gpsimd", lambda e: e.memset(self.onesbf[:, :], 1.0), inc=("ge", 1))
        P.op("gpsimd", lambda e: e.memset(self.epscol[:, :], 1e-5), inc=("ge", 1))
        tgb = self.din("tg_base", [128, 1024], percore=True)
        P.op("sync", lambda e: e.dma_start(out=self.tg[:, :], in_=tgb), inc=("xl", 16))
        self.ev_setup_xl = P.val("xl")
        self.ev_setup_wl = P.val("wl")
        self.ev_setup_ge = P.val("ge")

    def emit_ln(self, tag):
        P = self.P
        res, bigA, xlnT, ptr = self.res, self.bigA, self.xlnT, self.ptr
        sq_scr = xlnT[:, 0:4096].bitcast(F32)
        ss, sx = self.col("ln_ss"), self.col("ln_sx")
        mcol, e2col = self.col("ln_m"), self.col("ln_e2")
        vcol, sdcol = self.col("ln_v"), self.col("ln_sd")
        rcol, nmcol = self.col("ln_r"), self.col("ln_nm")
        xln = bigA
        start_v = P.val("ve")
        start_s = P.val("se")
        start_t = P.val("te")
        for i in range(NT):
            rsl = res[:, i * D:(i + 1) * D]
            w0 = [("xl", self.ev_res_loaded), ("ge", self.ev_setup_ge)]
            if i == 0:
                w0 = w0 + [("ve", start_v), ("se", start_s), ("te", start_t)]
            se = P.op("scalar", lambda e, rsl=rsl: e.activation(
                sq_scr, rsl, AF.Square, accum_out=ss), waits=w0, inc=("se", 1))
            v1 = P.op("vector", lambda e, rsl=rsl: e.reduce_sum(sx, rsl, axis=AX.X),
                      waits=w0, inc=("ve", 1))
            v2 = P.op("vector", lambda e: e.tensor_scalar_mul(mcol, sx, 1.0 / D),
                      waits=[("ve", v1)], inc=("ve", 1))
            P.op("vector", lambda e: e.tensor_scalar_mul(e2col, ss, 1.0 / D),
                 waits=[("se", se), ("ve", v2)], inc=("ve", 1))
            v4 = P.op("vector", lambda e: e.tensor_tensor(vcol, mcol, mcol, op=ALU.mult),
                      inc=("ve", 1))
            v5 = P.op("vector", lambda e: e.scalar_tensor_tensor(
                vcol, vcol, -1.0, e2col, op0=ALU.mult, op1=ALU.add),
                waits=[("ve", v4)], inc=("ve", 1))
            s2 = P.op("scalar", lambda e: e.activation(
                sdcol, vcol, AF.Sqrt, bias=self.epscol[:, :]),
                waits=[("ve", v5)], inc=("se", 1))
            v6 = P.op("vector", lambda e: e.reciprocal(rcol, sdcol),
                      waits=[("se", s2)], inc=("ve", 1))
            v7 = P.op("vector", lambda e: e.scalar_tensor_tensor(
                nmcol, mcol, -1.0, rcol, op0=ALU.mult, op1=ALU.mult),
                waits=[("ve", v6)], inc=("ve", 1))
            P.op("vector", lambda e, i=i, rsl=rsl: e.tensor_scalar(
                xln[:, i * D:(i + 1) * D], rsl, rcol, nmcol,
                op0=ALU.mult, op1=ALU.add),
                waits=[("ve", v7)], inc=("ve", 1))
        ev_xln = P.val("ve")

        last_copy = {0: 0, 1: 0}
        for dt in range(ND):
            bank = dt % 2
            tw = [("ve", ev_xln), ("wl", self.ev_setup_wl)]
            if last_copy[bank]:
                tw = tw + [("ve", last_copy[bank])]
            te = None
            for i in range(NT):
                te = P.op("tensor", lambda e, dt=dt, i=i, bank=bank: e.transpose(
                    ptr[bank][:, i * 128:(i + 1) * 128],
                    xln[:, i * D + dt * 128:i * D + (dt + 1) * 128],
                    self.identb[:, :]),
                    waits=tw if i == 0 else (),
                    inc=("te", 1) if i == NT - 1 else None)
            last_copy[bank] = P.op("vector", lambda e, dt=dt, bank=bank: e.tensor_copy(
                xlnT[:, dt * TLOC:(dt + 1) * TLOC], ptr[bank][:, :]),
                waits=[("te", te)], inc=("ve", 1))
        return ev_xln, P.val("ve")

    def l0_attn(self):
        P, nc = self.P, self.nc
        res, xlnT, bigA, bigB = self.res, self.xlnT, self.bigA, self.bigB
        scr, tg, pb = self.scr, self.tg, self.pb
        self.reset_cols()

        wr_h = self.din("l0a_wr", [16, 128, D])
        wk_h = self.din("l0a_wk", [16, 128, D])
        wv_h = self.din("l0a_wv", [16, 128, D])
        wo_h = self.din("l0a_wo", [16, 128, D])
        dec_h = self.din("l0a_decay", [128, ND])
        gnw_h = self.din("l0a_gnw", [128, ND])
        gnb_h = self.din("l0a_gnb", [128, ND])
        cmask_h = self.din("l0a_cmask", [128, 16], percore=True)

        cs_in = nc.dram_tensor("l0a_cs_in", [16, 128], F32, kind="Internal")
        cs_out = nc.dram_tensor("l0a_cs_out", [128, 128], F32, kind="Internal")
        gs_in = nc.dram_tensor("l0a_gs_in", [1, 2], F32, kind="Internal")
        gs_out = nc.dram_tensor("l0a_gs_out", [1, 2], F32, kind="Internal")

        dec_cm = self.col("dec_cm", ND)
        logd = self.col("logd", ND)
        gnw_cm = self.col("gnw_cm", ND)
        gnb_cm = self.col("gnb_cm", ND)
        stab = self.col("stab", 16)
        carry = self.col("carry", 16)
        carry2 = self.col("carry2", 1)
        sxt = self.col("sxt", ND)
        sqt = self.col("sqt", ND)
        Acol = self.col("Acl", ND)
        Bcol = self.col("Bcl", ND)
        cmask_sb = self.col("cmask", 16)

        P.op("sync", lambda e: e.dma_start(out=dec_cm, in_=dec_h), inc=("xl", 16))
        P.op("sync", lambda e: e.dma_start(out=gnw_cm, in_=gnw_h), inc=("xl", 16))
        P.op("sync", lambda e: e.dma_start(out=gnb_cm, in_=gnb_h), inc=("xl", 16))
        P.op("sync", lambda e: e.dma_start(out=cmask_sb, in_=cmask_h), inc=("xl", 16))
        ev_vecs = P.val("xl")

        se = P.op("scalar", lambda e: e.activation(logd, dec_cm, AF.Sigmoid),
                  waits=[("xl", ev_vecs)], inc=("se", 1))
        ve = P.op("vector", lambda e: e.tensor_scalar_max(logd, logd, 1e-7),
                  waits=[("se", se)], inc=("ve", 1))
        ev_logd_se = P.op("scalar", lambda e: e.activation(logd, logd, AF.Ln),
                          waits=[("ve", ve)], inc=("se", 1))

        ev_xln, ev_xlnT = self.emit_ln("l0a")

        wslot_ready = [0, 0]
        wslot_free_te = [0, 0]
        prev_epi_v = 0
        prev_epi_s = 0
        prev_sexp = 0
        for ct in range(ND):
            slot = ct % 2
            w = self.wbuf[slot]
            wts = [("te", wslot_free_te[slot])] if wslot_free_te[slot] else []
            P.op("gpsimd", lambda e, ct=ct, w=w: e.dma_start(
                out=w[:, 0:2048], in_=wk_h[ct]), waits=wts, inc=("wl", 16))
            P.op("gpsimd", lambda e, ct=ct, w=w: e.dma_start(
                out=w[:, 2048:4096], in_=wv_h[ct]), inc=("wl", 16))
            P.op("gpsimd", lambda e, ct=ct, w=w: e.dma_start(
                out=w[:, 4096:6144], in_=wr_h[ct]), inc=("wl", 16))
            wslot_ready[slot] = P.val("wl")

            te_mat = {}
            first = True
            for m in range(3):
                for h in range(2):
                    for dt in range(ND):
                        wts = []
                        if first:
                            wts = [("wl", wslot_ready[slot]), ("ve", ev_xlnT)]
                            if prev_epi_v:
                                wts += [("ve", prev_epi_v), ("se", prev_epi_s)]
                            first = False
                        te_mat[(m, h)] = P.op("tensor", lambda e, m=m, h=h, dt=dt, w=w: e.matmul(
                            pb[m * 2 + h][:, :],
                            w[:, m * 2048 + dt * 128:m * 2048 + (dt + 1) * 128],
                            xlnT[:, dt * TLOC + h * 512:dt * TLOC + (h + 1) * 512],
                            start=(dt == 0), stop=(dt == ND - 1)),
                            waits=wts,
                            inc=("te", 1) if dt == ND - 1 else None)
            wslot_free_te[slot] = te_mat[(2, 1)]

            vk = None
            for h in range(2):
                vk = P.op("vector", lambda e, h=h: e.tensor_copy(
                    scr[1][:, h * 512:(h + 1) * 512], pb[h][:, :]),
                    waits=[("te", te_mat[(0, h)])], inc=("ve", 1))
            va = P.op("vector", lambda e, ct=ct: e.tensor_scalar_mul(
                scr[2][:, :], tg[:, :], logd[:, ct:ct + 1]),
                waits=[("se", ev_logd_se), ("xl", self.ev_setup_xl)] +
                      ([("se", prev_sexp)] if prev_sexp else []),
                inc=("ve", 1))
            vkv = None
            for h in range(2):
                vkv = P.op("vector", lambda e, h=h: e.tensor_tensor(
                    scr[1][:, h * 512:(h + 1) * 512], pb[2 + h][:, :],
                    scr[1][:, h * 512:(h + 1) * 512], op=ALU.mult),
                    waits=[("te", te_mat[(1, h)]), ("ve", vk)], inc=("ve", 1))
            sr = None
            for h in range(2):
                sr = P.op("scalar", lambda e, ct=ct, h=h: e.activation(
                    bigB[:, ct * TLOC + h * 512:ct * TLOC + (h + 1) * 512],
                    pb[4 + h][:, :], AF.Sigmoid),
                    waits=[("te", te_mat[(2, h)])], inc=("se", 1))
            sexp = P.op("scalar", lambda e: e.activation(scr[3][:, :], scr[2][:, :], AF.Exp),
                        waits=[("ve", va)], inc=("se", 1))
            vc = P.op("vector", lambda e: e.tensor_scalar_max(
                scr[2][:, :], scr[3][:, :], 1e-10),
                waits=[("se", sexp)], inc=("ve", 1))
            vi = P.op("vector", lambda e: e.reciprocal(scr[3][:, :], scr[2][:, :]),
                      waits=[("ve", vc)], inc=("ve", 1))
            vs = P.op("vector", lambda e, ct=ct: e.tensor_tensor(
                bigA[:, ct * TLOC:(ct + 1) * TLOC], scr[1][:, :], scr[3][:, :],
                op=ALU.mult),
                waits=[("ve", vi), ("ve", vkv)], inc=("ve", 1))
            P.op("vector", lambda e, ct=ct: e.reduce_sum(
                stab[:, ct:ct + 1], bigA[:, ct * TLOC:(ct + 1) * TLOC], axis=AX.X),
                waits=[("ve", vs)], inc=("ve", 1))
            prev_epi_v, prev_epi_s, prev_sexp = vkv, sr, sexp
        ev_phaseA_v = P.val("ve")
        ev_phaseA_s = P.val("se")
        last_proj_te = P.val("te")

        teS = P.op("tensor", lambda e: e.transpose(
            pb[0][0:16, 0:128], stab, self.identf[:, :]),
            waits=[("ve", ev_phaseA_v)], inc=("te", 1))
        vS = P.op("vector", lambda e: e.tensor_copy(scr[2][0:16, 0:128],
                                                    pb[0][0:16, 0:128]),
                  waits=[("te", teS)], inc=("ve", 1))
        dS = P.op("sync", lambda e: e.dma_start(out=cs_in.ap(), in_=scr[2][0:16, 0:128]),
                  waits=[("ve", vS)], inc=("st", 16))
        cc = P.op("gpsimd", lambda e: e.collective_compute(
            "AllGather", ALU.bypass, replica_groups=[list(range(NCORES))],
            ins=[cs_in.ap().opt()], outs=[cs_out.ap().opt()]),
            waits=[("st", dS)], inc=("cc", 1))
        d2 = P.op("sync", lambda e: e.dma_start(out=scr[1][:, 0:128], in_=cs_out.ap()),
                  waits=[("cc", cc)], inc=("xl", 16))
        tc = P.op("tensor", lambda e: e.matmul(
            pb[1][:, 0:16], scr[1][:, 0:128], cmask_sb, start=True, stop=True),
            waits=[("xl", d2)], inc=("te", 1))
        ev_carry = P.op("vector", lambda e: e.tensor_copy(carry, pb[1][:, 0:16]),
                        waits=[("te", tc)], inc=("ve", 1))

        prev_sexp = 0
        for ct in range(ND):
            va = P.op("vector", lambda e, ct=ct: e.tensor_scalar_mul(
                scr[2][:, :], tg[:, :], logd[:, ct:ct + 1]),
                waits=([("ve", ev_carry)] if ct == 0 else []) +
                      ([("se", prev_sexp)] if prev_sexp else []),
                inc=("ve", 1))
            sexp = P.op("scalar", lambda e: e.activation(scr[3][:, :], scr[2][:, :], AF.Exp),
                        waits=[("ve", va), ("se", ev_phaseA_s)], inc=("se", 1))
            vs0 = P.op("vector", lambda e, ct=ct: e.tensor_tensor_scan(
                scr[0][:, 0:512], self.ones[:, :],
                bigA[:, ct * TLOC:ct * TLOC + 512],
                carry[:, ct:ct + 1], op0=ALU.mult, op1=ALU.add),
                waits=[("ve", ev_carry)] if ct == 0 else (), inc=("ve", 1))
            vcp = P.op("vector", lambda e: e.tensor_copy(carry2, scr[0][:, 511:512]),
                       waits=[("ve", vs0)], inc=("ve", 1))
            vs1 = P.op("vector", lambda e, ct=ct: e.tensor_tensor_scan(
                scr[0][:, 512:1024], self.ones[:, :],
                bigA[:, ct * TLOC + 512:(ct + 1) * TLOC],
                carry2, op0=ALU.mult, op1=ALU.add),
                waits=[("ve", vcp)], inc=("ve", 1))
            vst = P.op("vector", lambda e: e.tensor_tensor(
                scr[1][:, :], scr[0][:, :], scr[3][:, :], op=ALU.mult),
                waits=[("se", sexp), ("ve", vs1)], inc=("ve", 1))
            vo = P.op("vector", lambda e, ct=ct: e.tensor_tensor(
                bigA[:, ct * TLOC:(ct + 1) * TLOC], scr[1][:, :],
                bigB[:, ct * TLOC:(ct + 1) * TLOC], op=ALU.mult),
                waits=[("ve", vst)], inc=("ve", 1))
            P.op("scalar", lambda e, ct=ct: e.activation(
                scr[1][:, :], bigA[:, ct * TLOC:(ct + 1) * TLOC], AF.Square,
                accum_out=sqt[:, ct:ct + 1]),
                waits=[("ve", vo)], inc=("se", 1))
            P.op("vector", lambda e, ct=ct: e.reduce_sum(
                sxt[:, ct:ct + 1], bigA[:, ct * TLOC:(ct + 1) * TLOC], axis=AX.X),
                waits=[("ve", vo)], inc=("ve", 1))
            prev_sexp = sexp
        ev_phaseB_v = P.val("ve")
        ev_phaseB_s = P.val("se")

        P.op("tensor", lambda e: e.matmul(
            pb[2][0:1, 0:ND], self.ones[:, 0:1], sxt, start=True, stop=True),
            waits=[("ve", ev_phaseB_v), ("se", ev_phaseB_s),
                   ("ge", self.ev_setup_ge)])
        tS = P.op("tensor", lambda e: e.matmul(
            pb[2][0:1, ND:2 * ND], self.ones[:, 0:1], sqt, start=True, stop=True),
            inc=("te", 1))
        vg1 = P.op("vector", lambda e: e.reduce_sum(
            scr[2][0:1, 0:1], pb[2][0:1, 0:ND], axis=AX.X),
            waits=[("te", tS)], inc=("ve", 1))
        vg2 = P.op("vector", lambda e: e.reduce_sum(
            scr[2][0:1, 1:2], pb[2][0:1, ND:2 * ND], axis=AX.X),
            waits=[("ve", vg1)], inc=("ve", 1))
        dG = P.op("sync", lambda e: e.dma_start(out=gs_in.ap(), in_=scr[2][0:1, 0:2]),
                  waits=[("ve", vg2)], inc=("st", 16))
        ccG = P.op("gpsimd", lambda e: e.collective_compute(
            "AllReduce", ALU.add, replica_groups=[list(range(NCORES))],
            ins=[gs_in.ap().opt()], outs=[gs_out.ap().opt()]),
            waits=[("st", dG)], inc=("cc", 1))
        dG2 = P.op("sync", lambda e: e.dma_start(out=scr[2][0:1, 4:6], in_=gs_out.ap()),
                   waits=[("cc", ccG)], inc=("xl", 16))
        NELT = float(T * D)
        vm = P.op("vector", lambda e: e.tensor_scalar_mul(
            scr[2][0:1, 6:7], scr[2][0:1, 4:5], 1.0 / NELT),
            waits=[("xl", dG2)], inc=("ve", 1))
        ve2 = P.op("vector", lambda e: e.tensor_scalar_mul(
            scr[2][0:1, 7:8], scr[2][0:1, 5:6], 1.0 / NELT),
            waits=[("ve", vm)], inc=("ve", 1))
        vv = P.op("vector", lambda e: e.tensor_tensor(
            scr[2][0:1, 8:9], scr[2][0:1, 6:7], scr[2][0:1, 6:7], op=ALU.mult),
            waits=[("ve", ve2)], inc=("ve", 1))
        vv2 = P.op("vector", lambda e: e.scalar_tensor_tensor(
            scr[2][0:1, 8:9], scr[2][0:1, 8:9], -1.0, scr[2][0:1, 7:8],
            op0=ALU.mult, op1=ALU.add),
            waits=[("ve", vv)], inc=("ve", 1))
        sG = P.op("scalar", lambda e: e.activation(
            scr[2][0:1, 9:10], scr[2][0:1, 8:9], AF.Sqrt, bias=self.epscol[0:1, :]),
            waits=[("ve", vv2)], inc=("se", 1))
        vr = P.op("vector", lambda e: e.reciprocal(scr[2][0:1, 10:11], scr[2][0:1, 9:10]),
                  waits=[("se", sG)], inc=("ve", 1))
        vnm = P.op("vector", lambda e: e.tensor_tensor(
            scr[2][0:1, 11:12], scr[2][0:1, 6:7], scr[2][0:1, 10:11], op=ALU.mult),
            waits=[("ve", vr)], inc=("ve", 1))
        vnm2 = P.op("vector", lambda e: e.tensor_scalar_mul(
            scr[2][0:1, 11:12], scr[2][0:1, 11:12], -1.0),
            waits=[("ve", vnm)], inc=("ve", 1))
        tb = P.op("tensor", lambda e: e.matmul(
            pb[3][:, 0:2], self.ones[0:1, 0:128], scr[2][0:1, 10:12],
            start=True, stop=True),
            waits=[("ve", vnm2)], inc=("te", 1))
        vb = P.op("vector", lambda e: e.tensor_copy(scr[2][:, 16:18], pb[3][:, 0:2]),
                  waits=[("te", tb)], inc=("ve", 1))
        rsd_col = scr[2][:, 16:17]
        nmr_col = scr[2][:, 17:18]
        vA = P.op("vector", lambda e: e.tensor_scalar_mul(Acol, gnw_cm, rsd_col),
                  waits=[("ve", vb)], inc=("ve", 1))
        vB = P.op("vector", lambda e: e.tensor_scalar_mul(Bcol, gnw_cm, nmr_col),
                  waits=[("ve", vA)], inc=("ve", 1))
        ev_AB = P.op("vector", lambda e: e.tensor_tensor(Bcol, Bcol, gnb_cm, op=ALU.add),
                     waits=[("ve", vB)], inc=("ve", 1))

        normed = xlnT
        for ct in range(ND):
            P.op("vector", lambda e, ct=ct: e.tensor_scalar(
                normed[:, ct * TLOC:(ct + 1) * TLOC],
                bigA[:, ct * TLOC:(ct + 1) * TLOC],
                Acol[:, ct:ct + 1], Bcol[:, ct:ct + 1],
                op0=ALU.mult, op1=ALU.add),
                waits=[("ve", ev_AB)] if ct == 0 else (), inc=("ve", 1))
        ev_normed = P.val("ve")

        def wo_view(ct):
            if ct < 8:
                return bigB[:, ct * 2048:(ct + 1) * 2048]
            if ct < 11:
                return self.wbuf[0][:, (ct - 8) * 2048:(ct - 7) * 2048]
            if ct < 14:
                return self.wbuf[1][:, (ct - 11) * 2048:(ct - 10) * 2048]
            return self.scrp[:, (ct - 14) * 1024:(ct - 13) * 1024].bitcast(BF16)

        for ct in range(ND):
            wts = []
            if ct == 0:
                wts = [("ve", ev_phaseB_v), ("se", ev_phaseB_s),
                       ("te", last_proj_te)]
            P.op("gpsimd", lambda e, ct=ct: e.dma_start(out=wo_view(ct), in_=wo_h[ct]),
                 waits=wts, inc=("wl", 16))
        ev_wo = P.val("wl")

        prev_evac = 0
        for i in range(NT):
            te_last = None
            for ct in range(ND):
                for q in range(4):
                    wts = []
                    if ct == 0 and q == 0:
                        wts = [("wl", ev_wo), ("ve", ev_normed)]
                        if prev_evac:
                            wts.append(("ve", prev_evac))
                    te_last = P.op("tensor", lambda e, i=i, ct=ct, q=q: e.matmul(
                        pb[q][:, :],
                        normed[:, ct * TLOC + i * 128:ct * TLOC + (i + 1) * 128],
                        wo_view(ct)[:, q * 512:(q + 1) * 512],
                        start=(ct == 0), stop=(ct == ND - 1)),
                        waits=wts,
                        inc=("te", 1) if (ct == ND - 1 and q == 3) else None)
            for q in range(4):
                prev_evac = P.op("vector", lambda e, i=i, q=q: e.tensor_tensor(
                    res[:, i * D + q * 512:i * D + (q + 1) * 512],
                    pb[q][:, :],
                    res[:, i * D + q * 512:i * D + (q + 1) * 512], op=ALU.add),
                    waits=[("te", te_last)] if q == 0 else (), inc=("ve", 1))

    def l0_ffn(self):
        P, nc = self.P, self.nc
        res, xlnT, bigA, bigB = self.res, self.xlnT, self.bigA, self.bigB
        scr, pb, ptr = self.scr, self.pb, self.ptr
        self.reset_cols()
        HSC = 0.08838834764831845

        wg_h = self.din("l0f_wg", [128, 256])
        wq_h = self.din("l0f_wqkv", [48, 128, D])
        qidx_h = self.din("l0f_qidx", [128, 16], percore=True, dtype=I32)
        yidx_h = self.din("l0f_yidx", [128, 64], percore=True, dtype=I32)
        lidx_h = self.din("l0f_lidx", [128, 1], percore=True, dtype=I32)

        gsel = nc.dram_tensor("l0f_gsel", [32, TLOC], F32, kind="Internal")
        kv_in = nc.dram_tensor("l0f_kv_in", [256, TLOC], BF16, kind="Internal")
        kv_out = nc.dram_tensor("l0f_kv_out", [2048, TLOC], BF16, kind="Internal")
        qag_in = nc.dram_tensor("l0f_qag_in", [2048, TLOC], BF16, kind="Internal")
        qag_out = nc.dram_tensor("l0f_qag_out", [16384, TLOC], BF16, kind="Internal")
        yag_in = nc.dram_tensor("l0f_yag_in", [T, 256], BF16, kind="Internal")
        yag_out = nc.dram_tensor("l0f_yag_out", [T * NCORES, 256], BF16, kind="Internal")
        lag_in = nc.dram_tensor("l0f_lag_in", [128, 128], F32, kind="Internal")
        lag_out = nc.dram_tensor("l0f_lag_out", [1024, 128], F32, kind="Internal")

        wg_sb = self.col("wg_sb", 128).bitcast(BF16)
        qidx = self.col("qidx", 16).bitcast(I32)
        yidx = self.col("yidx", 64).bitcast(I32)
        lidx = self.col("lidx", 1).bitcast(I32)
        negm = self.col("negm")
        ssum = self.col("ssum")
        topc = self.col("topc")

        ent = [("ve", P.val("ve")), ("se", P.val("se")), ("te", P.val("te"))]
        P.op("sync", lambda e: e.dma_start(out=qidx, in_=qidx_h), waits=ent,
             inc=("xl", 16))
        P.op("sync", lambda e: e.dma_start(out=yidx, in_=yidx_h), waits=ent,
             inc=("xl", 16))
        P.op("sync", lambda e: e.dma_start(out=lidx, in_=lidx_h), waits=ent,
             inc=("xl", 16))
        P.op("gpsimd", lambda e: e.dma_start(out=wg_sb, in_=wg_h), waits=ent,
             inc=("wl", 16))
        ev_wg = P.val("wl")

        ev_xln, ev_xlnT = self.emit_ln("l0f")

        ind_scr = scr[2][:, 0:16]
        gt1_scr = scr[2][:, 16:32]
        prev_tr_copy = 0
        exp_ev = 0
        for i in range(NT):
            te_g = None
            for dt in range(ND):
                wts = []
                if dt == 0:
                    wts = [("ve", ev_xlnT), ("wl", ev_wg)]
                    if exp_ev:
                        wts.append(("se", exp_ev))
                te_g = P.op("tensor", lambda e, i=i, dt=dt: e.matmul(
                    pb[4][:, 0:16],
                    xlnT[:, dt * TLOC + i * 128:dt * TLOC + (i + 1) * 128],
                    wg_sb[:, dt * 16:(dt + 1) * 16],
                    start=(dt == 0), stop=(dt == ND - 1)),
                    waits=wts, inc=("te", 1) if dt == ND - 1 else None)
            vng = P.op("vector", lambda e: e.reduce_max(
                negm, pb[4][:, 0:16], axis=AX.X, negate=True),
                waits=[("te", te_g)], inc=("ve", 1))
            vind = P.op("vector", lambda e: e.tensor_scalar(
                ind_scr, pb[4][:, 0:16], negm, 0.0, op0=ALU.add, op1=ALU.is_ge),
                waits=[("ve", vng)], inc=("ve", 1))
            exp_ev = P.op("scalar", lambda e: e.activation(
                scr[3][:, 0:16], pb[4][:, 0:16], AF.Exp, bias=negm),
                waits=[("ve", vind)], inc=("se", 1))
            vsum = P.op("vector", lambda e: e.reduce_sum(
                ssum, scr[3][:, 0:16], axis=AX.X),
                waits=[("se", exp_ev)], inc=("ve", 1))
            vtop = P.op("vector", lambda e: e.reciprocal(topc, ssum),
                        waits=[("ve", vsum)], inc=("ve", 1))
            vgt1 = P.op("vector", lambda e: e.tensor_scalar_mul(
                gt1_scr, ind_scr, topc),
                waits=[("ve", vtop)], inc=("ve", 1))
            wts = [("ve", vgt1)]
            if prev_tr_copy:
                wts.append(("ve", prev_tr_copy))
            te_t = P.op("tensor", lambda e: e.transpose(
                pb[5][0:32, 0:128], scr[2][:, 0:32], self.identf[:, :]),
                waits=wts, inc=("te", 1))
            prev_tr_copy = P.op("vector", lambda e, i=i: e.tensor_copy(
                scr[1][0:32, i * 128:(i + 1) * 128], pb[5][0:32, 0:128]),
                waits=[("te", te_t)], inc=("ve", 1))
        dsel = P.op("sync", lambda e: e.dma_start(out=gsel.ap(), in_=scr[1][0:32, :]),
                    waits=[("ve", prev_tr_copy)], inc=("st", 16))

        kacc, vacc = scr[0], scr[1]
        wslot_ready = [0, 0]
        wslot_free = [0, 0]
        prev_bc_consumed = 0
        prev_evac_s = 0
        prev_evac_v = 0
        last_q_evac = 0
        for et in range(48):
            slot = et % 2
            w = self.wbuf[slot][:, 0:2048]
            wts = [("te", wslot_free[slot])] if wslot_free[slot] else []
            if et == 0:
                wts.append(("st", dsel))
            P.op("gpsimd", lambda e, et=et, w=w: e.dma_start(out=w, in_=wq_h[et]),
                 waits=wts, inc=("wl", 16))
            wslot_ready[slot] = P.val("wl")
            role = et // 16
            hh = et % 16
            ev_bc = 0
            if role > 0:
                grow = hh if role == 1 else 16 + hh
                wts = [("st", dsel)]
                if prev_bc_consumed:
                    wts.append(("ve", prev_bc_consumed))
                ev_bc = P.op("gpsimd", lambda e, grow=grow: e.dma_start(
                    out=scr[3][:, :],
                    in_=gsel.ap()[grow:grow + 1, :].to_broadcast([128, TLOC])),
                    waits=wts, inc=("wl", 16))
            te_m = {}
            for th in range(2):
                for dt in range(ND):
                    wts = []
                    if dt == 0 and th == 0:
                        wts = [("wl", wslot_ready[slot]), ("ve", ev_xlnT)]
                        if prev_evac_s:
                            wts.append(("se", prev_evac_s))
                        if prev_evac_v:
                            wts.append(("ve", prev_evac_v))
                    te_m[th] = P.op("tensor", lambda e, th=th, dt=dt, w=w, slot=slot: e.matmul(
                        pb[slot * 2 + th][:, :],
                        w[:, dt * 128:(dt + 1) * 128],
                        xlnT[:, dt * TLOC + th * 512:dt * TLOC + (th + 1) * 512],
                        start=(dt == 0), stop=(dt == ND - 1)),
                        waits=wts, inc=("te", 1) if dt == ND - 1 else None)
            wslot_free[slot] = te_m[1]
            if role == 0:
                ev = None
                for th in range(2):
                    ev = P.op("scalar", lambda e, hh=hh, th=th, slot=slot: e.activation(
                        bigA[:, hh * TLOC + th * 512:hh * TLOC + (th + 1) * 512],
                        pb[slot * 2 + th][:, :], AF.Copy, scale=HSC),
                        waits=[("te", te_m[th])], inc=("se", 1))
                prev_evac_s = ev
                last_q_evac = ev
                prev_evac_v = 0
            else:
                acc = kacc if role == 1 else vacc
                ev = None
                for th in range(2):
                    sl = slice(th * 512, (th + 1) * 512)
                    if hh == 0:
                        ev = P.op("vector", lambda e, th=th, slot=slot, acc=acc, sl=sl: e.tensor_tensor(
                            acc[:, sl], pb[slot * 2 + th][:, :], scr[3][:, sl],
                            op=ALU.mult),
                            waits=[("te", te_m[th]), ("wl", ev_bc)], inc=("ve", 1))
                    else:
                        ev = P.op("vector", lambda e, th=th, slot=slot, sl=sl: e.tensor_tensor(
                            scr[2][:, sl], pb[slot * 2 + th][:, :], scr[3][:, sl],
                            op=ALU.mult),
                            waits=[("te", te_m[th]), ("wl", ev_bc)], inc=("ve", 1))
                        ev = P.op("vector", lambda e, acc=acc, sl=sl: e.tensor_tensor(
                            acc[:, sl], acc[:, sl], scr[2][:, sl], op=ALU.add),
                            waits=[("ve", ev)], inc=("ve", 1))
                prev_bc_consumed = ev
                prev_evac_v = ev
                prev_evac_s = 0
        ev_sel_v = P.val("ve")

        ksh_bf = self.wbuf[0][:, 2048:3072]
        vsh_bf = self.wbuf[0][:, 3072:4096]
        vtok_bf = self.wbuf[0][:, 4096:5120]
        sk = P.op("scalar", lambda e: e.activation(ksh_bf, kacc[:, :], AF.Copy),
                  waits=[("ve", ev_sel_v)], inc=("se", 1))
        sv = P.op("scalar", lambda e: e.activation(vsh_bf, vacc[:, :], AF.Copy),
                  waits=[("ve", ev_sel_v)], inc=("se", 1))
        te_v = None
        for j in range(NT):
            te_v = P.op("tensor", lambda e, j=j: e.transpose(
                ptr[0][:, j * 128:(j + 1) * 128],
                vsh_bf[:, j * 128:(j + 1) * 128], self.identb[:, :]),
                waits=[("se", sv)] if j == 0 else (),
                inc=("te", 1) if j == NT - 1 else None)
        vvt = P.op("vector", lambda e: e.tensor_copy(vtok_bf, ptr[0][:, :]),
                   waits=[("te", te_v)], inc=("ve", 1))
        d_k = P.op("sync", lambda e: e.dma_start(out=kv_in.ap()[0:128, :], in_=ksh_bf),
                   waits=[("se", sk)], inc=("st", 16))
        d_v = P.op("sync", lambda e: e.dma_start(out=kv_in.ap()[128:256, :], in_=vtok_bf),
                   waits=[("ve", vvt)], inc=("st", 16))
        dq = None
        for hh in range(16):
            dq = P.op("sync", lambda e, hh=hh: e.dma_start(
                out=qag_in.ap()[hh * 128:(hh + 1) * 128, :],
                in_=bigA[:, hh * TLOC:(hh + 1) * TLOC]),
                waits=[("se", last_q_evac)] if hh == 0 else (), inc=("st", 16))
        cc1 = P.op("gpsimd", lambda e: e.collective_compute(
            "AllGather", ALU.bypass, replica_groups=[list(range(NCORES))],
            ins=[kv_in.ap().opt()], outs=[kv_out.ap().opt()]),
            waits=[("st", d_v), ("st", d_k)], inc=("cc", 1))
        cc2 = P.op("gpsimd", lambda e: e.collective_compute(
            "AllGather", ALU.bypass, replica_groups=[list(range(NCORES))],
            ins=[qag_in.ap().opt()], outs=[qag_out.ap().opt()]),
            waits=[("st", dq)], inc=("cc", 1))

        dkv = None
        for g in range(NQC):
            r, half = g // 2, g % 2
            dkv = P.op("sync", lambda e, g=g, r=r, half=half: e.dma_start(
                out=bigB[:, g * 512:(g + 1) * 512],
                in_=kv_out.ap()[256 * r:256 * r + 128, half * 512:(half + 1) * 512]),
                waits=[("cc", cc1)] if g == 0 else (), inc=("xl", 16))
            dkv = P.op("sync", lambda e, g=g, r=r, half=half: e.dma_start(
                out=bigB[:, 8192 + g * 512:8192 + (g + 1) * 512],
                in_=kv_out.ap()[256 * r + 128:256 * r + 256,
                                half * 512:(half + 1) * 512]),
                inc=("xl", 16))
        ev_kv_own = P.val("xl")
        for lh in range(2):
            for r in range(NCORES):
                P.op("gpsimd", lambda e, lh=lh, r=r: e.indirect_dma_start(
                    out=bigA[:, lh * 8192 + r * TLOC:lh * 8192 + (r + 1) * TLOC],
                    out_offset=None,
                    in_=qag_out.ap()[0:128, :],
                    in_offset=IndirectOffsetOnAxis(qidx[:, lh * 8 + r:lh * 8 + r + 1], 0)),
                    waits=[("cc", cc2)] if (lh == 0 and r == 0) else (),
                    inc=("wl", 16))
        ev_q_own = P.val("wl")
        if DEBUG_L0F:
            dbg_gsel = nc.dram_tensor("dbg_gsel", [32, 1024], F32, kind="ExternalOutput").ap()
            P.op("sync", lambda e: e.dma_start(out=dbg_gsel, in_=gsel.ap()),
                 waits=[("st", dsel)], inc=("st", 16))
            dbg_q = nc.dram_tensor("dbg_q", [128, 16384], BF16, kind="ExternalOutput").ap()
            dbg_kv = nc.dram_tensor("dbg_kv", [128, 16384], BF16, kind="ExternalOutput").ap()
            dbg_ksel = nc.dram_tensor("dbg_ksel", [128, 1024], F32, kind="ExternalOutput").ap()
            dbg_vsel = nc.dram_tensor("dbg_vsel", [128, 1024], F32, kind="ExternalOutput").ap()
            P.op("sync", lambda e: e.dma_start(out=dbg_q, in_=bigA[:, :]),
                 waits=[("wl", ev_q_own)], inc=("st", 16))
            P.op("sync", lambda e: e.dma_start(out=dbg_kv, in_=bigB[:, :]),
                 waits=[("xl", ev_kv_own)], inc=("st", 16))
            P.op("sync", lambda e: e.dma_start(out=dbg_ksel, in_=kacc[:, :]),
                 waits=[("ve", ev_sel_v)], inc=("st", 16))
            P.op("sync", lambda e: e.dma_start(out=dbg_vsel, in_=vacc[:, :]),
                 waits=[("ve", ev_sel_v)], inc=("st", 16))

        ebuf = scr[0].bitcast(BF16)
        lrow = scr[1][0:1, 0:512]
        lrow_i = scr[1][0:1, 512:1024]
        prev_oevac_s = 0
        prev_levac_v = 0
        dl = 0
        for lh in range(2):
            for qg in range(NQC):
                first_mm = True
                nsub = 4 * (qg + 1)
                sub = 0
                ev_exp = {}
                ev_msk = {}
                for kg in range(qg + 1):
                    for j in range(4):
                        ping = sub % 2
                        wts = []
                        if first_mm:
                            wts = [("wl", ev_q_own), ("xl", ev_kv_own)]
                            if prev_oevac_s:
                                wts += [("se", prev_oevac_s), ("ve", prev_levac_v)]
                            first_mm = False
                        elif sub >= 2:
                            wts = [("se", ev_exp[sub - 2])]
                            if (sub - 2) in ev_msk:
                                wts.append(("ge", ev_msk[sub - 2]))
                        P.op("tensor", lambda e, ping=ping, kg=kg, j=j, lh=lh, qg=qg: e.matmul(
                            pb[ping][:, :],
                            bigB[:, kg * 512 + j * 128:kg * 512 + (j + 1) * 128],
                            bigA[:, lh * 8192 + qg * 512:lh * 8192 + (qg + 1) * 512],
                            start=True, stop=True),
                            waits=wts, inc=("te", 1))
                        te_s = P.val("te")
                        eslot = ping
                        ev_exp[sub] = P.op("scalar", lambda e, ping=ping, eslot=eslot: e.activation(
                            ebuf[:, eslot * 512:(eslot + 1) * 512], pb[ping][:, :], AF.Exp),
                            waits=[("te", te_s)], inc=("se", 1))
                        if kg == qg:
                            ev_msk[sub] = P.op("gpsimd", lambda e, j=j, eslot=eslot: e.affine_select(
                                ebuf[:, eslot * 512:(eslot + 1) * 512],
                                ebuf[:, eslot * 512:(eslot + 1) * 512],
                                pattern=[[1, 512]], compare_op=ALU.is_ge, fill=0.0,
                                base=-(j * 128), channel_multiplier=-1),
                                waits=[("se", ev_exp[sub])], inc=("ge", 1))
                        wts_pv = [("se", ev_exp[sub])]
                        if sub in ev_msk:
                            wts_pv.append(("ge", ev_msk[sub]))
                        P.op("tensor", lambda e, kg=kg, j=j, eslot=eslot, sub=sub, nsub=nsub: e.matmul(
                            pb[4][:, :],
                            bigB[:, 8192 + kg * 512 + j * 128:8192 + kg * 512 + (j + 1) * 128],
                            ebuf[:, eslot * 512:(eslot + 1) * 512],
                            start=(sub == 0), stop=(sub == nsub - 1)),
                            waits=wts_pv)
                        P.op("tensor", lambda e, eslot=eslot, sub=sub, nsub=nsub: e.matmul(
                            pb[5][0:1, :], self.onesbf[:, 0:1],
                            ebuf[:, eslot * 512:(eslot + 1) * 512],
                            start=(sub == 0), stop=(sub == nsub - 1)),
                            inc=("te", 1))
                        sub += 1
                te_fin = P.val("te")
                prev_oevac_s = P.op("scalar", lambda e, lh=lh, qg=qg: e.activation(
                    xlnT[:, lh * 8192 + qg * 512:lh * 8192 + (qg + 1) * 512],
                    pb[4][:, :], AF.Copy),
                    waits=[("te", te_fin)], inc=("se", 1))
                vl = P.op("vector", lambda e: e.tensor_copy(lrow, pb[5][0:1, :]),
                          waits=[("te", te_fin)], inc=("ve", 1))
                prev_levac_v = P.op("vector", lambda e: e.reciprocal(lrow_i, lrow),
                                    waits=[("ve", vl)] + ([("st", dl)] if dl else []),
                                    inc=("ve", 1))
                dl = P.op("sync", lambda e, lh=lh, qg=qg: e.dma_start(
                    out=lag_in.ap()[lh * 64 + qg * 4:lh * 64 + (qg + 1) * 4, :],
                    in_=lrow_i),
                    waits=[("ve", prev_levac_v)], inc=("st", 16))

        prev_copy = 0
        dyi = None
        for lh in range(2):
            for b4 in range(16):
                te_y = None
                for j in range(4):
                    a = b4 * 4 + j
                    wts = []
                    if j == 0:
                        wts = [("se", prev_oevac_s)]
                        if prev_copy:
                            wts.append(("ve", prev_copy))
                    te_y = P.op("tensor", lambda e, lh=lh, a=a, j=j: e.transpose(
                        ptr[1][:, j * 128:(j + 1) * 128],
                        xlnT[:, lh * 8192 + a * 128:lh * 8192 + (a + 1) * 128],
                        self.identb[:, :]),
                        waits=wts, inc=("te", 1) if j == 3 else None)
                prev_copy = P.op("vector", lambda e: e.tensor_copy(
                    scr[2].bitcast(BF16)[:, 0:512], ptr[1][:, 0:512]),
                    waits=[("te", te_y)] + ([("st", dyi)] if dyi else []),
                    inc=("ve", 1))
                dyi = P.op("sync", lambda e, lh=lh, b4=b4: e.dma_start(
                    out=yag_in.ap()[b4 * 512:(b4 + 1) * 512,
                                    lh * 128:(lh + 1) * 128].rearrange(
                                        "(jj p) d -> p jj d", p=128),
                    in_=scr[2].bitcast(BF16)[:, 0:512].rearrange(
                        "p (jj d) -> p jj d", d=128)),
                    waits=[("ve", prev_copy)], inc=("st", 16))
        if DEBUG_L0F:
            dbg_y = nc.dram_tensor("dbg_y", [128, 16384], BF16, kind="ExternalOutput").ap()
            dbg_l = nc.dram_tensor("dbg_l", [128, 128], F32, kind="ExternalOutput").ap()
            P.op("sync", lambda e: e.dma_start(out=dbg_y, in_=xlnT[:, :]),
                 waits=[("se", prev_oevac_s)], inc=("st", 16))
            P.op("sync", lambda e: e.dma_start(out=dbg_l, in_=lag_in.ap()),
                 waits=[("st", dl)], inc=("st", 16))
        cc3 = P.op("gpsimd", lambda e: e.collective_compute(
            "AllGather", ALU.bypass, replica_groups=[list(range(NCORES))],
            ins=[yag_in.ap().opt()], outs=[yag_out.ap().opt()]),
            waits=[("st", dyi), ("st", dl)], inc=("cc", 1))
        cc4 = P.op("gpsimd", lambda e: e.collective_compute(
            "AllGather", ALU.bypass, replica_groups=[list(range(NCORES))],
            ins=[lag_in.ap().opt()], outs=[lag_out.ap().opt()]),
            inc=("cc", 1))

        for i in range(NT):
            for r in range(NCORES):
                P.op("gpsimd", lambda e, i=i, r=r: e.indirect_dma_start(
                    out=bigA[:, i * 2048 + r * 256:i * 2048 + (r + 1) * 256],
                    out_offset=None,
                    in_=yag_out.ap()[0:128, :],
                    in_offset=IndirectOffsetOnAxis(yidx[:, i * 8 + r:i * 8 + r + 1], 0)),
                    waits=[("cc", cc3)] if (i == 0 and r == 0) else (),
                    inc=("wl", 16))
        P.op("gpsimd", lambda e: e.indirect_dma_start(
            out=scr[1][:, 0:128], out_offset=None,
            in_=lag_out.ap()[0:128, :],
            in_offset=IndirectOffsetOnAxis(lidx[:, 0:1], 0)),
            waits=[("cc", cc4)], inc=("wl", 16))
        ev_y_own = P.val("wl")
        te_l = P.op("tensor", lambda e: e.transpose(
            pb[0][:, 0:128], scr[1][:, 0:128], self.identf[:, :]),
            waits=[("wl", ev_y_own)], inc=("te", 1))
        v_l = P.op("vector", lambda e: e.tensor_copy(scr[2][:, 0:128], pb[0][:, 0:128]),
                   waits=[("te", te_l)], inc=("ve", 1))
        linv_cols = scr[2][:, 0:128]
        if DEBUG_L0F:
            dbg_yret = nc.dram_tensor("dbg_yret", [128, 16384], BF16, kind="ExternalOutput").ap()
            dbg_linv = nc.dram_tensor("dbg_linv", [128, 128], F32, kind="ExternalOutput").ap()
            P.op("sync", lambda e: e.dma_start(out=dbg_yret, in_=bigA[:, :]),
                 waits=[("wl", ev_y_own)], inc=("st", 16))
            P.op("sync", lambda e: e.dma_start(out=dbg_linv, in_=linv_cols),
                 waits=[("ve", v_l)], inc=("st", 16))
        for i in range(NT):
            for h in range(16):
                r, sub = h // 2, h % 2
                P.op("vector", lambda e, i=i, h=h, r=r, sub=sub: e.scalar_tensor_tensor(
                    res[:, i * D + h * 128:i * D + (h + 1) * 128],
                    bigA[:, i * 2048 + r * 256 + sub * 128:i * 2048 + r * 256 + (sub + 1) * 128],
                    linv_cols[:, h * 8 + i:h * 8 + i + 1],
                    res[:, i * D + h * 128:i * D + (h + 1) * 128],
                    op0=ALU.mult, op1=ALU.add),
                    waits=[("ve", v_l), ("wl", ev_y_own)] if (i == 0 and h == 0) else (),
                    inc=("ve", 1))

    def l1_attn(self):
        P, nc = self.P, self.nc
        res, xlnT, bigA, bigB = self.res, self.xlnT, self.bigA, self.bigB
        scr, pb, ptr = self.scr, self.pb, self.ptr
        wb0, wb1 = self.wbuf
        self.reset_cols()
        SC = 0.125

        wqk_h = self.din("l1a_wqk", [24, 128, 2048])
        wv_h = self.din("l1a_wv", [3, 128, 8192])
        wo_h = self.din("l1a_wo", [24, 128, 1024])
        fkv_h = self.din("l1a_fkv", [128, 1920])
        pkb_h = self.din("l1a_pkb", [128, 768])
        pvb_h = self.din("l1a_pvb", [128, 1024])
        idx_h = self.din("l1a_idx", [128, 288], percore=True, dtype=I32)

        qk_in = nc.dram_tensor("l1a_qk_in", [3072, TLOC], BF16, kind="Internal")
        qk_out = nc.dram_tensor("l1a_qk_out", [24576, TLOC], BF16, kind="Internal")
        v_in = nc.dram_tensor("l1a_vin", [24576, 64], BF16, kind="Internal")
        v_out = nc.dram_tensor("l1a_vout", [196608, 64], BF16, kind="Internal")
        yag_in = nc.dram_tensor("l1a_yag_in", [T, 192], BF16, kind="Internal")
        yag_out = nc.dram_tensor("l1a_yag_out", [T * NCORES, 192], BF16, kind="Internal")

        ve_ent = P.val("ve")
        se_ent = P.val("se")
        te_ent = P.val("te")
        xl_ent = P.val("xl")
        ge_ent = P.val("ge")
        linv = self.col("l1a_linv", 4)

        tgI = self.tg.bitcast(I32)
        P.op("sync", lambda e: e.dma_start(out=tgI[:, 0:288], in_=idx_h),
             waits=[("ve", ve_ent)], inc=("xl", 16))
        ev_idx = P.val("xl")
        P.op("gpsimd", lambda e: e.dma_start(out=wb0[:, 0:1920], in_=fkv_h),
             waits=[("te", te_ent)], inc=("wl", 16))
        P.op("gpsimd", lambda e: e.dma_start(out=wb0[:, 1920:2688], in_=pkb_h),
             inc=("wl", 16))
        P.op("gpsimd", lambda e: e.dma_start(out=wb0[:, 2688:3712], in_=pvb_h),
             inc=("wl", 16))
        ev_tabs = P.val("wl")

        sbf = self.scrp.bitcast(BF16)
        cK, cV = sbf[:, 0:1024], sbf[:, 1024:2048]
        P.op("gpsimd", lambda e: e.memset(sbf[:, 0:2048], 0.0),
             waits=[("ve", ve_ent), ("se", se_ent), ("te", te_ent)], inc=("ge", 1))
        ev_mset = P.val("ge")

        ev_xln, ev_xlnT = self.emit_ln("l1a")
        se_ln = P.val("se")

        prev_cevac = 0
        for th in range(2):
            specs = [(pb[4], 0, 32, 2, 0), (pb[4], 64, 128, 8, 896),
                     (pb[5], 0, 32, 2, 448), (pb[5], 64, 128, 8, 1408)]
            evs = []
            for si, (bank, p0, p1, dt0, c0) in enumerate(specs):
                w = p1 - p0
                n = 16 - dt0
                te_c = None
                for i, dt in enumerate(range(dt0, 16)):
                    wts = []
                    if i == 0 and si == 0:
                        wts = [("wl", ev_tabs), ("ve", ev_xlnT), ("se", se_ln)]
                        if prev_cevac:
                            wts.append(("se", prev_cevac))
                    rhs = xlnT[:, dt * TLOC + th * 512:dt * TLOC + (th + 1) * 512]
                    te_c = P.op("tensor", lambda e, bank=bank, p0=p0, p1=p1, c0=c0,
                                i=i, w=w, n=n, rhs=rhs: e.matmul(
                        bank[p0:p1, :],
                        wb0[:, c0 + i * w:c0 + (i + 1) * w],
                        rhs, start=(i == 0), stop=(i == n - 1)),
                        waits=wts, inc=("te", 1) if i == n - 1 else None)
                evs.append(te_c)
            for si, (bank, p0, p1, dt0, c0) in enumerate(specs):
                dst = cK if si < 2 else cV
                prev_cevac = P.op("scalar", lambda e, bank=bank, p0=p0, p1=p1,
                                  dst=dst, th=th: e.activation(
                    dst[p0:p1, th * 512:(th + 1) * 512], bank[p0:p1, :], AF.Copy),
                    waits=[("te", evs[si])] + ([("ge", ev_mset)] if th == 0 and si == 0 else []),
                    inc=("se", 1))
        ev_c_done = P.val("se")

        ring = [sbf[:, 2048 + s * 1024:2048 + (s + 1) * 1024] for s in range(6)]
        ring_free = [0] * 6
        ystage_free = [0] * 8
        wslot_ready = [0, 0, 0]
        wslot_free = [0, 0, 0]
        psum_free = [0, 0, 0, 0]
        d_qk = 0
        P.op("gpsimd", lambda e: e.dma_start(out=bigB[:, 0:8192], in_=wv_h[0]),
             waits=[("te", te_ent), ("xl", xl_ent), ("ge", ge_ent)], inc=("wl", 16))
        vslot_ready = [P.val("wl"), 0]
        P.op("gpsimd", lambda e: e.dma_start(out=bigB[:, 8192:16384], in_=wv_h[1]),
             inc=("wl", 16))
        vslot_ready[1] = P.val("wl")
        vslot_free = [0, 0]

        for ot in range(24):
            slot = ot % 3
            w = wb1[:, slot * 2048:(slot + 1) * 2048]
            wts = [("te", wslot_free[slot])] if wslot_free[slot] else []
            if ot == 0:
                wts.append(("te", te_ent))
            P.op("gpsimd", lambda e, ot=ot, w=w: e.dma_start(out=w, in_=wqk_h[ot]),
                 waits=wts, inc=("wl", 16))
            wslot_ready[slot] = P.val("wl")
            te_m = {}
            corr = 12 <= ot < 18
            for th in range(2):
                pbi = (ot % 2) * 2 + th
                for dt in range(ND):
                    wts = []
                    if dt == 0:
                        wts = [("wl", wslot_ready[slot])]
                        if ot == 0 and th == 0:
                            wts += [("ve", ev_xlnT), ("se", se_ln)]
                        if psum_free[pbi]:
                            wts.append(("se", psum_free[pbi]))
                        if ot == 12 and th == 0:
                            wts.append(("se", ev_c_done))
                    te_m[th] = P.op("tensor", lambda e, pbi=pbi, w=w, dt=dt, th=th,
                                    corr=corr: e.matmul(
                        pb[pbi][:, :], w[:, dt * 128:(dt + 1) * 128],
                        xlnT[:, dt * TLOC + th * 512:dt * TLOC + (th + 1) * 512],
                        start=(dt == 0), stop=(dt == ND - 1 and not corr)),
                        waits=wts,
                        inc=("te", 1) if (dt == ND - 1 and not corr) else None)
                if corr:
                    otp = ot - 12
                    te_m[th] = P.op("tensor", lambda e, pbi=pbi, otp=otp, th=th: e.matmul(
                        pb[pbi][:, :],
                        wb0[:, 1920 + otp * 128:1920 + (otp + 1) * 128],
                        cK[:, th * 512:(th + 1) * 512],
                        start=False, stop=True), inc=("te", 1))
            wslot_free[slot] = te_m[1]
            rs = ot % 6
            ev2 = None
            for th in range(2):
                pbi = (ot % 2) * 2 + th
                wts = [("te", te_m[th])]
                if th == 0 and ring_free[rs]:
                    wts.append(("st", ring_free[rs]))
                ev2 = P.op("scalar", lambda e, rs=rs, th=th, pbi=pbi, ot=ot: e.activation(
                    ring[rs][:, th * 512:(th + 1) * 512], pb[pbi][:, :], AF.Copy,
                    scale=(SC if ot < 12 else 1.0)),
                    waits=wts, inc=("se", 1))
                psum_free[pbi] = ev2
            d_qk = P.op("sync", lambda e, ot=ot, rs=rs: e.dma_start(
                out=qk_in.ap()[ot * 128:(ot + 1) * 128, :], in_=ring[rs][:, :]),
                waits=[("se", ev2)], inc=("st", 16))
            ring_free[rs] = d_qk

        d_v = 0
        vc_last_te = {}
        for vc in range(3):
            half = vc % 2
            if vc == 2:
                P.op("gpsimd", lambda e: e.dma_start(
                    out=bigB[:, 0:8192], in_=wv_h[2]),
                    waits=[("te", vc_last_te[0])], inc=("wl", 16))
                vslot_ready[0] = P.val("wl")
            wv = bigB[:, half * 8192:(half + 1) * 8192]
            for tt in range(NT):
                pbi = tt % 4
                wts0 = [("wl", vslot_ready[half])]
                if psum_free[pbi]:
                    wts0.append(("se", psum_free[pbi]))
                if vc == 0 and tt == 0:
                    wts0 += [("se", ev_c_done), ("wl", ev_tabs)]
                first_done = False
                if vc < 2:
                    P.op("tensor", lambda e, pbi=pbi, tt=tt, vc=vc: e.matmul(
                        pb[pbi][:, :], cV[:, tt * 128:(tt + 1) * 128],
                        wb0[:, 2688 + vc * 512:2688 + (vc + 1) * 512],
                        start=True, stop=False), waits=wts0)
                    first_done = True
                te_v = None
                for dt in range(ND):
                    wts = [] if (first_done or dt > 0) else wts0
                    te_v = P.op("tensor", lambda e, pbi=pbi, dt=dt, tt=tt, wv=wv,
                                vc=vc: e.matmul(
                        pb[pbi][:, :],
                        xlnT[:, dt * TLOC + tt * 128:dt * TLOC + (tt + 1) * 128],
                        wv[:, dt * 512:(dt + 1) * 512],
                        start=(dt == 0 and vc == 2), stop=(dt == ND - 1)),
                        waits=wts, inc=("te", 1) if dt == ND - 1 else None)
                vc_last_te[vc] = te_v
                rs = (vc * 8 + tt) % 6
                wts = [("te", te_v)]
                if ring_free[rs]:
                    wts.append(("st", ring_free[rs]))
                ev3 = P.op("scalar", lambda e, rs=rs, pbi=pbi: e.activation(
                    ring[rs][:, 0:512], pb[pbi][:, :], AF.Copy),
                    waits=wts, inc=("se", 1))
                psum_free[pbi] = ev3
                for hl in range(8):
                    d_v = P.op("sync", lambda e, vc=vc, tt=tt, hl=hl, rs=rs: e.dma_start(
                        out=v_in.ap()[(vc * 8 + hl) * 1024 + tt * 128:
                                      (vc * 8 + hl) * 1024 + (tt + 1) * 128, :],
                        in_=ring[rs][:, hl * 64:(hl + 1) * 64]),
                        waits=[("se", ev3)] if hl == 0 else (), inc=("st", 16))
                ring_free[rs] = d_v
            vslot_free[half] = vc_last_te[vc]
        ev_p4 = P.val("te")

        cc_qk = P.op("gpsimd", lambda e: e.collective_compute(
            "AllGather", ALU.bypass, replica_groups=[list(range(NCORES))],
            ins=[qk_in.ap().opt()], outs=[qk_out.ap().opt()]),
            waits=[("st", d_qk)], inc=("cc", 1))
        cc_v = P.op("gpsimd", lambda e: e.collective_compute(
            "AllGather", ALU.bypass, replica_groups=[list(range(NCORES))],
            ins=[v_in.ap().opt()], outs=[v_out.ap().opt()]),
            waits=[("st", d_v)], inc=("cc", 1))

        for ht in range(12):
            P.op("gpsimd", lambda e, ht=ht: e.dma_start(
                out=xlnT[:, ht * 1024:(ht + 1) * 1024], in_=wo_h[ht]),
                waits=[("te", ev_p4)] if ht == 0 else (), inc=("wl", 16))
        ev_wo0 = P.val("wl")

        for r in range(NCORES):
            P.op("gpsimd", lambda e, r=r: e.indirect_dma_start(
                out=bigA[:, r * TLOC:(r + 1) * TLOC], out_offset=None,
                in_=qk_out.ap()[0:128, :],
                in_offset=IndirectOffsetOnAxis(tgI[:, 256 + r:257 + r], 0)),
                waits=[("cc", cc_qk), ("xl", ev_idx)] if r == 0 else (),
                inc=("wl", 16))
        for r in range(NCORES):
            P.op("gpsimd", lambda e, r=r: e.indirect_dma_start(
                out=bigA[0:64, 8192 + r * TLOC:8192 + (r + 1) * TLOC],
                out_offset=None,
                in_=qk_out.ap()[0:64, :],
                in_offset=IndirectOffsetOnAxis(tgI[0:64, 264 + r:265 + r], 0)),
                inc=("wl", 16))
        for r in range(NCORES):
            P.op("gpsimd", lambda e, r=r: e.indirect_dma_start(
                out=bigB[:, r * TLOC:(r + 1) * TLOC], out_offset=None,
                in_=qk_out.ap()[0:128, :],
                in_offset=IndirectOffsetOnAxis(tgI[:, 272 + r:273 + r], 0)),
                waits=[("te", ev_p4)] if r == 0 else (), inc=("wl", 16))
        for r in range(NCORES):
            P.op("gpsimd", lambda e, r=r: e.indirect_dma_start(
                out=bigB[0:64, 8192 + r * TLOC:8192 + (r + 1) * TLOC],
                out_offset=None,
                in_=qk_out.ap()[0:64, :],
                in_offset=IndirectOffsetOnAxis(tgI[0:64, 280 + r:281 + r], 0)),
                inc=("wl", 16))
        for r in range(NCORES):
            for hl in range(3):
                for tt in range(NT):
                    gt = r * NT + tt
                    wts = []
                    if r == 0 and hl == 0 and tt == 0:
                        wts = [("cc", cc_v), ("te", ev_p4)]
                    P.op("gpsimd", lambda e, r=r, hl=hl, tt=tt, gt=gt: e.indirect_dma_start(
                        out=self.wbuf[gt // 32][:, (gt % 32) * 192 + hl * 64:
                                                (gt % 32) * 192 + (hl + 1) * 64],
                        out_offset=None,
                        in_=v_out.ap()[0:128, :],
                        in_offset=IndirectOffsetOnAxis(
                            tgI[:, (r * 3 + hl) * 8 + tt:(r * 3 + hl) * 8 + tt + 1], 0)),
                        waits=wts, inc=("wl", 16))
        ev_gath = P.val("wl")

        def qap(lh, qc):
            if lh < 2:
                return bigA[lh * 64:(lh + 1) * 64, qc * 512:(qc + 1) * 512]
            return bigA[0:64, 8192 + qc * 512:8192 + (qc + 1) * 512]

        def kap(lh, gt):
            if lh < 2:
                return bigB[lh * 64:(lh + 1) * 64, gt * 128:(gt + 1) * 128]
            return bigB[0:64, 8192 + gt * 128:8192 + (gt + 1) * 128]

        def vap(lh, gt):
            return self.wbuf[gt // 32][:, (gt % 32) * 192 + lh * 64:
                                       (gt % 32) * 192 + (lh + 1) * 64]

        ebuf = sbf[:, 0:1024]
        Oev = scr[1][:, 0:512]
        prev_oevac = {2: 0, 3: 0}
        prev_exp_last = 0
        ev_tr_last = 0
        tr_free = {4: 0, 5: 0}
        ys = [sbf[:, 4096 + s * 64:4096 + (s + 1) * 64] for s in range(8)]
        ysc = 0
        dy = 0
        first_flash = True
        for lh in range(3):
            for qc in range(NQC):
                ob = 2 + ((lh * NQC + qc) % 2)
                nk = 4 * (qc + 1)
                ev_exp = {}
                ev_msk = {}
                my_exp_last = 0
                for gt in range(nk):
                    ping = gt % 2
                    wts = []
                    if gt == 0:
                        if first_flash:
                            wts = [("wl", ev_gath)]
                            first_flash = False
                        if prev_exp_last:
                            wts.append(("se", prev_exp_last))
                    elif gt >= 2:
                        wts = [("se", ev_exp[gt - 2])]
                        if (gt - 2) in ev_msk:
                            wts.append(("ge", ev_msk[gt - 2]))
                    P.op("tensor", lambda e, ping=ping, lh=lh, gt=gt, qc=qc: e.matmul(
                        pb[ping][:, :], kap(lh, gt), qap(lh, qc),
                        start=True, stop=True), waits=wts, inc=("te", 1))
                    te_s = P.val("te")
                    ev_exp[gt] = P.op("scalar", lambda e, ping=ping: e.activation(
                        ebuf[:, ping * 512:(ping + 1) * 512], pb[ping][:, :],
                        AF.Exp), waits=[("te", te_s)], inc=("se", 1))
                    my_exp_last = ev_exp[gt]
                    if gt // 4 == qc:
                        ev_msk[gt] = P.op("gpsimd", lambda e, gt=gt, ping=ping: e.affine_select(
                            ebuf[:, ping * 512:(ping + 1) * 512],
                            ebuf[:, ping * 512:(ping + 1) * 512],
                            pattern=[[1, 512]], compare_op=ALU.is_ge, fill=0.0,
                            base=-((gt % 4) * 128), channel_multiplier=-1),
                            waits=[("se", ev_exp[gt])], inc=("ge", 1))
                    wts_pv = [("se", ev_exp[gt])]
                    if gt in ev_msk:
                        wts_pv.append(("ge", ev_msk[gt]))
                    if gt == 0 and prev_oevac[ob]:
                        wts_pv.append(("ve", prev_oevac[ob]))
                    P.op("tensor", lambda e, lh=lh, gt=gt, ping=ping, ob=ob,
                         nk=nk: e.matmul(
                        pb[ob][0:64, :], vap(lh, gt),
                        ebuf[:, ping * 512:(ping + 1) * 512],
                        start=(gt == 0), stop=(gt == nk - 1)), waits=wts_pv)
                    P.op("tensor", lambda e, gt=gt, ping=ping, ob=ob, nk=nk: e.matmul(
                        pb[ob][64:65, :], self.onesbf[:, 0:1],
                        ebuf[:, ping * 512:(ping + 1) * 512],
                        start=(gt == 0), stop=(gt == nk - 1)), inc=("te", 1))
                prev_exp_last = my_exp_last
                te_fin = P.val("te")
                oev = P.op("vector", lambda e, ob=ob: e.tensor_copy(
                    Oev[0:65, :], pb[ob][0:65, :]),
                    waits=[("te", te_fin)] + ([("te", ev_tr_last)] if ev_tr_last else []),
                    inc=("ve", 1))
                prev_oevac[ob] = oev
                for j in range(4):
                    tp = 4 + (j % 2)
                    wts = [("ve", oev)] if j == 0 else []
                    if tr_free[tp]:
                        wts.append(("ve", tr_free[tp]))
                    ev_tr_last = P.op("tensor", lambda e, tp=tp, j=j: e.transpose(
                        pb[tp][:, 0:65], Oev[0:65, j * 128:(j + 1) * 128],
                        self.identf[0:65, 0:65]),
                        waits=wts, inc=("te", 1))
                    rec = P.op("vector", lambda e, tp=tp, j=j: e.reciprocal(
                        linv[:, j:j + 1], pb[tp][:, 64:65]),
                        waits=[("te", ev_tr_last)], inc=("ve", 1))
                    s = ysc % 8
                    ysc += 1
                    wts = [("ve", rec)]
                    if ystage_free[s]:
                        wts.append(("st", ystage_free[s]))
                    evm = P.op("vector", lambda e, tp=tp, j=j, s=s: e.tensor_scalar_mul(
                        ys[s][:, :], pb[tp][:, 0:64], linv[:, j:j + 1]),
                        waits=wts, inc=("ve", 1))
                    tr_free[tp] = evm
                    dy = P.op("sync", lambda e, lh=lh, qc=qc, j=j, s=s: e.dma_start(
                        out=yag_in.ap()[qc * 512 + j * 128:qc * 512 + (j + 1) * 128,
                                        lh * 64:(lh + 1) * 64],
                        in_=ys[s][:, :]),
                        waits=[("ve", evm)], inc=("st", 16))
                    ystage_free[s] = dy
        te_flash = P.val("te")
        se_flash = P.val("se")

        for ht in range(12):
            dst = wb0 if ht < 6 else wb1
            P.op("gpsimd", lambda e, ht=ht, dst=dst: e.dma_start(
                out=dst[:, (ht % 6) * 1024:(ht % 6 + 1) * 1024], in_=wo_h[12 + ht]),
                waits=[("te", te_flash)] if ht == 0 else (), inc=("wl", 16))
        ev_wo1 = P.val("wl")

        cc_y = P.op("gpsimd", lambda e: e.collective_compute(
            "AllGather", ALU.bypass, replica_groups=[list(range(NCORES))],
            ins=[yag_in.ap().opt()], outs=[yag_out.ap().opt()]),
            waits=[("st", dy)], inc=("cc", 1))

        for tt in range(NT):
            for r in range(NCORES):
                wts = []
                if tt == 0 and r == 0:
                    wts = [("cc", cc_y), ("te", te_flash)]
                P.op("gpsimd", lambda e, tt=tt, r=r: e.indirect_dma_start(
                    out=bigB[:, tt * 1536 + r * 192:tt * 1536 + (r + 1) * 192],
                    out_offset=None,
                    in_=yag_out.ap()[0:128, :],
                    in_offset=IndirectOffsetOnAxis(
                        tgI[:, 192 + tt * 8 + r:192 + tt * 8 + r + 1], 0)),
                    waits=wts, inc=("wl", 16))
        ev_ygath = P.val("wl")

        ev_ytr = {}
        prev_ycopy = {0: 0, 1: 0}
        for tt in range(NT):
            for ht in range(12):
                b = ht % 2
                wts = [("wl", ev_ygath)] if (tt == 0 and ht == 0) else []
                if prev_ycopy[b]:
                    wts.append(("ve", prev_ycopy[b]))
                te_y = P.op("tensor", lambda e, tt=tt, ht=ht, b=b: e.transpose(
                    ptr[b][:, 0:128],
                    bigB[:, tt * 1536 + ht * 128:tt * 1536 + (ht + 1) * 128],
                    self.identb[:, :]), waits=wts, inc=("te", 1))
                prev_ycopy[b] = P.op("vector", lambda e, tt=tt, ht=ht, b=b: e.tensor_copy(
                    bigA[:, tt * 1536 + ht * 128:tt * 1536 + (ht + 1) * 128],
                    ptr[b][:, 0:128]), waits=[("te", te_y)], inc=("ve", 1))
            ev_ytr[tt] = prev_ycopy[(11) % 2]

        wo_free = [0, 0]
        for tt in range(NT):
            for dc in range(4):
                pbi = (tt * 4 + dc) % 2
                te_o = None
                for ht in range(12):
                    wts = []
                    if ht == 0:
                        wts = [("ve", ev_ytr[tt])]
                        if tt == 0 and dc == 0:
                            wts.append(("se", se_flash))
                        wts.append(("wl", ev_wo0 if dc < 2 else ev_wo1))
                        if wo_free[pbi]:
                            wts.append(("ve", wo_free[pbi]))
                    if dc < 2:
                        rhs = xlnT[:, ht * 1024 + (dc % 2) * 512:
                                   ht * 1024 + (dc % 2) * 512 + 512]
                    else:
                        src = wb0 if ht < 6 else wb1
                        rhs = src[:, (ht % 6) * 1024 + (dc % 2) * 512:
                                  (ht % 6) * 1024 + (dc % 2) * 512 + 512]
                    te_o = P.op("tensor", lambda e, pbi=pbi, tt=tt, ht=ht,
                                rhs=rhs: e.matmul(
                        pb[pbi][:, :],
                        bigA[:, tt * 1536 + ht * 128:tt * 1536 + (ht + 1) * 128],
                        rhs, start=(ht == 0), stop=(ht == 11)),
                        waits=wts, inc=("te", 1) if ht == 11 else None)
                wo_free[pbi] = P.op("vector", lambda e, pbi=pbi, tt=tt, dc=dc: e.tensor_tensor(
                    res[:, tt * D + dc * 512:tt * D + (dc + 1) * 512],
                    pb[pbi][:, :],
                    res[:, tt * D + dc * 512:tt * D + (dc + 1) * 512],
                    op=ALU.add), waits=[("te", te_o)], inc=("ve", 1))

    def l1_ffn(self):
        P, nc = self.P, self.nc
        res, xlnT, bigA, bigB = self.res, self.xlnT, self.bigA, self.bigB
        scr, pb = self.scr, self.pb
        self.reset_cols()
        NI = 11
        NSI = 22

        rw_h = self.din("l1f_rw", [128, 128])
        gw_h = self.din("l1f_gw", [8, NI, 128, D])
        uw_h = self.din("l1f_uw", [8, NI, 128, D])
        dw_h = self.din("l1f_dw", [8, NI, 128, D])
        sgw_h = self.din("l1f_sgw", [NSI, 128, D])
        suw_h = self.din("l1f_suw", [NSI, 128, D])
        sdw_h = self.din("l1f_sdw", [NSI, 128, D])
        sgb_h = self.din("l1f_sgb", [128, NSI])
        sub_h = self.din("l1f_sub", [128, NSI])
        sdb_h = self.din("l1f_sdb", [1, D])

        rw_sb = self.col("rw_sb", 64).bitcast(BF16)
        ctab = self.col("ctab", 64)
        sgb_cm = self.col("sgb", NSI)
        sub_cm = self.col("sub", NSI)
        nm1 = self.col("nm1")
        nm2 = self.col("nm2")
        w1c = self.col("w1c")
        w2c = self.col("w2c")
        dcol = self.col("dcol")

        ent = [("ve", P.val("ve")), ("se", P.val("se")), ("te", P.val("te"))]
        self._l1f_te_ent = P.val("te")
        P.op("gpsimd", lambda e: e.dma_start(out=rw_sb, in_=rw_h), waits=ent,
             inc=("wl", 16))
        P.op("sync", lambda e: e.dma_start(out=sgb_cm, in_=sgb_h), waits=ent,
             inc=("xl", 16))
        P.op("sync", lambda e: e.dma_start(out=sub_cm, in_=sub_h), waits=ent,
             inc=("xl", 16))
        ev_rw = P.val("wl")
        ev_b = P.val("xl")
        if DEBUG_L0F:
            dbg_b0 = nc.dram_tensor("dbg_b0", [128, 124], F32, kind="ExternalOutput").ap()
            P.op("sync", lambda e: e.dma_start(out=dbg_b0, in_=scr[2][:, 900:1024]),
                 inc=("st", 16))

        ev_xln, ev_xlnT = self.emit_ln("l1f")

        c1s = scr[2][:, 0:8]
        lms = scr[2][:, 8:16]
        c2s = scr[2][:, 16:24]
        t1s = scr[2][:, 24:32]
        prev_comb = 0
        for tt in range(NT):
            te_r = None
            for dt in range(ND):
                wts = []
                if dt == 0:
                    wts = [("ve", ev_xlnT), ("wl", ev_rw)]
                    if prev_comb:
                        wts.append(("ve", prev_comb))
                te_r = P.op("tensor", lambda e, tt=tt, dt=dt: e.matmul(
                    pb[4][:, 0:8],
                    xlnT[:, dt * TLOC + tt * 128:dt * TLOC + (tt + 1) * 128],
                    rw_sb[:, dt * 8:(dt + 1) * 8],
                    start=(dt == 0), stop=(dt == ND - 1)),
                    waits=wts, inc=("te", 1) if dt == ND - 1 else None)
            v1 = P.op("vector", lambda e: e.reduce_max(
                nm1, pb[4][:, 0:8], axis=AX.X, negate=True),
                waits=[("te", te_r)], inc=("ve", 1))
            v2 = P.op("vector", lambda e: e.tensor_scalar(
                c1s, pb[4][:, 0:8], nm1, 0.0, op0=ALU.add, op1=ALU.is_ge),
                waits=[("ve", v1)], inc=("ve", 1))
            v3 = P.op("vector", lambda e: e.scalar_tensor_tensor(
                lms, c1s, -1e30, pb[4][:, 0:8], op0=ALU.mult, op1=ALU.add),
                waits=[("ve", v2)], inc=("ve", 1))
            v4 = P.op("vector", lambda e: e.reduce_max(
                nm2, lms, axis=AX.X, negate=True),
                waits=[("ve", v3)], inc=("ve", 1))
            v5 = P.op("vector", lambda e: e.tensor_scalar(
                c2s, lms, nm2, 0.0, op0=ALU.add, op1=ALU.is_ge),
                waits=[("ve", v4)], inc=("ve", 1))
            v6 = P.op("vector", lambda e: e.tensor_tensor(
                dcol, nm2, nm1, op=ALU.subtract),
                waits=[("ve", v5)], inc=("ve", 1))
            s1 = P.op("scalar", lambda e: e.activation(w1c, dcol, AF.Sigmoid),
                      waits=[("ve", v6)], inc=("se", 1))
            v7 = P.op("vector", lambda e: e.tensor_scalar(
                w2c, w1c, -1.0, 1.0, op0=ALU.mult, op1=ALU.add),
                waits=[("se", s1)], inc=("ve", 1))
            v8 = P.op("vector", lambda e: e.tensor_scalar_mul(t1s, c1s, w1c),
                      waits=[("ve", v7)], inc=("ve", 1))
            prev_comb = P.op("vector", lambda e, tt=tt: e.scalar_tensor_tensor(
                ctab[:, tt * 8:(tt + 1) * 8], c2s, w2c, t1s,
                op0=ALU.mult, op1=ALU.add),
                waits=[("ve", v8)], inc=("ve", 1))
        ev_ctab = prev_comb

        silu_bf = scr[0].bitcast(BF16)
        up_bf = scr[1].bitcast(BF16)
        self._l1f_evb = ev_b

        def ffn_pass(nit, gsrc, usrc, dsrc, gbias, ubias, comb_col, sdb_src,
                     first_wait_v, first_wait_s):
            wfree = [0, 0]
            prev_sil = 0
            prev_hmul = 0
            for it in range(nit):
                slot = it % 2
                w = self.wbuf[slot]
                wts = [("te", wfree[slot])] if wfree[slot] else []
                if it == 0:
                    wts += [("ve", first_wait_v), ("se", first_wait_s)]
                    if not wfree[slot] and self._l1f_te_ent:
                        wts.append(("te", self._l1f_te_ent))
                P.op("gpsimd", lambda e, it=it, w=w: e.dma_start(
                    out=w[:, 0:2048], in_=gsrc(it)), waits=wts, inc=("wl", 16))
                P.op("gpsimd", lambda e, it=it, w=w: e.dma_start(
                    out=w[:, 2048:4096], in_=usrc(it)), inc=("wl", 16))
                wready = P.val("wl")
                te_g = {}
                te_u = {}
                for th in range(2):
                    for dt in range(ND):
                        wts = []
                        if th == 0 and dt == 0:
                            wts = [("wl", wready), ("ve", ev_xlnT)]
                            if prev_hmul:
                                wts.append(("ve", prev_hmul))
                            if prev_sil:
                                wts.append(("se", prev_sil))
                        te_g[th] = P.op("tensor", lambda e, th=th, dt=dt, w=w: e.matmul(
                            pb[th][:, :], w[:, dt * 128:(dt + 1) * 128],
                            xlnT[:, dt * TLOC + th * 512:dt * TLOC + (th + 1) * 512],
                            start=(dt == 0), stop=(dt == ND - 1)),
                            waits=wts, inc=("te", 1) if dt == ND - 1 else None)
                for th in range(2):
                    for dt in range(ND):
                        te_u[th] = P.op("tensor", lambda e, th=th, dt=dt, w=w: e.matmul(
                            pb[2 + th][:, :], w[:, 2048 + dt * 128:2048 + (dt + 1) * 128],
                            xlnT[:, dt * TLOC + th * 512:dt * TLOC + (th + 1) * 512],
                            start=(dt == 0), stop=(dt == ND - 1)),
                            inc=("te", 1) if dt == ND - 1 else None)
                wfree[slot] = te_u[1]
                for th in range(2):
                    kw = {}
                    if gbias is not None:
                        kw["bias"] = gbias(it)
                    prev_sil = P.op("scalar", lambda e, th=th, it=it, kw=kw: e.activation(
                        silu_bf[:, th * 512:(th + 1) * 512], pb[th][:, :],
                        AF.Silu, **kw),
                        waits=[("te", te_g[th])] +
                              ([("xl", self._l1f_evb)] if kw else []),
                        inc=("se", 1))
                    if ubias is not None:
                        vub = P.op("vector", lambda e, th=th, it=it: e.tensor_scalar(
                            up_bf[:, th * 512:(th + 1) * 512], pb[2 + th][:, :],
                            ubias(it), None, op0=ALU.add),
                            waits=[("te", te_u[th]), ("xl", self._l1f_evb)],
                            inc=("ve", 1))
                        prev_hmul = P.op("vector", lambda e, th=th, it=it: e.tensor_tensor(
                            bigA[:, it * TLOC + th * 512:it * TLOC + (th + 1) * 512],
                            up_bf[:, th * 512:(th + 1) * 512],
                            silu_bf[:, th * 512:(th + 1) * 512], op=ALU.mult),
                            waits=[("se", prev_sil), ("ve", vub)], inc=("ve", 1))
                    else:
                        prev_hmul = P.op("vector", lambda e, th=th, it=it: e.tensor_tensor(
                            bigA[:, it * TLOC + th * 512:it * TLOC + (th + 1) * 512],
                            pb[2 + th][:, :],
                            silu_bf[:, th * 512:(th + 1) * 512], op=ALU.mult),
                            waits=[("se", prev_sil), ("te", te_u[th])], inc=("ve", 1))
            if DEBUG_L0F and not hasattr(self, "_l1f_d2"):
                self._l1f_d2 = True
                dbg_up2 = nc.dram_tensor("dbg_up2", [128, 2048], BF16, kind="ExternalOutput").ap()
                dbg_bias = nc.dram_tensor("dbg_bias", [128, 124], F32, kind="ExternalOutput").ap()
                P.op("sync", lambda e: e.dma_start(out=dbg_up2, in_=up_bf[:, :]),
                     waits=[("ve", prev_hmul)], inc=("st", 16))
                P.op("sync", lambda e: e.dma_start(out=dbg_bias, in_=scr[2][:, 900:1024]),
                     waits=[("ve", prev_hmul)], inc=("st", 16))
            ev_h = prev_hmul
            piece = 1024
            nphase = D // piece
            prev_ev = 0
            dmaD = None
            for ph in range(nphase):
                for it in range(nit):
                    wts = []
                    if it == 0:
                        wts = [("ve", ev_h)]
                        if prev_ev:
                            wts.append(("ve", prev_ev))
                        if dmaD:
                            wts.append(dmaD)
                    P.op("gpsimd", lambda e, ph=ph, it=it: e.dma_start(
                        out=bigB[:, it * piece:(it + 1) * piece],
                        in_=dsrc(it)[:, ph * piece:(ph + 1) * piece]),
                        waits=wts, inc=("wl", 16))
                dmaD = ("wl", P.val("wl"))
                if sdb_src is not None:
                    dsb = P.op("sync", lambda e, ph=ph: e.dma_start(
                        out=scr[3][0:1, 0:piece],
                        in_=sdb_src[0:1, ph * piece:(ph + 1) * piece]),
                        waits=[("ve", prev_ev)] if prev_ev else [("ve", ev_h)],
                        inc=("xl", 16))
                    dmaD2 = ("xl", dsb)
                else:
                    dmaD2 = None
                nq = piece // 512
                for tt in range(NT):
                    te_d = {}
                    for it in range(nit):
                        for q2 in range(nq):
                            wts = []
                            if it == 0 and q2 == 0:
                                wts = [dmaD, ("ve", ev_h)] + ([dmaD2] if dmaD2 else [])
                                if tt == 0 and ph == 0:
                                    wts.append(("ve", ev_ctab))
                                if prev_ev:
                                    wts.append(("ve", prev_ev))
                            te_d[q2] = P.op("tensor", lambda e, it=it, q2=q2, tt=tt: e.matmul(
                                pb[4 + q2][:, :],
                                bigA[:, it * TLOC + tt * 128:it * TLOC + (tt + 1) * 128],
                                bigB[:, it * piece + q2 * 512:it * piece + (q2 + 1) * 512],
                                start=(it == 0), stop=(it == nit - 1 and sdb_src is None)),
                                waits=wts,
                                inc=("te", 1) if (it == nit - 1 and q2 == nq - 1 and sdb_src is None) else None)
                    if sdb_src is not None:
                        for q2 in range(nq):
                            te_d[q2] = P.op("tensor", lambda e, q2=q2: e.matmul(
                                pb[4 + q2][:, :], self.ones[0:1, 0:128],
                                scr[3][0:1, q2 * 512:(q2 + 1) * 512],
                                start=False, stop=True),
                                inc=("te", 1) if q2 == nq - 1 else None)
                    for q2 in range(nq):
                        col0 = tt * D + ph * piece + q2 * 512
                        if comb_col is not None:
                            prev_ev = P.op("vector", lambda e, tt=tt, q2=q2, col0=col0: e.scalar_tensor_tensor(
                                res[:, col0:col0 + 512], pb[4 + q2][:, :],
                                comb_col(tt), res[:, col0:col0 + 512],
                                op0=ALU.mult, op1=ALU.add),
                                waits=[("te", te_d[nq - 1])] if q2 == 0 else (),
                                inc=("ve", 1))
                        else:
                            prev_ev = P.op("vector", lambda e, tt=tt, q2=q2, col0=col0: e.tensor_tensor(
                                res[:, col0:col0 + 512], pb[4 + q2][:, :],
                                res[:, col0:col0 + 512], op=ALU.add),
                                waits=[("te", te_d[nq - 1])] if q2 == 0 else (),
                                inc=("ve", 1))
            return prev_ev, P.val("se")

        fw_v, fw_s = ev_ctab, P.val("se")
        for e_i in range(L1F_NEXP):
            fw_v, fw_s = ffn_pass(
                NI,
                lambda it, e_i=e_i: gw_h[e_i, it],
                lambda it, e_i=e_i: uw_h[e_i, it],
                lambda it, e_i=e_i: dw_h[e_i, it],
                None, None,
                lambda tt, e_i=e_i: ctab[:, tt * 8 + e_i:tt * 8 + e_i + 1],
                None, fw_v, fw_s)
        if not L1F_SHARED:
            return
        fw_v, fw_s = ffn_pass(
            L1F_NIT or NI,
            lambda it: sgw_h[it], lambda it: suw_h[it], lambda it: sdw_h[it],
            lambda it: sgb_cm[:, it:it + 1], lambda it: sub_cm[:, it:it + 1],
            None, (sdb_h if not L1F_NOSDB else None),
            fw_v, fw_s)
        if not L1F_SH2:
            return
        ffn_pass(
            NSI - NI,
            lambda it: sgw_h[NI + it], lambda it: suw_h[NI + it],
            lambda it: sdw_h[NI + it],
            lambda it: sgb_cm[:, NI + it:NI + it + 1],
            lambda it: sub_cm[:, NI + it:NI + it + 1],
            None, None,
            fw_v, fw_s)

    def finish(self):
        P = self.P
        allv = P.val("ve")
        for i in range(NT):
            P.op("sync", lambda e, i=i: e.dma_start(
                out=self.y_out[i * 128:(i + 1) * 128, :],
                in_=self.res[:, i * D:(i + 1) * D]),
                waits=[("ve", allv)] if i == 0 else (), inc=("st", 16))
        final = P.val("st")
        P.op("sync", lambda e: e.nop(), waits=[("st", final)])


def prep_host(inp):
    H = {"ident_h": np.eye(128, dtype=np.float32)}
    f = lambda k: np.asarray(inp[k], dtype=np.float32)
    H["l0a_wr"] = _retile_dmajor(f("L0_attn_Wr"))
    H["l0a_wk"] = _retile_dmajor(f("L0_attn_Wk"))
    H["l0a_wv"] = _retile_dmajor(f("L0_attn_Wv"))
    H["l0a_wo"] = _rows_block(np.ascontiguousarray(f("L0_attn_Wo").T))
    H["l0a_decay"] = _cm_vec(f("L0_attn_decay"))
    H["l0a_gnw"] = _cm_vec(f("L0_attn_gn_weight"))
    H["l0a_gnb"] = _cm_vec(f("L0_attn_gn_bias"))
    wg = f("L0_ffn_W_gate")
    H["l0f_wg"] = np.ascontiguousarray(
        wg.reshape(16, 16, 128).transpose(2, 1, 0).reshape(128, 256))
    H["l0f_wqkv"] = _retile_dmajor(f("L0_ffn_W_qkv"))

    WQ, WK, WV, WO = (f("L1_attn_W_Q"), f("L1_attn_W_K"),
                      f("L1_attn_W_V"), f("L1_attn_W_O"))
    wq_t = WQ.reshape(16, 128, 12, 128).transpose(2, 1, 0, 3).reshape(12, 128, 2048)
    wk_t = WK.reshape(16, 128, 12, 128).transpose(2, 1, 0, 3).reshape(12, 128, 2048)
    H["l1a_wqk"] = np.ascontiguousarray(np.concatenate([wq_t, wk_t], axis=0))
    H["l1a_wv"] = np.ascontiguousarray(
        WV.reshape(16, 128, 3, 512).transpose(2, 1, 0, 3).reshape(3, 128, 8192))
    H["l1a_wo"] = np.ascontiguousarray(
        WO.reshape(12, 128, 2, 1024).transpose(2, 0, 1, 3).reshape(24, 128, 1024))
    fkv = np.zeros((128, 1920), dtype=np.float32)
    fkv[:, 0:448] = f("L1_attn_FK0").reshape(14, 128, 32).transpose(1, 0, 2).reshape(128, 448)
    fkv[:, 448:896] = f("L1_attn_FV0").reshape(14, 128, 32).transpose(1, 0, 2).reshape(128, 448)
    fkv[:, 896:1408] = f("L1_attn_FK1").reshape(8, 128, 64).transpose(1, 0, 2).reshape(128, 512)
    fkv[:, 1408:1920] = f("L1_attn_FV1").reshape(8, 128, 64).transpose(1, 0, 2).reshape(128, 512)
    H["l1a_fkv"] = fkv

    def _pblk(P0, P1, width):
        tab = np.zeros((128, width), dtype=np.float32)
        for h in range(4):
            for r in range(8):
                tab[h * 8 + r, h * 64:(h + 1) * 64] = P0[h, r]
        for hl in range(8):
            for r in range(8):
                tab[64 + hl * 8 + r, 256 + hl * 64:256 + (hl + 1) * 64] = P1[hl, r]
        return tab
    H["l1a_pkb"] = _pblk(f("L1_attn_PK0"), f("L1_attn_PK1"), 768)
    H["l1a_pvb"] = _pblk(f("L1_attn_PV0"), f("L1_attn_PV1"), 1024)

    rw = f("L1_ffn_router_w")
    H["l1f_rw"] = np.ascontiguousarray(
        rw.reshape(8, 16, 128).transpose(2, 1, 0).reshape(128, 128))
    gw = f("L1_ffn_gate_w"); uw = f("L1_ffn_up_w"); dw = f("L1_ffn_down_w")
    H["l1f_gw"] = np.stack([_retile_dmajor(np.ascontiguousarray(gw[e].T))
                            for e in range(8)])
    H["l1f_uw"] = np.stack([_retile_dmajor(np.ascontiguousarray(uw[e].T))
                            for e in range(8)])
    H["l1f_dw"] = np.stack([_rows_block(dw[e]) for e in range(8)])
    H["l1f_sgw"] = _retile_dmajor(np.ascontiguousarray(f("L1_ffn_s_gate_w").T))
    H["l1f_suw"] = _retile_dmajor(np.ascontiguousarray(f("L1_ffn_s_up_w").T))
    H["l1f_sdw"] = _rows_block(f("L1_ffn_s_down_w"))
    H["l1f_sgb"] = _cm_vec(f("L1_ffn_s_gate_b"))
    H["l1f_sub"] = _cm_vec(f("L1_ffn_s_up_b"))
    H["l1f_sdb"] = np.ascontiguousarray(f("L1_ffn_s_down_b").reshape(1, D))
    HP = [dict() for _ in range(NCORES)]
    for c in range(NCORES):
        tgv = np.arange(c * TLOC, (c + 1) * TLOC).astype(np.float32)
        HP[c]["tg_base"] = np.ascontiguousarray(
            np.broadcast_to(tgv[None, :], (128, TLOC)).copy())
        m = np.zeros((128, 16), dtype=np.float32)
        for r in range(NCORES):
            if r < c:
                for ct in range(16):
                    m[16 * r + ct, ct] = 1.0
        HP[c]["l0a_cmask"] = m
        p = np.arange(128)
        qidx = np.zeros((128, 16), dtype=np.int32)
        for lh in range(2):
            for r in range(NCORES):
                qidx[:, lh * 8 + r] = r * 2048 + (2 * c + lh) * 128 + p
        HP[c]["l0f_qidx"] = qidx
        yidx = np.zeros((128, 64), dtype=np.int32)
        for i in range(NT):
            for r in range(NCORES):
                yidx[:, i * 8 + r] = r * T + c * TLOC + i * 128 + p
        HP[c]["l0f_yidx"] = yidx
        lidx = np.zeros((128, 1), dtype=np.int32)
        for q in range(128):
            h, i = q // 8, q % 8
            lidx[q, 0] = 128 * (h // 2) + (h % 2) * 64 + 8 * c + i
        HP[c]["l0f_lidx"] = lidx
        aidx = np.zeros((128, 288), dtype=np.int32)
        for r in range(NCORES):
            for hl in range(3):
                h = 3 * c + hl
                for i in range(NT):
                    aidx[:, (r * 3 + hl) * 8 + i] = (r * 24576 + (h // 8) * 8192 +
                                                     (h % 8) * 1024 + i * 128 + p)
        for i in range(NT):
            for r in range(NCORES):
                aidx[:, 192 + i * 8 + r] = r * T + c * TLOC + i * 128 + p
        for r in range(NCORES):
            aidx[:, 256 + r] = r * 3072 + c * 192 + p
            aidx[0:64, 264 + r] = r * 3072 + c * 192 + 128 + p[0:64]
            aidx[:, 272 + r] = r * 3072 + 1536 + c * 192 + p
            aidx[0:64, 280 + r] = r * 3072 + 1536 + c * 192 + 128 + p[0:64]
        HP[c]["l1a_idx"] = aidx
    return H, HP


_BUILD_CACHE = {}


def _get_builder(blocks):
    key = tuple(blocks)
    if key not in _BUILD_CACHE:
        _BUILD_CACHE[key] = B(list(blocks)).build()
    return _BUILD_CACHE[key]


def run_group(blocks, x_cur_shards, H, HP):
    b = _get_builder(blocks)
    in_maps = []
    for c in range(NCORES):
        m = {}
        for n in b.shared_names:
            m[n] = H[n]
        for n in b.percore_names:
            m[n] = HP[c][n] if n != "x_in" else x_cur_shards[c]
        in_maps.append(m)
    res = run_bass_kernel_spmd(b.nc, in_maps, core_ids=list(range(NCORES)))
    global LAST_RESULTS
    LAST_RESULTS = res.results
    return [np.asarray(r["y_out"]) for r in res.results]


def kernel(**inputs):
    H, HP = prep_host(inputs)
    x = np.ascontiguousarray(np.asarray(inputs["L0_attn_x"], dtype=np.float32)[0])
    shards = shard_tokens(x)
    for group in PLAN:
        shards = run_group(group, shards, H, HP)
    y = unshard_tokens(shards)
    return y.reshape(1, T, D)


# version 04
# speedup vs baseline: 1.0621x; 1.0621x over previous
import numpy as np
from collections import defaultdict
from contextlib import ExitStack

import concourse.bass as bass
import concourse.mybir as mybir
from concourse.bass_core_rust import IndirectOffsetOnAxis
from concourse.bass_utils import run_bass_kernel_spmd

F32 = mybir.dt.float32
BF16 = mybir.dt.bfloat16
I32 = mybir.dt.int32
AF = mybir.ActivationFunctionType
ALU = mybir.AluOpType
AX = mybir.AxisListType

NCORES = 8
T = 8192
D = 2048
TLOC = T // NCORES
NT = TLOC // 128
ND = D // 128
QC = 512
NQC = T // QC

PLAN = [[0, 1, 2, 3]]
DEBUG_L0F = False
L1F_NEXP = 8
L1F_SHARED = True
L1F_SH2 = True
L1F_NOSDB = False
L1F_NIT = None
LAST_RESULTS = None


class Prog:
    ENGINES = ("sync", "gpsimd", "scalar", "vector", "tensor")

    def __init__(self, nc, ctx):
        self.nc = nc
        self.ctx = ctx
        self.q = {e: [] for e in self.ENGINES}
        self.cnt = defaultdict(int)
        self.sems = {}

    def sem(self, name):
        if name not in self.sems:
            self.sems[name] = self.ctx.enter_context(self.nc.semaphore(name))
        return self.sems[name]

    def op(self, eng, build, waits=(), inc=None):
        waits = tuple((self.sem(s), int(v)) for (s, v) in waits if v)
        newv = None
        if inc is not None:
            sname, n = inc
            self.cnt[sname] += n
            newv = self.cnt[sname]
            semh = self.sem(sname)

            def closure(e, build=build, waits=waits, semh=semh, n=n):
                for (sh, v) in waits:
                    e.wait_ge(sh, v)
                build(e).then_inc(semh, n)
        else:
            def closure(e, build=build, waits=waits):
                for (sh, v) in waits:
                    e.wait_ge(sh, v)
                build(e)
        self.q[eng].append(closure)
        return newv

    def val(self, name):
        return self.cnt[name]

    def run_queues(self):
        with self.nc.Block() as block:
            block.sync(lambda e: [f(e) for f in self.q["sync"]])
            block.gpsimd(lambda e: [f(e) for f in self.q["gpsimd"]])
            block.scalar(lambda e: [f(e) for f in self.q["scalar"]])
            block.vector(lambda e: [f(e) for f in self.q["vector"]])
            block.tensor(lambda e: [f(e) for f in self.q["tensor"]])


def _retile_dmajor(W_ed):
    e_out, d_in = W_ed.shape
    nct, ndt = e_out // 128, d_in // 128
    arr = W_ed.reshape(nct, 128, ndt, 128)
    arr = arr.transpose(0, 3, 2, 1)
    return np.ascontiguousarray(arr).reshape(nct, 128, d_in)


def _rows_block(W_de):
    d_in, e_out = W_de.shape
    return np.ascontiguousarray(W_de.reshape(d_in // 128, 128, e_out))


def _cm_vec(v):
    return np.ascontiguousarray(np.asarray(v).reshape(-1, 128).T)


def shard_tokens(x_full):
    return [np.ascontiguousarray(x_full[c * TLOC:(c + 1) * TLOC])
            for c in range(NCORES)]


def unshard_tokens(shards):
    return np.concatenate(shards, axis=0)


class B:
    def __init__(self, blocks):
        self.blocks = blocks
        self.nc = bass.Bass("TRN2", target_bir_lowering=False, debug=False)
        self.ctx = ExitStack()
        self.P = Prog(self.nc, self.ctx)
        self.shared_names = []
        self.percore_names = []

    def din(self, name, shape, percore=False, dtype=F32):
        (self.percore_names if percore else self.shared_names).append(name)
        return self.nc.dram_tensor(name, list(shape), dtype, kind="ExternalInput").ap()

    def sbuf(self, name, shape, dtype):
        return self.ctx.enter_context(self.nc.sbuf_tensor(name, list(shape), dtype))

    def psum(self, name, shape, dtype=F32):
        return self.ctx.enter_context(self.nc.psum_tensor(name, list(shape), dtype))

    def col(self, name, width=1):
        if name not in self.cols:
            assert self._small_off + width <= 256, "smalls overflow"
            self.cols[name] = self.smalls[:, self._small_off:self._small_off + width]
            self._small_off += width
        return self.cols[name]

    def reset_cols(self):
        self._small_off = 0
        self.cols = {}

    def build(self):
        nc = self.nc
        self.x_in = self.din("x_in", [TLOC, D], percore=True)
        self.y_out = nc.dram_tensor("y_out", [TLOC, D], F32, kind="ExternalOutput").ap()

        self.res = self.sbuf("res", [128, NT * D], F32)
        self.xlnT = self.sbuf("xlnT", [128, ND * TLOC], BF16)
        self.bigA = self.sbuf("bigA", [128, ND * TLOC], BF16)
        self.bigB = self.sbuf("bigB", [128, ND * TLOC], BF16)
        self.wbuf = [self.sbuf("wbuf0", [128, 6144], BF16),
                     self.sbuf("wbuf1", [128, 6144], BF16)]
        self.scrp = self.sbuf("scrp", [128, 4 * 1024], F32)
        self.tg = self.sbuf("tg", [128, 1024], F32)
        self.identb = self.sbuf("identb", [128, 128], BF16)
        self.identf = self.sbuf("identf", [128, 128], F32)
        self.ones = self.sbuf("onesb", [128, 512], F32)
        self.onesbf = self.sbuf("onesbf", [128, 8], BF16)
        self.epscol = self.sbuf("epscol", [128, 1], F32)
        self.smalls = self.sbuf("smalls", [128, 256], F32)
        self.pb = [self.psum(f"pb{i}", [128, 512], F32) for i in range(6)]
        self.ptr = [self.psum(f"ptr{i}", [128, 1024], BF16) for i in range(2)]
        self.scr = [self.scrp[:, i * 1024:(i + 1) * 1024] for i in range(4)]
        self.reset_cols()

        self.setup()
        if 0 in self.blocks:
            self.l0_attn()
        if 1 in self.blocks:
            self.l0_ffn()
        if 2 in self.blocks:
            self.l1_attn()
        if 3 in self.blocks:
            self.l1_ffn()
        self.finish()
        self.P.run_queues()
        nc.finalize()
        return self

    def setup(self):
        P = self.P
        for i in range(NT):
            P.op("sync", lambda e, i=i: e.dma_start(
                out=self.res[:, i * D:(i + 1) * D],
                in_=self.x_in[i * 128:(i + 1) * 128, :]), inc=("xl", 16))
        self.ev_res_loaded = P.val("xl")

        ident_h = self.din("ident_h", [128, 128])
        P.op("gpsimd", lambda e: e.dma_start(out=self.identb[:, :], in_=ident_h),
             inc=("wl", 16))
        P.op("gpsimd", lambda e: e.dma_start(out=self.identf[:, :], in_=ident_h),
             inc=("wl", 16))
        P.op("gpsimd", lambda e: e.memset(self.ones[:, :], 1.0), inc=("ge", 1))
        P.op("gpsimd", lambda e: e.memset(self.onesbf[:, :], 1.0), inc=("ge", 1))
        P.op("gpsimd", lambda e: e.memset(self.epscol[:, :], 1e-5), inc=("ge", 1))
        tgb = self.din("tg_base", [128, 1024], percore=True)
        P.op("sync", lambda e: e.dma_start(out=self.tg[:, :], in_=tgb), inc=("xl", 16))
        self.ev_setup_xl = P.val("xl")
        self.ev_setup_wl = P.val("wl")
        self.ev_setup_ge = P.val("ge")

    def emit_ln(self, tag):
        P = self.P
        res, bigA, xlnT, ptr = self.res, self.bigA, self.xlnT, self.ptr
        sq_scr = xlnT[:, 0:4096].bitcast(F32)
        ss, sx = self.col("ln_ss"), self.col("ln_sx")
        mcol, e2col = self.col("ln_m"), self.col("ln_e2")
        vcol, sdcol = self.col("ln_v"), self.col("ln_sd")
        rcol, nmcol = self.col("ln_r"), self.col("ln_nm")
        xln = bigA
        start_v = P.val("ve")
        start_s = P.val("se")
        start_t = P.val("te")
        for i in range(NT):
            rsl = res[:, i * D:(i + 1) * D]
            w0 = [("xl", self.ev_res_loaded), ("ge", self.ev_setup_ge)]
            if i == 0:
                w0 = w0 + [("ve", start_v), ("se", start_s), ("te", start_t)]
            se = P.op("scalar", lambda e, rsl=rsl: e.activation(
                sq_scr, rsl, AF.Square, accum_out=ss), waits=w0, inc=("se", 1))
            v1 = P.op("vector", lambda e, rsl=rsl: e.reduce_sum(sx, rsl, axis=AX.X),
                      waits=w0, inc=("ve", 1))
            v2 = P.op("vector", lambda e: e.tensor_scalar_mul(mcol, sx, 1.0 / D),
                      waits=[("ve", v1)], inc=("ve", 1))
            P.op("vector", lambda e: e.tensor_scalar_mul(e2col, ss, 1.0 / D),
                 waits=[("se", se), ("ve", v2)], inc=("ve", 1))
            v4 = P.op("vector", lambda e: e.tensor_tensor(vcol, mcol, mcol, op=ALU.mult),
                      inc=("ve", 1))
            v5 = P.op("vector", lambda e: e.scalar_tensor_tensor(
                vcol, vcol, -1.0, e2col, op0=ALU.mult, op1=ALU.add),
                waits=[("ve", v4)], inc=("ve", 1))
            s2 = P.op("scalar", lambda e: e.activation(
                sdcol, vcol, AF.Sqrt, bias=self.epscol[:, :]),
                waits=[("ve", v5)], inc=("se", 1))
            v6 = P.op("vector", lambda e: e.reciprocal(rcol, sdcol),
                      waits=[("se", s2)], inc=("ve", 1))
            v7 = P.op("vector", lambda e: e.scalar_tensor_tensor(
                nmcol, mcol, -1.0, rcol, op0=ALU.mult, op1=ALU.mult),
                waits=[("ve", v6)], inc=("ve", 1))
            P.op("vector", lambda e, i=i, rsl=rsl: e.tensor_scalar(
                xln[:, i * D:(i + 1) * D], rsl, rcol, nmcol,
                op0=ALU.mult, op1=ALU.add),
                waits=[("ve", v7)], inc=("ve", 1))
        ev_xln = P.val("ve")

        last_copy = {0: 0, 1: 0}
        for dt in range(ND):
            bank = dt % 2
            tw = [("ve", ev_xln), ("wl", self.ev_setup_wl)]
            if last_copy[bank]:
                tw = tw + [("ve", last_copy[bank])]
            te = None
            for i in range(NT):
                te = P.op("tensor", lambda e, dt=dt, i=i, bank=bank: e.transpose(
                    ptr[bank][:, i * 128:(i + 1) * 128],
                    xln[:, i * D + dt * 128:i * D + (dt + 1) * 128],
                    self.identb[:, :]),
                    waits=tw if i == 0 else (),
                    inc=("te", 1) if i == NT - 1 else None)
            last_copy[bank] = P.op("vector", lambda e, dt=dt, bank=bank: e.tensor_copy(
                xlnT[:, dt * TLOC:(dt + 1) * TLOC], ptr[bank][:, :]),
                waits=[("te", te)], inc=("ve", 1))
        return ev_xln, P.val("ve")

    def l0_attn(self):
        P, nc = self.P, self.nc
        res, xlnT, bigA, bigB = self.res, self.xlnT, self.bigA, self.bigB
        scr, tg, pb = self.scr, self.tg, self.pb
        self.reset_cols()

        wr_h = self.din("l0a_wr", [16, 128, D])
        wk_h = self.din("l0a_wk", [16, 128, D])
        wv_h = self.din("l0a_wv", [16, 128, D])
        wo_h = self.din("l0a_wo", [16, 128, D])
        dec_h = self.din("l0a_decay", [128, ND])
        gnw_h = self.din("l0a_gnw", [128, ND])
        gnb_h = self.din("l0a_gnb", [128, ND])
        cmask_h = self.din("l0a_cmask", [128, 16], percore=True)

        cs_in = nc.dram_tensor("l0a_cs_in", [16, 128], F32, kind="Internal")
        cs_out = nc.dram_tensor("l0a_cs_out", [128, 128], F32, kind="Internal")
        gs_in = nc.dram_tensor("l0a_gs_in", [1, 2], F32, kind="Internal")
        gs_out = nc.dram_tensor("l0a_gs_out", [1, 2], F32, kind="Internal")

        dec_cm = self.col("dec_cm", ND)
        logd = self.col("logd", ND)
        gnw_cm = self.col("gnw_cm", ND)
        gnb_cm = self.col("gnb_cm", ND)
        stab = self.col("stab", 16)
        carry = self.col("carry", 16)
        carry2 = self.col("carry2", 1)
        sxt = self.col("sxt", ND)
        sqt = self.col("sqt", ND)
        Acol = self.col("Acl", ND)
        Bcol = self.col("Bcl", ND)
        cmask_sb = self.col("cmask", 16)

        P.op("sync", lambda e: e.dma_start(out=dec_cm, in_=dec_h), inc=("xl", 16))
        P.op("sync", lambda e: e.dma_start(out=gnw_cm, in_=gnw_h), inc=("xl", 16))
        P.op("sync", lambda e: e.dma_start(out=gnb_cm, in_=gnb_h), inc=("xl", 16))
        P.op("sync", lambda e: e.dma_start(out=cmask_sb, in_=cmask_h), inc=("xl", 16))
        ev_vecs = P.val("xl")

        se = P.op("scalar", lambda e: e.activation(logd, dec_cm, AF.Sigmoid),
                  waits=[("xl", ev_vecs)], inc=("se", 1))
        ve = P.op("vector", lambda e: e.tensor_scalar_max(logd, logd, 1e-7),
                  waits=[("se", se)], inc=("ve", 1))
        ev_logd_se = P.op("scalar", lambda e: e.activation(logd, logd, AF.Ln),
                          waits=[("ve", ve)], inc=("se", 1))

        ev_xln, ev_xlnT = self.emit_ln("l0a")

        wslot_ready = [0, 0]
        wslot_free_te = [0, 0]
        prev_epi_v = 0
        prev_epi_s = 0
        prev_sexp = 0
        for ct in range(ND):
            slot = ct % 2
            w = self.wbuf[slot]
            wts = [("te", wslot_free_te[slot])] if wslot_free_te[slot] else []
            P.op("gpsimd", lambda e, ct=ct, w=w: e.dma_start(
                out=w[:, 0:2048], in_=wk_h[ct]), waits=wts, inc=("wl", 16))
            P.op("gpsimd", lambda e, ct=ct, w=w: e.dma_start(
                out=w[:, 2048:4096], in_=wv_h[ct]), inc=("wl", 16))
            P.op("gpsimd", lambda e, ct=ct, w=w: e.dma_start(
                out=w[:, 4096:6144], in_=wr_h[ct]), inc=("wl", 16))
            wslot_ready[slot] = P.val("wl")

            te_mat = {}
            first = True
            for m in range(3):
                for h in range(2):
                    for dt in range(ND):
                        wts = []
                        if first:
                            wts = [("wl", wslot_ready[slot]), ("ve", ev_xlnT)]
                            if prev_epi_v:
                                wts += [("ve", prev_epi_v), ("se", prev_epi_s)]
                            first = False
                        te_mat[(m, h)] = P.op("tensor", lambda e, m=m, h=h, dt=dt, w=w: e.matmul(
                            pb[m * 2 + h][:, :],
                            w[:, m * 2048 + dt * 128:m * 2048 + (dt + 1) * 128],
                            xlnT[:, dt * TLOC + h * 512:dt * TLOC + (h + 1) * 512],
                            start=(dt == 0), stop=(dt == ND - 1)),
                            waits=wts,
                            inc=("te", 1) if dt == ND - 1 else None)
            wslot_free_te[slot] = te_mat[(2, 1)]

            vk = None
            for h in range(2):
                vk = P.op("vector", lambda e, h=h: e.tensor_copy(
                    scr[1][:, h * 512:(h + 1) * 512], pb[h][:, :]),
                    waits=[("te", te_mat[(0, h)])], inc=("ve", 1))
            va = P.op("vector", lambda e, ct=ct: e.tensor_scalar_mul(
                scr[2][:, :], tg[:, :], logd[:, ct:ct + 1]),
                waits=[("se", ev_logd_se), ("xl", self.ev_setup_xl)] +
                      ([("se", prev_sexp)] if prev_sexp else []),
                inc=("ve", 1))
            vkv = None
            for h in range(2):
                vkv = P.op("vector", lambda e, h=h: e.tensor_tensor(
                    scr[1][:, h * 512:(h + 1) * 512], pb[2 + h][:, :],
                    scr[1][:, h * 512:(h + 1) * 512], op=ALU.mult),
                    waits=[("te", te_mat[(1, h)]), ("ve", vk)], inc=("ve", 1))
            sr = None
            for h in range(2):
                sr = P.op("scalar", lambda e, ct=ct, h=h: e.activation(
                    bigB[:, ct * TLOC + h * 512:ct * TLOC + (h + 1) * 512],
                    pb[4 + h][:, :], AF.Sigmoid),
                    waits=[("te", te_mat[(2, h)])], inc=("se", 1))
            sexp = P.op("scalar", lambda e: e.activation(scr[3][:, :], scr[2][:, :], AF.Exp),
                        waits=[("ve", va)], inc=("se", 1))
            vc = P.op("vector", lambda e: e.tensor_scalar_max(
                scr[2][:, :], scr[3][:, :], 1e-10),
                waits=[("se", sexp)], inc=("ve", 1))
            vi = P.op("vector", lambda e: e.reciprocal(scr[3][:, :], scr[2][:, :]),
                      waits=[("ve", vc)], inc=("ve", 1))
            vs = P.op("vector", lambda e, ct=ct: e.tensor_tensor(
                bigA[:, ct * TLOC:(ct + 1) * TLOC], scr[1][:, :], scr[3][:, :],
                op=ALU.mult),
                waits=[("ve", vi), ("ve", vkv)], inc=("ve", 1))
            P.op("vector", lambda e, ct=ct: e.reduce_sum(
                stab[:, ct:ct + 1], bigA[:, ct * TLOC:(ct + 1) * TLOC], axis=AX.X),
                waits=[("ve", vs)], inc=("ve", 1))
            prev_epi_v, prev_epi_s, prev_sexp = vkv, sr, sexp
        ev_phaseA_v = P.val("ve")
        ev_phaseA_s = P.val("se")
        last_proj_te = P.val("te")

        teS = P.op("tensor", lambda e: e.transpose(
            pb[0][0:16, 0:128], stab, self.identf[:, :]),
            waits=[("ve", ev_phaseA_v)], inc=("te", 1))
        vS = P.op("vector", lambda e: e.tensor_copy(scr[2][0:16, 0:128],
                                                    pb[0][0:16, 0:128]),
                  waits=[("te", teS)], inc=("ve", 1))
        dS = P.op("sync", lambda e: e.dma_start(out=cs_in.ap(), in_=scr[2][0:16, 0:128]),
                  waits=[("ve", vS)], inc=("st", 16))
        cc = P.op("gpsimd", lambda e: e.collective_compute(
            "AllGather", ALU.bypass, replica_groups=[list(range(NCORES))],
            ins=[cs_in.ap().opt()], outs=[cs_out.ap().opt()]),
            waits=[("st", dS)], inc=("cc", 1))
        d2 = P.op("sync", lambda e: e.dma_start(out=scr[1][:, 0:128], in_=cs_out.ap()),
                  waits=[("cc", cc)], inc=("xl", 16))
        tc = P.op("tensor", lambda e: e.matmul(
            pb[1][:, 0:16], scr[1][:, 0:128], cmask_sb, start=True, stop=True),
            waits=[("xl", d2)], inc=("te", 1))
        ev_carry = P.op("vector", lambda e: e.tensor_copy(carry, pb[1][:, 0:16]),
                        waits=[("te", tc)], inc=("ve", 1))

        prev_sexp = 0
        for ct in range(ND):
            va = P.op("vector", lambda e, ct=ct: e.tensor_scalar_mul(
                scr[2][:, :], tg[:, :], logd[:, ct:ct + 1]),
                waits=([("ve", ev_carry)] if ct == 0 else []) +
                      ([("se", prev_sexp)] if prev_sexp else []),
                inc=("ve", 1))
            sexp = P.op("scalar", lambda e: e.activation(scr[3][:, :], scr[2][:, :], AF.Exp),
                        waits=[("ve", va), ("se", ev_phaseA_s)], inc=("se", 1))
            vs0 = P.op("vector", lambda e, ct=ct: e.tensor_tensor_scan(
                scr[0][:, 0:512], self.ones[:, :],
                bigA[:, ct * TLOC:ct * TLOC + 512],
                carry[:, ct:ct + 1], op0=ALU.mult, op1=ALU.add),
                waits=[("ve", ev_carry)] if ct == 0 else (), inc=("ve", 1))
            vcp = P.op("vector", lambda e: e.tensor_copy(carry2, scr[0][:, 511:512]),
                       waits=[("ve", vs0)], inc=("ve", 1))
            vs1 = P.op("vector", lambda e, ct=ct: e.tensor_tensor_scan(
                scr[0][:, 512:1024], self.ones[:, :],
                bigA[:, ct * TLOC + 512:(ct + 1) * TLOC],
                carry2, op0=ALU.mult, op1=ALU.add),
                waits=[("ve", vcp)], inc=("ve", 1))
            vst = P.op("vector", lambda e: e.tensor_tensor(
                scr[1][:, :], scr[0][:, :], scr[3][:, :], op=ALU.mult),
                waits=[("se", sexp), ("ve", vs1)], inc=("ve", 1))
            vo = P.op("vector", lambda e, ct=ct: e.tensor_tensor(
                bigA[:, ct * TLOC:(ct + 1) * TLOC], scr[1][:, :],
                bigB[:, ct * TLOC:(ct + 1) * TLOC], op=ALU.mult),
                waits=[("ve", vst)], inc=("ve", 1))
            P.op("scalar", lambda e, ct=ct: e.activation(
                scr[1][:, :], bigA[:, ct * TLOC:(ct + 1) * TLOC], AF.Square,
                accum_out=sqt[:, ct:ct + 1]),
                waits=[("ve", vo)], inc=("se", 1))
            P.op("vector", lambda e, ct=ct: e.reduce_sum(
                sxt[:, ct:ct + 1], bigA[:, ct * TLOC:(ct + 1) * TLOC], axis=AX.X),
                waits=[("ve", vo)], inc=("ve", 1))
            prev_sexp = sexp
        ev_phaseB_v = P.val("ve")
        ev_phaseB_s = P.val("se")

        P.op("tensor", lambda e: e.matmul(
            pb[2][0:1, 0:ND], self.ones[:, 0:1], sxt, start=True, stop=True),
            waits=[("ve", ev_phaseB_v), ("se", ev_phaseB_s),
                   ("ge", self.ev_setup_ge)])
        tS = P.op("tensor", lambda e: e.matmul(
            pb[2][0:1, ND:2 * ND], self.ones[:, 0:1], sqt, start=True, stop=True),
            inc=("te", 1))
        vg1 = P.op("vector", lambda e: e.reduce_sum(
            scr[2][0:1, 0:1], pb[2][0:1, 0:ND], axis=AX.X),
            waits=[("te", tS)], inc=("ve", 1))
        vg2 = P.op("vector", lambda e: e.reduce_sum(
            scr[2][0:1, 1:2], pb[2][0:1, ND:2 * ND], axis=AX.X),
            waits=[("ve", vg1)], inc=("ve", 1))
        dG = P.op("sync", lambda e: e.dma_start(out=gs_in.ap(), in_=scr[2][0:1, 0:2]),
                  waits=[("ve", vg2)], inc=("st", 16))
        ccG = P.op("gpsimd", lambda e: e.collective_compute(
            "AllReduce", ALU.add, replica_groups=[list(range(NCORES))],
            ins=[gs_in.ap().opt()], outs=[gs_out.ap().opt()]),
            waits=[("st", dG)], inc=("cc", 1))
        dG2 = P.op("sync", lambda e: e.dma_start(out=scr[2][0:1, 4:6], in_=gs_out.ap()),
                   waits=[("cc", ccG)], inc=("xl", 16))
        NELT = float(T * D)
        vm = P.op("vector", lambda e: e.tensor_scalar_mul(
            scr[2][0:1, 6:7], scr[2][0:1, 4:5], 1.0 / NELT),
            waits=[("xl", dG2)], inc=("ve", 1))
        ve2 = P.op("vector", lambda e: e.tensor_scalar_mul(
            scr[2][0:1, 7:8], scr[2][0:1, 5:6], 1.0 / NELT),
            waits=[("ve", vm)], inc=("ve", 1))
        vv = P.op("vector", lambda e: e.tensor_tensor(
            scr[2][0:1, 8:9], scr[2][0:1, 6:7], scr[2][0:1, 6:7], op=ALU.mult),
            waits=[("ve", ve2)], inc=("ve", 1))
        vv2 = P.op("vector", lambda e: e.scalar_tensor_tensor(
            scr[2][0:1, 8:9], scr[2][0:1, 8:9], -1.0, scr[2][0:1, 7:8],
            op0=ALU.mult, op1=ALU.add),
            waits=[("ve", vv)], inc=("ve", 1))
        sG = P.op("scalar", lambda e: e.activation(
            scr[2][0:1, 9:10], scr[2][0:1, 8:9], AF.Sqrt, bias=self.epscol[0:1, :]),
            waits=[("ve", vv2)], inc=("se", 1))
        vr = P.op("vector", lambda e: e.reciprocal(scr[2][0:1, 10:11], scr[2][0:1, 9:10]),
                  waits=[("se", sG)], inc=("ve", 1))
        vnm = P.op("vector", lambda e: e.tensor_tensor(
            scr[2][0:1, 11:12], scr[2][0:1, 6:7], scr[2][0:1, 10:11], op=ALU.mult),
            waits=[("ve", vr)], inc=("ve", 1))
        vnm2 = P.op("vector", lambda e: e.tensor_scalar_mul(
            scr[2][0:1, 11:12], scr[2][0:1, 11:12], -1.0),
            waits=[("ve", vnm)], inc=("ve", 1))
        tb = P.op("tensor", lambda e: e.matmul(
            pb[3][:, 0:2], self.ones[0:1, 0:128], scr[2][0:1, 10:12],
            start=True, stop=True),
            waits=[("ve", vnm2)], inc=("te", 1))
        vb = P.op("vector", lambda e: e.tensor_copy(scr[2][:, 16:18], pb[3][:, 0:2]),
                  waits=[("te", tb)], inc=("ve", 1))
        rsd_col = scr[2][:, 16:17]
        nmr_col = scr[2][:, 17:18]
        vA = P.op("vector", lambda e: e.tensor_scalar_mul(Acol, gnw_cm, rsd_col),
                  waits=[("ve", vb)], inc=("ve", 1))
        vB = P.op("vector", lambda e: e.tensor_scalar_mul(Bcol, gnw_cm, nmr_col),
                  waits=[("ve", vA)], inc=("ve", 1))
        ev_AB = P.op("vector", lambda e: e.tensor_tensor(Bcol, Bcol, gnb_cm, op=ALU.add),
                     waits=[("ve", vB)], inc=("ve", 1))

        normed = xlnT
        for ct in range(ND):
            P.op("vector", lambda e, ct=ct: e.tensor_scalar(
                normed[:, ct * TLOC:(ct + 1) * TLOC],
                bigA[:, ct * TLOC:(ct + 1) * TLOC],
                Acol[:, ct:ct + 1], Bcol[:, ct:ct + 1],
                op0=ALU.mult, op1=ALU.add),
                waits=[("ve", ev_AB)] if ct == 0 else (), inc=("ve", 1))
        ev_normed = P.val("ve")

        def wo_view(ct):
            if ct < 8:
                return bigB[:, ct * 2048:(ct + 1) * 2048]
            if ct < 11:
                return self.wbuf[0][:, (ct - 8) * 2048:(ct - 7) * 2048]
            if ct < 14:
                return self.wbuf[1][:, (ct - 11) * 2048:(ct - 10) * 2048]
            return self.scrp[:, (ct - 14) * 1024:(ct - 13) * 1024].bitcast(BF16)

        for ct in range(ND):
            wts = []
            if ct == 0:
                wts = [("ve", ev_phaseB_v), ("se", ev_phaseB_s),
                       ("te", last_proj_te)]
            P.op("gpsimd", lambda e, ct=ct: e.dma_start(out=wo_view(ct), in_=wo_h[ct]),
                 waits=wts, inc=("wl", 16))
        ev_wo = P.val("wl")

        prev_evac = 0
        for i in range(NT):
            te_last = None
            for ct in range(ND):
                for q in range(4):
                    wts = []
                    if ct == 0 and q == 0:
                        wts = [("wl", ev_wo), ("ve", ev_normed)]
                        if prev_evac:
                            wts.append(("ve", prev_evac))
                    te_last = P.op("tensor", lambda e, i=i, ct=ct, q=q: e.matmul(
                        pb[q][:, :],
                        normed[:, ct * TLOC + i * 128:ct * TLOC + (i + 1) * 128],
                        wo_view(ct)[:, q * 512:(q + 1) * 512],
                        start=(ct == 0), stop=(ct == ND - 1)),
                        waits=wts,
                        inc=("te", 1) if (ct == ND - 1 and q == 3) else None)
            for q in range(4):
                prev_evac = P.op("vector", lambda e, i=i, q=q: e.tensor_tensor(
                    res[:, i * D + q * 512:i * D + (q + 1) * 512],
                    pb[q][:, :],
                    res[:, i * D + q * 512:i * D + (q + 1) * 512], op=ALU.add),
                    waits=[("te", te_last)] if q == 0 else (), inc=("ve", 1))

    def l0_ffn(self):
        P, nc = self.P, self.nc
        res, xlnT, bigA, bigB = self.res, self.xlnT, self.bigA, self.bigB
        scr, pb, ptr = self.scr, self.pb, self.ptr
        self.reset_cols()
        HSC = 0.08838834764831845

        wg_h = self.din("l0f_wg", [128, 256])
        wq_h = self.din("l0f_wqkv", [48, 128, D])
        qidx_h = self.din("l0f_qidx", [128, 16], percore=True, dtype=I32)
        yidx_h = self.din("l0f_yidx", [128, 64], percore=True, dtype=I32)
        lidx_h = self.din("l0f_lidx", [128, 1], percore=True, dtype=I32)

        gsel = nc.dram_tensor("l0f_gsel", [32, TLOC], F32, kind="Internal")
        kv_in = nc.dram_tensor("l0f_kv_in", [256, TLOC], BF16, kind="Internal")
        kv_out = nc.dram_tensor("l0f_kv_out", [2048, TLOC], BF16, kind="Internal")
        qag_in = nc.dram_tensor("l0f_qag_in", [2048, TLOC], BF16, kind="Internal")
        qag_out = nc.dram_tensor("l0f_qag_out", [16384, TLOC], BF16, kind="Internal")
        yag_in = [nc.dram_tensor(f"l0f_yag_in{h}", [T, 128], BF16, kind="Internal")
                  for h in range(2)]
        yag_out = [nc.dram_tensor(f"l0f_yag_out{h}", [T * NCORES, 128], BF16,
                                  kind="Internal") for h in range(2)]
        lag_in = nc.dram_tensor("l0f_lag_in", [128, 128], F32, kind="Internal")
        lag_out = nc.dram_tensor("l0f_lag_out", [1024, 128], F32, kind="Internal")

        wg_sb = self.col("wg_sb", 128).bitcast(BF16)
        qidx = self.col("qidx", 16).bitcast(I32)
        yidx = self.col("yidx", 64).bitcast(I32)
        lidx = self.col("lidx", 1).bitcast(I32)
        negm = self.col("negm")
        ssum = self.col("ssum")
        topc = self.col("topc")

        ent = [("ve", P.val("ve")), ("se", P.val("se")), ("te", P.val("te"))]
        P.op("sync", lambda e: e.dma_start(out=qidx, in_=qidx_h), waits=ent,
             inc=("xl", 16))
        P.op("sync", lambda e: e.dma_start(out=yidx, in_=yidx_h), waits=ent,
             inc=("xl", 16))
        P.op("sync", lambda e: e.dma_start(out=lidx, in_=lidx_h), waits=ent,
             inc=("xl", 16))
        P.op("gpsimd", lambda e: e.dma_start(out=wg_sb, in_=wg_h), waits=ent,
             inc=("wl", 16))
        ev_wg = P.val("wl")

        ev_xln, ev_xlnT = self.emit_ln("l0f")

        ind_scr = scr[2][:, 0:16]
        gt1_scr = scr[2][:, 16:32]
        prev_tr_copy = 0
        exp_ev = 0
        for i in range(NT):
            te_g = None
            for dt in range(ND):
                wts = []
                if dt == 0:
                    wts = [("ve", ev_xlnT), ("wl", ev_wg)]
                    if exp_ev:
                        wts.append(("se", exp_ev))
                te_g = P.op("tensor", lambda e, i=i, dt=dt: e.matmul(
                    pb[4][:, 0:16],
                    xlnT[:, dt * TLOC + i * 128:dt * TLOC + (i + 1) * 128],
                    wg_sb[:, dt * 16:(dt + 1) * 16],
                    start=(dt == 0), stop=(dt == ND - 1)),
                    waits=wts, inc=("te", 1) if dt == ND - 1 else None)
            vng = P.op("vector", lambda e: e.reduce_max(
                negm, pb[4][:, 0:16], axis=AX.X, negate=True),
                waits=[("te", te_g)], inc=("ve", 1))
            vind = P.op("vector", lambda e: e.tensor_scalar(
                ind_scr, pb[4][:, 0:16], negm, 0.0, op0=ALU.add, op1=ALU.is_ge),
                waits=[("ve", vng)], inc=("ve", 1))
            exp_ev = P.op("scalar", lambda e: e.activation(
                scr[3][:, 0:16], pb[4][:, 0:16], AF.Exp, bias=negm),
                waits=[("ve", vind)], inc=("se", 1))
            vsum = P.op("vector", lambda e: e.reduce_sum(
                ssum, scr[3][:, 0:16], axis=AX.X),
                waits=[("se", exp_ev)], inc=("ve", 1))
            vtop = P.op("vector", lambda e: e.reciprocal(topc, ssum),
                        waits=[("ve", vsum)], inc=("ve", 1))
            vgt1 = P.op("vector", lambda e: e.tensor_scalar_mul(
                gt1_scr, ind_scr, topc),
                waits=[("ve", vtop)], inc=("ve", 1))
            wts = [("ve", vgt1)]
            if prev_tr_copy:
                wts.append(("ve", prev_tr_copy))
            te_t = P.op("tensor", lambda e: e.transpose(
                pb[5][0:32, 0:128], scr[2][:, 0:32], self.identf[:, :]),
                waits=wts, inc=("te", 1))
            prev_tr_copy = P.op("vector", lambda e, i=i: e.tensor_copy(
                scr[1][0:32, i * 128:(i + 1) * 128], pb[5][0:32, 0:128]),
                waits=[("te", te_t)], inc=("ve", 1))
        dsel = P.op("sync", lambda e: e.dma_start(out=gsel.ap(), in_=scr[1][0:32, :]),
                    waits=[("ve", prev_tr_copy)], inc=("st", 16))

        kacc, vacc = scr[0], scr[1]
        wslot_ready = [0, 0]
        wslot_free = [0, 0]
        prev_bc_consumed = 0
        prev_evac_s = 0
        prev_evac_v = 0
        last_q_evac = 0
        for et in range(48):
            slot = et % 2
            w = self.wbuf[slot][:, 0:2048]
            wts = [("te", wslot_free[slot])] if wslot_free[slot] else []
            if et == 0:
                wts.append(("st", dsel))
            P.op("gpsimd", lambda e, et=et, w=w: e.dma_start(out=w, in_=wq_h[et]),
                 waits=wts, inc=("wl", 16))
            wslot_ready[slot] = P.val("wl")
            role = et // 16
            hh = et % 16
            ev_bc = 0
            if role > 0:
                grow = hh if role == 1 else 16 + hh
                wts = [("st", dsel)]
                if prev_bc_consumed:
                    wts.append(("ve", prev_bc_consumed))
                ev_bc = P.op("gpsimd", lambda e, grow=grow: e.dma_start(
                    out=scr[3][:, :],
                    in_=gsel.ap()[grow:grow + 1, :].to_broadcast([128, TLOC])),
                    waits=wts, inc=("wl", 16))
            te_m = {}
            for th in range(2):
                for dt in range(ND):
                    wts = []
                    if dt == 0 and th == 0:
                        wts = [("wl", wslot_ready[slot]), ("ve", ev_xlnT)]
                        if prev_evac_s:
                            wts.append(("se", prev_evac_s))
                        if prev_evac_v:
                            wts.append(("ve", prev_evac_v))
                    te_m[th] = P.op("tensor", lambda e, th=th, dt=dt, w=w, slot=slot: e.matmul(
                        pb[slot * 2 + th][:, :],
                        w[:, dt * 128:(dt + 1) * 128],
                        xlnT[:, dt * TLOC + th * 512:dt * TLOC + (th + 1) * 512],
                        start=(dt == 0), stop=(dt == ND - 1)),
                        waits=wts, inc=("te", 1) if dt == ND - 1 else None)
            wslot_free[slot] = te_m[1]
            if role == 0:
                ev = None
                for th in range(2):
                    ev = P.op("scalar", lambda e, hh=hh, th=th, slot=slot: e.activation(
                        bigA[:, hh * TLOC + th * 512:hh * TLOC + (th + 1) * 512],
                        pb[slot * 2 + th][:, :], AF.Copy, scale=HSC),
                        waits=[("te", te_m[th])], inc=("se", 1))
                prev_evac_s = ev
                last_q_evac = ev
                prev_evac_v = 0
            else:
                acc = kacc if role == 1 else vacc
                ev = None
                for th in range(2):
                    sl = slice(th * 512, (th + 1) * 512)
                    if hh == 0:
                        ev = P.op("vector", lambda e, th=th, slot=slot, acc=acc, sl=sl: e.tensor_tensor(
                            acc[:, sl], pb[slot * 2 + th][:, :], scr[3][:, sl],
                            op=ALU.mult),
                            waits=[("te", te_m[th]), ("wl", ev_bc)], inc=("ve", 1))
                    else:
                        ev = P.op("vector", lambda e, th=th, slot=slot, sl=sl: e.tensor_tensor(
                            scr[2][:, sl], pb[slot * 2 + th][:, :], scr[3][:, sl],
                            op=ALU.mult),
                            waits=[("te", te_m[th]), ("wl", ev_bc)], inc=("ve", 1))
                        ev = P.op("vector", lambda e, acc=acc, sl=sl: e.tensor_tensor(
                            acc[:, sl], acc[:, sl], scr[2][:, sl], op=ALU.add),
                            waits=[("ve", ev)], inc=("ve", 1))
                prev_bc_consumed = ev
                prev_evac_v = ev
                prev_evac_s = 0
        ev_sel_v = P.val("ve")

        ksh_bf = self.wbuf[0][:, 2048:3072]
        vsh_bf = self.wbuf[0][:, 3072:4096]
        vtok_bf = self.wbuf[0][:, 4096:5120]
        sk = P.op("scalar", lambda e: e.activation(ksh_bf, kacc[:, :], AF.Copy),
                  waits=[("ve", ev_sel_v)], inc=("se", 1))
        sv = P.op("scalar", lambda e: e.activation(vsh_bf, vacc[:, :], AF.Copy),
                  waits=[("ve", ev_sel_v)], inc=("se", 1))
        te_v = None
        for j in range(NT):
            te_v = P.op("tensor", lambda e, j=j: e.transpose(
                ptr[0][:, j * 128:(j + 1) * 128],
                vsh_bf[:, j * 128:(j + 1) * 128], self.identb[:, :]),
                waits=[("se", sv)] if j == 0 else (),
                inc=("te", 1) if j == NT - 1 else None)
        vvt = P.op("vector", lambda e: e.tensor_copy(vtok_bf, ptr[0][:, :]),
                   waits=[("te", te_v)], inc=("ve", 1))
        d_k = P.op("sync", lambda e: e.dma_start(out=kv_in.ap()[0:128, :], in_=ksh_bf),
                   waits=[("se", sk)], inc=("st", 16))
        d_v = P.op("sync", lambda e: e.dma_start(out=kv_in.ap()[128:256, :], in_=vtok_bf),
                   waits=[("ve", vvt)], inc=("st", 16))
        dq = None
        for hh in range(16):
            dq = P.op("sync", lambda e, hh=hh: e.dma_start(
                out=qag_in.ap()[hh * 128:(hh + 1) * 128, :],
                in_=bigA[:, hh * TLOC:(hh + 1) * TLOC]),
                waits=[("se", last_q_evac)] if hh == 0 else (), inc=("st", 16))
        cc1 = P.op("gpsimd", lambda e: e.collective_compute(
            "AllGather", ALU.bypass, replica_groups=[list(range(NCORES))],
            ins=[kv_in.ap().opt()], outs=[kv_out.ap().opt()]),
            waits=[("st", d_v), ("st", d_k)], inc=("cc", 1))
        cc2 = P.op("gpsimd", lambda e: e.collective_compute(
            "AllGather", ALU.bypass, replica_groups=[list(range(NCORES))],
            ins=[qag_in.ap().opt()], outs=[qag_out.ap().opt()]),
            waits=[("st", dq)], inc=("cc", 1))

        dkv = None
        for g in range(NQC):
            r, half = g // 2, g % 2
            dkv = P.op("sync", lambda e, g=g, r=r, half=half: e.dma_start(
                out=bigB[:, g * 512:(g + 1) * 512],
                in_=kv_out.ap()[256 * r:256 * r + 128, half * 512:(half + 1) * 512]),
                waits=[("cc", cc1)] if g == 0 else (), inc=("xl", 16))
            dkv = P.op("sync", lambda e, g=g, r=r, half=half: e.dma_start(
                out=bigB[:, 8192 + g * 512:8192 + (g + 1) * 512],
                in_=kv_out.ap()[256 * r + 128:256 * r + 256,
                                half * 512:(half + 1) * 512]),
                inc=("xl", 16))
        ev_kv_own = P.val("xl")
        for lh in range(2):
            for r in range(NCORES):
                P.op("gpsimd", lambda e, lh=lh, r=r: e.indirect_dma_start(
                    out=bigA[:, lh * 8192 + r * TLOC:lh * 8192 + (r + 1) * TLOC],
                    out_offset=None,
                    in_=qag_out.ap()[0:128, :],
                    in_offset=IndirectOffsetOnAxis(qidx[:, lh * 8 + r:lh * 8 + r + 1], 0)),
                    waits=[("cc", cc2)] if (lh == 0 and r == 0) else (),
                    inc=("wl", 16))
        ev_q_own = P.val("wl")
        if DEBUG_L0F:
            dbg_gsel = nc.dram_tensor("dbg_gsel", [32, 1024], F32, kind="ExternalOutput").ap()
            P.op("sync", lambda e: e.dma_start(out=dbg_gsel, in_=gsel.ap()),
                 waits=[("st", dsel)], inc=("st", 16))
            dbg_q = nc.dram_tensor("dbg_q", [128, 16384], BF16, kind="ExternalOutput").ap()
            dbg_kv = nc.dram_tensor("dbg_kv", [128, 16384], BF16, kind="ExternalOutput").ap()
            dbg_ksel = nc.dram_tensor("dbg_ksel", [128, 1024], F32, kind="ExternalOutput").ap()
            dbg_vsel = nc.dram_tensor("dbg_vsel", [128, 1024], F32, kind="ExternalOutput").ap()
            P.op("sync", lambda e: e.dma_start(out=dbg_q, in_=bigA[:, :]),
                 waits=[("wl", ev_q_own)], inc=("st", 16))
            P.op("sync", lambda e: e.dma_start(out=dbg_kv, in_=bigB[:, :]),
                 waits=[("xl", ev_kv_own)], inc=("st", 16))
            P.op("sync", lambda e: e.dma_start(out=dbg_ksel, in_=kacc[:, :]),
                 waits=[("ve", ev_sel_v)], inc=("st", 16))
            P.op("sync", lambda e: e.dma_start(out=dbg_vsel, in_=vacc[:, :]),
                 waits=[("ve", ev_sel_v)], inc=("st", 16))

        ebuf = scr[0].bitcast(BF16)
        lrow = scr[1][0:1, 0:512]
        lrow_i = scr[1][0:1, 512:1024]
        prev_oevac_s = 0
        prev_levac_v = 0
        dl = 0
        prev_copy = 0
        dyi = None
        cc3 = {}
        for lh in range(2):
            for qg in range(NQC):
                first_mm = True
                nsub = 4 * (qg + 1)
                sub = 0
                ev_exp = {}
                ev_msk = {}
                for kg in range(qg + 1):
                    for j in range(4):
                        ping = sub % 2
                        wts = []
                        if first_mm:
                            wts = [("wl", ev_q_own), ("xl", ev_kv_own)]
                            if prev_oevac_s:
                                wts += [("se", prev_oevac_s), ("ve", prev_levac_v)]
                            first_mm = False
                        elif sub >= 2:
                            wts = [("se", ev_exp[sub - 2])]
                            if (sub - 2) in ev_msk:
                                wts.append(("ge", ev_msk[sub - 2]))
                        P.op("tensor", lambda e, ping=ping, kg=kg, j=j, lh=lh, qg=qg: e.matmul(
                            pb[ping][:, :],
                            bigB[:, kg * 512 + j * 128:kg * 512 + (j + 1) * 128],
                            bigA[:, lh * 8192 + qg * 512:lh * 8192 + (qg + 1) * 512],
                            start=True, stop=True),
                            waits=wts, inc=("te", 1))
                        te_s = P.val("te")
                        eslot = ping
                        ev_exp[sub] = P.op("scalar", lambda e, ping=ping, eslot=eslot: e.activation(
                            ebuf[:, eslot * 512:(eslot + 1) * 512], pb[ping][:, :], AF.Exp),
                            waits=[("te", te_s)], inc=("se", 1))
                        if kg == qg:
                            ev_msk[sub] = P.op("gpsimd", lambda e, j=j, eslot=eslot: e.affine_select(
                                ebuf[:, eslot * 512:(eslot + 1) * 512],
                                ebuf[:, eslot * 512:(eslot + 1) * 512],
                                pattern=[[1, 512]], compare_op=ALU.is_ge, fill=0.0,
                                base=-(j * 128), channel_multiplier=-1),
                                waits=[("se", ev_exp[sub])], inc=("ge", 1))
                        wts_pv = [("se", ev_exp[sub])]
                        if sub in ev_msk:
                            wts_pv.append(("ge", ev_msk[sub]))
                        P.op("tensor", lambda e, kg=kg, j=j, eslot=eslot, sub=sub, nsub=nsub: e.matmul(
                            pb[4][:, :],
                            bigB[:, 8192 + kg * 512 + j * 128:8192 + kg * 512 + (j + 1) * 128],
                            ebuf[:, eslot * 512:(eslot + 1) * 512],
                            start=(sub == 0), stop=(sub == nsub - 1)),
                            waits=wts_pv)
                        P.op("tensor", lambda e, eslot=eslot, sub=sub, nsub=nsub: e.matmul(
                            pb[5][0:1, :], self.onesbf[:, 0:1],
                            ebuf[:, eslot * 512:(eslot + 1) * 512],
                            start=(sub == 0), stop=(sub == nsub - 1)),
                            inc=("te", 1))
                        sub += 1
                te_fin = P.val("te")
                prev_oevac_s = P.op("scalar", lambda e, lh=lh, qg=qg: e.activation(
                    xlnT[:, lh * 8192 + qg * 512:lh * 8192 + (qg + 1) * 512],
                    pb[4][:, :], AF.Copy),
                    waits=[("te", te_fin)], inc=("se", 1))
                vl = P.op("vector", lambda e: e.tensor_copy(lrow, pb[5][0:1, :]),
                          waits=[("te", te_fin)], inc=("ve", 1))
                prev_levac_v = P.op("vector", lambda e: e.reciprocal(lrow_i, lrow),
                                    waits=[("ve", vl)] + ([("st", dl)] if dl else []),
                                    inc=("ve", 1))
                dl = P.op("sync", lambda e, lh=lh, qg=qg: e.dma_start(
                    out=lag_in.ap()[lh * 64 + qg * 4:lh * 64 + (qg + 1) * 4, :],
                    in_=lrow_i),
                    waits=[("ve", prev_levac_v)], inc=("st", 16))
            for b4 in range(16):
                te_y = None
                for j in range(4):
                    a = b4 * 4 + j
                    wts = []
                    if j == 0:
                        wts = [("se", prev_oevac_s)]
                        if prev_copy:
                            wts.append(("ve", prev_copy))
                    te_y = P.op("tensor", lambda e, lh=lh, a=a, j=j: e.transpose(
                        ptr[1][:, j * 128:(j + 1) * 128],
                        xlnT[:, lh * 8192 + a * 128:lh * 8192 + (a + 1) * 128],
                        self.identb[:, :]),
                        waits=wts, inc=("te", 1) if j == 3 else None)
                prev_copy = P.op("vector", lambda e: e.tensor_copy(
                    scr[2].bitcast(BF16)[:, 0:512], ptr[1][:, 0:512]),
                    waits=[("te", te_y)] + ([("st", dyi)] if dyi else []),
                    inc=("ve", 1))
                dyi = P.op("sync", lambda e, lh=lh, b4=b4: e.dma_start(
                    out=yag_in[lh].ap()[b4 * 512:(b4 + 1) * 512, :].rearrange(
                        "(jj p) d -> p jj d", p=128),
                    in_=scr[2].bitcast(BF16)[:, 0:512].rearrange(
                        "p (jj d) -> p jj d", d=128)),
                    waits=[("ve", prev_copy)], inc=("st", 16))
            cc3[lh] = P.op("gpsimd", lambda e, lh=lh: e.collective_compute(
                "AllGather", ALU.bypass, replica_groups=[list(range(NCORES))],
                ins=[yag_in[lh].ap().opt()], outs=[yag_out[lh].ap().opt()]),
                waits=[("st", dyi)], inc=("cc", 1))
        if DEBUG_L0F:
            dbg_y = nc.dram_tensor("dbg_y", [128, 16384], BF16, kind="ExternalOutput").ap()
            dbg_l = nc.dram_tensor("dbg_l", [128, 128], F32, kind="ExternalOutput").ap()
            P.op("sync", lambda e: e.dma_start(out=dbg_y, in_=xlnT[:, :]),
                 waits=[("se", prev_oevac_s)], inc=("st", 16))
            P.op("sync", lambda e: e.dma_start(out=dbg_l, in_=lag_in.ap()),
                 waits=[("st", dl)], inc=("st", 16))
        cc4 = P.op("gpsimd", lambda e: e.collective_compute(
            "AllGather", ALU.bypass, replica_groups=[list(range(NCORES))],
            ins=[lag_in.ap().opt()], outs=[lag_out.ap().opt()]),
            waits=[("st", dl)], inc=("cc", 1))

        for lh in range(2):
            for i in range(NT):
                for r in range(NCORES):
                    P.op("gpsimd", lambda e, lh=lh, i=i, r=r: e.indirect_dma_start(
                        out=bigA[:, i * 2048 + r * 256 + lh * 128:
                                 i * 2048 + r * 256 + (lh + 1) * 128],
                        out_offset=None,
                        in_=yag_out[lh].ap()[0:128, :],
                        in_offset=IndirectOffsetOnAxis(yidx[:, i * 8 + r:i * 8 + r + 1], 0)),
                        waits=[("cc", cc3[lh])] if (i == 0 and r == 0) else (),
                        inc=("wl", 16))
        P.op("gpsimd", lambda e: e.indirect_dma_start(
            out=scr[1][:, 0:128], out_offset=None,
            in_=lag_out.ap()[0:128, :],
            in_offset=IndirectOffsetOnAxis(lidx[:, 0:1], 0)),
            waits=[("cc", cc4)], inc=("wl", 16))
        ev_y_own = P.val("wl")
        te_l = P.op("tensor", lambda e: e.transpose(
            pb[0][:, 0:128], scr[1][:, 0:128], self.identf[:, :]),
            waits=[("wl", ev_y_own)], inc=("te", 1))
        v_l = P.op("vector", lambda e: e.tensor_copy(scr[2][:, 0:128], pb[0][:, 0:128]),
                   waits=[("te", te_l)], inc=("ve", 1))
        linv_cols = scr[2][:, 0:128]
        if DEBUG_L0F:
            dbg_yret = nc.dram_tensor("dbg_yret", [128, 16384], BF16, kind="ExternalOutput").ap()
            dbg_linv = nc.dram_tensor("dbg_linv", [128, 128], F32, kind="ExternalOutput").ap()
            P.op("sync", lambda e: e.dma_start(out=dbg_yret, in_=bigA[:, :]),
                 waits=[("wl", ev_y_own)], inc=("st", 16))
            P.op("sync", lambda e: e.dma_start(out=dbg_linv, in_=linv_cols),
                 waits=[("ve", v_l)], inc=("st", 16))
        for i in range(NT):
            for h in range(16):
                r, sub = h // 2, h % 2
                P.op("vector", lambda e, i=i, h=h, r=r, sub=sub: e.scalar_tensor_tensor(
                    res[:, i * D + h * 128:i * D + (h + 1) * 128],
                    bigA[:, i * 2048 + r * 256 + sub * 128:i * 2048 + r * 256 + (sub + 1) * 128],
                    linv_cols[:, h * 8 + i:h * 8 + i + 1],
                    res[:, i * D + h * 128:i * D + (h + 1) * 128],
                    op0=ALU.mult, op1=ALU.add),
                    waits=[("ve", v_l), ("wl", ev_y_own)] if (i == 0 and h == 0) else (),
                    inc=("ve", 1))

    def l1_attn(self):
        P, nc = self.P, self.nc
        res, xlnT, bigA, bigB = self.res, self.xlnT, self.bigA, self.bigB
        scr, pb, ptr = self.scr, self.pb, self.ptr
        wb0, wb1 = self.wbuf
        self.reset_cols()
        SC = 0.125

        wqk_h = self.din("l1a_wqk", [24, 128, 2048])
        wv_h = self.din("l1a_wv", [3, 128, 8192])
        wo_h = self.din("l1a_wo", [24, 128, 1024])
        fkv_h = self.din("l1a_fkv", [128, 1920])
        pkb_h = self.din("l1a_pkb", [128, 768])
        pvb_h = self.din("l1a_pvb", [128, 1024])
        idx_h = self.din("l1a_idx", [128, 288], percore=True, dtype=I32)

        qk_in = nc.dram_tensor("l1a_qk_in", [3072, TLOC], BF16, kind="Internal")
        qk_out = nc.dram_tensor("l1a_qk_out", [24576, TLOC], BF16, kind="Internal")
        v_in = nc.dram_tensor("l1a_vin", [24576, 64], BF16, kind="Internal")
        v_out = nc.dram_tensor("l1a_vout", [196608, 64], BF16, kind="Internal")
        yag_in = [nc.dram_tensor(f"l1a_yag_in{h}", [T, 64], BF16, kind="Internal")
                  for h in range(3)]
        yag_out = [nc.dram_tensor(f"l1a_yag_out{h}", [T * NCORES, 64], BF16,
                                  kind="Internal") for h in range(3)]

        ve_ent = P.val("ve")
        se_ent = P.val("se")
        te_ent = P.val("te")
        xl_ent = P.val("xl")
        ge_ent = P.val("ge")
        linv = self.col("l1a_linv", 4)

        tgI = self.tg.bitcast(I32)
        P.op("sync", lambda e: e.dma_start(out=tgI[:, 0:288], in_=idx_h),
             waits=[("ve", ve_ent)], inc=("xl", 16))
        ev_idx = P.val("xl")
        P.op("gpsimd", lambda e: e.dma_start(out=wb0[:, 0:1920], in_=fkv_h),
             waits=[("te", te_ent)], inc=("wl", 16))
        P.op("gpsimd", lambda e: e.dma_start(out=wb0[:, 1920:2688], in_=pkb_h),
             inc=("wl", 16))
        P.op("gpsimd", lambda e: e.dma_start(out=wb0[:, 2688:3712], in_=pvb_h),
             inc=("wl", 16))
        ev_tabs = P.val("wl")

        sbf = self.scrp.bitcast(BF16)
        cK, cV = sbf[:, 0:1024], sbf[:, 1024:2048]
        P.op("gpsimd", lambda e: e.memset(sbf[:, 0:2048], 0.0),
             waits=[("ve", ve_ent), ("se", se_ent), ("te", te_ent)], inc=("ge", 1))
        ev_mset = P.val("ge")

        ev_xln, ev_xlnT = self.emit_ln("l1a")
        se_ln = P.val("se")

        prev_cevac = 0
        for th in range(2):
            specs = [(pb[4], 0, 32, 2, 0), (pb[4], 64, 128, 8, 896),
                     (pb[5], 0, 32, 2, 448), (pb[5], 64, 128, 8, 1408)]
            evs = []
            for si, (bank, p0, p1, dt0, c0) in enumerate(specs):
                w = p1 - p0
                n = 16 - dt0
                te_c = None
                for i, dt in enumerate(range(dt0, 16)):
                    wts = []
                    if i == 0 and si == 0:
                        wts = [("wl", ev_tabs), ("ve", ev_xlnT), ("se", se_ln)]
                        if prev_cevac:
                            wts.append(("se", prev_cevac))
                    rhs = xlnT[:, dt * TLOC + th * 512:dt * TLOC + (th + 1) * 512]
                    te_c = P.op("tensor", lambda e, bank=bank, p0=p0, p1=p1, c0=c0,
                                i=i, w=w, n=n, rhs=rhs: e.matmul(
                        bank[p0:p1, :],
                        wb0[:, c0 + i * w:c0 + (i + 1) * w],
                        rhs, start=(i == 0), stop=(i == n - 1)),
                        waits=wts, inc=("te", 1) if i == n - 1 else None)
                evs.append(te_c)
            for si, (bank, p0, p1, dt0, c0) in enumerate(specs):
                dst = cK if si < 2 else cV
                prev_cevac = P.op("scalar", lambda e, bank=bank, p0=p0, p1=p1,
                                  dst=dst, th=th: e.activation(
                    dst[p0:p1, th * 512:(th + 1) * 512], bank[p0:p1, :], AF.Copy),
                    waits=[("te", evs[si])] + ([("ge", ev_mset)] if th == 0 and si == 0 else []),
                    inc=("se", 1))
        ev_c_done = P.val("se")

        ring = [sbf[:, 2048 + s * 1024:2048 + (s + 1) * 1024] for s in range(6)]
        ring_free = [0] * 6
        ystage_free = [0] * 8
        wslot_ready = [0, 0, 0]
        wslot_free = [0, 0, 0]
        psum_free = [0, 0, 0, 0]
        d_qk = 0
        P.op("gpsimd", lambda e: e.dma_start(out=bigB[:, 0:8192], in_=wv_h[0]),
             waits=[("te", te_ent), ("xl", xl_ent), ("ge", ge_ent)], inc=("wl", 16))
        vslot_ready = [P.val("wl"), 0]
        P.op("gpsimd", lambda e: e.dma_start(out=bigB[:, 8192:16384], in_=wv_h[1]),
             inc=("wl", 16))
        vslot_ready[1] = P.val("wl")
        vslot_free = [0, 0]

        for ot in range(24):
            slot = ot % 3
            w = wb1[:, slot * 2048:(slot + 1) * 2048]
            wts = [("te", wslot_free[slot])] if wslot_free[slot] else []
            if ot == 0:
                wts.append(("te", te_ent))
            P.op("gpsimd", lambda e, ot=ot, w=w: e.dma_start(out=w, in_=wqk_h[ot]),
                 waits=wts, inc=("wl", 16))
            wslot_ready[slot] = P.val("wl")
            te_m = {}
            corr = 12 <= ot < 18
            for th in range(2):
                pbi = (ot % 2) * 2 + th
                for dt in range(ND):
                    wts = []
                    if dt == 0:
                        wts = [("wl", wslot_ready[slot])]
                        if ot == 0 and th == 0:
                            wts += [("ve", ev_xlnT), ("se", se_ln)]
                        if psum_free[pbi]:
                            wts.append(("se", psum_free[pbi]))
                        if ot == 12 and th == 0:
                            wts.append(("se", ev_c_done))
                    te_m[th] = P.op("tensor", lambda e, pbi=pbi, w=w, dt=dt, th=th,
                                    corr=corr: e.matmul(
                        pb[pbi][:, :], w[:, dt * 128:(dt + 1) * 128],
                        xlnT[:, dt * TLOC + th * 512:dt * TLOC + (th + 1) * 512],
                        start=(dt == 0), stop=(dt == ND - 1 and not corr)),
                        waits=wts,
                        inc=("te", 1) if (dt == ND - 1 and not corr) else None)
                if corr:
                    otp = ot - 12
                    te_m[th] = P.op("tensor", lambda e, pbi=pbi, otp=otp, th=th: e.matmul(
                        pb[pbi][:, :],
                        wb0[:, 1920 + otp * 128:1920 + (otp + 1) * 128],
                        cK[:, th * 512:(th + 1) * 512],
                        start=False, stop=True), inc=("te", 1))
            wslot_free[slot] = te_m[1]
            rs = ot % 6
            ev2 = None
            for th in range(2):
                pbi = (ot % 2) * 2 + th
                wts = [("te", te_m[th])]
                if th == 0 and ring_free[rs]:
                    wts.append(("st", ring_free[rs]))
                ev2 = P.op("scalar", lambda e, rs=rs, th=th, pbi=pbi, ot=ot: e.activation(
                    ring[rs][:, th * 512:(th + 1) * 512], pb[pbi][:, :], AF.Copy,
                    scale=(SC if ot < 12 else 1.0)),
                    waits=wts, inc=("se", 1))
                psum_free[pbi] = ev2
            d_qk = P.op("sync", lambda e, ot=ot, rs=rs: e.dma_start(
                out=qk_in.ap()[ot * 128:(ot + 1) * 128, :], in_=ring[rs][:, :]),
                waits=[("se", ev2)], inc=("st", 16))
            ring_free[rs] = d_qk

        d_v = 0
        vc_last_te = {}
        for vc in range(3):
            half = vc % 2
            if vc == 2:
                P.op("gpsimd", lambda e: e.dma_start(
                    out=bigB[:, 0:8192], in_=wv_h[2]),
                    waits=[("te", vc_last_te[0])], inc=("wl", 16))
                vslot_ready[0] = P.val("wl")
            wv = bigB[:, half * 8192:(half + 1) * 8192]
            for tt in range(NT):
                pbi = tt % 4
                wts0 = [("wl", vslot_ready[half])]
                if psum_free[pbi]:
                    wts0.append(("se", psum_free[pbi]))
                if vc == 0 and tt == 0:
                    wts0 += [("se", ev_c_done), ("wl", ev_tabs)]
                first_done = False
                if vc < 2:
                    P.op("tensor", lambda e, pbi=pbi, tt=tt, vc=vc: e.matmul(
                        pb[pbi][:, :], cV[:, tt * 128:(tt + 1) * 128],
                        wb0[:, 2688 + vc * 512:2688 + (vc + 1) * 512],
                        start=True, stop=False), waits=wts0)
                    first_done = True
                te_v = None
                for dt in range(ND):
                    wts = [] if (first_done or dt > 0) else wts0
                    te_v = P.op("tensor", lambda e, pbi=pbi, dt=dt, tt=tt, wv=wv,
                                vc=vc: e.matmul(
                        pb[pbi][:, :],
                        xlnT[:, dt * TLOC + tt * 128:dt * TLOC + (tt + 1) * 128],
                        wv[:, dt * 512:(dt + 1) * 512],
                        start=(dt == 0 and vc == 2), stop=(dt == ND - 1)),
                        waits=wts, inc=("te", 1) if dt == ND - 1 else None)
                vc_last_te[vc] = te_v
                rs = (vc * 8 + tt) % 6
                wts = [("te", te_v)]
                if ring_free[rs]:
                    wts.append(("st", ring_free[rs]))
                ev3 = P.op("scalar", lambda e, rs=rs, pbi=pbi: e.activation(
                    ring[rs][:, 0:512], pb[pbi][:, :], AF.Copy),
                    waits=wts, inc=("se", 1))
                psum_free[pbi] = ev3
                for hl in range(8):
                    d_v = P.op("sync", lambda e, vc=vc, tt=tt, hl=hl, rs=rs: e.dma_start(
                        out=v_in.ap()[(vc * 8 + hl) * 1024 + tt * 128:
                                      (vc * 8 + hl) * 1024 + (tt + 1) * 128, :],
                        in_=ring[rs][:, hl * 64:(hl + 1) * 64]),
                        waits=[("se", ev3)] if hl == 0 else (), inc=("st", 16))
                ring_free[rs] = d_v
            vslot_free[half] = vc_last_te[vc]
        ev_p4 = P.val("te")

        cc_qk = P.op("gpsimd", lambda e: e.collective_compute(
            "AllGather", ALU.bypass, replica_groups=[list(range(NCORES))],
            ins=[qk_in.ap().opt()], outs=[qk_out.ap().opt()]),
            waits=[("st", d_qk)], inc=("cc", 1))
        cc_v = P.op("gpsimd", lambda e: e.collective_compute(
            "AllGather", ALU.bypass, replica_groups=[list(range(NCORES))],
            ins=[v_in.ap().opt()], outs=[v_out.ap().opt()]),
            waits=[("st", d_v)], inc=("cc", 1))

        for ht in range(12):
            P.op("gpsimd", lambda e, ht=ht: e.dma_start(
                out=xlnT[:, ht * 1024:(ht + 1) * 1024], in_=wo_h[ht]),
                waits=[("te", ev_p4)] if ht == 0 else (), inc=("wl", 16))
        ev_wo0 = P.val("wl")

        for r in range(NCORES):
            P.op("gpsimd", lambda e, r=r: e.indirect_dma_start(
                out=bigA[:, r * TLOC:(r + 1) * TLOC], out_offset=None,
                in_=qk_out.ap()[0:128, :],
                in_offset=IndirectOffsetOnAxis(tgI[:, 256 + r:257 + r], 0)),
                waits=[("cc", cc_qk), ("xl", ev_idx)] if r == 0 else (),
                inc=("wl", 16))
        for r in range(NCORES):
            P.op("gpsimd", lambda e, r=r: e.indirect_dma_start(
                out=bigA[0:64, 8192 + r * TLOC:8192 + (r + 1) * TLOC],
                out_offset=None,
                in_=qk_out.ap()[0:64, :],
                in_offset=IndirectOffsetOnAxis(tgI[0:64, 264 + r:265 + r], 0)),
                inc=("wl", 16))
        for r in range(NCORES):
            P.op("gpsimd", lambda e, r=r: e.indirect_dma_start(
                out=bigB[:, r * TLOC:(r + 1) * TLOC], out_offset=None,
                in_=qk_out.ap()[0:128, :],
                in_offset=IndirectOffsetOnAxis(tgI[:, 272 + r:273 + r], 0)),
                waits=[("te", ev_p4)] if r == 0 else (), inc=("wl", 16))
        for r in range(NCORES):
            P.op("gpsimd", lambda e, r=r: e.indirect_dma_start(
                out=bigB[0:64, 8192 + r * TLOC:8192 + (r + 1) * TLOC],
                out_offset=None,
                in_=qk_out.ap()[0:64, :],
                in_offset=IndirectOffsetOnAxis(tgI[0:64, 280 + r:281 + r], 0)),
                inc=("wl", 16))
        def vbase(gt):
            if gt < 31:
                return self.wbuf[0][:, gt * 195:gt * 195 + 195]
            if gt < 62:
                return self.wbuf[1][:, (gt - 31) * 195:(gt - 31) * 195 + 195]
            return sbf[:, 1024 + (gt - 62) * 195:1024 + (gt - 62) * 195 + 195]

        P.op("gpsimd", lambda e: e.memset(wb0[:, :], 1.0),
             waits=[("te", ev_p4)], inc=("ge", 1))
        P.op("gpsimd", lambda e: e.memset(wb1[:, :], 1.0), inc=("ge", 1))
        P.op("gpsimd", lambda e: e.memset(sbf[:, 1024:1414], 1.0), inc=("ge", 1))
        ev_vones = P.val("ge")
        for r in range(NCORES):
            for hl in range(3):
                for tt in range(NT):
                    gt = r * NT + tt
                    wts = []
                    if r == 0 and hl == 0 and tt == 0:
                        wts = [("cc", cc_v), ("te", ev_p4), ("ge", ev_vones)]
                    P.op("gpsimd", lambda e, r=r, hl=hl, tt=tt, gt=gt: e.indirect_dma_start(
                        out=vbase(gt)[:, hl * 65:hl * 65 + 64],
                        out_offset=None,
                        in_=v_out.ap()[0:128, :],
                        in_offset=IndirectOffsetOnAxis(
                            tgI[:, (r * 3 + hl) * 8 + tt:(r * 3 + hl) * 8 + tt + 1], 0)),
                        waits=wts, inc=("wl", 16))
        ev_gath = P.val("wl")

        def qap(lh, qc):
            if lh < 2:
                return bigA[lh * 64:(lh + 1) * 64, qc * 512:(qc + 1) * 512]
            return bigA[0:64, 8192 + qc * 512:8192 + (qc + 1) * 512]

        def kap(lh, gt):
            if lh < 2:
                return bigB[lh * 64:(lh + 1) * 64, gt * 128:(gt + 1) * 128]
            return bigB[0:64, 8192 + gt * 128:8192 + (gt + 1) * 128]

        def vap(lh, gt):
            return vbase(gt)[:, lh * 65:(lh + 1) * 65]

        ebuf = sbf[:, 0:1024]
        Oev = scr[1][:, 0:512]
        prev_oevac = {2: 0, 3: 0}
        prev_exp_last = 0
        ev_tr_last = 0
        tr_free = {4: 0, 5: 0}
        ys = [sbf[:, 4096 + s * 64:4096 + (s + 1) * 64] for s in range(8)]
        ysc = 0
        dy = 0
        cc_yh = {}
        first_flash = True
        for lh in range(3):
            for qc in range(NQC):
                ob = 2 + ((lh * NQC + qc) % 2)
                nk = 4 * (qc + 1)
                ev_exp = {}
                ev_msk = {}
                my_exp_last = 0
                for gt in range(nk):
                    ping = gt % 2
                    wts = []
                    if gt == 0:
                        if first_flash:
                            wts = [("wl", ev_gath)]
                            first_flash = False
                        if prev_exp_last:
                            wts.append(("se", prev_exp_last))
                    elif gt >= 2:
                        wts = [("se", ev_exp[gt - 2])]
                        if (gt - 2) in ev_msk:
                            wts.append(("ge", ev_msk[gt - 2]))
                    P.op("tensor", lambda e, ping=ping, lh=lh, gt=gt, qc=qc: e.matmul(
                        pb[ping][:, :], kap(lh, gt), qap(lh, qc),
                        start=True, stop=True), waits=wts, inc=("te", 1))
                    te_s = P.val("te")
                    ev_exp[gt] = P.op("scalar", lambda e, ping=ping: e.activation(
                        ebuf[:, ping * 512:(ping + 1) * 512], pb[ping][:, :],
                        AF.Exp), waits=[("te", te_s)], inc=("se", 1))
                    my_exp_last = ev_exp[gt]
                    if gt // 4 == qc:
                        ev_msk[gt] = P.op("gpsimd", lambda e, gt=gt, ping=ping: e.affine_select(
                            ebuf[:, ping * 512:(ping + 1) * 512],
                            ebuf[:, ping * 512:(ping + 1) * 512],
                            pattern=[[1, 512]], compare_op=ALU.is_ge, fill=0.0,
                            base=-((gt % 4) * 128), channel_multiplier=-1),
                            waits=[("se", ev_exp[gt])], inc=("ge", 1))
                    wts_pv = [("se", ev_exp[gt])]
                    if gt in ev_msk:
                        wts_pv.append(("ge", ev_msk[gt]))
                    if gt == 0 and prev_oevac[ob]:
                        wts_pv.append(("ve", prev_oevac[ob]))
                    P.op("tensor", lambda e, lh=lh, gt=gt, ping=ping, ob=ob,
                         nk=nk: e.matmul(
                        pb[ob][0:65, :], vap(lh, gt),
                        ebuf[:, ping * 512:(ping + 1) * 512],
                        start=(gt == 0), stop=(gt == nk - 1)),
                        waits=wts_pv, inc=("te", 1))
                prev_exp_last = my_exp_last
                te_fin = P.val("te")
                oev = P.op("vector", lambda e, ob=ob: e.tensor_copy(
                    Oev[0:65, :], pb[ob][0:65, :]),
                    waits=[("te", te_fin)] + ([("te", ev_tr_last)] if ev_tr_last else []),
                    inc=("ve", 1))
                prev_oevac[ob] = oev
                for j in range(4):
                    tp = 4 + (j % 2)
                    wts = [("ve", oev)] if j == 0 else []
                    if tr_free[tp]:
                        wts.append(("ve", tr_free[tp]))
                    ev_tr_last = P.op("tensor", lambda e, tp=tp, j=j: e.transpose(
                        pb[tp][:, 0:65], Oev[0:65, j * 128:(j + 1) * 128],
                        self.identf[0:65, 0:65]),
                        waits=wts, inc=("te", 1))
                    rec = P.op("vector", lambda e, tp=tp, j=j: e.reciprocal(
                        linv[:, j:j + 1], pb[tp][:, 64:65]),
                        waits=[("te", ev_tr_last)], inc=("ve", 1))
                    s = ysc % 8
                    ysc += 1
                    wts = [("ve", rec)]
                    if ystage_free[s]:
                        wts.append(("st", ystage_free[s]))
                    evm = P.op("vector", lambda e, tp=tp, j=j, s=s: e.tensor_scalar_mul(
                        ys[s][:, :], pb[tp][:, 0:64], linv[:, j:j + 1]),
                        waits=wts, inc=("ve", 1))
                    tr_free[tp] = evm
                    dy = P.op("sync", lambda e, lh=lh, qc=qc, j=j, s=s: e.dma_start(
                        out=yag_in[lh].ap()[qc * 512 + j * 128:
                                            qc * 512 + (j + 1) * 128, :],
                        in_=ys[s][:, :]),
                        waits=[("ve", evm)], inc=("st", 16))
                    ystage_free[s] = dy
            cc_yh[lh] = P.op("gpsimd", lambda e, lh=lh: e.collective_compute(
                "AllGather", ALU.bypass, replica_groups=[list(range(NCORES))],
                ins=[yag_in[lh].ap().opt()], outs=[yag_out[lh].ap().opt()]),
                waits=[("st", dy)], inc=("cc", 1))
        te_flash = P.val("te")
        se_flash = P.val("se")

        for ht in range(12):
            dst = wb0 if ht < 6 else wb1
            P.op("gpsimd", lambda e, ht=ht, dst=dst: e.dma_start(
                out=dst[:, (ht % 6) * 1024:(ht % 6 + 1) * 1024], in_=wo_h[12 + ht]),
                waits=[("te", te_flash)] if ht == 0 else (), inc=("wl", 16))
        ev_wo1 = P.val("wl")

        for hl in range(3):
            for tt in range(NT):
                for r in range(NCORES):
                    wts = []
                    if tt == 0 and r == 0:
                        wts = [("cc", cc_yh[hl])]
                        if hl == 0:
                            wts.append(("te", te_flash))
                    P.op("gpsimd", lambda e, hl=hl, tt=tt, r=r: e.indirect_dma_start(
                        out=bigB[:, tt * 1536 + r * 192 + hl * 64:
                                 tt * 1536 + r * 192 + (hl + 1) * 64],
                        out_offset=None,
                        in_=yag_out[hl].ap()[0:128, :],
                        in_offset=IndirectOffsetOnAxis(
                            tgI[:, 192 + tt * 8 + r:192 + tt * 8 + r + 1], 0)),
                        waits=wts, inc=("wl", 16))
        ev_ygath = P.val("wl")

        ev_ytr = {}
        prev_ycopy = {0: 0, 1: 0}
        for tt in range(NT):
            for ht in range(12):
                b = ht % 2
                wts = [("wl", ev_ygath)] if (tt == 0 and ht == 0) else []
                if prev_ycopy[b]:
                    wts.append(("ve", prev_ycopy[b]))
                te_y = P.op("tensor", lambda e, tt=tt, ht=ht, b=b: e.transpose(
                    ptr[b][:, 0:128],
                    bigB[:, tt * 1536 + ht * 128:tt * 1536 + (ht + 1) * 128],
                    self.identb[:, :]), waits=wts, inc=("te", 1))
                prev_ycopy[b] = P.op("vector", lambda e, tt=tt, ht=ht, b=b: e.tensor_copy(
                    bigA[:, tt * 1536 + ht * 128:tt * 1536 + (ht + 1) * 128],
                    ptr[b][:, 0:128]), waits=[("te", te_y)], inc=("ve", 1))
            ev_ytr[tt] = prev_ycopy[(11) % 2]

        wo_free = [0, 0]
        for tt in range(NT):
            for dc in range(4):
                pbi = (tt * 4 + dc) % 2
                te_o = None
                for ht in range(12):
                    wts = []
                    if ht == 0:
                        wts = [("ve", ev_ytr[tt])]
                        if tt == 0 and dc == 0:
                            wts.append(("se", se_flash))
                        wts.append(("wl", ev_wo0 if dc < 2 else ev_wo1))
                        if wo_free[pbi]:
                            wts.append(("ve", wo_free[pbi]))
                    if dc < 2:
                        rhs = xlnT[:, ht * 1024 + (dc % 2) * 512:
                                   ht * 1024 + (dc % 2) * 512 + 512]
                    else:
                        src = wb0 if ht < 6 else wb1
                        rhs = src[:, (ht % 6) * 1024 + (dc % 2) * 512:
                                  (ht % 6) * 1024 + (dc % 2) * 512 + 512]
                    te_o = P.op("tensor", lambda e, pbi=pbi, tt=tt, ht=ht,
                                rhs=rhs: e.matmul(
                        pb[pbi][:, :],
                        bigA[:, tt * 1536 + ht * 128:tt * 1536 + (ht + 1) * 128],
                        rhs, start=(ht == 0), stop=(ht == 11)),
                        waits=wts, inc=("te", 1) if ht == 11 else None)
                wo_free[pbi] = P.op("vector", lambda e, pbi=pbi, tt=tt, dc=dc: e.tensor_tensor(
                    res[:, tt * D + dc * 512:tt * D + (dc + 1) * 512],
                    pb[pbi][:, :],
                    res[:, tt * D + dc * 512:tt * D + (dc + 1) * 512],
                    op=ALU.add), waits=[("te", te_o)], inc=("ve", 1))

    def l1_ffn(self):
        P, nc = self.P, self.nc
        res, xlnT, bigA, bigB = self.res, self.xlnT, self.bigA, self.bigB
        scr, pb = self.scr, self.pb
        self.reset_cols()
        NI = 11
        NSI = 22

        rw_h = self.din("l1f_rw", [128, 128])
        gw_h = self.din("l1f_gw", [8, NI, 128, D])
        uw_h = self.din("l1f_uw", [8, NI, 128, D])
        dw_h = self.din("l1f_dw", [8, NI, 128, D])
        sgw_h = self.din("l1f_sgw", [NSI, 128, D])
        suw_h = self.din("l1f_suw", [NSI, 128, D])
        sdw_h = self.din("l1f_sdw", [NSI, 128, D])
        sgb_h = self.din("l1f_sgb", [128, NSI])
        sub_h = self.din("l1f_sub", [128, NSI])
        sdb_h = self.din("l1f_sdb", [1, D])

        rw_sb = self.col("rw_sb", 64).bitcast(BF16)
        ctab = self.col("ctab", 64)
        sgb_cm = self.col("sgb", NSI)
        sub_cm = self.col("sub", NSI)
        nm1 = self.col("nm1")
        nm2 = self.col("nm2")
        w1c = self.col("w1c")
        w2c = self.col("w2c")
        dcol = self.col("dcol")

        ent = [("ve", P.val("ve")), ("se", P.val("se")), ("te", P.val("te"))]
        self._l1f_te_ent = P.val("te")
        P.op("gpsimd", lambda e: e.dma_start(out=rw_sb, in_=rw_h), waits=ent,
             inc=("wl", 16))
        P.op("sync", lambda e: e.dma_start(out=sgb_cm, in_=sgb_h), waits=ent,
             inc=("xl", 16))
        P.op("sync", lambda e: e.dma_start(out=sub_cm, in_=sub_h), waits=ent,
             inc=("xl", 16))
        ev_rw = P.val("wl")
        ev_b = P.val("xl")
        if DEBUG_L0F:
            dbg_b0 = nc.dram_tensor("dbg_b0", [128, 124], F32, kind="ExternalOutput").ap()
            P.op("sync", lambda e: e.dma_start(out=dbg_b0, in_=scr[2][:, 900:1024]),
                 inc=("st", 16))

        ev_xln, ev_xlnT = self.emit_ln("l1f")

        c1s = scr[2][:, 0:8]
        lms = scr[2][:, 8:16]
        c2s = scr[2][:, 16:24]
        t1s = scr[2][:, 24:32]
        prev_comb = 0
        for tt in range(NT):
            te_r = None
            for dt in range(ND):
                wts = []
                if dt == 0:
                    wts = [("ve", ev_xlnT), ("wl", ev_rw)]
                    if prev_comb:
                        wts.append(("ve", prev_comb))
                te_r = P.op("tensor", lambda e, tt=tt, dt=dt: e.matmul(
                    pb[4][:, 0:8],
                    xlnT[:, dt * TLOC + tt * 128:dt * TLOC + (tt + 1) * 128],
                    rw_sb[:, dt * 8:(dt + 1) * 8],
                    start=(dt == 0), stop=(dt == ND - 1)),
                    waits=wts, inc=("te", 1) if dt == ND - 1 else None)
            v1 = P.op("vector", lambda e: e.reduce_max(
                nm1, pb[4][:, 0:8], axis=AX.X, negate=True),
                waits=[("te", te_r)], inc=("ve", 1))
            v2 = P.op("vector", lambda e: e.tensor_scalar(
                c1s, pb[4][:, 0:8], nm1, 0.0, op0=ALU.add, op1=ALU.is_ge),
                waits=[("ve", v1)], inc=("ve", 1))
            v3 = P.op("vector", lambda e: e.scalar_tensor_tensor(
                lms, c1s, -1e30, pb[4][:, 0:8], op0=ALU.mult, op1=ALU.add),
                waits=[("ve", v2)], inc=("ve", 1))
            v4 = P.op("vector", lambda e: e.reduce_max(
                nm2, lms, axis=AX.X, negate=True),
                waits=[("ve", v3)], inc=("ve", 1))
            v5 = P.op("vector", lambda e: e.tensor_scalar(
                c2s, lms, nm2, 0.0, op0=ALU.add, op1=ALU.is_ge),
                waits=[("ve", v4)], inc=("ve", 1))
            v6 = P.op("vector", lambda e: e.tensor_tensor(
                dcol, nm2, nm1, op=ALU.subtract),
                waits=[("ve", v5)], inc=("ve", 1))
            s1 = P.op("scalar", lambda e: e.activation(w1c, dcol, AF.Sigmoid),
                      waits=[("ve", v6)], inc=("se", 1))
            v7 = P.op("vector", lambda e: e.tensor_scalar(
                w2c, w1c, -1.0, 1.0, op0=ALU.mult, op1=ALU.add),
                waits=[("se", s1)], inc=("ve", 1))
            v8 = P.op("vector", lambda e: e.tensor_scalar_mul(t1s, c1s, w1c),
                      waits=[("ve", v7)], inc=("ve", 1))
            prev_comb = P.op("vector", lambda e, tt=tt: e.scalar_tensor_tensor(
                ctab[:, tt * 8:(tt + 1) * 8], c2s, w2c, t1s,
                op0=ALU.mult, op1=ALU.add),
                waits=[("ve", v8)], inc=("ve", 1))
        ev_ctab = prev_comb

        silu_bf = scr[0].bitcast(BF16)
        up_bf = scr[1].bitcast(BF16)
        self._l1f_evb = ev_b

        def ffn_pass(nit, gsrc, usrc, dsrc, gbias, ubias, comb_col, sdb_src,
                     first_wait_v, first_wait_s):
            wfree = [0, 0]
            prev_sil = 0
            prev_hmul = 0
            for it in range(nit):
                slot = it % 2
                w = self.wbuf[slot]
                wts = [("te", wfree[slot])] if wfree[slot] else []
                if it == 0:
                    wts += [("ve", first_wait_v), ("se", first_wait_s)]
                    if not wfree[slot] and self._l1f_te_ent:
                        wts.append(("te", self._l1f_te_ent))
                P.op("gpsimd", lambda e, it=it, w=w: e.dma_start(
                    out=w[:, 0:2048], in_=gsrc(it)), waits=wts, inc=("wl", 16))
                P.op("gpsimd", lambda e, it=it, w=w: e.dma_start(
                    out=w[:, 2048:4096], in_=usrc(it)), inc=("wl", 16))
                wready = P.val("wl")
                te_g = {}
                te_u = {}
                for th in range(2):
                    for dt in range(ND):
                        wts = []
                        if th == 0 and dt == 0:
                            wts = [("wl", wready), ("ve", ev_xlnT)]
                            if prev_hmul:
                                wts.append(("ve", prev_hmul))
                            if prev_sil:
                                wts.append(("se", prev_sil))
                        te_g[th] = P.op("tensor", lambda e, th=th, dt=dt, w=w: e.matmul(
                            pb[th][:, :], w[:, dt * 128:(dt + 1) * 128],
                            xlnT[:, dt * TLOC + th * 512:dt * TLOC + (th + 1) * 512],
                            start=(dt == 0), stop=(dt == ND - 1)),
                            waits=wts, inc=("te", 1) if dt == ND - 1 else None)
                for th in range(2):
                    for dt in range(ND):
                        te_u[th] = P.op("tensor", lambda e, th=th, dt=dt, w=w: e.matmul(
                            pb[2 + th][:, :], w[:, 2048 + dt * 128:2048 + (dt + 1) * 128],
                            xlnT[:, dt * TLOC + th * 512:dt * TLOC + (th + 1) * 512],
                            start=(dt == 0), stop=(dt == ND - 1)),
                            inc=("te", 1) if dt == ND - 1 else None)
                wfree[slot] = te_u[1]
                for th in range(2):
                    kw = {}
                    if gbias is not None:
                        kw["bias"] = gbias(it)
                    prev_sil = P.op("scalar", lambda e, th=th, it=it, kw=kw: e.activation(
                        silu_bf[:, th * 512:(th + 1) * 512], pb[th][:, :],
                        AF.Silu, **kw),
                        waits=[("te", te_g[th])] +
                              ([("xl", self._l1f_evb)] if kw else []),
                        inc=("se", 1))
                    if ubias is not None:
                        vub = P.op("vector", lambda e, th=th, it=it: e.tensor_scalar(
                            up_bf[:, th * 512:(th + 1) * 512], pb[2 + th][:, :],
                            ubias(it), None, op0=ALU.add),
                            waits=[("te", te_u[th]), ("xl", self._l1f_evb)],
                            inc=("ve", 1))
                        prev_hmul = P.op("vector", lambda e, th=th, it=it: e.tensor_tensor(
                            bigA[:, it * TLOC + th * 512:it * TLOC + (th + 1) * 512],
                            up_bf[:, th * 512:(th + 1) * 512],
                            silu_bf[:, th * 512:(th + 1) * 512], op=ALU.mult),
                            waits=[("se", prev_sil), ("ve", vub)], inc=("ve", 1))
                    else:
                        prev_hmul = P.op("vector", lambda e, th=th, it=it: e.tensor_tensor(
                            bigA[:, it * TLOC + th * 512:it * TLOC + (th + 1) * 512],
                            pb[2 + th][:, :],
                            silu_bf[:, th * 512:(th + 1) * 512], op=ALU.mult),
                            waits=[("se", prev_sil), ("te", te_u[th])], inc=("ve", 1))
            if DEBUG_L0F and not hasattr(self, "_l1f_d2"):
                self._l1f_d2 = True
                dbg_up2 = nc.dram_tensor("dbg_up2", [128, 2048], BF16, kind="ExternalOutput").ap()
                dbg_bias = nc.dram_tensor("dbg_bias", [128, 124], F32, kind="ExternalOutput").ap()
                P.op("sync", lambda e: e.dma_start(out=dbg_up2, in_=up_bf[:, :]),
                     waits=[("ve", prev_hmul)], inc=("st", 16))
                P.op("sync", lambda e: e.dma_start(out=dbg_bias, in_=scr[2][:, 900:1024]),
                     waits=[("ve", prev_hmul)], inc=("st", 16))
            ev_h = prev_hmul
            piece = 1024
            nphase = D // piece
            prev_ev = 0
            dmaD = None
            for ph in range(nphase):
                for it in range(nit):
                    wts = []
                    if it == 0:
                        wts = [("ve", ev_h)]
                        if prev_ev:
                            wts.append(("ve", prev_ev))
                        if dmaD:
                            wts.append(dmaD)
                    P.op("gpsimd", lambda e, ph=ph, it=it: e.dma_start(
                        out=bigB[:, it * piece:(it + 1) * piece],
                        in_=dsrc(it)[:, ph * piece:(ph + 1) * piece]),
                        waits=wts, inc=("wl", 16))
                dmaD = ("wl", P.val("wl"))
                if sdb_src is not None:
                    dsb = P.op("sync", lambda e, ph=ph: e.dma_start(
                        out=scr[3][0:1, 0:piece],
                        in_=sdb_src[0:1, ph * piece:(ph + 1) * piece]),
                        waits=[("ve", prev_ev)] if prev_ev else [("ve", ev_h)],
                        inc=("xl", 16))
                    dmaD2 = ("xl", dsb)
                else:
                    dmaD2 = None
                nq = piece // 512
                for tt in range(NT):
                    te_d = {}
                    for it in range(nit):
                        for q2 in range(nq):
                            wts = []
                            if it == 0 and q2 == 0:
                                wts = [dmaD, ("ve", ev_h)] + ([dmaD2] if dmaD2 else [])
                                if tt == 0 and ph == 0:
                                    wts.append(("ve", ev_ctab))
                                if prev_ev:
                                    wts.append(("ve", prev_ev))
                            te_d[q2] = P.op("tensor", lambda e, it=it, q2=q2, tt=tt: e.matmul(
                                pb[4 + q2][:, :],
                                bigA[:, it * TLOC + tt * 128:it * TLOC + (tt + 1) * 128],
                                bigB[:, it * piece + q2 * 512:it * piece + (q2 + 1) * 512],
                                start=(it == 0), stop=(it == nit - 1 and sdb_src is None)),
                                waits=wts,
                                inc=("te", 1) if (it == nit - 1 and q2 == nq - 1 and sdb_src is None) else None)
                    if sdb_src is not None:
                        for q2 in range(nq):
                            te_d[q2] = P.op("tensor", lambda e, q2=q2: e.matmul(
                                pb[4 + q2][:, :], self.ones[0:1, 0:128],
                                scr[3][0:1, q2 * 512:(q2 + 1) * 512],
                                start=False, stop=True),
                                inc=("te", 1) if q2 == nq - 1 else None)
                    for q2 in range(nq):
                        col0 = tt * D + ph * piece + q2 * 512
                        if comb_col is not None:
                            prev_ev = P.op("vector", lambda e, tt=tt, q2=q2, col0=col0: e.scalar_tensor_tensor(
                                res[:, col0:col0 + 512], pb[4 + q2][:, :],
                                comb_col(tt), res[:, col0:col0 + 512],
                                op0=ALU.mult, op1=ALU.add),
                                waits=[("te", te_d[nq - 1])] if q2 == 0 else (),
                                inc=("ve", 1))
                        else:
                            prev_ev = P.op("vector", lambda e, tt=tt, q2=q2, col0=col0: e.tensor_tensor(
                                res[:, col0:col0 + 512], pb[4 + q2][:, :],
                                res[:, col0:col0 + 512], op=ALU.add),
                                waits=[("te", te_d[nq - 1])] if q2 == 0 else (),
                                inc=("ve", 1))
            return prev_ev, P.val("se")

        fw_v, fw_s = ev_ctab, P.val("se")
        for e_i in range(L1F_NEXP):
            fw_v, fw_s = ffn_pass(
                NI,
                lambda it, e_i=e_i: gw_h[e_i, it],
                lambda it, e_i=e_i: uw_h[e_i, it],
                lambda it, e_i=e_i: dw_h[e_i, it],
                None, None,
                lambda tt, e_i=e_i: ctab[:, tt * 8 + e_i:tt * 8 + e_i + 1],
                None, fw_v, fw_s)
        if not L1F_SHARED:
            return
        fw_v, fw_s = ffn_pass(
            L1F_NIT or NI,
            lambda it: sgw_h[it], lambda it: suw_h[it], lambda it: sdw_h[it],
            lambda it: sgb_cm[:, it:it + 1], lambda it: sub_cm[:, it:it + 1],
            None, (sdb_h if not L1F_NOSDB else None),
            fw_v, fw_s)
        if not L1F_SH2:
            return
        ffn_pass(
            NSI - NI,
            lambda it: sgw_h[NI + it], lambda it: suw_h[NI + it],
            lambda it: sdw_h[NI + it],
            lambda it: sgb_cm[:, NI + it:NI + it + 1],
            lambda it: sub_cm[:, NI + it:NI + it + 1],
            None, None,
            fw_v, fw_s)

    def finish(self):
        P = self.P
        allv = P.val("ve")
        for i in range(NT):
            P.op("sync", lambda e, i=i: e.dma_start(
                out=self.y_out[i * 128:(i + 1) * 128, :],
                in_=self.res[:, i * D:(i + 1) * D]),
                waits=[("ve", allv)] if i == 0 else (), inc=("st", 16))
        final = P.val("st")
        P.op("sync", lambda e: e.nop(), waits=[("st", final)])


def prep_host(inp):
    H = {"ident_h": np.eye(128, dtype=np.float32)}
    f = lambda k: np.asarray(inp[k], dtype=np.float32)
    H["l0a_wr"] = _retile_dmajor(f("L0_attn_Wr"))
    H["l0a_wk"] = _retile_dmajor(f("L0_attn_Wk"))
    H["l0a_wv"] = _retile_dmajor(f("L0_attn_Wv"))
    H["l0a_wo"] = _rows_block(np.ascontiguousarray(f("L0_attn_Wo").T))
    H["l0a_decay"] = _cm_vec(f("L0_attn_decay"))
    H["l0a_gnw"] = _cm_vec(f("L0_attn_gn_weight"))
    H["l0a_gnb"] = _cm_vec(f("L0_attn_gn_bias"))
    wg = f("L0_ffn_W_gate")
    H["l0f_wg"] = np.ascontiguousarray(
        wg.reshape(16, 16, 128).transpose(2, 1, 0).reshape(128, 256))
    H["l0f_wqkv"] = _retile_dmajor(f("L0_ffn_W_qkv"))

    WQ, WK, WV, WO = (f("L1_attn_W_Q"), f("L1_attn_W_K"),
                      f("L1_attn_W_V"), f("L1_attn_W_O"))
    wq_t = WQ.reshape(16, 128, 12, 128).transpose(2, 1, 0, 3).reshape(12, 128, 2048)
    wk_t = WK.reshape(16, 128, 12, 128).transpose(2, 1, 0, 3).reshape(12, 128, 2048)
    H["l1a_wqk"] = np.ascontiguousarray(np.concatenate([wq_t, wk_t], axis=0))
    H["l1a_wv"] = np.ascontiguousarray(
        WV.reshape(16, 128, 3, 512).transpose(2, 1, 0, 3).reshape(3, 128, 8192))
    H["l1a_wo"] = np.ascontiguousarray(
        WO.reshape(12, 128, 2, 1024).transpose(2, 0, 1, 3).reshape(24, 128, 1024))
    fkv = np.zeros((128, 1920), dtype=np.float32)
    fkv[:, 0:448] = f("L1_attn_FK0").reshape(14, 128, 32).transpose(1, 0, 2).reshape(128, 448)
    fkv[:, 448:896] = f("L1_attn_FV0").reshape(14, 128, 32).transpose(1, 0, 2).reshape(128, 448)
    fkv[:, 896:1408] = f("L1_attn_FK1").reshape(8, 128, 64).transpose(1, 0, 2).reshape(128, 512)
    fkv[:, 1408:1920] = f("L1_attn_FV1").reshape(8, 128, 64).transpose(1, 0, 2).reshape(128, 512)
    H["l1a_fkv"] = fkv

    def _pblk(P0, P1, width):
        tab = np.zeros((128, width), dtype=np.float32)
        for h in range(4):
            for r in range(8):
                tab[h * 8 + r, h * 64:(h + 1) * 64] = P0[h, r]
        for hl in range(8):
            for r in range(8):
                tab[64 + hl * 8 + r, 256 + hl * 64:256 + (hl + 1) * 64] = P1[hl, r]
        return tab
    H["l1a_pkb"] = _pblk(f("L1_attn_PK0"), f("L1_attn_PK1"), 768)
    H["l1a_pvb"] = _pblk(f("L1_attn_PV0"), f("L1_attn_PV1"), 1024)

    rw = f("L1_ffn_router_w")
    H["l1f_rw"] = np.ascontiguousarray(
        rw.reshape(8, 16, 128).transpose(2, 1, 0).reshape(128, 128))
    gw = f("L1_ffn_gate_w"); uw = f("L1_ffn_up_w"); dw = f("L1_ffn_down_w")
    H["l1f_gw"] = np.stack([_retile_dmajor(np.ascontiguousarray(gw[e].T))
                            for e in range(8)])
    H["l1f_uw"] = np.stack([_retile_dmajor(np.ascontiguousarray(uw[e].T))
                            for e in range(8)])
    H["l1f_dw"] = np.stack([_rows_block(dw[e]) for e in range(8)])
    H["l1f_sgw"] = _retile_dmajor(np.ascontiguousarray(f("L1_ffn_s_gate_w").T))
    H["l1f_suw"] = _retile_dmajor(np.ascontiguousarray(f("L1_ffn_s_up_w").T))
    H["l1f_sdw"] = _rows_block(f("L1_ffn_s_down_w"))
    H["l1f_sgb"] = _cm_vec(f("L1_ffn_s_gate_b"))
    H["l1f_sub"] = _cm_vec(f("L1_ffn_s_up_b"))
    H["l1f_sdb"] = np.ascontiguousarray(f("L1_ffn_s_down_b").reshape(1, D))
    HP = [dict() for _ in range(NCORES)]
    for c in range(NCORES):
        tgv = np.arange(c * TLOC, (c + 1) * TLOC).astype(np.float32)
        HP[c]["tg_base"] = np.ascontiguousarray(
            np.broadcast_to(tgv[None, :], (128, TLOC)).copy())
        m = np.zeros((128, 16), dtype=np.float32)
        for r in range(NCORES):
            if r < c:
                for ct in range(16):
                    m[16 * r + ct, ct] = 1.0
        HP[c]["l0a_cmask"] = m
        p = np.arange(128)
        qidx = np.zeros((128, 16), dtype=np.int32)
        for lh in range(2):
            for r in range(NCORES):
                qidx[:, lh * 8 + r] = r * 2048 + (2 * c + lh) * 128 + p
        HP[c]["l0f_qidx"] = qidx
        yidx = np.zeros((128, 64), dtype=np.int32)
        for i in range(NT):
            for r in range(NCORES):
                yidx[:, i * 8 + r] = r * T + c * TLOC + i * 128 + p
        HP[c]["l0f_yidx"] = yidx
        lidx = np.zeros((128, 1), dtype=np.int32)
        for q in range(128):
            h, i = q // 8, q % 8
            lidx[q, 0] = 128 * (h // 2) + (h % 2) * 64 + 8 * c + i
        HP[c]["l0f_lidx"] = lidx
        aidx = np.zeros((128, 288), dtype=np.int32)
        for r in range(NCORES):
            for hl in range(3):
                h = 3 * c + hl
                for i in range(NT):
                    aidx[:, (r * 3 + hl) * 8 + i] = (r * 24576 + (h // 8) * 8192 +
                                                     (h % 8) * 1024 + i * 128 + p)
        for i in range(NT):
            for r in range(NCORES):
                aidx[:, 192 + i * 8 + r] = r * T + c * TLOC + i * 128 + p
        for r in range(NCORES):
            aidx[:, 256 + r] = r * 3072 + c * 192 + p
            aidx[0:64, 264 + r] = r * 3072 + c * 192 + 128 + p[0:64]
            aidx[:, 272 + r] = r * 3072 + 1536 + c * 192 + p
            aidx[0:64, 280 + r] = r * 3072 + 1536 + c * 192 + 128 + p[0:64]
        HP[c]["l1a_idx"] = aidx
    return H, HP


_BUILD_CACHE = {}


def _get_builder(blocks):
    key = tuple(blocks)
    if key not in _BUILD_CACHE:
        _BUILD_CACHE[key] = B(list(blocks)).build()
    return _BUILD_CACHE[key]


def run_group(blocks, x_cur_shards, H, HP):
    b = _get_builder(blocks)
    in_maps = []
    for c in range(NCORES):
        m = {}
        for n in b.shared_names:
            m[n] = H[n]
        for n in b.percore_names:
            m[n] = HP[c][n] if n != "x_in" else x_cur_shards[c]
        in_maps.append(m)
    res = run_bass_kernel_spmd(b.nc, in_maps, core_ids=list(range(NCORES)))
    global LAST_RESULTS
    LAST_RESULTS = res.results
    return [np.asarray(r["y_out"]) for r in res.results]


def kernel(**inputs):
    H, HP = prep_host(inputs)
    x = np.ascontiguousarray(np.asarray(inputs["L0_attn_x"], dtype=np.float32)[0])
    shards = shard_tokens(x)
    for group in PLAN:
        shards = run_group(group, shards, H, HP)
    y = unshard_tokens(shards)
    return y.reshape(1, T, D)


# version 05
# speedup vs baseline: 1.2260x; 1.1543x over previous
import numpy as np
from collections import defaultdict
from contextlib import ExitStack

import concourse.bass as bass
import concourse.mybir as mybir
from concourse.bass_core_rust import IndirectOffsetOnAxis
from concourse.bass_utils import run_bass_kernel_spmd

F32 = mybir.dt.float32
BF16 = mybir.dt.bfloat16
I32 = mybir.dt.int32
AF = mybir.ActivationFunctionType
ALU = mybir.AluOpType
AX = mybir.AxisListType

NCORES = 8
T = 8192
D = 2048
TLOC = T // NCORES
NT = TLOC // 128
ND = D // 128
QC = 512
NQC = T // QC

PLAN = [[0, 1, 2, 3]]
DEBUG_L0F = False
L1F_NEXP = 8
L1F_SHARED = True
L1F_SH2 = True
L1F_NOSDB = False
L1F_NIT = None
L1A_NODUP = False
L1A_NOLOOK = False
DEBUG_L1A = False
LAST_RESULTS = None


class Prog:
    ENGINES = ("sync", "gpsimd", "scalar", "vector", "tensor")

    def __init__(self, nc, ctx):
        self.nc = nc
        self.ctx = ctx
        self.q = {e: [] for e in self.ENGINES}
        self.cnt = defaultdict(int)
        self.sems = {}

    def sem(self, name):
        if name not in self.sems:
            self.sems[name] = self.ctx.enter_context(self.nc.semaphore(name))
        return self.sems[name]

    def op(self, eng, build, waits=(), inc=None):
        waits = tuple((self.sem(s), int(v)) for (s, v) in waits if v)
        newv = None
        if inc is not None:
            sname, n = inc
            self.cnt[sname] += n
            newv = self.cnt[sname]
            semh = self.sem(sname)

            def closure(e, build=build, waits=waits, semh=semh, n=n):
                for (sh, v) in waits:
                    e.wait_ge(sh, v)
                build(e).then_inc(semh, n)
        else:
            def closure(e, build=build, waits=waits):
                for (sh, v) in waits:
                    e.wait_ge(sh, v)
                build(e)
        self.q[eng].append(closure)
        return newv

    def val(self, name):
        return self.cnt[name]

    def run_queues(self):
        with self.nc.Block() as block:
            block.sync(lambda e: [f(e) for f in self.q["sync"]])
            block.gpsimd(lambda e: [f(e) for f in self.q["gpsimd"]])
            block.scalar(lambda e: [f(e) for f in self.q["scalar"]])
            block.vector(lambda e: [f(e) for f in self.q["vector"]])
            block.tensor(lambda e: [f(e) for f in self.q["tensor"]])


def _retile_dmajor(W_ed):
    e_out, d_in = W_ed.shape
    nct, ndt = e_out // 128, d_in // 128
    arr = W_ed.reshape(nct, 128, ndt, 128)
    arr = arr.transpose(0, 3, 2, 1)
    return np.ascontiguousarray(arr).reshape(nct, 128, d_in)


def _rows_block(W_de):
    d_in, e_out = W_de.shape
    return np.ascontiguousarray(W_de.reshape(d_in // 128, 128, e_out))


def _cm_vec(v):
    return np.ascontiguousarray(np.asarray(v).reshape(-1, 128).T)


def shard_tokens(x_full):
    return [np.ascontiguousarray(x_full[c * TLOC:(c + 1) * TLOC])
            for c in range(NCORES)]


def unshard_tokens(shards):
    return np.concatenate(shards, axis=0)


class B:
    def __init__(self, blocks):
        self.blocks = blocks
        self.nc = bass.Bass("TRN2", target_bir_lowering=False, debug=False)
        self.ctx = ExitStack()
        self.P = Prog(self.nc, self.ctx)
        self.shared_names = []
        self.percore_names = []

    def din(self, name, shape, percore=False, dtype=F32):
        (self.percore_names if percore else self.shared_names).append(name)
        return self.nc.dram_tensor(name, list(shape), dtype, kind="ExternalInput").ap()

    def sbuf(self, name, shape, dtype):
        return self.ctx.enter_context(self.nc.sbuf_tensor(name, list(shape), dtype))

    def psum(self, name, shape, dtype=F32):
        return self.ctx.enter_context(self.nc.psum_tensor(name, list(shape), dtype))

    def col(self, name, width=1):
        if name not in self.cols:
            assert self._small_off + width <= 256, "smalls overflow"
            self.cols[name] = self.smalls[:, self._small_off:self._small_off + width]
            self._small_off += width
        return self.cols[name]

    def reset_cols(self):
        self._small_off = 0
        self.cols = {}

    def build(self):
        nc = self.nc
        self.x_in = self.din("x_in", [TLOC, D], percore=True)
        self.y_out = nc.dram_tensor("y_out", [TLOC, D], F32, kind="ExternalOutput").ap()

        self.res = self.sbuf("res", [128, NT * D], F32)
        self.xlnT = self.sbuf("xlnT", [128, ND * TLOC], BF16)
        self.bigA = self.sbuf("bigA", [128, ND * TLOC], BF16)
        self.bigB = self.sbuf("bigB", [128, ND * TLOC], BF16)
        self.wbuf = [self.sbuf("wbuf0", [128, 6144], BF16),
                     self.sbuf("wbuf1", [128, 6144], BF16)]
        self.scrp = self.sbuf("scrp", [128, 4 * 1024], F32)
        self.tg = self.sbuf("tg", [128, 1024], F32)
        self.identb = self.sbuf("identb", [128, 128], BF16)
        self.identf = self.sbuf("identf", [128, 128], F32)
        self.ones = self.sbuf("onesb", [128, 512], F32)
        self.onesbf = self.sbuf("onesbf", [128, 8], BF16)
        self.epscol = self.sbuf("epscol", [128, 1], F32)
        self.smalls = self.sbuf("smalls", [128, 256], F32)
        self.pb = [self.psum(f"pb{i}", [128, 512], F32) for i in range(6)]
        self.ptr = [self.psum(f"ptr{i}", [128, 1024], BF16) for i in range(2)]
        self.scr = [self.scrp[:, i * 1024:(i + 1) * 1024] for i in range(4)]
        self.reset_cols()

        self.setup()
        if 0 in self.blocks:
            self.l0_attn()
        if 1 in self.blocks:
            self.l0_ffn()
        if 2 in self.blocks:
            self.l1_attn()
        if 3 in self.blocks:
            self.l1_ffn()
        self.finish()
        self.P.run_queues()
        nc.finalize()
        return self

    def setup(self):
        P = self.P
        for i in range(NT):
            P.op("sync", lambda e, i=i: e.dma_start(
                out=self.res[:, i * D:(i + 1) * D],
                in_=self.x_in[i * 128:(i + 1) * 128, :]), inc=("xl", 16))
        self.ev_res_loaded = P.val("xl")

        ident_h = self.din("ident_h", [128, 128])
        P.op("gpsimd", lambda e: e.dma_start(out=self.identb[:, :], in_=ident_h),
             inc=("wl", 16))
        P.op("gpsimd", lambda e: e.dma_start(out=self.identf[:, :], in_=ident_h),
             inc=("wl", 16))
        P.op("gpsimd", lambda e: e.memset(self.ones[:, :], 1.0), inc=("ge", 1))
        P.op("gpsimd", lambda e: e.memset(self.onesbf[:, :], 1.0), inc=("ge", 1))
        P.op("gpsimd", lambda e: e.memset(self.epscol[:, :], 1e-5), inc=("ge", 1))
        tgb = self.din("tg_base", [128, 1024], percore=True)
        P.op("sync", lambda e: e.dma_start(out=self.tg[:, :], in_=tgb), inc=("xl", 16))
        self.ev_setup_xl = P.val("xl")
        self.ev_setup_wl = P.val("wl")
        self.ev_setup_ge = P.val("ge")

    def emit_ln(self, tag):
        P = self.P
        res, bigA, xlnT, ptr = self.res, self.bigA, self.xlnT, self.ptr
        sq_scr = xlnT[:, 0:4096].bitcast(F32)
        ss, sx = self.col("ln_ss"), self.col("ln_sx")
        mcol, e2col = self.col("ln_m"), self.col("ln_e2")
        vcol, sdcol = self.col("ln_v"), self.col("ln_sd")
        rcol, nmcol = self.col("ln_r"), self.col("ln_nm")
        xln = bigA
        start_v = P.val("ve")
        start_s = P.val("se")
        start_t = P.val("te")
        for i in range(NT):
            rsl = res[:, i * D:(i + 1) * D]
            w0 = [("xl", self.ev_res_loaded), ("ge", self.ev_setup_ge)]
            if i == 0:
                w0 = w0 + [("ve", start_v), ("se", start_s), ("te", start_t)]
            se = P.op("scalar", lambda e, rsl=rsl: e.activation(
                sq_scr, rsl, AF.Square, accum_out=ss), waits=w0, inc=("se", 1))
            v1 = P.op("vector", lambda e, rsl=rsl: e.reduce_sum(sx, rsl, axis=AX.X),
                      waits=w0, inc=("ve", 1))
            v2 = P.op("vector", lambda e: e.tensor_scalar_mul(mcol, sx, 1.0 / D),
                      waits=[("ve", v1)], inc=("ve", 1))
            P.op("vector", lambda e: e.tensor_scalar_mul(e2col, ss, 1.0 / D),
                 waits=[("se", se), ("ve", v2)], inc=("ve", 1))
            v4 = P.op("vector", lambda e: e.tensor_tensor(vcol, mcol, mcol, op=ALU.mult),
                      inc=("ve", 1))
            v5 = P.op("vector", lambda e: e.scalar_tensor_tensor(
                vcol, vcol, -1.0, e2col, op0=ALU.mult, op1=ALU.add),
                waits=[("ve", v4)], inc=("ve", 1))
            s2 = P.op("scalar", lambda e: e.activation(
                sdcol, vcol, AF.Sqrt, bias=self.epscol[:, :]),
                waits=[("ve", v5)], inc=("se", 1))
            v6 = P.op("vector", lambda e: e.reciprocal(rcol, sdcol),
                      waits=[("se", s2)], inc=("ve", 1))
            v7 = P.op("vector", lambda e: e.scalar_tensor_tensor(
                nmcol, mcol, -1.0, rcol, op0=ALU.mult, op1=ALU.mult),
                waits=[("ve", v6)], inc=("ve", 1))
            P.op("vector", lambda e, i=i, rsl=rsl: e.tensor_scalar(
                xln[:, i * D:(i + 1) * D], rsl, rcol, nmcol,
                op0=ALU.mult, op1=ALU.add),
                waits=[("ve", v7)], inc=("ve", 1))
        ev_xln = P.val("ve")

        last_copy = {0: 0, 1: 0}
        for dt in range(ND):
            bank = dt % 2
            tw = [("ve", ev_xln), ("wl", self.ev_setup_wl)]
            if last_copy[bank]:
                tw = tw + [("ve", last_copy[bank])]
            te = None
            for i in range(NT):
                te = P.op("tensor", lambda e, dt=dt, i=i, bank=bank: e.transpose(
                    ptr[bank][:, i * 128:(i + 1) * 128],
                    xln[:, i * D + dt * 128:i * D + (dt + 1) * 128],
                    self.identb[:, :]),
                    waits=tw if i == 0 else (),
                    inc=("te", 1) if i == NT - 1 else None)
            last_copy[bank] = P.op("vector", lambda e, dt=dt, bank=bank: e.tensor_copy(
                xlnT[:, dt * TLOC:(dt + 1) * TLOC], ptr[bank][:, :]),
                waits=[("te", te)], inc=("ve", 1))
        return ev_xln, P.val("ve")

    def l0_attn(self):
        P, nc = self.P, self.nc
        res, xlnT, bigA, bigB = self.res, self.xlnT, self.bigA, self.bigB
        scr, tg, pb = self.scr, self.tg, self.pb
        self.reset_cols()

        wr_h = self.din("l0a_wr", [16, 128, D])
        wk_h = self.din("l0a_wk", [16, 128, D])
        wv_h = self.din("l0a_wv", [16, 128, D])
        wo_h = self.din("l0a_wo", [16, 128, D])
        dec_h = self.din("l0a_decay", [128, ND])
        gnw_h = self.din("l0a_gnw", [128, ND])
        gnb_h = self.din("l0a_gnb", [128, ND])
        cmask_h = self.din("l0a_cmask", [128, 16], percore=True)

        cs_in = nc.dram_tensor("l0a_cs_in", [16, 128], F32, kind="Internal")
        cs_out = nc.dram_tensor("l0a_cs_out", [128, 128], F32, kind="Internal")
        gs_in = nc.dram_tensor("l0a_gs_in", [1, 2], F32, kind="Internal")
        gs_out = nc.dram_tensor("l0a_gs_out", [1, 2], F32, kind="Internal")

        dec_cm = self.col("dec_cm", ND)
        logd = self.col("logd", ND)
        gnw_cm = self.col("gnw_cm", ND)
        gnb_cm = self.col("gnb_cm", ND)
        stab = self.col("stab", 16)
        carry = self.col("carry", 16)
        carry2 = self.col("carry2", 1)
        sxt = self.col("sxt", ND)
        sqt = self.col("sqt", ND)
        Acol = self.col("Acl", ND)
        Bcol = self.col("Bcl", ND)
        cmask_sb = self.col("cmask", 16)

        P.op("sync", lambda e: e.dma_start(out=dec_cm, in_=dec_h), inc=("xl", 16))
        P.op("sync", lambda e: e.dma_start(out=gnw_cm, in_=gnw_h), inc=("xl", 16))
        P.op("sync", lambda e: e.dma_start(out=gnb_cm, in_=gnb_h), inc=("xl", 16))
        P.op("sync", lambda e: e.dma_start(out=cmask_sb, in_=cmask_h), inc=("xl", 16))
        ev_vecs = P.val("xl")

        se = P.op("scalar", lambda e: e.activation(logd, dec_cm, AF.Sigmoid),
                  waits=[("xl", ev_vecs)], inc=("se", 1))
        ve = P.op("vector", lambda e: e.tensor_scalar_max(logd, logd, 1e-7),
                  waits=[("se", se)], inc=("ve", 1))
        ev_logd_se = P.op("scalar", lambda e: e.activation(logd, logd, AF.Ln),
                          waits=[("ve", ve)], inc=("se", 1))

        ev_xln, ev_xlnT = self.emit_ln("l0a")

        wslot_ready = [0, 0]
        wslot_free_te = [0, 0]
        prev_epi_v = 0
        prev_epi_s = 0
        prev_sexp = 0
        for ct in range(ND):
            slot = ct % 2
            w = self.wbuf[slot]
            wts = [("te", wslot_free_te[slot])] if wslot_free_te[slot] else []
            P.op("gpsimd", lambda e, ct=ct, w=w: e.dma_start(
                out=w[:, 0:2048], in_=wk_h[ct]), waits=wts, inc=("wl", 16))
            P.op("gpsimd", lambda e, ct=ct, w=w: e.dma_start(
                out=w[:, 2048:4096], in_=wv_h[ct]), inc=("wl", 16))
            P.op("gpsimd", lambda e, ct=ct, w=w: e.dma_start(
                out=w[:, 4096:6144], in_=wr_h[ct]), inc=("wl", 16))
            wslot_ready[slot] = P.val("wl")

            te_mat = {}
            first = True
            for m in range(3):
                for h in range(2):
                    for dt in range(ND):
                        wts = []
                        if first:
                            wts = [("wl", wslot_ready[slot]), ("ve", ev_xlnT)]
                            if prev_epi_v:
                                wts += [("ve", prev_epi_v), ("se", prev_epi_s)]
                            first = False
                        te_mat[(m, h)] = P.op("tensor", lambda e, m=m, h=h, dt=dt, w=w: e.matmul(
                            pb[m * 2 + h][:, :],
                            w[:, m * 2048 + dt * 128:m * 2048 + (dt + 1) * 128],
                            xlnT[:, dt * TLOC + h * 512:dt * TLOC + (h + 1) * 512],
                            start=(dt == 0), stop=(dt == ND - 1)),
                            waits=wts,
                            inc=("te", 1) if dt == ND - 1 else None)
            wslot_free_te[slot] = te_mat[(2, 1)]

            vk = None
            for h in range(2):
                vk = P.op("vector", lambda e, h=h: e.tensor_copy(
                    scr[1][:, h * 512:(h + 1) * 512], pb[h][:, :]),
                    waits=[("te", te_mat[(0, h)])], inc=("ve", 1))
            va = P.op("vector", lambda e, ct=ct: e.tensor_scalar_mul(
                scr[2][:, :], tg[:, :], logd[:, ct:ct + 1]),
                waits=[("se", ev_logd_se), ("xl", self.ev_setup_xl)] +
                      ([("se", prev_sexp)] if prev_sexp else []),
                inc=("ve", 1))
            vkv = None
            for h in range(2):
                vkv = P.op("vector", lambda e, h=h: e.tensor_tensor(
                    scr[1][:, h * 512:(h + 1) * 512], pb[2 + h][:, :],
                    scr[1][:, h * 512:(h + 1) * 512], op=ALU.mult),
                    waits=[("te", te_mat[(1, h)]), ("ve", vk)], inc=("ve", 1))
            sr = None
            for h in range(2):
                sr = P.op("scalar", lambda e, ct=ct, h=h: e.activation(
                    bigB[:, ct * TLOC + h * 512:ct * TLOC + (h + 1) * 512],
                    pb[4 + h][:, :], AF.Sigmoid),
                    waits=[("te", te_mat[(2, h)])], inc=("se", 1))
            sexp = P.op("scalar", lambda e: e.activation(scr[3][:, :], scr[2][:, :], AF.Exp),
                        waits=[("ve", va)], inc=("se", 1))
            vc = P.op("vector", lambda e: e.tensor_scalar_max(
                scr[2][:, :], scr[3][:, :], 1e-10),
                waits=[("se", sexp)], inc=("ve", 1))
            vi = P.op("vector", lambda e: e.reciprocal(scr[3][:, :], scr[2][:, :]),
                      waits=[("ve", vc)], inc=("ve", 1))
            vs = P.op("vector", lambda e, ct=ct: e.tensor_tensor(
                bigA[:, ct * TLOC:(ct + 1) * TLOC], scr[1][:, :], scr[3][:, :],
                op=ALU.mult),
                waits=[("ve", vi), ("ve", vkv)], inc=("ve", 1))
            P.op("vector", lambda e, ct=ct: e.reduce_sum(
                stab[:, ct:ct + 1], bigA[:, ct * TLOC:(ct + 1) * TLOC], axis=AX.X),
                waits=[("ve", vs)], inc=("ve", 1))
            prev_epi_v, prev_epi_s, prev_sexp = vkv, sr, sexp
        ev_phaseA_v = P.val("ve")
        ev_phaseA_s = P.val("se")
        last_proj_te = P.val("te")

        teS = P.op("tensor", lambda e: e.transpose(
            pb[0][0:16, 0:128], stab, self.identf[:, :]),
            waits=[("ve", ev_phaseA_v)], inc=("te", 1))
        vS = P.op("vector", lambda e: e.tensor_copy(scr[2][0:16, 0:128],
                                                    pb[0][0:16, 0:128]),
                  waits=[("te", teS)], inc=("ve", 1))
        dS = P.op("sync", lambda e: e.dma_start(out=cs_in.ap(), in_=scr[2][0:16, 0:128]),
                  waits=[("ve", vS)], inc=("st", 16))
        cc = P.op("gpsimd", lambda e: e.collective_compute(
            "AllGather", ALU.bypass, replica_groups=[list(range(NCORES))],
            ins=[cs_in.ap().opt()], outs=[cs_out.ap().opt()]),
            waits=[("st", dS)], inc=("cc", 1))
        d2 = P.op("sync", lambda e: e.dma_start(out=scr[1][:, 0:128], in_=cs_out.ap()),
                  waits=[("cc", cc)], inc=("xl", 16))
        tc = P.op("tensor", lambda e: e.matmul(
            pb[1][:, 0:16], scr[1][:, 0:128], cmask_sb, start=True, stop=True),
            waits=[("xl", d2)], inc=("te", 1))
        ev_carry = P.op("vector", lambda e: e.tensor_copy(carry, pb[1][:, 0:16]),
                        waits=[("te", tc)], inc=("ve", 1))

        prev_sexp = 0
        for ct in range(ND):
            va = P.op("vector", lambda e, ct=ct: e.tensor_scalar_mul(
                scr[2][:, :], tg[:, :], logd[:, ct:ct + 1]),
                waits=([("ve", ev_carry)] if ct == 0 else []) +
                      ([("se", prev_sexp)] if prev_sexp else []),
                inc=("ve", 1))
            sexp = P.op("scalar", lambda e: e.activation(scr[3][:, :], scr[2][:, :], AF.Exp),
                        waits=[("ve", va), ("se", ev_phaseA_s)], inc=("se", 1))
            vs0 = P.op("vector", lambda e, ct=ct: e.tensor_tensor_scan(
                scr[0][:, 0:512], self.ones[:, :],
                bigA[:, ct * TLOC:ct * TLOC + 512],
                carry[:, ct:ct + 1], op0=ALU.mult, op1=ALU.add),
                waits=[("ve", ev_carry)] if ct == 0 else (), inc=("ve", 1))
            vcp = P.op("vector", lambda e: e.tensor_copy(carry2, scr[0][:, 511:512]),
                       waits=[("ve", vs0)], inc=("ve", 1))
            vs1 = P.op("vector", lambda e, ct=ct: e.tensor_tensor_scan(
                scr[0][:, 512:1024], self.ones[:, :],
                bigA[:, ct * TLOC + 512:(ct + 1) * TLOC],
                carry2, op0=ALU.mult, op1=ALU.add),
                waits=[("ve", vcp)], inc=("ve", 1))
            vst = P.op("vector", lambda e: e.tensor_tensor(
                scr[1][:, :], scr[0][:, :], scr[3][:, :], op=ALU.mult),
                waits=[("se", sexp), ("ve", vs1)], inc=("ve", 1))
            vo = P.op("vector", lambda e, ct=ct: e.tensor_tensor(
                bigA[:, ct * TLOC:(ct + 1) * TLOC], scr[1][:, :],
                bigB[:, ct * TLOC:(ct + 1) * TLOC], op=ALU.mult),
                waits=[("ve", vst)], inc=("ve", 1))
            P.op("scalar", lambda e, ct=ct: e.activation(
                scr[1][:, :], bigA[:, ct * TLOC:(ct + 1) * TLOC], AF.Square,
                accum_out=sqt[:, ct:ct + 1]),
                waits=[("ve", vo)], inc=("se", 1))
            P.op("vector", lambda e, ct=ct: e.reduce_sum(
                sxt[:, ct:ct + 1], bigA[:, ct * TLOC:(ct + 1) * TLOC], axis=AX.X),
                waits=[("ve", vo)], inc=("ve", 1))
            prev_sexp = sexp
        ev_phaseB_v = P.val("ve")
        ev_phaseB_s = P.val("se")

        P.op("tensor", lambda e: e.matmul(
            pb[2][0:1, 0:ND], self.ones[:, 0:1], sxt, start=True, stop=True),
            waits=[("ve", ev_phaseB_v), ("se", ev_phaseB_s),
                   ("ge", self.ev_setup_ge)])
        tS = P.op("tensor", lambda e: e.matmul(
            pb[2][0:1, ND:2 * ND], self.ones[:, 0:1], sqt, start=True, stop=True),
            inc=("te", 1))
        vg1 = P.op("vector", lambda e: e.reduce_sum(
            scr[2][0:1, 0:1], pb[2][0:1, 0:ND], axis=AX.X),
            waits=[("te", tS)], inc=("ve", 1))
        vg2 = P.op("vector", lambda e: e.reduce_sum(
            scr[2][0:1, 1:2], pb[2][0:1, ND:2 * ND], axis=AX.X),
            waits=[("ve", vg1)], inc=("ve", 1))
        dG = P.op("sync", lambda e: e.dma_start(out=gs_in.ap(), in_=scr[2][0:1, 0:2]),
                  waits=[("ve", vg2)], inc=("st", 16))
        ccG = P.op("gpsimd", lambda e: e.collective_compute(
            "AllReduce", ALU.add, replica_groups=[list(range(NCORES))],
            ins=[gs_in.ap().opt()], outs=[gs_out.ap().opt()]),
            waits=[("st", dG)], inc=("cc", 1))
        dG2 = P.op("sync", lambda e: e.dma_start(out=scr[2][0:1, 4:6], in_=gs_out.ap()),
                   waits=[("cc", ccG)], inc=("xl", 16))
        NELT = float(T * D)
        vm = P.op("vector", lambda e: e.tensor_scalar_mul(
            scr[2][0:1, 6:7], scr[2][0:1, 4:5], 1.0 / NELT),
            waits=[("xl", dG2)], inc=("ve", 1))
        ve2 = P.op("vector", lambda e: e.tensor_scalar_mul(
            scr[2][0:1, 7:8], scr[2][0:1, 5:6], 1.0 / NELT),
            waits=[("ve", vm)], inc=("ve", 1))
        vv = P.op("vector", lambda e: e.tensor_tensor(
            scr[2][0:1, 8:9], scr[2][0:1, 6:7], scr[2][0:1, 6:7], op=ALU.mult),
            waits=[("ve", ve2)], inc=("ve", 1))
        vv2 = P.op("vector", lambda e: e.scalar_tensor_tensor(
            scr[2][0:1, 8:9], scr[2][0:1, 8:9], -1.0, scr[2][0:1, 7:8],
            op0=ALU.mult, op1=ALU.add),
            waits=[("ve", vv)], inc=("ve", 1))
        sG = P.op("scalar", lambda e: e.activation(
            scr[2][0:1, 9:10], scr[2][0:1, 8:9], AF.Sqrt, bias=self.epscol[0:1, :]),
            waits=[("ve", vv2)], inc=("se", 1))
        vr = P.op("vector", lambda e: e.reciprocal(scr[2][0:1, 10:11], scr[2][0:1, 9:10]),
                  waits=[("se", sG)], inc=("ve", 1))
        vnm = P.op("vector", lambda e: e.tensor_tensor(
            scr[2][0:1, 11:12], scr[2][0:1, 6:7], scr[2][0:1, 10:11], op=ALU.mult),
            waits=[("ve", vr)], inc=("ve", 1))
        vnm2 = P.op("vector", lambda e: e.tensor_scalar_mul(
            scr[2][0:1, 11:12], scr[2][0:1, 11:12], -1.0),
            waits=[("ve", vnm)], inc=("ve", 1))
        tb = P.op("tensor", lambda e: e.matmul(
            pb[3][:, 0:2], self.ones[0:1, 0:128], scr[2][0:1, 10:12],
            start=True, stop=True),
            waits=[("ve", vnm2)], inc=("te", 1))
        vb = P.op("vector", lambda e: e.tensor_copy(scr[2][:, 16:18], pb[3][:, 0:2]),
                  waits=[("te", tb)], inc=("ve", 1))
        rsd_col = scr[2][:, 16:17]
        nmr_col = scr[2][:, 17:18]
        vA = P.op("vector", lambda e: e.tensor_scalar_mul(Acol, gnw_cm, rsd_col),
                  waits=[("ve", vb)], inc=("ve", 1))
        vB = P.op("vector", lambda e: e.tensor_scalar_mul(Bcol, gnw_cm, nmr_col),
                  waits=[("ve", vA)], inc=("ve", 1))
        ev_AB = P.op("vector", lambda e: e.tensor_tensor(Bcol, Bcol, gnb_cm, op=ALU.add),
                     waits=[("ve", vB)], inc=("ve", 1))

        normed = xlnT
        for ct in range(ND):
            P.op("vector", lambda e, ct=ct: e.tensor_scalar(
                normed[:, ct * TLOC:(ct + 1) * TLOC],
                bigA[:, ct * TLOC:(ct + 1) * TLOC],
                Acol[:, ct:ct + 1], Bcol[:, ct:ct + 1],
                op0=ALU.mult, op1=ALU.add),
                waits=[("ve", ev_AB)] if ct == 0 else (), inc=("ve", 1))
        ev_normed = P.val("ve")

        def wo_view(ct):
            if ct < 8:
                return bigB[:, ct * 2048:(ct + 1) * 2048]
            if ct < 11:
                return self.wbuf[0][:, (ct - 8) * 2048:(ct - 7) * 2048]
            if ct < 14:
                return self.wbuf[1][:, (ct - 11) * 2048:(ct - 10) * 2048]
            return self.scrp[:, (ct - 14) * 1024:(ct - 13) * 1024].bitcast(BF16)

        for ct in range(ND):
            wts = []
            if ct == 0:
                wts = [("ve", ev_phaseB_v), ("se", ev_phaseB_s),
                       ("te", last_proj_te)]
            P.op("gpsimd", lambda e, ct=ct: e.dma_start(out=wo_view(ct), in_=wo_h[ct]),
                 waits=wts, inc=("wl", 16))
        ev_wo = P.val("wl")

        prev_evac = 0
        for i in range(NT):
            te_last = None
            for ct in range(ND):
                for q in range(4):
                    wts = []
                    if ct == 0 and q == 0:
                        wts = [("wl", ev_wo), ("ve", ev_normed)]
                        if prev_evac:
                            wts.append(("ve", prev_evac))
                    te_last = P.op("tensor", lambda e, i=i, ct=ct, q=q: e.matmul(
                        pb[q][:, :],
                        normed[:, ct * TLOC + i * 128:ct * TLOC + (i + 1) * 128],
                        wo_view(ct)[:, q * 512:(q + 1) * 512],
                        start=(ct == 0), stop=(ct == ND - 1)),
                        waits=wts,
                        inc=("te", 1) if (ct == ND - 1 and q == 3) else None)
            for q in range(4):
                prev_evac = P.op("vector", lambda e, i=i, q=q: e.tensor_tensor(
                    res[:, i * D + q * 512:i * D + (q + 1) * 512],
                    pb[q][:, :],
                    res[:, i * D + q * 512:i * D + (q + 1) * 512], op=ALU.add),
                    waits=[("te", te_last)] if q == 0 else (), inc=("ve", 1))

    def l0_ffn(self):
        P, nc = self.P, self.nc
        res, xlnT, bigA, bigB = self.res, self.xlnT, self.bigA, self.bigB
        scr, pb, ptr = self.scr, self.pb, self.ptr
        self.reset_cols()
        HSC = 0.08838834764831845

        wg_h = self.din("l0f_wg", [128, 256])
        wq_h = self.din("l0f_wqkv", [48, 128, D])
        qidx_h = self.din("l0f_qidx", [128, 16], percore=True, dtype=I32)
        yidx_h = self.din("l0f_yidx", [128, 64], percore=True, dtype=I32)
        lidx_h = self.din("l0f_lidx", [128, 1], percore=True, dtype=I32)

        gsel = nc.dram_tensor("l0f_gsel", [32, TLOC], F32, kind="Internal")
        kv_in = nc.dram_tensor("l0f_kv_in", [256, TLOC], BF16, kind="Internal")
        kv_out = nc.dram_tensor("l0f_kv_out", [2048, TLOC], BF16, kind="Internal")
        qag_in = nc.dram_tensor("l0f_qag_in", [2048, TLOC], BF16, kind="Internal")
        qag_out = nc.dram_tensor("l0f_qag_out", [16384, TLOC], BF16, kind="Internal")
        yag_in = [nc.dram_tensor(f"l0f_yag_in{h}", [T, 128], BF16, kind="Internal")
                  for h in range(2)]
        yag_out = [nc.dram_tensor(f"l0f_yag_out{h}", [T * NCORES, 128], BF16,
                                  kind="Internal") for h in range(2)]
        lag_in = nc.dram_tensor("l0f_lag_in", [128, 128], F32, kind="Internal")
        lag_out = nc.dram_tensor("l0f_lag_out", [1024, 128], F32, kind="Internal")

        wg_sb = self.col("wg_sb", 128).bitcast(BF16)
        qidx = self.col("qidx", 16).bitcast(I32)
        yidx = self.col("yidx", 64).bitcast(I32)
        lidx = self.col("lidx", 1).bitcast(I32)
        negm = self.col("negm")
        ssum = self.col("ssum")
        topc = self.col("topc")

        ent = [("ve", P.val("ve")), ("se", P.val("se")), ("te", P.val("te"))]
        P.op("sync", lambda e: e.dma_start(out=qidx, in_=qidx_h), waits=ent,
             inc=("xl", 16))
        P.op("sync", lambda e: e.dma_start(out=yidx, in_=yidx_h), waits=ent,
             inc=("xl", 16))
        P.op("sync", lambda e: e.dma_start(out=lidx, in_=lidx_h), waits=ent,
             inc=("xl", 16))
        P.op("gpsimd", lambda e: e.dma_start(out=wg_sb, in_=wg_h), waits=ent,
             inc=("wl", 16))
        ev_wg = P.val("wl")
        ones128 = self.tg.bitcast(BF16)[:, 1536:1664]
        P.op("gpsimd", lambda e: e.memset(ones128, 1.0), waits=ent, inc=("ge", 1))
        ev_o128 = P.val("ge")

        ev_xln, ev_xlnT = self.emit_ln("l0f")

        ind_scr = scr[2][:, 0:16]
        gt1_scr = scr[2][:, 16:32]
        prev_tr_copy = 0
        exp_ev = 0
        for i in range(NT):
            te_g = None
            for dt in range(ND):
                wts = []
                if dt == 0:
                    wts = [("ve", ev_xlnT), ("wl", ev_wg)]
                    if exp_ev:
                        wts.append(("se", exp_ev))
                te_g = P.op("tensor", lambda e, i=i, dt=dt: e.matmul(
                    pb[4][:, 0:16],
                    xlnT[:, dt * TLOC + i * 128:dt * TLOC + (i + 1) * 128],
                    wg_sb[:, dt * 16:(dt + 1) * 16],
                    start=(dt == 0), stop=(dt == ND - 1)),
                    waits=wts, inc=("te", 1) if dt == ND - 1 else None)
            vng = P.op("vector", lambda e: e.reduce_max(
                negm, pb[4][:, 0:16], axis=AX.X, negate=True),
                waits=[("te", te_g)], inc=("ve", 1))
            vind = P.op("vector", lambda e: e.tensor_scalar(
                ind_scr, pb[4][:, 0:16], negm, 0.0, op0=ALU.add, op1=ALU.is_ge),
                waits=[("ve", vng)], inc=("ve", 1))
            exp_ev = P.op("scalar", lambda e: e.activation(
                scr[3][:, 0:16], pb[4][:, 0:16], AF.Exp, bias=negm),
                waits=[("ve", vind)], inc=("se", 1))
            vsum = P.op("vector", lambda e: e.reduce_sum(
                ssum, scr[3][:, 0:16], axis=AX.X),
                waits=[("se", exp_ev)], inc=("ve", 1))
            vtop = P.op("vector", lambda e: e.reciprocal(topc, ssum),
                        waits=[("ve", vsum)], inc=("ve", 1))
            vgt1 = P.op("vector", lambda e: e.tensor_scalar_mul(
                gt1_scr, ind_scr, topc),
                waits=[("ve", vtop)], inc=("ve", 1))
            wts = [("ve", vgt1)]
            if prev_tr_copy:
                wts.append(("ve", prev_tr_copy))
            te_t = P.op("tensor", lambda e: e.transpose(
                pb[5][0:32, 0:128], scr[2][:, 0:32], self.identf[:, :]),
                waits=wts, inc=("te", 1))
            prev_tr_copy = P.op("vector", lambda e, i=i: e.tensor_copy(
                scr[1][0:32, i * 128:(i + 1) * 128], pb[5][0:32, 0:128]),
                waits=[("te", te_t)], inc=("ve", 1))
        dsel = P.op("sync", lambda e: e.dma_start(out=gsel.ap(), in_=scr[1][0:32, :]),
                    waits=[("ve", prev_tr_copy)], inc=("st", 16))

        kacc, vacc = scr[0], scr[1]
        wslot_ready = [0, 0]
        wslot_free = [0, 0]
        prev_bc_consumed = 0
        prev_evac_s = 0
        prev_evac_v = 0
        last_q_evac = 0
        for et in range(48):
            slot = et % 2
            w = self.wbuf[slot][:, 0:2048]
            wts = [("te", wslot_free[slot])] if wslot_free[slot] else []
            if et == 0:
                wts.append(("st", dsel))
            P.op("gpsimd", lambda e, et=et, w=w: e.dma_start(out=w, in_=wq_h[et]),
                 waits=wts, inc=("wl", 16))
            wslot_ready[slot] = P.val("wl")
            role = et // 16
            hh = et % 16
            ev_bc = 0
            if role > 0:
                grow = hh if role == 1 else 16 + hh
                wts = [("st", dsel)]
                if prev_bc_consumed:
                    wts.append(("ve", prev_bc_consumed))
                ev_bc = P.op("gpsimd", lambda e, grow=grow: e.dma_start(
                    out=scr[3][:, :],
                    in_=gsel.ap()[grow:grow + 1, :].to_broadcast([128, TLOC])),
                    waits=wts, inc=("wl", 16))
            te_m = {}
            for th in range(2):
                for dt in range(ND):
                    wts = []
                    if dt == 0 and th == 0:
                        wts = [("wl", wslot_ready[slot]), ("ve", ev_xlnT)]
                        if prev_evac_s:
                            wts.append(("se", prev_evac_s))
                        if prev_evac_v:
                            wts.append(("ve", prev_evac_v))
                    te_m[th] = P.op("tensor", lambda e, th=th, dt=dt, w=w, slot=slot: e.matmul(
                        pb[slot * 2 + th][:, :],
                        w[:, dt * 128:(dt + 1) * 128],
                        xlnT[:, dt * TLOC + th * 512:dt * TLOC + (th + 1) * 512],
                        start=(dt == 0), stop=(dt == ND - 1)),
                        waits=wts, inc=("te", 1) if dt == ND - 1 else None)
            wslot_free[slot] = te_m[1]
            if role == 0:
                ev = None
                for th in range(2):
                    ev = P.op("scalar", lambda e, hh=hh, th=th, slot=slot: e.activation(
                        bigA[:, hh * TLOC + th * 512:hh * TLOC + (th + 1) * 512],
                        pb[slot * 2 + th][:, :], AF.Copy, scale=HSC),
                        waits=[("te", te_m[th])], inc=("se", 1))
                prev_evac_s = ev
                last_q_evac = ev
                prev_evac_v = 0
            else:
                acc = kacc if role == 1 else vacc
                ev = None
                for th in range(2):
                    sl = slice(th * 512, (th + 1) * 512)
                    if hh == 0:
                        ev = P.op("vector", lambda e, th=th, slot=slot, acc=acc, sl=sl: e.tensor_tensor(
                            acc[:, sl], pb[slot * 2 + th][:, :], scr[3][:, sl],
                            op=ALU.mult),
                            waits=[("te", te_m[th]), ("wl", ev_bc)], inc=("ve", 1))
                    else:
                        ev = P.op("vector", lambda e, th=th, slot=slot, sl=sl: e.tensor_tensor(
                            scr[2][:, sl], pb[slot * 2 + th][:, :], scr[3][:, sl],
                            op=ALU.mult),
                            waits=[("te", te_m[th]), ("wl", ev_bc)], inc=("ve", 1))
                        ev = P.op("vector", lambda e, acc=acc, sl=sl: e.tensor_tensor(
                            acc[:, sl], acc[:, sl], scr[2][:, sl], op=ALU.add),
                            waits=[("ve", ev)], inc=("ve", 1))
                prev_bc_consumed = ev
                prev_evac_v = ev
                prev_evac_s = 0
        ev_sel_v = P.val("ve")

        ksh_bf = self.wbuf[0][:, 2048:3072]
        vsh_bf = self.wbuf[0][:, 3072:4096]
        vtok_bf = self.wbuf[0][:, 4096:5120]
        sk = P.op("scalar", lambda e: e.activation(ksh_bf, kacc[:, :], AF.Copy),
                  waits=[("ve", ev_sel_v)], inc=("se", 1))
        sv = P.op("scalar", lambda e: e.activation(vsh_bf, vacc[:, :], AF.Copy),
                  waits=[("ve", ev_sel_v)], inc=("se", 1))
        te_v = None
        for j in range(NT):
            te_v = P.op("tensor", lambda e, j=j: e.transpose(
                ptr[0][:, j * 128:(j + 1) * 128],
                vsh_bf[:, j * 128:(j + 1) * 128], self.identb[:, :]),
                waits=[("se", sv)] if j == 0 else (),
                inc=("te", 1) if j == NT - 1 else None)
        vvt = P.op("vector", lambda e: e.tensor_copy(vtok_bf, ptr[0][:, :]),
                   waits=[("te", te_v)], inc=("ve", 1))
        d_k = P.op("sync", lambda e: e.dma_start(out=kv_in.ap()[0:128, :], in_=ksh_bf),
                   waits=[("se", sk)], inc=("st", 16))
        d_v = P.op("sync", lambda e: e.dma_start(out=kv_in.ap()[128:256, :], in_=vtok_bf),
                   waits=[("ve", vvt)], inc=("st", 16))
        dq = None
        for hh in range(16):
            dq = P.op("sync", lambda e, hh=hh: e.dma_start(
                out=qag_in.ap()[hh * 128:(hh + 1) * 128, :],
                in_=bigA[:, hh * TLOC:(hh + 1) * TLOC]),
                waits=[("se", last_q_evac)] if hh == 0 else (), inc=("st", 16))
        cc1 = P.op("gpsimd", lambda e: e.collective_compute(
            "AllGather", ALU.bypass, replica_groups=[list(range(NCORES))],
            ins=[kv_in.ap().opt()], outs=[kv_out.ap().opt()]),
            waits=[("st", d_v), ("st", d_k)], inc=("cc", 1))
        cc2 = P.op("gpsimd", lambda e: e.collective_compute(
            "AllGather", ALU.bypass, replica_groups=[list(range(NCORES))],
            ins=[qag_in.ap().opt()], outs=[qag_out.ap().opt()]),
            waits=[("st", dq)], inc=("cc", 1))

        dkv = None
        for g in range(NQC):
            r, half = g // 2, g % 2
            dkv = P.op("sync", lambda e, g=g, r=r, half=half: e.dma_start(
                out=bigB[:, g * 512:(g + 1) * 512],
                in_=kv_out.ap()[256 * r:256 * r + 128, half * 512:(half + 1) * 512]),
                waits=[("cc", cc1)] if g == 0 else (), inc=("xl", 16))
            dkv = P.op("sync", lambda e, g=g, r=r, half=half: e.dma_start(
                out=bigB[:, 8192 + g * 512:8192 + (g + 1) * 512],
                in_=kv_out.ap()[256 * r + 128:256 * r + 256,
                                half * 512:(half + 1) * 512]),
                inc=("xl", 16))
        ev_kv_own = P.val("xl")
        for lh in range(2):
            for r in range(NCORES):
                P.op("gpsimd", lambda e, lh=lh, r=r: e.indirect_dma_start(
                    out=bigA[:, lh * 8192 + r * TLOC:lh * 8192 + (r + 1) * TLOC],
                    out_offset=None,
                    in_=qag_out.ap()[0:128, :],
                    in_offset=IndirectOffsetOnAxis(qidx[:, lh * 8 + r:lh * 8 + r + 1], 0)),
                    waits=[("cc", cc2)] if (lh == 0 and r == 0) else (),
                    inc=("wl", 16))
        ev_q_own = P.val("wl")
        if DEBUG_L0F:
            dbg_gsel = nc.dram_tensor("dbg_gsel", [32, 1024], F32, kind="ExternalOutput").ap()
            P.op("sync", lambda e: e.dma_start(out=dbg_gsel, in_=gsel.ap()),
                 waits=[("st", dsel)], inc=("st", 16))
            dbg_q = nc.dram_tensor("dbg_q", [128, 16384], BF16, kind="ExternalOutput").ap()
            dbg_kv = nc.dram_tensor("dbg_kv", [128, 16384], BF16, kind="ExternalOutput").ap()
            dbg_ksel = nc.dram_tensor("dbg_ksel", [128, 1024], F32, kind="ExternalOutput").ap()
            dbg_vsel = nc.dram_tensor("dbg_vsel", [128, 1024], F32, kind="ExternalOutput").ap()
            P.op("sync", lambda e: e.dma_start(out=dbg_q, in_=bigA[:, :]),
                 waits=[("wl", ev_q_own)], inc=("st", 16))
            P.op("sync", lambda e: e.dma_start(out=dbg_kv, in_=bigB[:, :]),
                 waits=[("xl", ev_kv_own)], inc=("st", 16))
            P.op("sync", lambda e: e.dma_start(out=dbg_ksel, in_=kacc[:, :]),
                 waits=[("ve", ev_sel_v)], inc=("st", 16))
            P.op("sync", lambda e: e.dma_start(out=dbg_vsel, in_=vacc[:, :]),
                 waits=[("ve", ev_sel_v)], inc=("st", 16))

        ebuf = scr[0].bitcast(BF16)
        lrow = scr[1][0:1, 0:512]
        lrow_i = scr[1][0:1, 512:1024]
        prev_oevac_s = 0
        prev_levac_v = 0
        dl = 0
        prev_copy = 0
        dyi = None
        cc3 = {}
        for lh in range(2):
            for qg in range(NQC):
                first_mm = True
                nsub = 4 * (qg + 1)
                sub = 0
                ev_exp = {}
                ev_msk = {}
                for kg in range(qg + 1):
                    for j in range(4):
                        ping = sub % 2
                        wts = []
                        if first_mm:
                            wts = [("wl", ev_q_own), ("xl", ev_kv_own)]
                            if prev_oevac_s:
                                wts += [("se", prev_oevac_s), ("ve", prev_levac_v)]
                            first_mm = False
                        elif sub >= 2:
                            wts = [("se", ev_exp[sub - 2])]
                            if (sub - 2) in ev_msk:
                                wts.append(("ge", ev_msk[sub - 2]))
                        P.op("tensor", lambda e, ping=ping, kg=kg, j=j, lh=lh, qg=qg: e.matmul(
                            pb[ping][:, :],
                            bigB[:, kg * 512 + j * 128:kg * 512 + (j + 1) * 128],
                            bigA[:, lh * 8192 + qg * 512:lh * 8192 + (qg + 1) * 512],
                            start=True, stop=True),
                            waits=wts, inc=("te", 1))
                        te_s = P.val("te")
                        eslot = ping
                        ev_exp[sub] = P.op("scalar", lambda e, ping=ping, eslot=eslot: e.activation(
                            ebuf[:, eslot * 512:(eslot + 1) * 512], pb[ping][:, :], AF.Exp),
                            waits=[("te", te_s)], inc=("se", 1))
                        if kg == qg:
                            ev_msk[sub] = P.op("gpsimd", lambda e, j=j, eslot=eslot: e.affine_select(
                                ebuf[:, eslot * 512:(eslot + 1) * 512],
                                ebuf[:, eslot * 512:(eslot + 1) * 512],
                                pattern=[[1, 512]], compare_op=ALU.is_ge, fill=0.0,
                                base=-(j * 128), channel_multiplier=-1),
                                waits=[("se", ev_exp[sub])], inc=("ge", 1))
                        wts_pv = [("se", ev_exp[sub])]
                        if sub in ev_msk:
                            wts_pv.append(("ge", ev_msk[sub]))
                        P.op("tensor", lambda e, kg=kg, j=j, eslot=eslot, sub=sub, nsub=nsub: e.matmul(
                            pb[4][:, :],
                            bigB[:, 8192 + kg * 512 + j * 128:8192 + kg * 512 + (j + 1) * 128],
                            ebuf[:, eslot * 512:(eslot + 1) * 512],
                            start=(sub == 0), stop=(sub == nsub - 1)),
                            waits=wts_pv)
                        P.op("tensor", lambda e, eslot=eslot, sub=sub, nsub=nsub: e.matmul(
                            pb[5][:, :], ones128,
                            ebuf[:, eslot * 512:(eslot + 1) * 512],
                            start=(sub == 0), stop=(sub == nsub - 1)),
                            waits=[("ge", ev_o128)] if (lh == 0 and qg == 0 and sub == 0) else (),
                            inc=("te", 1))
                        sub += 1
                te_fin = P.val("te")
                prev_oevac_s = P.op("scalar", lambda e, lh=lh, qg=qg: e.activation(
                    xlnT[:, lh * 8192 + qg * 512:lh * 8192 + (qg + 1) * 512],
                    pb[4][:, :], AF.Copy),
                    waits=[("te", te_fin)], inc=("se", 1))
                vl = P.op("vector", lambda e: e.tensor_copy(lrow, pb[5][0:1, :]),
                          waits=[("te", te_fin)], inc=("ve", 1))
                prev_levac_v = P.op("vector", lambda e: e.reciprocal(lrow_i, lrow),
                                    waits=[("ve", vl)] + ([("st", dl)] if dl else []),
                                    inc=("ve", 1))
                dl = P.op("sync", lambda e, lh=lh, qg=qg: e.dma_start(
                    out=lag_in.ap()[lh * 64 + qg * 4:lh * 64 + (qg + 1) * 4, :],
                    in_=lrow_i),
                    waits=[("ve", prev_levac_v)], inc=("st", 16))
            for b4 in range(16):
                te_y = None
                for j in range(4):
                    a = b4 * 4 + j
                    wts = []
                    if j == 0:
                        wts = [("se", prev_oevac_s)]
                        if prev_copy:
                            wts.append(("ve", prev_copy))
                    te_y = P.op("tensor", lambda e, lh=lh, a=a, j=j: e.transpose(
                        ptr[1][:, j * 128:(j + 1) * 128],
                        xlnT[:, lh * 8192 + a * 128:lh * 8192 + (a + 1) * 128],
                        self.identb[:, :]),
                        waits=wts, inc=("te", 1) if j == 3 else None)
                prev_copy = P.op("vector", lambda e: e.tensor_copy(
                    scr[2].bitcast(BF16)[:, 0:512], ptr[1][:, 0:512]),
                    waits=[("te", te_y)] + ([("st", dyi)] if dyi else []),
                    inc=("ve", 1))
                dyi = P.op("sync", lambda e, lh=lh, b4=b4: e.dma_start(
                    out=yag_in[lh].ap()[b4 * 512:(b4 + 1) * 512, :].rearrange(
                        "(jj p) d -> p jj d", p=128),
                    in_=scr[2].bitcast(BF16)[:, 0:512].rearrange(
                        "p (jj d) -> p jj d", d=128)),
                    waits=[("ve", prev_copy)], inc=("st", 16))
            cc3[lh] = P.op("gpsimd", lambda e, lh=lh: e.collective_compute(
                "AllGather", ALU.bypass, replica_groups=[list(range(NCORES))],
                ins=[yag_in[lh].ap().opt()], outs=[yag_out[lh].ap().opt()]),
                waits=[("st", dyi)], inc=("cc", 1))
        if DEBUG_L0F:
            dbg_y = nc.dram_tensor("dbg_y", [128, 16384], BF16, kind="ExternalOutput").ap()
            dbg_l = nc.dram_tensor("dbg_l", [128, 128], F32, kind="ExternalOutput").ap()
            P.op("sync", lambda e: e.dma_start(out=dbg_y, in_=xlnT[:, :]),
                 waits=[("se", prev_oevac_s)], inc=("st", 16))
            P.op("sync", lambda e: e.dma_start(out=dbg_l, in_=lag_in.ap()),
                 waits=[("st", dl)], inc=("st", 16))
        cc4 = P.op("gpsimd", lambda e: e.collective_compute(
            "AllGather", ALU.bypass, replica_groups=[list(range(NCORES))],
            ins=[lag_in.ap().opt()], outs=[lag_out.ap().opt()]),
            waits=[("st", dl)], inc=("cc", 1))

        for lh in range(2):
            for i in range(NT):
                for r in range(NCORES):
                    P.op("gpsimd", lambda e, lh=lh, i=i, r=r: e.indirect_dma_start(
                        out=bigA[:, i * 2048 + r * 256 + lh * 128:
                                 i * 2048 + r * 256 + (lh + 1) * 128],
                        out_offset=None,
                        in_=yag_out[lh].ap()[0:128, :],
                        in_offset=IndirectOffsetOnAxis(yidx[:, i * 8 + r:i * 8 + r + 1], 0)),
                        waits=[("cc", cc3[lh])] if (i == 0 and r == 0) else (),
                        inc=("wl", 16))
        P.op("gpsimd", lambda e: e.indirect_dma_start(
            out=scr[1][:, 0:128], out_offset=None,
            in_=lag_out.ap()[0:128, :],
            in_offset=IndirectOffsetOnAxis(lidx[:, 0:1], 0)),
            waits=[("cc", cc4)], inc=("wl", 16))
        ev_y_own = P.val("wl")
        te_l = P.op("tensor", lambda e: e.transpose(
            pb[0][:, 0:128], scr[1][:, 0:128], self.identf[:, :]),
            waits=[("wl", ev_y_own)], inc=("te", 1))
        v_l = P.op("vector", lambda e: e.tensor_copy(scr[2][:, 0:128], pb[0][:, 0:128]),
                   waits=[("te", te_l)], inc=("ve", 1))
        linv_cols = scr[2][:, 0:128]
        if DEBUG_L0F:
            dbg_yret = nc.dram_tensor("dbg_yret", [128, 16384], BF16, kind="ExternalOutput").ap()
            dbg_linv = nc.dram_tensor("dbg_linv", [128, 128], F32, kind="ExternalOutput").ap()
            P.op("sync", lambda e: e.dma_start(out=dbg_yret, in_=bigA[:, :]),
                 waits=[("wl", ev_y_own)], inc=("st", 16))
            P.op("sync", lambda e: e.dma_start(out=dbg_linv, in_=linv_cols),
                 waits=[("ve", v_l)], inc=("st", 16))
        for i in range(NT):
            for h in range(16):
                r, sub = h // 2, h % 2
                P.op("vector", lambda e, i=i, h=h, r=r, sub=sub: e.scalar_tensor_tensor(
                    res[:, i * D + h * 128:i * D + (h + 1) * 128],
                    bigA[:, i * 2048 + r * 256 + sub * 128:i * 2048 + r * 256 + (sub + 1) * 128],
                    linv_cols[:, h * 8 + i:h * 8 + i + 1],
                    res[:, i * D + h * 128:i * D + (h + 1) * 128],
                    op0=ALU.mult, op1=ALU.add),
                    waits=[("ve", v_l), ("wl", ev_y_own)] if (i == 0 and h == 0) else (),
                    inc=("ve", 1))

    def l1_attn(self):
        P, nc = self.P, self.nc
        res, xlnT, bigA, bigB = self.res, self.xlnT, self.bigA, self.bigB
        scr, pb, ptr = self.scr, self.pb, self.ptr
        wb0, wb1 = self.wbuf
        self.reset_cols()
        SC = 0.125

        wqk_h = self.din("l1a_wqk", [24, 128, 2048])
        wv_h = self.din("l1a_wv", [3, 128, 8192])
        wo_h = self.din("l1a_wo", [24, 128, 1024])
        fkv_h = self.din("l1a_fkv", [128, 1920])
        pkb_h = self.din("l1a_pkb", [128, 768])
        pvb_h = self.din("l1a_pvb", [128, 1024])
        idx_h = self.din("l1a_idx", [128, 288], percore=True, dtype=I32)

        qk_in = nc.dram_tensor("l1a_qk_in", [3072, TLOC], BF16, kind="Internal")
        qk_out = nc.dram_tensor("l1a_qk_out", [24576, TLOC], BF16, kind="Internal")
        v_in = nc.dram_tensor("l1a_vin", [24576, 64], BF16, kind="Internal")
        v_out = nc.dram_tensor("l1a_vout", [196608, 64], BF16, kind="Internal")
        yag_in = [nc.dram_tensor(f"l1a_yag_in{h}", [T, 64], BF16, kind="Internal")
                  for h in range(3)]
        yag_out = [nc.dram_tensor(f"l1a_yag_out{h}", [T * NCORES, 64], BF16,
                                  kind="Internal") for h in range(3)]

        ve_ent = P.val("ve")
        se_ent = P.val("se")
        te_ent = P.val("te")
        xl_ent = P.val("xl")
        ge_ent = P.val("ge")
        linv = self.col("l1a_linv", 4)

        tgI = self.tg.bitcast(I32)
        P.op("sync", lambda e: e.dma_start(out=tgI[:, 0:288], in_=idx_h),
             waits=[("ve", ve_ent)], inc=("xl", 16))
        ev_idx = P.val("xl")
        P.op("gpsimd", lambda e: e.dma_start(out=wb0[:, 0:1920], in_=fkv_h),
             waits=[("te", te_ent)], inc=("wl", 16))
        P.op("gpsimd", lambda e: e.dma_start(out=wb0[:, 1920:2688], in_=pkb_h),
             inc=("wl", 16))
        P.op("gpsimd", lambda e: e.dma_start(out=wb0[:, 2688:3712], in_=pvb_h),
             inc=("wl", 16))
        ev_tabs = P.val("wl")

        sbf = self.scrp.bitcast(BF16)
        cK, cV = sbf[:, 0:1024], sbf[:, 1024:2048]
        P.op("gpsimd", lambda e: e.memset(sbf[:, 0:2048], 0.0),
             waits=[("ve", ve_ent), ("se", se_ent), ("te", te_ent)], inc=("ge", 1))
        ev_mset = P.val("ge")

        ev_xln, ev_xlnT = self.emit_ln("l1a")
        se_ln = P.val("se")

        prev_cevac = 0
        for th in range(2):
            specs = [(pb[4], 0, 32, 2, 0), (pb[4], 64, 128, 8, 896),
                     (pb[5], 0, 32, 2, 448), (pb[5], 64, 128, 8, 1408)]
            evs = []
            for si, (bank, p0, p1, dt0, c0) in enumerate(specs):
                w = p1 - p0
                n = 16 - dt0
                te_c = None
                for i, dt in enumerate(range(dt0, 16)):
                    wts = []
                    if i == 0 and si == 0:
                        wts = [("wl", ev_tabs), ("ve", ev_xlnT), ("se", se_ln)]
                        if prev_cevac:
                            wts.append(("se", prev_cevac))
                    rhs = xlnT[:, dt * TLOC + th * 512:dt * TLOC + (th + 1) * 512]
                    te_c = P.op("tensor", lambda e, bank=bank, p0=p0, p1=p1, c0=c0,
                                i=i, w=w, n=n, rhs=rhs: e.matmul(
                        bank[p0:p1, :],
                        wb0[:, c0 + i * w:c0 + (i + 1) * w],
                        rhs, start=(i == 0), stop=(i == n - 1)),
                        waits=wts, inc=("te", 1) if i == n - 1 else None)
                evs.append(te_c)
            for si, (bank, p0, p1, dt0, c0) in enumerate(specs):
                dst = cK if si < 2 else cV
                prev_cevac = P.op("scalar", lambda e, bank=bank, p0=p0, p1=p1,
                                  dst=dst, th=th: e.activation(
                    dst[p0:p1, th * 512:(th + 1) * 512], bank[p0:p1, :], AF.Copy),
                    waits=[("te", evs[si])] + ([("ge", ev_mset)] if th == 0 and si == 0 else []),
                    inc=("se", 1))
        ev_c_done = P.val("se")

        ring = [sbf[:, 2048 + s * 1024:2048 + (s + 1) * 1024] for s in range(6)]
        ring_free = [0] * 6
        ystage_free = [0] * 8
        wslot_ready = [0, 0, 0]
        wslot_free = [0, 0, 0]
        psum_free = [0, 0, 0, 0]
        d_qk = 0
        P.op("gpsimd", lambda e: e.dma_start(out=bigB[:, 0:8192], in_=wv_h[0]),
             waits=[("te", te_ent), ("xl", xl_ent), ("ge", ge_ent)], inc=("wl", 16))
        vslot_ready = [P.val("wl"), 0]
        P.op("gpsimd", lambda e: e.dma_start(out=bigB[:, 8192:16384], in_=wv_h[1]),
             inc=("wl", 16))
        vslot_ready[1] = P.val("wl")
        vslot_free = [0, 0]

        for ot in range(24):
            slot = ot % 3
            w = wb1[:, slot * 2048:(slot + 1) * 2048]
            wts = [("te", wslot_free[slot])] if wslot_free[slot] else []
            if ot == 0:
                wts.append(("te", te_ent))
            P.op("gpsimd", lambda e, ot=ot, w=w: e.dma_start(out=w, in_=wqk_h[ot]),
                 waits=wts, inc=("wl", 16))
            wslot_ready[slot] = P.val("wl")
            te_m = {}
            corr = 12 <= ot < 18
            for th in range(2):
                pbi = (ot % 2) * 2 + th
                for dt in range(ND):
                    wts = []
                    if dt == 0:
                        wts = [("wl", wslot_ready[slot])]
                        if ot == 0 and th == 0:
                            wts += [("ve", ev_xlnT), ("se", se_ln)]
                        if psum_free[pbi]:
                            wts.append(("se", psum_free[pbi]))
                        if ot == 12 and th == 0:
                            wts.append(("se", ev_c_done))
                    te_m[th] = P.op("tensor", lambda e, pbi=pbi, w=w, dt=dt, th=th,
                                    corr=corr: e.matmul(
                        pb[pbi][:, :], w[:, dt * 128:(dt + 1) * 128],
                        xlnT[:, dt * TLOC + th * 512:dt * TLOC + (th + 1) * 512],
                        start=(dt == 0), stop=(dt == ND - 1 and not corr)),
                        waits=wts,
                        inc=("te", 1) if (dt == ND - 1 and not corr) else None)
                if corr:
                    otp = ot - 12
                    te_m[th] = P.op("tensor", lambda e, pbi=pbi, otp=otp, th=th: e.matmul(
                        pb[pbi][:, :],
                        wb0[:, 1920 + otp * 128:1920 + (otp + 1) * 128],
                        cK[:, th * 512:(th + 1) * 512],
                        start=False, stop=True), inc=("te", 1))
            wslot_free[slot] = te_m[1]
            rs = ot % 6
            ev2 = None
            for th in range(2):
                pbi = (ot % 2) * 2 + th
                wts = [("te", te_m[th])]
                if th == 0 and ring_free[rs]:
                    wts.append(("st", ring_free[rs]))
                ev2 = P.op("scalar", lambda e, rs=rs, th=th, pbi=pbi, ot=ot: e.activation(
                    ring[rs][:, th * 512:(th + 1) * 512], pb[pbi][:, :], AF.Copy,
                    scale=(SC if ot < 12 else 1.0)),
                    waits=wts, inc=("se", 1))
                psum_free[pbi] = ev2
            d_qk = P.op("sync", lambda e, ot=ot, rs=rs: e.dma_start(
                out=qk_in.ap()[ot * 128:(ot + 1) * 128, :], in_=ring[rs][:, :]),
                waits=[("se", ev2)], inc=("st", 16))
            ring_free[rs] = d_qk

        d_v = 0
        vc_last_te = {}
        for vc in range(3):
            half = vc % 2
            if vc == 2:
                P.op("gpsimd", lambda e: e.dma_start(
                    out=bigB[:, 0:8192], in_=wv_h[2]),
                    waits=[("te", vc_last_te[0])], inc=("wl", 16))
                vslot_ready[0] = P.val("wl")
            wv = bigB[:, half * 8192:(half + 1) * 8192]
            for tt in range(NT):
                pbi = tt % 4
                wts0 = [("wl", vslot_ready[half])]
                if psum_free[pbi]:
                    wts0.append(("se", psum_free[pbi]))
                if vc == 0 and tt == 0:
                    wts0 += [("se", ev_c_done), ("wl", ev_tabs)]
                first_done = False
                if vc < 2:
                    P.op("tensor", lambda e, pbi=pbi, tt=tt, vc=vc: e.matmul(
                        pb[pbi][:, :], cV[:, tt * 128:(tt + 1) * 128],
                        wb0[:, 2688 + vc * 512:2688 + (vc + 1) * 512],
                        start=True, stop=False), waits=wts0)
                    first_done = True
                te_v = None
                for dt in range(ND):
                    wts = [] if (first_done or dt > 0) else wts0
                    te_v = P.op("tensor", lambda e, pbi=pbi, dt=dt, tt=tt, wv=wv,
                                vc=vc: e.matmul(
                        pb[pbi][:, :],
                        xlnT[:, dt * TLOC + tt * 128:dt * TLOC + (tt + 1) * 128],
                        wv[:, dt * 512:(dt + 1) * 512],
                        start=(dt == 0 and vc == 2), stop=(dt == ND - 1)),
                        waits=wts, inc=("te", 1) if dt == ND - 1 else None)
                vc_last_te[vc] = te_v
                rs = (vc * 8 + tt) % 6
                wts = [("te", te_v)]
                if ring_free[rs]:
                    wts.append(("st", ring_free[rs]))
                ev3 = P.op("scalar", lambda e, rs=rs, pbi=pbi: e.activation(
                    ring[rs][:, 0:512], pb[pbi][:, :], AF.Copy),
                    waits=wts, inc=("se", 1))
                psum_free[pbi] = ev3
                for hl in range(8):
                    d_v = P.op("sync", lambda e, vc=vc, tt=tt, hl=hl, rs=rs: e.dma_start(
                        out=v_in.ap()[(vc * 8 + hl) * 1024 + tt * 128:
                                      (vc * 8 + hl) * 1024 + (tt + 1) * 128, :],
                        in_=ring[rs][:, hl * 64:(hl + 1) * 64]),
                        waits=[("se", ev3)] if hl == 0 else (), inc=("st", 16))
                ring_free[rs] = d_v
            vslot_free[half] = vc_last_te[vc]
        ev_p4 = P.val("te")

        cc_qk = P.op("gpsimd", lambda e: e.collective_compute(
            "AllGather", ALU.bypass, replica_groups=[list(range(NCORES))],
            ins=[qk_in.ap().opt()], outs=[qk_out.ap().opt()]),
            waits=[("st", d_qk)], inc=("cc", 1))
        cc_v = P.op("gpsimd", lambda e: e.collective_compute(
            "AllGather", ALU.bypass, replica_groups=[list(range(NCORES))],
            ins=[v_in.ap().opt()], outs=[v_out.ap().opt()]),
            waits=[("st", d_v)], inc=("cc", 1))

        for ht in range(12):
            P.op("gpsimd", lambda e, ht=ht: e.dma_start(
                out=xlnT[:, ht * 1024:(ht + 1) * 1024], in_=wo_h[ht]),
                waits=[("te", ev_p4)] if ht == 0 else (), inc=("wl", 16))
        ev_wo0 = P.val("wl")

        for r in range(NCORES):
            P.op("gpsimd", lambda e, r=r: e.indirect_dma_start(
                out=bigA[:, r * TLOC:(r + 1) * TLOC], out_offset=None,
                in_=qk_out.ap()[0:128, :],
                in_offset=IndirectOffsetOnAxis(tgI[:, 256 + r:257 + r], 0)),
                waits=[("cc", cc_qk), ("xl", ev_idx)] if r == 0 else (),
                inc=("wl", 16))
        for r in range(NCORES):
            P.op("gpsimd", lambda e, r=r: e.indirect_dma_start(
                out=bigA[0:64, 8192 + r * TLOC:8192 + (r + 1) * TLOC],
                out_offset=None,
                in_=qk_out.ap()[0:64, :],
                in_offset=IndirectOffsetOnAxis(tgI[0:64, 264 + r:265 + r], 0)),
                inc=("wl", 16))
        for r in range(NCORES):
            P.op("gpsimd", lambda e, r=r: e.indirect_dma_start(
                out=bigB[:, r * TLOC:(r + 1) * TLOC], out_offset=None,
                in_=qk_out.ap()[0:128, :],
                in_offset=IndirectOffsetOnAxis(tgI[:, 272 + r:273 + r], 0)),
                waits=[("te", ev_p4)] if r == 0 else (), inc=("wl", 16))
        for r in range(NCORES):
            P.op("gpsimd", lambda e, r=r: e.indirect_dma_start(
                out=bigB[0:64, 8192 + r * TLOC:8192 + (r + 1) * TLOC],
                out_offset=None,
                in_=qk_out.ap()[0:64, :],
                in_offset=IndirectOffsetOnAxis(tgI[0:64, 280 + r:281 + r], 0)),
                inc=("wl", 16))
        for r in range(NCORES):
            P.op("gpsimd", lambda e, r=r: e.indirect_dma_start(
                out=bigA[64:128, 8192 + r * TLOC:8192 + (r + 1) * TLOC],
                out_offset=None,
                in_=qk_out.ap()[0:64, :],
                in_offset=IndirectOffsetOnAxis(tgI[0:64, 264 + r:265 + r], 0)),
                inc=("wl", 16))
        for r in range(NCORES):
            P.op("gpsimd", lambda e, r=r: e.indirect_dma_start(
                out=bigB[64:128, 8192 + r * TLOC:8192 + (r + 1) * TLOC],
                out_offset=None,
                in_=qk_out.ap()[0:64, :],
                in_offset=IndirectOffsetOnAxis(tgI[0:64, 280 + r:281 + r], 0)),
                inc=("wl", 16))
        def vbase(gt):
            if gt < 31:
                return self.wbuf[0][:, gt * 195:gt * 195 + 195]
            if gt < 62:
                return self.wbuf[1][:, (gt - 31) * 195:(gt - 31) * 195 + 195]
            return sbf[:, 1024 + (gt - 62) * 195:1024 + (gt - 62) * 195 + 195]

        P.op("gpsimd", lambda e: e.memset(wb0[:, :], 1.0),
             waits=[("te", ev_p4)], inc=("ge", 1))
        P.op("gpsimd", lambda e: e.memset(wb1[:, :], 1.0), inc=("ge", 1))
        P.op("gpsimd", lambda e: e.memset(sbf[:, 1024:1414], 1.0), inc=("ge", 1))
        ev_vones = P.val("ge")
        for r in range(NCORES):
            for hl in range(3):
                for tt in range(NT):
                    gt = r * NT + tt
                    wts = []
                    if r == 0 and hl == 0 and tt == 0:
                        wts = [("cc", cc_v), ("te", ev_p4), ("ge", ev_vones)]
                    P.op("gpsimd", lambda e, r=r, hl=hl, tt=tt, gt=gt: e.indirect_dma_start(
                        out=vbase(gt)[:, hl * 65:hl * 65 + 64],
                        out_offset=None,
                        in_=v_out.ap()[0:128, :],
                        in_offset=IndirectOffsetOnAxis(
                            tgI[:, (r * 3 + hl) * 8 + tt:(r * 3 + hl) * 8 + tt + 1], 0)),
                        waits=wts, inc=("wl", 16))
        ev_gath = P.val("wl")

        def qap(lh, qc):
            if lh < 2:
                return bigA[lh * 64:(lh + 1) * 64, qc * 512:(qc + 1) * 512]
            return bigA[0:64, 8192 + qc * 512:8192 + (qc + 1) * 512]

        def kap(lh, gt):
            if lh < 2:
                return bigB[lh * 64:(lh + 1) * 64, gt * 128:(gt + 1) * 128]
            return bigB[0:64, 8192 + gt * 128:8192 + (gt + 1) * 128]

        def vap(lh, gt):
            return vbase(gt)[:, lh * 65:(lh + 1) * 65]

        def qap2(half, qc):
            return bigA[half * 64:half * 64 + 64,
                        8192 + qc * 512:8192 + (qc + 1) * 512]

        def kap2(half, gt):
            return bigB[half * 64:half * 64 + 64,
                        8192 + gt * 128:8192 + (gt + 1) * 128]

        eslots = [sbf[:, 0:512], sbf[:, 512:1024],
                  sbf[:, 3072:3584], sbf[:, 3584:4096]]
        Oev = scr[1][:, 0:512]
        ys = [sbf[:, 4096 + s * 64:4096 + (s + 1) * 64] for s in range(8)]
        stf = dict(ysc=0, dy=0, oevac=0, trlast=0)
        tr_cons = {1: 0, 3: 0}
        cc_yh = {}

        def finalize(h, lh, qc, te_fin, tbank):
            oev = P.op("vector", lambda e, h=h: e.tensor_copy(
                Oev[0:65, :], pb[4 + h][0:65, :]),
                waits=[("te", te_fin)] +
                      ([("te", stf["trlast"])] if stf["trlast"] else []),
                inc=("ve", 1))
            stf["oevac"] = oev
            for j in range(4):
                wts = [("ve", oev)] if j == 0 else []
                if tr_cons[tbank]:
                    wts.append(("ve", tr_cons[tbank]))
                stf["trlast"] = P.op("tensor", lambda e, tbank=tbank, j=j: e.transpose(
                    pb[tbank][:, 0:65], Oev[0:65, j * 128:(j + 1) * 128],
                    self.identf[0:65, 0:65]),
                    waits=wts, inc=("te", 1))
                rec = P.op("vector", lambda e, tbank=tbank, j=j: e.reciprocal(
                    linv[:, j:j + 1], pb[tbank][:, 64:65]),
                    waits=[("te", stf["trlast"])], inc=("ve", 1))
                s = stf["ysc"] % 8
                stf["ysc"] += 1
                wts = [("ve", rec)]
                if ystage_free[s]:
                    wts.append(("st", ystage_free[s]))
                evm = P.op("vector", lambda e, tbank=tbank, j=j, s=s: e.tensor_scalar_mul(
                    ys[s][:, :], pb[tbank][:, 0:64], linv[:, j:j + 1]),
                    waits=wts, inc=("ve", 1))
                tr_cons[tbank] = evm
                stf["dy"] = P.op("sync", lambda e, lh=lh, qc=qc, j=j, s=s: e.dma_start(
                    out=yag_in[lh].ap()[qc * 512 + j * 128:qc * 512 + (j + 1) * 128, :],
                    in_=ys[s][:, :]),
                    waits=[("ve", evm)], inc=("st", 16))
                ystage_free[s] = stf["dy"]

        first_flash = [True]
        prev_exp_last = [0]
        for qc in range(NQC):
            nk = 4 * (qc + 1)
            exp_ev = {}
            msk_ev = {}
            pv_te = {}

            def emit_pv(h, gt, nk=nk):
                wts = [("se", exp_ev[(h, gt)])]
                if (h, gt) in msk_ev:
                    wts.append(("ge", msk_ev[(h, gt)]))
                if gt == 0 and stf["oevac"]:
                    wts.append(("ve", stf["oevac"]))
                sl = eslots[h * 2 + (gt % 2)]
                pv_te[(h, gt)] = P.op("tensor", lambda e, h=h, gt=gt, sl=sl,
                                      nk=nk: e.matmul(
                    pb[4 + h][0:65, :], vap(h, gt), sl,
                    start=(gt == 0), stop=(gt == nk - 1)),
                    waits=wts, inc=("te", 1))

            prev_batch = []
            for b in range(nk // 2):
                batch = []
                te_s = {}
                for j2 in range(2):
                    gt = 2 * b + j2
                    for h in range(2):
                        bi = h * 2 + (gt % 2)
                        bank = pb[bi]
                        wts = []
                        if first_flash[0]:
                            wts.append(("wl", ev_gath))
                            first_flash[0] = False
                        if gt >= 2:
                            wts.append(("se", exp_ev[(h, gt - 2)]))
                        elif prev_exp_last[0]:
                            wts.append(("se", prev_exp_last[0]))
                        if gt == 1 and tr_cons[bi]:
                            wts.append(("ve", tr_cons[bi]))
                        P.op("tensor", lambda e, h=h, gt=gt, qc=qc, bank=bank: e.matmul(
                            bank[:, :], kap(h, gt), qap(h, qc),
                            start=True, stop=True), waits=wts, inc=("te", 1))
                        te_s[(h, gt)] = P.val("te")
                        batch.append((h, gt))
                for (h, gt) in prev_batch:
                    emit_pv(h, gt)
                prev_batch = batch
                for (h, gt) in batch:
                    sl = eslots[h * 2 + (gt % 2)]
                    bank = pb[h * 2 + (gt % 2)]
                    ew = [("te", pv_te[(h, gt - 2)])] if (h, gt - 2) in pv_te \
                        else [("te", te_s[(h, gt)])]
                    exp_ev[(h, gt)] = P.op("scalar", lambda e, bank=bank,
                                           sl=sl: e.activation(
                        sl, bank[:, :], AF.Exp), waits=ew, inc=("se", 1))
                    if gt // 4 == qc:
                        msk_ev[(h, gt)] = P.op("gpsimd", lambda e, gt=gt,
                                               sl=sl: e.affine_select(
                            sl, sl, pattern=[[1, 512]], compare_op=ALU.is_ge,
                            fill=0.0, base=-((gt % 4) * 128),
                            channel_multiplier=-1),
                            waits=[("se", exp_ev[(h, gt)])], inc=("ge", 1))
            for (h, gt) in prev_batch:
                emit_pv(h, gt)
            prev_exp_last[0] = exp_ev[(1, nk - 1)]
            te_fin = P.val("te")
            finalize(0, 0, qc, te_fin, 1)
            finalize(1, 1, qc, te_fin, 3)
        for lh in range(2):
            cc_yh[lh] = P.op("gpsimd", lambda e, lh=lh: e.collective_compute(
                "AllGather", ALU.bypass, replica_groups=[list(range(NCORES))],
                ins=[yag_in[lh].ap().opt()], outs=[yag_out[lh].ap().opt()]),
                waits=[("st", stf["dy"])], inc=("cc", 1))

        bankmap = [0, 2, 1, 3]
        for qc in range(NQC):
            nk = 4 * (qc + 1)
            exp_ev = {}
            msk_ev = {}
            pv_te = {}

            def emit_pv2(gt, nk=nk):
                wts = [("se", exp_ev[gt])]
                if gt in msk_ev:
                    wts.append(("ge", msk_ev[gt]))
                if gt == 0 and stf["oevac"]:
                    wts.append(("ve", stf["oevac"]))
                sl = eslots[gt % 4]
                pv_te[gt] = P.op("tensor", lambda e, gt=gt, sl=sl, nk=nk: e.matmul(
                    pb[4][0:65, :], vap(2, gt), sl,
                    start=(gt == 0), stop=(gt == nk - 1)),
                    waits=wts, inc=("te", 1))

            prev_batch = []
            for b in range(nk // 4):
                batch = []
                te_s = {}
                for j4 in range(4):
                    gt = 4 * b + j4
                    bi = bankmap[gt % 4]
                    bank = pb[bi]
                    half = gt % 2 if not L1A_NODUP else 0
                    wts = []
                    if gt >= 4:
                        wts.append(("se", exp_ev[gt - 4]))
                    elif prev_exp_last[0]:
                        wts.append(("se", prev_exp_last[0]))
                    if bi in (1, 3) and gt <= 3 and tr_cons[bi]:
                        wts.append(("ve", tr_cons[bi]))
                    P.op("tensor", lambda e, gt=gt, qc=qc, bank=bank,
                         half=half: e.matmul(
                        bank[:, :], kap2(half, gt), qap2(half, qc),
                        start=True, stop=True), waits=wts, inc=("te", 1))
                    te_s[gt] = P.val("te")
                    batch.append(gt)
                for gt in prev_batch:
                    emit_pv2(gt)
                prev_batch = batch
                for gt in batch:
                    bank, sl = pb[bankmap[gt % 4]], eslots[gt % 4]
                    ew = [("te", pv_te[gt - 4])] if (gt - 4) in pv_te \
                        else [("te", te_s[gt])]
                    exp_ev[gt] = P.op("scalar", lambda e, bank=bank,
                                      sl=sl: e.activation(
                        sl, bank[:, :], AF.Exp), waits=ew, inc=("se", 1))
                    if gt // 4 == qc:
                        msk_ev[gt] = P.op("gpsimd", lambda e, gt=gt,
                                          sl=sl: e.affine_select(
                            sl, sl, pattern=[[1, 512]], compare_op=ALU.is_ge,
                            fill=0.0, base=-((gt % 4) * 128),
                            channel_multiplier=-1),
                            waits=[("se", exp_ev[gt])], inc=("ge", 1))
            for gt in prev_batch:
                emit_pv2(gt)
            prev_exp_last[0] = exp_ev[nk - 1]
            te_fin = P.val("te")
            finalize(0, 2, qc, te_fin, 1)
        cc_yh[2] = P.op("gpsimd", lambda e: e.collective_compute(
            "AllGather", ALU.bypass, replica_groups=[list(range(NCORES))],
            ins=[yag_in[2].ap().opt()], outs=[yag_out[2].ap().opt()]),
            waits=[("st", stf["dy"])], inc=("cc", 1))
        te_flash = P.val("te")
        se_flash = P.val("se")

        for ht in range(12):
            dst = wb0 if ht < 6 else wb1
            P.op("gpsimd", lambda e, ht=ht, dst=dst: e.dma_start(
                out=dst[:, (ht % 6) * 1024:(ht % 6 + 1) * 1024], in_=wo_h[12 + ht]),
                waits=[("te", te_flash)] if ht == 0 else (), inc=("wl", 16))
        ev_wo1 = P.val("wl")

        for hl in range(3):
            for tt in range(NT):
                for r in range(NCORES):
                    wts = []
                    if tt == 0 and r == 0:
                        wts = [("cc", cc_yh[hl])]
                        if hl == 0:
                            wts.append(("te", te_flash))
                    P.op("gpsimd", lambda e, hl=hl, tt=tt, r=r: e.indirect_dma_start(
                        out=bigB[:, tt * 1536 + r * 192 + hl * 64:
                                 tt * 1536 + r * 192 + (hl + 1) * 64],
                        out_offset=None,
                        in_=yag_out[hl].ap()[0:128, :],
                        in_offset=IndirectOffsetOnAxis(
                            tgI[:, 192 + tt * 8 + r:192 + tt * 8 + r + 1], 0)),
                        waits=wts, inc=("wl", 16))
        ev_ygath = P.val("wl")
        if DEBUG_L1A:
            dbg_yall = nc.dram_tensor("dbg_yall", [128, 12288], BF16,
                                      kind="ExternalOutput").ap()
            P.op("sync", lambda e: e.dma_start(out=dbg_yall, in_=bigB[:, 0:12288]),
                 waits=[("wl", ev_ygath)], inc=("st", 16))

        ev_ytr = {}
        prev_ycopy = {0: 0, 1: 0}
        for tt in range(NT):
            for ht in range(12):
                b = ht % 2
                wts = [("wl", ev_ygath)] if (tt == 0 and ht == 0) else []
                if prev_ycopy[b]:
                    wts.append(("ve", prev_ycopy[b]))
                te_y = P.op("tensor", lambda e, tt=tt, ht=ht, b=b: e.transpose(
                    ptr[b][:, 0:128],
                    bigB[:, tt * 1536 + ht * 128:tt * 1536 + (ht + 1) * 128],
                    self.identb[:, :]), waits=wts, inc=("te", 1))
                prev_ycopy[b] = P.op("vector", lambda e, tt=tt, ht=ht, b=b: e.tensor_copy(
                    bigA[:, tt * 1536 + ht * 128:tt * 1536 + (ht + 1) * 128],
                    ptr[b][:, 0:128]), waits=[("te", te_y)], inc=("ve", 1))
            ev_ytr[tt] = prev_ycopy[(11) % 2]

        wo_free = [0, 0]
        for tt in range(NT):
            for dc in range(4):
                pbi = (tt * 4 + dc) % 2
                te_o = None
                for ht in range(12):
                    wts = []
                    if ht == 0:
                        wts = [("ve", ev_ytr[tt])]
                        if tt == 0 and dc == 0:
                            wts.append(("se", se_flash))
                        wts.append(("wl", ev_wo0 if dc < 2 else ev_wo1))
                        if wo_free[pbi]:
                            wts.append(("ve", wo_free[pbi]))
                    if dc < 2:
                        rhs = xlnT[:, ht * 1024 + (dc % 2) * 512:
                                   ht * 1024 + (dc % 2) * 512 + 512]
                    else:
                        src = wb0 if ht < 6 else wb1
                        rhs = src[:, (ht % 6) * 1024 + (dc % 2) * 512:
                                  (ht % 6) * 1024 + (dc % 2) * 512 + 512]
                    te_o = P.op("tensor", lambda e, pbi=pbi, tt=tt, ht=ht,
                                rhs=rhs: e.matmul(
                        pb[pbi][:, :],
                        bigA[:, tt * 1536 + ht * 128:tt * 1536 + (ht + 1) * 128],
                        rhs, start=(ht == 0), stop=(ht == 11)),
                        waits=wts, inc=("te", 1) if ht == 11 else None)
                wo_free[pbi] = P.op("vector", lambda e, pbi=pbi, tt=tt, dc=dc: e.tensor_tensor(
                    res[:, tt * D + dc * 512:tt * D + (dc + 1) * 512],
                    pb[pbi][:, :],
                    res[:, tt * D + dc * 512:tt * D + (dc + 1) * 512],
                    op=ALU.add), waits=[("te", te_o)], inc=("ve", 1))

    def l1_ffn(self):
        P, nc = self.P, self.nc
        res, xlnT, bigA, bigB = self.res, self.xlnT, self.bigA, self.bigB
        scr, pb = self.scr, self.pb
        self.reset_cols()
        NI = 11
        NSI = 22

        rw_h = self.din("l1f_rw", [128, 128])
        gw_h = self.din("l1f_gw", [8, NI, 128, D])
        uw_h = self.din("l1f_uw", [8, NI, 128, D])
        dw_h = self.din("l1f_dw", [8, NI, 128, D])
        sgw_h = self.din("l1f_sgw", [NSI, 128, D])
        suw_h = self.din("l1f_suw", [NSI, 128, D])
        sdw_h = self.din("l1f_sdw", [NSI, 128, D])
        sgb_h = self.din("l1f_sgb", [128, NSI])
        sub_h = self.din("l1f_sub", [128, NSI])
        sdb_h = self.din("l1f_sdb", [1, D])

        rw_sb = self.col("rw_sb", 64).bitcast(BF16)
        ctab = self.col("ctab", 64)
        sgb_cm = self.col("sgb", NSI)
        sub_cm = self.col("sub", NSI)
        nm1 = self.col("nm1")
        nm2 = self.col("nm2")
        w1c = self.col("w1c")
        w2c = self.col("w2c")
        dcol = self.col("dcol")

        ent = [("ve", P.val("ve")), ("se", P.val("se")), ("te", P.val("te"))]
        self._l1f_te_ent = P.val("te")
        P.op("gpsimd", lambda e: e.dma_start(out=rw_sb, in_=rw_h), waits=ent,
             inc=("wl", 16))
        P.op("sync", lambda e: e.dma_start(out=sgb_cm, in_=sgb_h), waits=ent,
             inc=("xl", 16))
        P.op("sync", lambda e: e.dma_start(out=sub_cm, in_=sub_h), waits=ent,
             inc=("xl", 16))
        ev_rw = P.val("wl")
        ev_b = P.val("xl")
        if DEBUG_L0F:
            dbg_b0 = nc.dram_tensor("dbg_b0", [128, 124], F32, kind="ExternalOutput").ap()
            P.op("sync", lambda e: e.dma_start(out=dbg_b0, in_=scr[2][:, 900:1024]),
                 inc=("st", 16))

        ev_xln, ev_xlnT = self.emit_ln("l1f")

        c1s = scr[2][:, 0:8]
        lms = scr[2][:, 8:16]
        c2s = scr[2][:, 16:24]
        t1s = scr[2][:, 24:32]
        prev_comb = 0
        for tt in range(NT):
            te_r = None
            for dt in range(ND):
                wts = []
                if dt == 0:
                    wts = [("ve", ev_xlnT), ("wl", ev_rw)]
                    if prev_comb:
                        wts.append(("ve", prev_comb))
                te_r = P.op("tensor", lambda e, tt=tt, dt=dt: e.matmul(
                    pb[4][:, 0:8],
                    xlnT[:, dt * TLOC + tt * 128:dt * TLOC + (tt + 1) * 128],
                    rw_sb[:, dt * 8:(dt + 1) * 8],
                    start=(dt == 0), stop=(dt == ND - 1)),
                    waits=wts, inc=("te", 1) if dt == ND - 1 else None)
            v1 = P.op("vector", lambda e: e.reduce_max(
                nm1, pb[4][:, 0:8], axis=AX.X, negate=True),
                waits=[("te", te_r)], inc=("ve", 1))
            v2 = P.op("vector", lambda e: e.tensor_scalar(
                c1s, pb[4][:, 0:8], nm1, 0.0, op0=ALU.add, op1=ALU.is_ge),
                waits=[("ve", v1)], inc=("ve", 1))
            v3 = P.op("vector", lambda e: e.scalar_tensor_tensor(
                lms, c1s, -1e30, pb[4][:, 0:8], op0=ALU.mult, op1=ALU.add),
                waits=[("ve", v2)], inc=("ve", 1))
            v4 = P.op("vector", lambda e: e.reduce_max(
                nm2, lms, axis=AX.X, negate=True),
                waits=[("ve", v3)], inc=("ve", 1))
            v5 = P.op("vector", lambda e: e.tensor_scalar(
                c2s, lms, nm2, 0.0, op0=ALU.add, op1=ALU.is_ge),
                waits=[("ve", v4)], inc=("ve", 1))
            v6 = P.op("vector", lambda e: e.tensor_tensor(
                dcol, nm2, nm1, op=ALU.subtract),
                waits=[("ve", v5)], inc=("ve", 1))
            s1 = P.op("scalar", lambda e: e.activation(w1c, dcol, AF.Sigmoid),
                      waits=[("ve", v6)], inc=("se", 1))
            v7 = P.op("vector", lambda e: e.tensor_scalar(
                w2c, w1c, -1.0, 1.0, op0=ALU.mult, op1=ALU.add),
                waits=[("se", s1)], inc=("ve", 1))
            v8 = P.op("vector", lambda e: e.tensor_scalar_mul(t1s, c1s, w1c),
                      waits=[("ve", v7)], inc=("ve", 1))
            prev_comb = P.op("vector", lambda e, tt=tt: e.scalar_tensor_tensor(
                ctab[:, tt * 8:(tt + 1) * 8], c2s, w2c, t1s,
                op0=ALU.mult, op1=ALU.add),
                waits=[("ve", v8)], inc=("ve", 1))
        ev_ctab = prev_comb

        silu_bf = scr[0].bitcast(BF16)
        up_bf = scr[1].bitcast(BF16)
        self._l1f_evb = ev_b

        def ffn_pass(nit, gsrc, usrc, dsrc, gbias, ubias, comb_col, sdb_src,
                     first_wait_v, first_wait_s):
            wfree = [0, 0]
            prev_sil = 0
            prev_hmul = 0
            for it in range(nit):
                slot = it % 2
                w = self.wbuf[slot]
                wts = [("te", wfree[slot])] if wfree[slot] else []
                if it == 0:
                    wts += [("ve", first_wait_v), ("se", first_wait_s)]
                    if not wfree[slot] and self._l1f_te_ent:
                        wts.append(("te", self._l1f_te_ent))
                P.op("gpsimd", lambda e, it=it, w=w: e.dma_start(
                    out=w[:, 0:2048], in_=gsrc(it)), waits=wts, inc=("wl", 16))
                P.op("gpsimd", lambda e, it=it, w=w: e.dma_start(
                    out=w[:, 2048:4096], in_=usrc(it)), inc=("wl", 16))
                wready = P.val("wl")
                te_g = {}
                te_u = {}
                for th in range(2):
                    for dt in range(ND):
                        wts = []
                        if th == 0 and dt == 0:
                            wts = [("wl", wready), ("ve", ev_xlnT)]
                            if prev_hmul:
                                wts.append(("ve", prev_hmul))
                            if prev_sil:
                                wts.append(("se", prev_sil))
                        te_g[th] = P.op("tensor", lambda e, th=th, dt=dt, w=w: e.matmul(
                            pb[th][:, :], w[:, dt * 128:(dt + 1) * 128],
                            xlnT[:, dt * TLOC + th * 512:dt * TLOC + (th + 1) * 512],
                            start=(dt == 0), stop=(dt == ND - 1)),
                            waits=wts, inc=("te", 1) if dt == ND - 1 else None)
                for th in range(2):
                    for dt in range(ND):
                        te_u[th] = P.op("tensor", lambda e, th=th, dt=dt, w=w: e.matmul(
                            pb[2 + th][:, :], w[:, 2048 + dt * 128:2048 + (dt + 1) * 128],
                            xlnT[:, dt * TLOC + th * 512:dt * TLOC + (th + 1) * 512],
                            start=(dt == 0), stop=(dt == ND - 1)),
                            inc=("te", 1) if dt == ND - 1 else None)
                wfree[slot] = te_u[1]
                for th in range(2):
                    kw = {}
                    if gbias is not None:
                        kw["bias"] = gbias(it)
                    prev_sil = P.op("scalar", lambda e, th=th, it=it, kw=kw: e.activation(
                        silu_bf[:, th * 512:(th + 1) * 512], pb[th][:, :],
                        AF.Silu, **kw),
                        waits=[("te", te_g[th])] +
                              ([("xl", self._l1f_evb)] if kw else []),
                        inc=("se", 1))
                    if ubias is not None:
                        vub = P.op("vector", lambda e, th=th, it=it: e.tensor_scalar(
                            up_bf[:, th * 512:(th + 1) * 512], pb[2 + th][:, :],
                            ubias(it), None, op0=ALU.add),
                            waits=[("te", te_u[th]), ("xl", self._l1f_evb)],
                            inc=("ve", 1))
                        prev_hmul = P.op("vector", lambda e, th=th, it=it: e.tensor_tensor(
                            bigA[:, it * TLOC + th * 512:it * TLOC + (th + 1) * 512],
                            up_bf[:, th * 512:(th + 1) * 512],
                            silu_bf[:, th * 512:(th + 1) * 512], op=ALU.mult),
                            waits=[("se", prev_sil), ("ve", vub)], inc=("ve", 1))
                    else:
                        prev_hmul = P.op("vector", lambda e, th=th, it=it: e.tensor_tensor(
                            bigA[:, it * TLOC + th * 512:it * TLOC + (th + 1) * 512],
                            pb[2 + th][:, :],
                            silu_bf[:, th * 512:(th + 1) * 512], op=ALU.mult),
                            waits=[("se", prev_sil), ("te", te_u[th])], inc=("ve", 1))
            if DEBUG_L0F and not hasattr(self, "_l1f_d2"):
                self._l1f_d2 = True
                dbg_up2 = nc.dram_tensor("dbg_up2", [128, 2048], BF16, kind="ExternalOutput").ap()
                dbg_bias = nc.dram_tensor("dbg_bias", [128, 124], F32, kind="ExternalOutput").ap()
                P.op("sync", lambda e: e.dma_start(out=dbg_up2, in_=up_bf[:, :]),
                     waits=[("ve", prev_hmul)], inc=("st", 16))
                P.op("sync", lambda e: e.dma_start(out=dbg_bias, in_=scr[2][:, 900:1024]),
                     waits=[("ve", prev_hmul)], inc=("st", 16))
            ev_h = prev_hmul
            piece = 1024
            nphase = D // piece
            prev_ev = 0
            dmaD = None
            for ph in range(nphase):
                for it in range(nit):
                    wts = []
                    if it == 0:
                        wts = [("ve", ev_h)]
                        if prev_ev:
                            wts.append(("ve", prev_ev))
                        if dmaD:
                            wts.append(dmaD)
                    P.op("gpsimd", lambda e, ph=ph, it=it: e.dma_start(
                        out=bigB[:, it * piece:(it + 1) * piece],
                        in_=dsrc(it)[:, ph * piece:(ph + 1) * piece]),
                        waits=wts, inc=("wl", 16))
                dmaD = ("wl", P.val("wl"))
                if sdb_src is not None:
                    dsb = P.op("gpsimd", lambda e, ph=ph: e.dma_start(
                        out=scr[3][:, 0:piece],
                        in_=sdb_src[0:1, ph * piece:(ph + 1) * piece]
                        .to_broadcast([128, piece])),
                        waits=[("ve", prev_ev)] if prev_ev else [("ve", ev_h)],
                        inc=("wl", 16))
                    dmaD2 = ("wl", dsb)
                else:
                    dmaD2 = None
                nq = piece // 512
                for tt in range(NT):
                    te_d = {}
                    for it in range(nit):
                        for q2 in range(nq):
                            wts = []
                            if it == 0 and q2 == 0:
                                wts = [dmaD, ("ve", ev_h)] + ([dmaD2] if dmaD2 else [])
                                if tt == 0 and ph == 0:
                                    wts.append(("ve", ev_ctab))
                                if prev_ev:
                                    wts.append(("ve", prev_ev))
                            te_d[q2] = P.op("tensor", lambda e, it=it, q2=q2, tt=tt: e.matmul(
                                pb[4 + q2][:, :],
                                bigA[:, it * TLOC + tt * 128:it * TLOC + (tt + 1) * 128],
                                bigB[:, it * piece + q2 * 512:it * piece + (q2 + 1) * 512],
                                start=(it == 0), stop=(it == nit - 1)),
                                waits=wts,
                                inc=("te", 1) if (it == nit - 1 and q2 == nq - 1) else None)
                    if sdb_src is not None:
                        for q2 in range(nq):
                            P.op("vector", lambda e, q2=q2: e.tensor_tensor(
                                pb[4 + q2][:, :], pb[4 + q2][:, :],
                                scr[3][:, q2 * 512:(q2 + 1) * 512], op=ALU.add),
                                waits=([("te", te_d[nq - 1]), dmaD2]
                                       if q2 == 0 else []),
                                inc=("ve", 1))
                    for q2 in range(nq):
                        col0 = tt * D + ph * piece + q2 * 512
                        if comb_col is not None:
                            prev_ev = P.op("vector", lambda e, tt=tt, q2=q2, col0=col0: e.scalar_tensor_tensor(
                                res[:, col0:col0 + 512], pb[4 + q2][:, :],
                                comb_col(tt), res[:, col0:col0 + 512],
                                op0=ALU.mult, op1=ALU.add),
                                waits=[("te", te_d[nq - 1])] if q2 == 0 else (),
                                inc=("ve", 1))
                        else:
                            prev_ev = P.op("vector", lambda e, tt=tt, q2=q2, col0=col0: e.tensor_tensor(
                                res[:, col0:col0 + 512], pb[4 + q2][:, :],
                                res[:, col0:col0 + 512], op=ALU.add),
                                waits=[("te", te_d[nq - 1])] if q2 == 0 else (),
                                inc=("ve", 1))
            return prev_ev, P.val("se")

        fw_v, fw_s = ev_ctab, P.val("se")
        for e_i in range(L1F_NEXP):
            fw_v, fw_s = ffn_pass(
                NI,
                lambda it, e_i=e_i: gw_h[e_i, it],
                lambda it, e_i=e_i: uw_h[e_i, it],
                lambda it, e_i=e_i: dw_h[e_i, it],
                None, None,
                lambda tt, e_i=e_i: ctab[:, tt * 8 + e_i:tt * 8 + e_i + 1],
                None, fw_v, fw_s)
        if not L1F_SHARED:
            return
        fw_v, fw_s = ffn_pass(
            L1F_NIT or NI,
            lambda it: sgw_h[it], lambda it: suw_h[it], lambda it: sdw_h[it],
            lambda it: sgb_cm[:, it:it + 1], lambda it: sub_cm[:, it:it + 1],
            None, (sdb_h if not L1F_NOSDB else None),
            fw_v, fw_s)
        if not L1F_SH2:
            return
        ffn_pass(
            NSI - NI,
            lambda it: sgw_h[NI + it], lambda it: suw_h[NI + it],
            lambda it: sdw_h[NI + it],
            lambda it: sgb_cm[:, NI + it:NI + it + 1],
            lambda it: sub_cm[:, NI + it:NI + it + 1],
            None, None,
            fw_v, fw_s)

    def finish(self):
        P = self.P
        allv = P.val("ve")
        for i in range(NT):
            P.op("sync", lambda e, i=i: e.dma_start(
                out=self.y_out[i * 128:(i + 1) * 128, :],
                in_=self.res[:, i * D:(i + 1) * D]),
                waits=[("ve", allv)] if i == 0 else (), inc=("st", 16))
        final = P.val("st")
        P.op("sync", lambda e: e.nop(), waits=[("st", final)])


def prep_host(inp):
    H = {"ident_h": np.eye(128, dtype=np.float32)}
    f = lambda k: np.asarray(inp[k], dtype=np.float32)
    H["l0a_wr"] = _retile_dmajor(f("L0_attn_Wr"))
    H["l0a_wk"] = _retile_dmajor(f("L0_attn_Wk"))
    H["l0a_wv"] = _retile_dmajor(f("L0_attn_Wv"))
    H["l0a_wo"] = _rows_block(np.ascontiguousarray(f("L0_attn_Wo").T))
    H["l0a_decay"] = _cm_vec(f("L0_attn_decay"))
    H["l0a_gnw"] = _cm_vec(f("L0_attn_gn_weight"))
    H["l0a_gnb"] = _cm_vec(f("L0_attn_gn_bias"))
    wg = f("L0_ffn_W_gate")
    H["l0f_wg"] = np.ascontiguousarray(
        wg.reshape(16, 16, 128).transpose(2, 1, 0).reshape(128, 256))
    H["l0f_wqkv"] = _retile_dmajor(f("L0_ffn_W_qkv"))

    WQ, WK, WV, WO = (f("L1_attn_W_Q"), f("L1_attn_W_K"),
                      f("L1_attn_W_V"), f("L1_attn_W_O"))
    wq_t = WQ.reshape(16, 128, 12, 128).transpose(2, 1, 0, 3).reshape(12, 128, 2048)
    wk_t = WK.reshape(16, 128, 12, 128).transpose(2, 1, 0, 3).reshape(12, 128, 2048)
    H["l1a_wqk"] = np.ascontiguousarray(np.concatenate([wq_t, wk_t], axis=0))
    H["l1a_wv"] = np.ascontiguousarray(
        WV.reshape(16, 128, 3, 512).transpose(2, 1, 0, 3).reshape(3, 128, 8192))
    H["l1a_wo"] = np.ascontiguousarray(
        WO.reshape(12, 128, 2, 1024).transpose(2, 0, 1, 3).reshape(24, 128, 1024))
    fkv = np.zeros((128, 1920), dtype=np.float32)
    fkv[:, 0:448] = f("L1_attn_FK0").reshape(14, 128, 32).transpose(1, 0, 2).reshape(128, 448)
    fkv[:, 448:896] = f("L1_attn_FV0").reshape(14, 128, 32).transpose(1, 0, 2).reshape(128, 448)
    fkv[:, 896:1408] = f("L1_attn_FK1").reshape(8, 128, 64).transpose(1, 0, 2).reshape(128, 512)
    fkv[:, 1408:1920] = f("L1_attn_FV1").reshape(8, 128, 64).transpose(1, 0, 2).reshape(128, 512)
    H["l1a_fkv"] = fkv

    def _pblk(P0, P1, width):
        tab = np.zeros((128, width), dtype=np.float32)
        for h in range(4):
            for r in range(8):
                tab[h * 8 + r, h * 64:(h + 1) * 64] = P0[h, r]
        for hl in range(8):
            for r in range(8):
                tab[64 + hl * 8 + r, 256 + hl * 64:256 + (hl + 1) * 64] = P1[hl, r]
        return tab
    H["l1a_pkb"] = _pblk(f("L1_attn_PK0"), f("L1_attn_PK1"), 768)
    H["l1a_pvb"] = _pblk(f("L1_attn_PV0"), f("L1_attn_PV1"), 1024)

    rw = f("L1_ffn_router_w")
    H["l1f_rw"] = np.ascontiguousarray(
        rw.reshape(8, 16, 128).transpose(2, 1, 0).reshape(128, 128))
    gw = f("L1_ffn_gate_w"); uw = f("L1_ffn_up_w"); dw = f("L1_ffn_down_w")
    H["l1f_gw"] = np.stack([_retile_dmajor(np.ascontiguousarray(gw[e].T))
                            for e in range(8)])
    H["l1f_uw"] = np.stack([_retile_dmajor(np.ascontiguousarray(uw[e].T))
                            for e in range(8)])
    H["l1f_dw"] = np.stack([_rows_block(dw[e]) for e in range(8)])
    H["l1f_sgw"] = _retile_dmajor(np.ascontiguousarray(f("L1_ffn_s_gate_w").T))
    H["l1f_suw"] = _retile_dmajor(np.ascontiguousarray(f("L1_ffn_s_up_w").T))
    H["l1f_sdw"] = _rows_block(f("L1_ffn_s_down_w"))
    H["l1f_sgb"] = _cm_vec(f("L1_ffn_s_gate_b"))
    H["l1f_sub"] = _cm_vec(f("L1_ffn_s_up_b"))
    H["l1f_sdb"] = np.ascontiguousarray(f("L1_ffn_s_down_b").reshape(1, D))
    HP = [dict() for _ in range(NCORES)]
    for c in range(NCORES):
        tgv = np.arange(c * TLOC, (c + 1) * TLOC).astype(np.float32)
        HP[c]["tg_base"] = np.ascontiguousarray(
            np.broadcast_to(tgv[None, :], (128, TLOC)).copy())
        m = np.zeros((128, 16), dtype=np.float32)
        for r in range(NCORES):
            if r < c:
                for ct in range(16):
                    m[16 * r + ct, ct] = 1.0
        HP[c]["l0a_cmask"] = m
        p = np.arange(128)
        qidx = np.zeros((128, 16), dtype=np.int32)
        for lh in range(2):
            for r in range(NCORES):
                qidx[:, lh * 8 + r] = r * 2048 + (2 * c + lh) * 128 + p
        HP[c]["l0f_qidx"] = qidx
        yidx = np.zeros((128, 64), dtype=np.int32)
        for i in range(NT):
            for r in range(NCORES):
                yidx[:, i * 8 + r] = r * T + c * TLOC + i * 128 + p
        HP[c]["l0f_yidx"] = yidx
        lidx = np.zeros((128, 1), dtype=np.int32)
        for q in range(128):
            h, i = q // 8, q % 8
            lidx[q, 0] = 128 * (h // 2) + (h % 2) * 64 + 8 * c + i
        HP[c]["l0f_lidx"] = lidx
        aidx = np.zeros((128, 288), dtype=np.int32)
        for r in range(NCORES):
            for hl in range(3):
                h = 3 * c + hl
                for i in range(NT):
                    aidx[:, (r * 3 + hl) * 8 + i] = (r * 24576 + (h // 8) * 8192 +
                                                     (h % 8) * 1024 + i * 128 + p)
        for i in range(NT):
            for r in range(NCORES):
                aidx[:, 192 + i * 8 + r] = r * T + c * TLOC + i * 128 + p
        for r in range(NCORES):
            aidx[:, 256 + r] = r * 3072 + c * 192 + p
            aidx[0:64, 264 + r] = r * 3072 + c * 192 + 128 + p[0:64]
            aidx[:, 272 + r] = r * 3072 + 1536 + c * 192 + p
            aidx[0:64, 280 + r] = r * 3072 + 1536 + c * 192 + 128 + p[0:64]
        HP[c]["l1a_idx"] = aidx
    return H, HP


_BUILD_CACHE = {}


def _get_builder(blocks):
    key = tuple(blocks)
    if key not in _BUILD_CACHE:
        _BUILD_CACHE[key] = B(list(blocks)).build()
    return _BUILD_CACHE[key]


def run_group(blocks, x_cur_shards, H, HP):
    b = _get_builder(blocks)
    in_maps = []
    for c in range(NCORES):
        m = {}
        for n in b.shared_names:
            m[n] = H[n]
        for n in b.percore_names:
            m[n] = HP[c][n] if n != "x_in" else x_cur_shards[c]
        in_maps.append(m)
    res = run_bass_kernel_spmd(b.nc, in_maps, core_ids=list(range(NCORES)))
    global LAST_RESULTS
    LAST_RESULTS = res.results
    return [np.asarray(r["y_out"]) for r in res.results]


def kernel(**inputs):
    H, HP = prep_host(inputs)
    x = np.ascontiguousarray(np.asarray(inputs["L0_attn_x"], dtype=np.float32)[0])
    shards = shard_tokens(x)
    for group in PLAN:
        shards = run_group(group, shards, H, HP)
    y = unshard_tokens(shards)
    return y.reshape(1, T, D)


# version 07
# speedup vs baseline: 1.4209x; 1.1590x over previous
import numpy as np
from collections import defaultdict
from contextlib import ExitStack

import concourse.bass as bass
import concourse.mybir as mybir
from concourse.bass_core_rust import IndirectOffsetOnAxis
from concourse.bass_utils import run_bass_kernel_spmd

F32 = mybir.dt.float32
BF16 = mybir.dt.bfloat16
I32 = mybir.dt.int32
AF = mybir.ActivationFunctionType
ALU = mybir.AluOpType
AX = mybir.AxisListType

NCORES = 8
T = 8192
D = 2048
TLOC = T // NCORES
NT = TLOC // 128
ND = D // 128
QC = 512
NQC = T // QC

PLAN = [[0, 1, 2, 3]]
DEBUG_L0F = False
L1F_NEXP = 8
L1F_SHARED = True
L1F_SH2 = True
L1F_NOSDB = False
L1F_NIT = None
L1A_NODUP = False
L1A_NOLOOK = False
DEBUG_L1A = False
LAST_RESULTS = None


class Prog:
    ENGINES = ("sync", "gpsimd", "scalar", "vector", "tensor")

    def __init__(self, nc, ctx):
        self.nc = nc
        self.ctx = ctx
        self.q = {e: [] for e in self.ENGINES}
        self.cnt = defaultdict(int)
        self.sems = {}

    def sem(self, name):
        if name not in self.sems:
            self.sems[name] = self.ctx.enter_context(self.nc.semaphore(name))
        return self.sems[name]

    def op(self, eng, build, waits=(), inc=None):
        waits = tuple((self.sem(s), int(v)) for (s, v) in waits if v)
        newv = None
        if inc is not None:
            sname, n = inc
            self.cnt[sname] += n
            newv = self.cnt[sname]
            semh = self.sem(sname)

            def closure(e, build=build, waits=waits, semh=semh, n=n):
                for (sh, v) in waits:
                    e.wait_ge(sh, v)
                build(e).then_inc(semh, n)
        else:
            def closure(e, build=build, waits=waits):
                for (sh, v) in waits:
                    e.wait_ge(sh, v)
                build(e)
        self.q[eng].append(closure)
        return newv

    def val(self, name):
        return self.cnt[name]

    def run_queues(self):
        with self.nc.Block() as block:
            block.sync(lambda e: [f(e) for f in self.q["sync"]])
            block.gpsimd(lambda e: [f(e) for f in self.q["gpsimd"]])
            block.scalar(lambda e: [f(e) for f in self.q["scalar"]])
            block.vector(lambda e: [f(e) for f in self.q["vector"]])
            block.tensor(lambda e: [f(e) for f in self.q["tensor"]])


def _retile_dmajor(W_ed):
    e_out, d_in = W_ed.shape
    nct, ndt = e_out // 128, d_in // 128
    arr = W_ed.reshape(nct, 128, ndt, 128)
    arr = arr.transpose(0, 3, 2, 1)
    return np.ascontiguousarray(arr).reshape(nct, 128, d_in)


def _rows_block(W_de):
    d_in, e_out = W_de.shape
    return np.ascontiguousarray(W_de.reshape(d_in // 128, 128, e_out))


def _cm_vec(v):
    return np.ascontiguousarray(np.asarray(v).reshape(-1, 128).T)


def shard_tokens(x_full):
    return [np.ascontiguousarray(x_full[c * TLOC:(c + 1) * TLOC])
            for c in range(NCORES)]


def unshard_tokens(shards):
    return np.concatenate(shards, axis=0)


class B:
    def __init__(self, blocks):
        self.blocks = blocks
        self.nc = bass.Bass("TRN2", target_bir_lowering=False, debug=False)
        self.ctx = ExitStack()
        self.P = Prog(self.nc, self.ctx)
        self.shared_names = []
        self.percore_names = []

    def din(self, name, shape, percore=False, dtype=F32):
        (self.percore_names if percore else self.shared_names).append(name)
        return self.nc.dram_tensor(name, list(shape), dtype, kind="ExternalInput").ap()

    def sbuf(self, name, shape, dtype):
        return self.ctx.enter_context(self.nc.sbuf_tensor(name, list(shape), dtype))

    def psum(self, name, shape, dtype=F32):
        return self.ctx.enter_context(self.nc.psum_tensor(name, list(shape), dtype))

    def col(self, name, width=1):
        if name not in self.cols:
            assert self._small_off + width <= 256, "smalls overflow"
            self.cols[name] = self.smalls[:, self._small_off:self._small_off + width]
            self._small_off += width
        return self.cols[name]

    def reset_cols(self):
        self._small_off = 0
        self.cols = {}

    def build(self):
        nc = self.nc
        self.x_in = self.din("x_in", [TLOC, D], percore=True)
        self.y_out = nc.dram_tensor("y_out", [TLOC, D], F32, kind="ExternalOutput").ap()

        self.res = self.sbuf("res", [128, NT * D], F32)
        self.xlnT = self.sbuf("xlnT", [128, ND * TLOC], BF16)
        self.bigA = self.sbuf("bigA", [128, ND * TLOC], BF16)
        self.bigB = self.sbuf("bigB", [128, ND * TLOC], BF16)
        self.wbuf = [self.sbuf("wbuf0", [128, 6144], BF16),
                     self.sbuf("wbuf1", [128, 6144], BF16)]
        self.scrp = self.sbuf("scrp", [128, 4 * 1024], F32)
        self.tg = self.sbuf("tg", [128, 1024], F32)
        self.identb = self.sbuf("identb", [128, 128], BF16)
        self.identf = self.sbuf("identf", [128, 128], F32)
        self.ones = self.sbuf("onesb", [128, 512], F32)
        self.onesbf = self.sbuf("onesbf", [128, 8], BF16)
        self.epscol = self.sbuf("epscol", [128, 1], F32)
        self.smalls = self.sbuf("smalls", [128, 256], F32)
        self.pb = [self.psum(f"pb{i}", [128, 512], F32) for i in range(6)]
        self.ptr = [self.psum(f"ptr{i}", [128, 1024], BF16) for i in range(2)]
        self.scr = [self.scrp[:, i * 1024:(i + 1) * 1024] for i in range(4)]
        self.reset_cols()

        self.setup()
        if 0 in self.blocks:
            self.l0_attn()
        if 1 in self.blocks:
            self.l0_ffn()
        if 2 in self.blocks:
            self.l1_attn()
        if 3 in self.blocks:
            self.l1_ffn()
        self.finish()
        self.P.run_queues()
        nc.finalize()
        return self

    def setup(self):
        P = self.P
        for i in range(NT):
            P.op("sync", lambda e, i=i: e.dma_start(
                out=self.res[:, i * D:(i + 1) * D],
                in_=self.x_in[i * 128:(i + 1) * 128, :]), inc=("xl", 16))
        self.ev_res_loaded = P.val("xl")

        ident_h = self.din("ident_h", [128, 128])
        P.op("gpsimd", lambda e: e.dma_start(out=self.identb[:, :], in_=ident_h),
             inc=("wl", 16))
        P.op("gpsimd", lambda e: e.dma_start(out=self.identf[:, :], in_=ident_h),
             inc=("wl", 16))
        P.op("gpsimd", lambda e: e.memset(self.ones[:, :], 1.0), inc=("ge", 1))
        P.op("gpsimd", lambda e: e.memset(self.onesbf[:, :], 1.0), inc=("ge", 1))
        P.op("gpsimd", lambda e: e.memset(self.epscol[:, :], 1e-5), inc=("ge", 1))
        tgb = self.din("tg_base", [128, 1024], percore=True)
        P.op("sync", lambda e: e.dma_start(out=self.tg[:, :], in_=tgb), inc=("xl", 16))
        self.ev_setup_xl = P.val("xl")
        self.ev_setup_wl = P.val("wl")
        self.ev_setup_ge = P.val("ge")

    def emit_ln(self, tag):
        P = self.P
        res, bigA, xlnT, ptr = self.res, self.bigA, self.xlnT, self.ptr
        sq_scr = xlnT[:, 0:4096].bitcast(F32)
        ss, sx = self.col("ln_ss"), self.col("ln_sx")
        mcol, e2col = self.col("ln_m"), self.col("ln_e2")
        vcol, sdcol = self.col("ln_v"), self.col("ln_sd")
        rcol, nmcol = self.col("ln_r"), self.col("ln_nm")
        xln = bigA
        start_v = P.val("ve")
        start_s = P.val("se")
        start_t = P.val("te")
        for i in range(NT):
            rsl = res[:, i * D:(i + 1) * D]
            w0 = [("xl", self.ev_res_loaded), ("ge", self.ev_setup_ge)]
            if i == 0:
                w0 = w0 + [("ve", start_v), ("se", start_s), ("te", start_t)]
            se = P.op("scalar", lambda e, rsl=rsl: e.activation(
                sq_scr, rsl, AF.Square, accum_out=ss), waits=w0, inc=("se", 1))
            v1 = P.op("vector", lambda e, rsl=rsl: e.reduce_sum(sx, rsl, axis=AX.X),
                      waits=w0, inc=("ve", 1))
            v2 = P.op("vector", lambda e: e.tensor_scalar_mul(mcol, sx, 1.0 / D),
                      waits=[("ve", v1)], inc=("ve", 1))
            P.op("vector", lambda e: e.tensor_scalar_mul(e2col, ss, 1.0 / D),
                 waits=[("se", se), ("ve", v2)], inc=("ve", 1))
            v4 = P.op("vector", lambda e: e.tensor_tensor(vcol, mcol, mcol, op=ALU.mult),
                      inc=("ve", 1))
            v5 = P.op("vector", lambda e: e.scalar_tensor_tensor(
                vcol, vcol, -1.0, e2col, op0=ALU.mult, op1=ALU.add),
                waits=[("ve", v4)], inc=("ve", 1))
            s2 = P.op("scalar", lambda e: e.activation(
                sdcol, vcol, AF.Sqrt, bias=self.epscol[:, :]),
                waits=[("ve", v5)], inc=("se", 1))
            v6 = P.op("vector", lambda e: e.reciprocal(rcol, sdcol),
                      waits=[("se", s2)], inc=("ve", 1))
            v7 = P.op("vector", lambda e: e.scalar_tensor_tensor(
                nmcol, mcol, -1.0, rcol, op0=ALU.mult, op1=ALU.mult),
                waits=[("ve", v6)], inc=("ve", 1))
            P.op("vector", lambda e, i=i, rsl=rsl: e.tensor_scalar(
                xln[:, i * D:(i + 1) * D], rsl, rcol, nmcol,
                op0=ALU.mult, op1=ALU.add),
                waits=[("ve", v7)], inc=("ve", 1))
        ev_xln = P.val("ve")

        last_copy = {0: 0, 1: 0}
        for dt in range(ND):
            bank = dt % 2
            tw = [("ve", ev_xln), ("wl", self.ev_setup_wl)]
            if last_copy[bank]:
                tw = tw + [("ve", last_copy[bank])]
            te = None
            for i in range(NT):
                te = P.op("tensor", lambda e, dt=dt, i=i, bank=bank: e.transpose(
                    ptr[bank][:, i * 128:(i + 1) * 128],
                    xln[:, i * D + dt * 128:i * D + (dt + 1) * 128],
                    self.identb[:, :]),
                    waits=tw if i == 0 else (),
                    inc=("te", 1) if i == NT - 1 else None)
            last_copy[bank] = P.op("vector", lambda e, dt=dt, bank=bank: e.tensor_copy(
                xlnT[:, dt * TLOC:(dt + 1) * TLOC], ptr[bank][:, :]),
                waits=[("te", te)], inc=("ve", 1))
        return ev_xln, P.val("ve")

    def l0_attn(self):
        P, nc = self.P, self.nc
        res, xlnT, bigA, bigB = self.res, self.xlnT, self.bigA, self.bigB
        scr, tg, pb = self.scr, self.tg, self.pb
        self.reset_cols()

        wr_h = self.din("l0a_wr", [16, 128, D])
        wk_h = self.din("l0a_wk", [16, 128, D])
        wv_h = self.din("l0a_wv", [16, 128, D])
        wo_h = self.din("l0a_wo", [16, 128, D])
        dec_h = self.din("l0a_decay", [128, ND])
        gnw_h = self.din("l0a_gnw", [128, ND])
        gnb_h = self.din("l0a_gnb", [128, ND])
        cmask_h = self.din("l0a_cmask", [128, 16], percore=True)

        cs_in = nc.dram_tensor("l0a_cs_in", [16, 128], F32, kind="Internal")
        cs_out = nc.dram_tensor("l0a_cs_out", [128, 128], F32, kind="Internal")
        gs_in = nc.dram_tensor("l0a_gs_in", [1, 2], F32, kind="Internal")
        gs_out = nc.dram_tensor("l0a_gs_out", [1, 2], F32, kind="Internal")

        dec_cm = self.col("dec_cm", ND)
        logd = self.col("logd", ND)
        gnw_cm = self.col("gnw_cm", ND)
        gnb_cm = self.col("gnb_cm", ND)
        stab = self.col("stab", 16)
        carry = self.col("carry", 16)
        carry2 = self.col("carry2", 1)
        sxt = self.col("sxt", ND)
        sqt = self.col("sqt", ND)
        Acol = self.col("Acl", ND)
        Bcol = self.col("Bcl", ND)
        cmask_sb = self.col("cmask", 16)

        P.op("sync", lambda e: e.dma_start(out=dec_cm, in_=dec_h), inc=("xl", 16))
        P.op("sync", lambda e: e.dma_start(out=gnw_cm, in_=gnw_h), inc=("xl", 16))
        P.op("sync", lambda e: e.dma_start(out=gnb_cm, in_=gnb_h), inc=("xl", 16))
        P.op("sync", lambda e: e.dma_start(out=cmask_sb, in_=cmask_h), inc=("xl", 16))
        ev_vecs = P.val("xl")

        se = P.op("scalar", lambda e: e.activation(logd, dec_cm, AF.Sigmoid),
                  waits=[("xl", ev_vecs)], inc=("se", 1))
        ve = P.op("vector", lambda e: e.tensor_scalar_max(logd, logd, 1e-7),
                  waits=[("se", se)], inc=("ve", 1))
        ev_logd_se = P.op("scalar", lambda e: e.activation(logd, logd, AF.Ln),
                          waits=[("ve", ve)], inc=("se", 1))

        ev_xln, ev_xlnT = self.emit_ln("l0a")

        wslot_ready = [0, 0]
        wslot_free_te = [0, 0]
        prev_epi_v = 0
        prev_epi_s = 0
        prev_sexp = 0
        for ct in range(ND):
            slot = ct % 2
            w = self.wbuf[slot]
            wts = [("te", wslot_free_te[slot])] if wslot_free_te[slot] else []
            P.op("gpsimd", lambda e, ct=ct, w=w: e.dma_start(
                out=w[:, 0:2048], in_=wk_h[ct]), waits=wts, inc=("wl", 16))
            P.op("gpsimd", lambda e, ct=ct, w=w: e.dma_start(
                out=w[:, 2048:4096], in_=wv_h[ct]), inc=("wl", 16))
            P.op("gpsimd", lambda e, ct=ct, w=w: e.dma_start(
                out=w[:, 4096:6144], in_=wr_h[ct]), inc=("wl", 16))
            wslot_ready[slot] = P.val("wl")

            te_mat = {}
            first = True
            for m in range(3):
                for h in range(2):
                    for dt in range(ND):
                        wts = []
                        if first:
                            wts = [("wl", wslot_ready[slot]), ("ve", ev_xlnT)]
                            if prev_epi_v:
                                wts += [("ve", prev_epi_v), ("se", prev_epi_s)]
                            first = False
                        te_mat[(m, h)] = P.op("tensor", lambda e, m=m, h=h, dt=dt, w=w: e.matmul(
                            pb[m * 2 + h][:, :],
                            w[:, m * 2048 + dt * 128:m * 2048 + (dt + 1) * 128],
                            xlnT[:, dt * TLOC + h * 512:dt * TLOC + (h + 1) * 512],
                            start=(dt == 0), stop=(dt == ND - 1)),
                            waits=wts,
                            inc=("te", 1) if dt == ND - 1 else None)
            wslot_free_te[slot] = te_mat[(2, 1)]

            vk = None
            for h in range(2):
                vk = P.op("vector", lambda e, h=h: e.tensor_copy(
                    scr[1][:, h * 512:(h + 1) * 512], pb[h][:, :]),
                    waits=[("te", te_mat[(0, h)])], inc=("ve", 1))
            va = P.op("vector", lambda e, ct=ct: e.tensor_scalar_mul(
                scr[2][:, :], tg[:, :], logd[:, ct:ct + 1]),
                waits=[("se", ev_logd_se), ("xl", self.ev_setup_xl)] +
                      ([("se", prev_sexp)] if prev_sexp else []),
                inc=("ve", 1))
            vkv = None
            for h in range(2):
                vkv = P.op("vector", lambda e, h=h: e.tensor_tensor(
                    scr[1][:, h * 512:(h + 1) * 512], pb[2 + h][:, :],
                    scr[1][:, h * 512:(h + 1) * 512], op=ALU.mult),
                    waits=[("te", te_mat[(1, h)]), ("ve", vk)], inc=("ve", 1))
            sr = None
            for h in range(2):
                sr = P.op("scalar", lambda e, ct=ct, h=h: e.activation(
                    bigB[:, ct * TLOC + h * 512:ct * TLOC + (h + 1) * 512],
                    pb[4 + h][:, :], AF.Sigmoid),
                    waits=[("te", te_mat[(2, h)])], inc=("se", 1))
            sexp = P.op("scalar", lambda e: e.activation(scr[3][:, :], scr[2][:, :], AF.Exp),
                        waits=[("ve", va)], inc=("se", 1))
            vc = P.op("vector", lambda e: e.tensor_scalar_max(
                scr[2][:, :], scr[3][:, :], 1e-10),
                waits=[("se", sexp)], inc=("ve", 1))
            vi = P.op("vector", lambda e: e.reciprocal(scr[3][:, :], scr[2][:, :]),
                      waits=[("ve", vc)], inc=("ve", 1))
            vs = P.op("vector", lambda e, ct=ct: e.tensor_tensor(
                bigA[:, ct * TLOC:(ct + 1) * TLOC], scr[1][:, :], scr[3][:, :],
                op=ALU.mult),
                waits=[("ve", vi), ("ve", vkv)], inc=("ve", 1))
            P.op("vector", lambda e, ct=ct: e.reduce_sum(
                stab[:, ct:ct + 1], bigA[:, ct * TLOC:(ct + 1) * TLOC], axis=AX.X),
                waits=[("ve", vs)], inc=("ve", 1))
            prev_epi_v, prev_epi_s, prev_sexp = vkv, sr, sexp
        ev_phaseA_v = P.val("ve")
        ev_phaseA_s = P.val("se")
        last_proj_te = P.val("te")

        teS = P.op("tensor", lambda e: e.transpose(
            pb[0][0:16, 0:128], stab, self.identf[:, :]),
            waits=[("ve", ev_phaseA_v)], inc=("te", 1))
        vS = P.op("vector", lambda e: e.tensor_copy(scr[2][0:16, 0:128],
                                                    pb[0][0:16, 0:128]),
                  waits=[("te", teS)], inc=("ve", 1))
        dS = P.op("sync", lambda e: e.dma_start(out=cs_in.ap(), in_=scr[2][0:16, 0:128]),
                  waits=[("ve", vS)], inc=("st", 16))
        cc = P.op("gpsimd", lambda e: e.collective_compute(
            "AllGather", ALU.bypass, replica_groups=[list(range(NCORES))],
            ins=[cs_in.ap().opt()], outs=[cs_out.ap().opt()]),
            waits=[("st", dS)], inc=("cc", 1))
        d2 = P.op("sync", lambda e: e.dma_start(out=scr[1][:, 0:128], in_=cs_out.ap()),
                  waits=[("cc", cc)], inc=("xl", 16))
        tc = P.op("tensor", lambda e: e.matmul(
            pb[1][:, 0:16], scr[1][:, 0:128], cmask_sb, start=True, stop=True),
            waits=[("xl", d2)], inc=("te", 1))
        ev_carry = P.op("vector", lambda e: e.tensor_copy(carry, pb[1][:, 0:16]),
                        waits=[("te", tc)], inc=("ve", 1))

        prev_sexp = 0
        for ct in range(ND):
            va = P.op("vector", lambda e, ct=ct: e.tensor_scalar_mul(
                scr[2][:, :], tg[:, :], logd[:, ct:ct + 1]),
                waits=([("ve", ev_carry)] if ct == 0 else []) +
                      ([("se", prev_sexp)] if prev_sexp else []),
                inc=("ve", 1))
            sexp = P.op("scalar", lambda e: e.activation(scr[3][:, :], scr[2][:, :], AF.Exp),
                        waits=[("ve", va), ("se", ev_phaseA_s)], inc=("se", 1))
            vs0 = P.op("vector", lambda e, ct=ct: e.tensor_tensor_scan(
                scr[0][:, 0:512], self.ones[:, :],
                bigA[:, ct * TLOC:ct * TLOC + 512],
                carry[:, ct:ct + 1], op0=ALU.mult, op1=ALU.add),
                waits=[("ve", ev_carry)] if ct == 0 else (), inc=("ve", 1))
            vcp = P.op("vector", lambda e: e.tensor_copy(carry2, scr[0][:, 511:512]),
                       waits=[("ve", vs0)], inc=("ve", 1))
            vs1 = P.op("vector", lambda e, ct=ct: e.tensor_tensor_scan(
                scr[0][:, 512:1024], self.ones[:, :],
                bigA[:, ct * TLOC + 512:(ct + 1) * TLOC],
                carry2, op0=ALU.mult, op1=ALU.add),
                waits=[("ve", vcp)], inc=("ve", 1))
            vst = P.op("vector", lambda e: e.tensor_tensor(
                scr[1][:, :], scr[0][:, :], scr[3][:, :], op=ALU.mult),
                waits=[("se", sexp), ("ve", vs1)], inc=("ve", 1))
            vo = P.op("vector", lambda e, ct=ct: e.tensor_tensor(
                bigA[:, ct * TLOC:(ct + 1) * TLOC], scr[1][:, :],
                bigB[:, ct * TLOC:(ct + 1) * TLOC], op=ALU.mult),
                waits=[("ve", vst)], inc=("ve", 1))
            P.op("scalar", lambda e, ct=ct: e.activation(
                scr[1][:, :], bigA[:, ct * TLOC:(ct + 1) * TLOC], AF.Square,
                accum_out=sqt[:, ct:ct + 1]),
                waits=[("ve", vo)], inc=("se", 1))
            P.op("vector", lambda e, ct=ct: e.reduce_sum(
                sxt[:, ct:ct + 1], bigA[:, ct * TLOC:(ct + 1) * TLOC], axis=AX.X),
                waits=[("ve", vo)], inc=("ve", 1))
            prev_sexp = sexp
        ev_phaseB_v = P.val("ve")
        ev_phaseB_s = P.val("se")

        P.op("tensor", lambda e: e.matmul(
            pb[2][0:1, 0:ND], self.ones[:, 0:1], sxt, start=True, stop=True),
            waits=[("ve", ev_phaseB_v), ("se", ev_phaseB_s),
                   ("ge", self.ev_setup_ge)])
        tS = P.op("tensor", lambda e: e.matmul(
            pb[2][0:1, ND:2 * ND], self.ones[:, 0:1], sqt, start=True, stop=True),
            inc=("te", 1))
        vg1 = P.op("vector", lambda e: e.reduce_sum(
            scr[2][0:1, 0:1], pb[2][0:1, 0:ND], axis=AX.X),
            waits=[("te", tS)], inc=("ve", 1))
        vg2 = P.op("vector", lambda e: e.reduce_sum(
            scr[2][0:1, 1:2], pb[2][0:1, ND:2 * ND], axis=AX.X),
            waits=[("ve", vg1)], inc=("ve", 1))
        dG = P.op("sync", lambda e: e.dma_start(out=gs_in.ap(), in_=scr[2][0:1, 0:2]),
                  waits=[("ve", vg2)], inc=("st", 16))
        ccG = P.op("gpsimd", lambda e: e.collective_compute(
            "AllReduce", ALU.add, replica_groups=[list(range(NCORES))],
            ins=[gs_in.ap().opt()], outs=[gs_out.ap().opt()]),
            waits=[("st", dG)], inc=("cc", 1))
        dG2 = P.op("sync", lambda e: e.dma_start(out=scr[2][0:1, 4:6], in_=gs_out.ap()),
                   waits=[("cc", ccG)], inc=("xl", 16))
        NELT = float(T * D)
        vm = P.op("vector", lambda e: e.tensor_scalar_mul(
            scr[2][0:1, 6:7], scr[2][0:1, 4:5], 1.0 / NELT),
            waits=[("xl", dG2)], inc=("ve", 1))
        ve2 = P.op("vector", lambda e: e.tensor_scalar_mul(
            scr[2][0:1, 7:8], scr[2][0:1, 5:6], 1.0 / NELT),
            waits=[("ve", vm)], inc=("ve", 1))
        vv = P.op("vector", lambda e: e.tensor_tensor(
            scr[2][0:1, 8:9], scr[2][0:1, 6:7], scr[2][0:1, 6:7], op=ALU.mult),
            waits=[("ve", ve2)], inc=("ve", 1))
        vv2 = P.op("vector", lambda e: e.scalar_tensor_tensor(
            scr[2][0:1, 8:9], scr[2][0:1, 8:9], -1.0, scr[2][0:1, 7:8],
            op0=ALU.mult, op1=ALU.add),
            waits=[("ve", vv)], inc=("ve", 1))
        sG = P.op("scalar", lambda e: e.activation(
            scr[2][0:1, 9:10], scr[2][0:1, 8:9], AF.Sqrt, bias=self.epscol[0:1, :]),
            waits=[("ve", vv2)], inc=("se", 1))
        vr = P.op("vector", lambda e: e.reciprocal(scr[2][0:1, 10:11], scr[2][0:1, 9:10]),
                  waits=[("se", sG)], inc=("ve", 1))
        vnm = P.op("vector", lambda e: e.tensor_tensor(
            scr[2][0:1, 11:12], scr[2][0:1, 6:7], scr[2][0:1, 10:11], op=ALU.mult),
            waits=[("ve", vr)], inc=("ve", 1))
        vnm2 = P.op("vector", lambda e: e.tensor_scalar_mul(
            scr[2][0:1, 11:12], scr[2][0:1, 11:12], -1.0),
            waits=[("ve", vnm)], inc=("ve", 1))
        tb = P.op("tensor", lambda e: e.matmul(
            pb[3][:, 0:2], self.ones[0:1, 0:128], scr[2][0:1, 10:12],
            start=True, stop=True),
            waits=[("ve", vnm2)], inc=("te", 1))
        vb = P.op("vector", lambda e: e.tensor_copy(scr[2][:, 16:18], pb[3][:, 0:2]),
                  waits=[("te", tb)], inc=("ve", 1))
        rsd_col = scr[2][:, 16:17]
        nmr_col = scr[2][:, 17:18]
        vA = P.op("vector", lambda e: e.tensor_scalar_mul(Acol, gnw_cm, rsd_col),
                  waits=[("ve", vb)], inc=("ve", 1))
        vB = P.op("vector", lambda e: e.tensor_scalar_mul(Bcol, gnw_cm, nmr_col),
                  waits=[("ve", vA)], inc=("ve", 1))
        ev_AB = P.op("vector", lambda e: e.tensor_tensor(Bcol, Bcol, gnb_cm, op=ALU.add),
                     waits=[("ve", vB)], inc=("ve", 1))

        normed = xlnT
        for ct in range(ND):
            P.op("vector", lambda e, ct=ct: e.tensor_scalar(
                normed[:, ct * TLOC:(ct + 1) * TLOC],
                bigA[:, ct * TLOC:(ct + 1) * TLOC],
                Acol[:, ct:ct + 1], Bcol[:, ct:ct + 1],
                op0=ALU.mult, op1=ALU.add),
                waits=[("ve", ev_AB)] if ct == 0 else (), inc=("ve", 1))
        ev_normed = P.val("ve")

        def wo_view(ct):
            if ct < 8:
                return bigB[:, ct * 2048:(ct + 1) * 2048]
            if ct < 11:
                return self.wbuf[0][:, (ct - 8) * 2048:(ct - 7) * 2048]
            if ct < 14:
                return self.wbuf[1][:, (ct - 11) * 2048:(ct - 10) * 2048]
            return self.scrp[:, (ct - 14) * 1024:(ct - 13) * 1024].bitcast(BF16)

        for ct in range(ND):
            wts = []
            if ct == 0:
                wts = [("ve", ev_phaseB_v), ("se", ev_phaseB_s),
                       ("te", last_proj_te)]
            P.op("gpsimd", lambda e, ct=ct: e.dma_start(out=wo_view(ct), in_=wo_h[ct]),
                 waits=wts, inc=("wl", 16))
        ev_wo = P.val("wl")

        prev_evac = 0
        for i in range(NT):
            te_last = None
            for ct in range(ND):
                for q in range(4):
                    wts = []
                    if ct == 0 and q == 0:
                        wts = [("wl", ev_wo), ("ve", ev_normed)]
                        if prev_evac:
                            wts.append(("ve", prev_evac))
                    te_last = P.op("tensor", lambda e, i=i, ct=ct, q=q: e.matmul(
                        pb[q][:, :],
                        normed[:, ct * TLOC + i * 128:ct * TLOC + (i + 1) * 128],
                        wo_view(ct)[:, q * 512:(q + 1) * 512],
                        start=(ct == 0), stop=(ct == ND - 1)),
                        waits=wts,
                        inc=("te", 1) if (ct == ND - 1 and q == 3) else None)
            for q in range(4):
                prev_evac = P.op("vector", lambda e, i=i, q=q: e.tensor_tensor(
                    res[:, i * D + q * 512:i * D + (q + 1) * 512],
                    pb[q][:, :],
                    res[:, i * D + q * 512:i * D + (q + 1) * 512], op=ALU.add),
                    waits=[("te", te_last)] if q == 0 else (), inc=("ve", 1))

    def l0_ffn(self):
        P, nc = self.P, self.nc
        res, xlnT, bigA, bigB = self.res, self.xlnT, self.bigA, self.bigB
        scr, pb, ptr = self.scr, self.pb, self.ptr
        self.reset_cols()
        HSC = 0.08838834764831845

        wg_h = self.din("l0f_wg", [128, 256])
        wq_h = self.din("l0f_wqkv", [48, 128, D])
        qidx_h = self.din("l0f_qidx", [128, 16], percore=True, dtype=I32)
        yidx_h = self.din("l0f_yidx", [128, 64], percore=True, dtype=I32)
        lidx_h = self.din("l0f_lidx", [128, 1], percore=True, dtype=I32)

        gsel = nc.dram_tensor("l0f_gsel", [32, TLOC], F32, kind="Internal")
        kv_in = nc.dram_tensor("l0f_kv_in", [256, TLOC], BF16, kind="Internal")
        kv_out = nc.dram_tensor("l0f_kv_out", [2048, TLOC], BF16, kind="Internal")
        qag_in = nc.dram_tensor("l0f_qag_in", [2048, TLOC], BF16, kind="Internal")
        qag_out = nc.dram_tensor("l0f_qag_out", [16384, TLOC], BF16, kind="Internal")
        yag_in = [nc.dram_tensor(f"l0f_yag_in{h}", [T, 128], BF16, kind="Internal")
                  for h in range(2)]
        yag_out = [nc.dram_tensor(f"l0f_yag_out{h}", [T * NCORES, 128], BF16,
                                  kind="Internal") for h in range(2)]
        lag_in = nc.dram_tensor("l0f_lag_in", [128, 128], F32, kind="Internal")
        lag_out = nc.dram_tensor("l0f_lag_out", [1024, 128], F32, kind="Internal")

        wg_sb = self.col("wg_sb", 128).bitcast(BF16)
        qidx = self.col("qidx", 16).bitcast(I32)
        yidx = self.col("yidx", 64).bitcast(I32)
        lidx = self.col("lidx", 1).bitcast(I32)
        negm = self.col("negm")
        ssum = self.col("ssum")
        topc = self.col("topc")

        ent = [("ve", P.val("ve")), ("se", P.val("se")), ("te", P.val("te"))]
        P.op("sync", lambda e: e.dma_start(out=qidx, in_=qidx_h), waits=ent,
             inc=("xl", 16))
        P.op("sync", lambda e: e.dma_start(out=yidx, in_=yidx_h), waits=ent,
             inc=("xl", 16))
        P.op("sync", lambda e: e.dma_start(out=lidx, in_=lidx_h), waits=ent,
             inc=("xl", 16))
        P.op("gpsimd", lambda e: e.dma_start(out=wg_sb, in_=wg_h), waits=ent,
             inc=("wl", 16))
        ev_wg = P.val("wl")
        ones128 = self.tg.bitcast(BF16)[:, 1536:1664]
        P.op("gpsimd", lambda e: e.memset(ones128, 1.0), waits=ent, inc=("ge", 1))
        ev_o128 = P.val("ge")

        ev_xln, ev_xlnT = self.emit_ln("l0f")

        ind_scr = scr[2][:, 0:16]
        gt1_scr = scr[2][:, 16:32]
        prev_tr_copy = 0
        exp_ev = 0
        for i in range(NT):
            te_g = None
            for dt in range(ND):
                wts = []
                if dt == 0:
                    wts = [("ve", ev_xlnT), ("wl", ev_wg)]
                    if exp_ev:
                        wts.append(("se", exp_ev))
                te_g = P.op("tensor", lambda e, i=i, dt=dt: e.matmul(
                    pb[4][:, 0:16],
                    xlnT[:, dt * TLOC + i * 128:dt * TLOC + (i + 1) * 128],
                    wg_sb[:, dt * 16:(dt + 1) * 16],
                    start=(dt == 0), stop=(dt == ND - 1)),
                    waits=wts, inc=("te", 1) if dt == ND - 1 else None)
            vng = P.op("vector", lambda e: e.reduce_max(
                negm, pb[4][:, 0:16], axis=AX.X, negate=True),
                waits=[("te", te_g)], inc=("ve", 1))
            vind = P.op("vector", lambda e: e.tensor_scalar(
                ind_scr, pb[4][:, 0:16], negm, 0.0, op0=ALU.add, op1=ALU.is_ge),
                waits=[("ve", vng)], inc=("ve", 1))
            exp_ev = P.op("scalar", lambda e: e.activation(
                scr[3][:, 0:16], pb[4][:, 0:16], AF.Exp, bias=negm),
                waits=[("ve", vind)], inc=("se", 1))
            vsum = P.op("vector", lambda e: e.reduce_sum(
                ssum, scr[3][:, 0:16], axis=AX.X),
                waits=[("se", exp_ev)], inc=("ve", 1))
            vtop = P.op("vector", lambda e: e.reciprocal(topc, ssum),
                        waits=[("ve", vsum)], inc=("ve", 1))
            vgt1 = P.op("vector", lambda e: e.tensor_scalar_mul(
                gt1_scr, ind_scr, topc),
                waits=[("ve", vtop)], inc=("ve", 1))
            wts = [("ve", vgt1)]
            if prev_tr_copy:
                wts.append(("ve", prev_tr_copy))
            te_t = P.op("tensor", lambda e: e.transpose(
                pb[5][0:32, 0:128], scr[2][:, 0:32], self.identf[:, :]),
                waits=wts, inc=("te", 1))
            prev_tr_copy = P.op("vector", lambda e, i=i: e.tensor_copy(
                scr[1][0:32, i * 128:(i + 1) * 128], pb[5][0:32, 0:128]),
                waits=[("te", te_t)], inc=("ve", 1))
        dsel = P.op("sync", lambda e: e.dma_start(out=gsel.ap(), in_=scr[1][0:32, :]),
                    waits=[("ve", prev_tr_copy)], inc=("st", 16))

        kacc, vacc = scr[0], scr[1]
        wslot_ready = [0, 0]
        wslot_free = [0, 0]
        prev_bc_consumed = 0
        prev_evac_s = 0
        prev_evac_v = 0
        last_q_evac = 0
        for et in range(48):
            slot = et % 2
            w = self.wbuf[slot][:, 0:2048]
            wts = [("te", wslot_free[slot])] if wslot_free[slot] else []
            if et == 0:
                wts.append(("st", dsel))
            P.op("gpsimd", lambda e, et=et, w=w: e.dma_start(out=w, in_=wq_h[et]),
                 waits=wts, inc=("wl", 16))
            wslot_ready[slot] = P.val("wl")
            role = et // 16
            hh = et % 16
            ev_bc = 0
            if role > 0:
                grow = hh if role == 1 else 16 + hh
                wts = [("st", dsel)]
                if prev_bc_consumed:
                    wts.append(("ve", prev_bc_consumed))
                ev_bc = P.op("gpsimd", lambda e, grow=grow: e.dma_start(
                    out=scr[3][:, :],
                    in_=gsel.ap()[grow:grow + 1, :].to_broadcast([128, TLOC])),
                    waits=wts, inc=("wl", 16))
            te_m = {}
            for th in range(2):
                for dt in range(ND):
                    wts = []
                    if dt == 0 and th == 0:
                        wts = [("wl", wslot_ready[slot]), ("ve", ev_xlnT)]
                        if prev_evac_s:
                            wts.append(("se", prev_evac_s))
                        if prev_evac_v:
                            wts.append(("ve", prev_evac_v))
                    te_m[th] = P.op("tensor", lambda e, th=th, dt=dt, w=w, slot=slot: e.matmul(
                        pb[slot * 2 + th][:, :],
                        w[:, dt * 128:(dt + 1) * 128],
                        xlnT[:, dt * TLOC + th * 512:dt * TLOC + (th + 1) * 512],
                        start=(dt == 0), stop=(dt == ND - 1)),
                        waits=wts, inc=("te", 1) if dt == ND - 1 else None)
            wslot_free[slot] = te_m[1]
            if role == 0:
                ev = None
                for th in range(2):
                    ev = P.op("scalar", lambda e, hh=hh, th=th, slot=slot: e.activation(
                        bigA[:, hh * TLOC + th * 512:hh * TLOC + (th + 1) * 512],
                        pb[slot * 2 + th][:, :], AF.Copy, scale=HSC),
                        waits=[("te", te_m[th])], inc=("se", 1))
                prev_evac_s = ev
                last_q_evac = ev
                prev_evac_v = 0
            else:
                acc = kacc if role == 1 else vacc
                ev = None
                for th in range(2):
                    sl = slice(th * 512, (th + 1) * 512)
                    if hh == 0:
                        ev = P.op("vector", lambda e, th=th, slot=slot, acc=acc, sl=sl: e.tensor_tensor(
                            acc[:, sl], pb[slot * 2 + th][:, :], scr[3][:, sl],
                            op=ALU.mult),
                            waits=[("te", te_m[th]), ("wl", ev_bc)], inc=("ve", 1))
                    else:
                        ev = P.op("vector", lambda e, th=th, slot=slot, sl=sl: e.tensor_tensor(
                            scr[2][:, sl], pb[slot * 2 + th][:, :], scr[3][:, sl],
                            op=ALU.mult),
                            waits=[("te", te_m[th]), ("wl", ev_bc)], inc=("ve", 1))
                        ev = P.op("vector", lambda e, acc=acc, sl=sl: e.tensor_tensor(
                            acc[:, sl], acc[:, sl], scr[2][:, sl], op=ALU.add),
                            waits=[("ve", ev)], inc=("ve", 1))
                prev_bc_consumed = ev
                prev_evac_v = ev
                prev_evac_s = 0
        ev_sel_v = P.val("ve")

        ksh_bf = self.wbuf[0][:, 2048:3072]
        vsh_bf = self.wbuf[0][:, 3072:4096]
        vtok_bf = self.wbuf[0][:, 4096:5120]
        sk = P.op("scalar", lambda e: e.activation(ksh_bf, kacc[:, :], AF.Copy),
                  waits=[("ve", ev_sel_v)], inc=("se", 1))
        sv = P.op("scalar", lambda e: e.activation(vsh_bf, vacc[:, :], AF.Copy),
                  waits=[("ve", ev_sel_v)], inc=("se", 1))
        te_v = None
        for j in range(NT):
            te_v = P.op("tensor", lambda e, j=j: e.transpose(
                ptr[0][:, j * 128:(j + 1) * 128],
                vsh_bf[:, j * 128:(j + 1) * 128], self.identb[:, :]),
                waits=[("se", sv)] if j == 0 else (),
                inc=("te", 1) if j == NT - 1 else None)
        vvt = P.op("vector", lambda e: e.tensor_copy(vtok_bf, ptr[0][:, :]),
                   waits=[("te", te_v)], inc=("ve", 1))
        d_k = P.op("sync", lambda e: e.dma_start(out=kv_in.ap()[0:128, :], in_=ksh_bf),
                   waits=[("se", sk)], inc=("st", 16))
        d_v = P.op("sync", lambda e: e.dma_start(out=kv_in.ap()[128:256, :], in_=vtok_bf),
                   waits=[("ve", vvt)], inc=("st", 16))
        dq = None
        for hh in range(16):
            dq = P.op("sync", lambda e, hh=hh: e.dma_start(
                out=qag_in.ap()[hh * 128:(hh + 1) * 128, :],
                in_=bigA[:, hh * TLOC:(hh + 1) * TLOC]),
                waits=[("se", last_q_evac)] if hh == 0 else (), inc=("st", 16))
        cc1 = P.op("gpsimd", lambda e: e.collective_compute(
            "AllGather", ALU.bypass, replica_groups=[list(range(NCORES))],
            ins=[kv_in.ap().opt()], outs=[kv_out.ap().opt()]),
            waits=[("st", d_v), ("st", d_k)], inc=("cc", 1))
        cc2 = P.op("gpsimd", lambda e: e.collective_compute(
            "AllGather", ALU.bypass, replica_groups=[list(range(NCORES))],
            ins=[qag_in.ap().opt()], outs=[qag_out.ap().opt()]),
            waits=[("st", dq)], inc=("cc", 1))

        dkv = None
        for g in range(NQC):
            r, half = g // 2, g % 2
            dkv = P.op("sync", lambda e, g=g, r=r, half=half: e.dma_start(
                out=bigB[:, g * 512:(g + 1) * 512],
                in_=kv_out.ap()[256 * r:256 * r + 128, half * 512:(half + 1) * 512]),
                waits=[("cc", cc1)] if g == 0 else (), inc=("xl", 16))
            dkv = P.op("sync", lambda e, g=g, r=r, half=half: e.dma_start(
                out=bigB[:, 8192 + g * 512:8192 + (g + 1) * 512],
                in_=kv_out.ap()[256 * r + 128:256 * r + 256,
                                half * 512:(half + 1) * 512]),
                inc=("xl", 16))
        ev_kv_own = P.val("xl")
        for lh in range(2):
            for r in range(NCORES):
                P.op("gpsimd", lambda e, lh=lh, r=r: e.indirect_dma_start(
                    out=bigA[:, lh * 8192 + r * TLOC:lh * 8192 + (r + 1) * TLOC],
                    out_offset=None,
                    in_=qag_out.ap()[0:128, :],
                    in_offset=IndirectOffsetOnAxis(qidx[:, lh * 8 + r:lh * 8 + r + 1], 0)),
                    waits=[("cc", cc2)] if (lh == 0 and r == 0) else (),
                    inc=("wl", 16))
        ev_q_own = P.val("wl")
        if DEBUG_L0F:
            dbg_gsel = nc.dram_tensor("dbg_gsel", [32, 1024], F32, kind="ExternalOutput").ap()
            P.op("sync", lambda e: e.dma_start(out=dbg_gsel, in_=gsel.ap()),
                 waits=[("st", dsel)], inc=("st", 16))
            dbg_q = nc.dram_tensor("dbg_q", [128, 16384], BF16, kind="ExternalOutput").ap()
            dbg_kv = nc.dram_tensor("dbg_kv", [128, 16384], BF16, kind="ExternalOutput").ap()
            dbg_ksel = nc.dram_tensor("dbg_ksel", [128, 1024], F32, kind="ExternalOutput").ap()
            dbg_vsel = nc.dram_tensor("dbg_vsel", [128, 1024], F32, kind="ExternalOutput").ap()
            P.op("sync", lambda e: e.dma_start(out=dbg_q, in_=bigA[:, :]),
                 waits=[("wl", ev_q_own)], inc=("st", 16))
            P.op("sync", lambda e: e.dma_start(out=dbg_kv, in_=bigB[:, :]),
                 waits=[("xl", ev_kv_own)], inc=("st", 16))
            P.op("sync", lambda e: e.dma_start(out=dbg_ksel, in_=kacc[:, :]),
                 waits=[("ve", ev_sel_v)], inc=("st", 16))
            P.op("sync", lambda e: e.dma_start(out=dbg_vsel, in_=vacc[:, :]),
                 waits=[("ve", ev_sel_v)], inc=("st", 16))

        ebuf = scr[0].bitcast(BF16)
        lrow = scr[1][0:1, 0:512]
        lrow_i = scr[1][0:1, 512:1024]
        prev_oevac_s = 0
        prev_levac_v = 0
        dl = 0
        prev_copy = 0
        dyi = None
        cc3 = {}
        for lh in range(2):
            for qg in range(NQC):
                nsub = 4 * (qg + 1)
                ev_exp = {}
                ev_msk = {}
                te_s = {}

                def emit_s(sub, lh=lh, qg=qg):
                    ping = sub % 2
                    kg, j = sub // 4, sub % 4
                    wts = []
                    if sub == 0:
                        wts = [("wl", ev_q_own), ("xl", ev_kv_own)]
                        if prev_oevac_s:
                            wts += [("se", prev_oevac_s), ("ve", prev_levac_v)]
                    elif sub >= 2:
                        wts = [("se", ev_exp[sub - 2])]
                    P.op("tensor", lambda e, ping=ping, kg=kg, j=j, lh=lh, qg=qg: e.matmul(
                        pb[ping][:, :],
                        bigB[:, kg * 512 + j * 128:kg * 512 + (j + 1) * 128],
                        bigA[:, lh * 8192 + qg * 512:lh * 8192 + (qg + 1) * 512],
                        start=True, stop=True),
                        waits=wts, inc=("te", 1))
                    te_s[sub] = P.val("te")

                emit_s(0)
                for sub in range(nsub):
                    ping = sub % 2
                    kg, j = sub // 4, sub % 4
                    if sub + 1 < nsub:
                        emit_s(sub + 1)
                    ev_exp[sub] = P.op("scalar", lambda e, ping=ping: e.activation(
                        ebuf[:, ping * 512:(ping + 1) * 512], pb[ping][:, :], AF.Exp),
                        waits=[("te", te_s[sub])], inc=("se", 1))
                    if kg == qg:
                        ev_msk[sub] = P.op("gpsimd", lambda e, j=j, ping=ping: e.affine_select(
                            ebuf[:, ping * 512:(ping + 1) * 512],
                            ebuf[:, ping * 512:(ping + 1) * 512],
                            pattern=[[1, 512]], compare_op=ALU.is_ge, fill=0.0,
                            base=-(j * 128), channel_multiplier=-1),
                            waits=[("se", ev_exp[sub])], inc=("ge", 1))
                    wts_pv = [("se", ev_exp[sub])]
                    if sub in ev_msk:
                        wts_pv.append(("ge", ev_msk[sub]))
                    P.op("tensor", lambda e, kg=kg, j=j, ping=ping, sub=sub, nsub=nsub: e.matmul(
                        pb[4][:, :],
                        bigB[:, 8192 + kg * 512 + j * 128:8192 + kg * 512 + (j + 1) * 128],
                        ebuf[:, ping * 512:(ping + 1) * 512],
                        start=(sub == 0), stop=(sub == nsub - 1)),
                        waits=wts_pv)
                    P.op("tensor", lambda e, ping=ping, sub=sub, nsub=nsub: e.matmul(
                        pb[5][:, :], ones128,
                        ebuf[:, ping * 512:(ping + 1) * 512],
                        start=(sub == 0), stop=(sub == nsub - 1)),
                        waits=[("ge", ev_o128)] if (lh == 0 and qg == 0 and sub == 0) else (),
                        inc=("te", 1))
                te_fin = P.val("te")
                prev_oevac_s = P.op("scalar", lambda e, lh=lh, qg=qg: e.activation(
                    xlnT[:, lh * 8192 + qg * 512:lh * 8192 + (qg + 1) * 512],
                    pb[4][:, :], AF.Copy),
                    waits=[("te", te_fin)], inc=("se", 1))
                vl = P.op("vector", lambda e: e.tensor_copy(lrow, pb[5][0:1, :]),
                          waits=[("te", te_fin)], inc=("ve", 1))
                prev_levac_v = P.op("vector", lambda e: e.reciprocal(lrow_i, lrow),
                                    waits=[("ve", vl)] + ([("st", dl)] if dl else []),
                                    inc=("ve", 1))
                dl = P.op("sync", lambda e, lh=lh, qg=qg: e.dma_start(
                    out=lag_in.ap()[lh * 64 + qg * 4:lh * 64 + (qg + 1) * 4, :],
                    in_=lrow_i),
                    waits=[("ve", prev_levac_v)], inc=("st", 16))
            for b4 in range(16):
                te_y = None
                for j in range(4):
                    a = b4 * 4 + j
                    wts = []
                    if j == 0:
                        wts = [("se", prev_oevac_s)]
                        if prev_copy:
                            wts.append(("ve", prev_copy))
                    te_y = P.op("tensor", lambda e, lh=lh, a=a, j=j: e.transpose(
                        ptr[1][:, j * 128:(j + 1) * 128],
                        xlnT[:, lh * 8192 + a * 128:lh * 8192 + (a + 1) * 128],
                        self.identb[:, :]),
                        waits=wts, inc=("te", 1) if j == 3 else None)
                prev_copy = P.op("vector", lambda e: e.tensor_copy(
                    scr[2].bitcast(BF16)[:, 0:512], ptr[1][:, 0:512]),
                    waits=[("te", te_y)] + ([("st", dyi)] if dyi else []),
                    inc=("ve", 1))
                dyi = P.op("sync", lambda e, lh=lh, b4=b4: e.dma_start(
                    out=yag_in[lh].ap()[b4 * 512:(b4 + 1) * 512, :].rearrange(
                        "(jj p) d -> p jj d", p=128),
                    in_=scr[2].bitcast(BF16)[:, 0:512].rearrange(
                        "p (jj d) -> p jj d", d=128)),
                    waits=[("ve", prev_copy)], inc=("st", 16))
            cc3[lh] = P.op("gpsimd", lambda e, lh=lh: e.collective_compute(
                "AllGather", ALU.bypass, replica_groups=[list(range(NCORES))],
                ins=[yag_in[lh].ap().opt()], outs=[yag_out[lh].ap().opt()]),
                waits=[("st", dyi)], inc=("cc", 1))
        if DEBUG_L0F:
            dbg_y = nc.dram_tensor("dbg_y", [128, 16384], BF16, kind="ExternalOutput").ap()
            dbg_l = nc.dram_tensor("dbg_l", [128, 128], F32, kind="ExternalOutput").ap()
            P.op("sync", lambda e: e.dma_start(out=dbg_y, in_=xlnT[:, :]),
                 waits=[("se", prev_oevac_s)], inc=("st", 16))
            P.op("sync", lambda e: e.dma_start(out=dbg_l, in_=lag_in.ap()),
                 waits=[("st", dl)], inc=("st", 16))
        cc4 = P.op("gpsimd", lambda e: e.collective_compute(
            "AllGather", ALU.bypass, replica_groups=[list(range(NCORES))],
            ins=[lag_in.ap().opt()], outs=[lag_out.ap().opt()]),
            waits=[("st", dl)], inc=("cc", 1))

        for lh in range(2):
            for i in range(NT):
                for r in range(NCORES):
                    P.op("gpsimd", lambda e, lh=lh, i=i, r=r: e.indirect_dma_start(
                        out=bigA[:, i * 2048 + r * 256 + lh * 128:
                                 i * 2048 + r * 256 + (lh + 1) * 128],
                        out_offset=None,
                        in_=yag_out[lh].ap()[0:128, :],
                        in_offset=IndirectOffsetOnAxis(yidx[:, i * 8 + r:i * 8 + r + 1], 0)),
                        waits=[("cc", cc3[lh])] if (i == 0 and r == 0) else (),
                        inc=("wl", 16))
        P.op("gpsimd", lambda e: e.indirect_dma_start(
            out=scr[1][:, 0:128], out_offset=None,
            in_=lag_out.ap()[0:128, :],
            in_offset=IndirectOffsetOnAxis(lidx[:, 0:1], 0)),
            waits=[("cc", cc4)], inc=("wl", 16))
        ev_y_own = P.val("wl")
        te_l = P.op("tensor", lambda e: e.transpose(
            pb[0][:, 0:128], scr[1][:, 0:128], self.identf[:, :]),
            waits=[("wl", ev_y_own)], inc=("te", 1))
        v_l = P.op("vector", lambda e: e.tensor_copy(scr[2][:, 0:128], pb[0][:, 0:128]),
                   waits=[("te", te_l)], inc=("ve", 1))
        linv_cols = scr[2][:, 0:128]
        if DEBUG_L0F:
            dbg_yret = nc.dram_tensor("dbg_yret", [128, 16384], BF16, kind="ExternalOutput").ap()
            dbg_linv = nc.dram_tensor("dbg_linv", [128, 128], F32, kind="ExternalOutput").ap()
            P.op("sync", lambda e: e.dma_start(out=dbg_yret, in_=bigA[:, :]),
                 waits=[("wl", ev_y_own)], inc=("st", 16))
            P.op("sync", lambda e: e.dma_start(out=dbg_linv, in_=linv_cols),
                 waits=[("ve", v_l)], inc=("st", 16))
        for i in range(NT):
            for h in range(16):
                r, sub = h // 2, h % 2
                P.op("vector", lambda e, i=i, h=h, r=r, sub=sub: e.scalar_tensor_tensor(
                    res[:, i * D + h * 128:i * D + (h + 1) * 128],
                    bigA[:, i * 2048 + r * 256 + sub * 128:i * 2048 + r * 256 + (sub + 1) * 128],
                    linv_cols[:, h * 8 + i:h * 8 + i + 1],
                    res[:, i * D + h * 128:i * D + (h + 1) * 128],
                    op0=ALU.mult, op1=ALU.add),
                    waits=[("ve", v_l), ("wl", ev_y_own)] if (i == 0 and h == 0) else (),
                    inc=("ve", 1))

    def l1_attn(self):
        P, nc = self.P, self.nc
        res, xlnT, bigA, bigB = self.res, self.xlnT, self.bigA, self.bigB
        scr, pb, ptr = self.scr, self.pb, self.ptr
        wb0, wb1 = self.wbuf
        self.reset_cols()
        SC = 0.125

        wqk_h = self.din("l1a_wqk", [24, 128, 2048])
        wv_h = self.din("l1a_wv", [3, 128, 8192])
        wo_h = self.din("l1a_wo", [24, 128, 1024])
        fkv_h = self.din("l1a_fkv", [128, 1920])
        pkb_h = self.din("l1a_pkb", [128, 768])
        pvb_h = self.din("l1a_pvb", [128, 1024])
        idx_h = self.din("l1a_idx", [128, 288], percore=True, dtype=I32)

        qk_in = nc.dram_tensor("l1a_qk_in", [3072, TLOC], BF16, kind="Internal")
        qk_out = nc.dram_tensor("l1a_qk_out", [24576, TLOC], BF16, kind="Internal")
        v_in = nc.dram_tensor("l1a_vin", [24576, 64], BF16, kind="Internal")
        v_out = nc.dram_tensor("l1a_vout", [196608, 64], BF16, kind="Internal")
        yag_in = [nc.dram_tensor(f"l1a_yag_in{h}", [T, 64], BF16, kind="Internal")
                  for h in range(3)]
        yag_out = [nc.dram_tensor(f"l1a_yag_out{h}", [T * NCORES, 64], BF16,
                                  kind="Internal") for h in range(3)]

        ve_ent = P.val("ve")
        se_ent = P.val("se")
        te_ent = P.val("te")
        xl_ent = P.val("xl")
        ge_ent = P.val("ge")
        linv = self.col("l1a_linv", 4)

        tgI = self.tg.bitcast(I32)
        P.op("sync", lambda e: e.dma_start(out=tgI[:, 0:288], in_=idx_h),
             waits=[("ve", ve_ent)], inc=("xl", 16))
        ev_idx = P.val("xl")
        P.op("gpsimd", lambda e: e.dma_start(out=wb0[:, 0:1920], in_=fkv_h),
             waits=[("te", te_ent)], inc=("wl", 16))
        P.op("gpsimd", lambda e: e.dma_start(out=wb0[:, 1920:2688], in_=pkb_h),
             inc=("wl", 16))
        P.op("gpsimd", lambda e: e.dma_start(out=wb0[:, 2688:3712], in_=pvb_h),
             inc=("wl", 16))
        ev_tabs = P.val("wl")

        sbf = self.scrp.bitcast(BF16)
        cK, cV = sbf[:, 0:1024], sbf[:, 1024:2048]
        P.op("gpsimd", lambda e: e.memset(sbf[:, 0:2048], 0.0),
             waits=[("ve", ve_ent), ("se", se_ent), ("te", te_ent)], inc=("ge", 1))
        ev_mset = P.val("ge")

        ev_xln, ev_xlnT = self.emit_ln("l1a")
        se_ln = P.val("se")

        prev_cevac = 0
        for th in range(2):
            specs = [(pb[4], 0, 32, 2, 0), (pb[4], 64, 128, 8, 896),
                     (pb[5], 0, 32, 2, 448), (pb[5], 64, 128, 8, 1408)]
            evs = []
            for si, (bank, p0, p1, dt0, c0) in enumerate(specs):
                w = p1 - p0
                n = 16 - dt0
                te_c = None
                for i, dt in enumerate(range(dt0, 16)):
                    wts = []
                    if i == 0 and si == 0:
                        wts = [("wl", ev_tabs), ("ve", ev_xlnT), ("se", se_ln)]
                        if prev_cevac:
                            wts.append(("se", prev_cevac))
                    rhs = xlnT[:, dt * TLOC + th * 512:dt * TLOC + (th + 1) * 512]
                    te_c = P.op("tensor", lambda e, bank=bank, p0=p0, p1=p1, c0=c0,
                                i=i, w=w, n=n, rhs=rhs: e.matmul(
                        bank[p0:p1, :],
                        wb0[:, c0 + i * w:c0 + (i + 1) * w],
                        rhs, start=(i == 0), stop=(i == n - 1)),
                        waits=wts, inc=("te", 1) if i == n - 1 else None)
                evs.append(te_c)
            for si, (bank, p0, p1, dt0, c0) in enumerate(specs):
                dst = cK if si < 2 else cV
                prev_cevac = P.op("scalar", lambda e, bank=bank, p0=p0, p1=p1,
                                  dst=dst, th=th: e.activation(
                    dst[p0:p1, th * 512:(th + 1) * 512], bank[p0:p1, :], AF.Copy),
                    waits=[("te", evs[si])] + ([("ge", ev_mset)] if th == 0 and si == 0 else []),
                    inc=("se", 1))
        ev_c_done = P.val("se")

        ring = [sbf[:, 2048 + s * 1024:2048 + (s + 1) * 1024] for s in range(6)]
        ring_free = [0] * 6
        ystage_free = [0] * 8
        wslot_ready = [0, 0, 0]
        wslot_free = [0, 0, 0]
        psum_free = [0, 0, 0, 0]
        d_qk = 0
        P.op("gpsimd", lambda e: e.dma_start(out=bigB[:, 0:8192], in_=wv_h[0]),
             waits=[("te", te_ent), ("xl", xl_ent), ("ge", ge_ent)], inc=("wl", 16))
        vslot_ready = [P.val("wl"), 0]
        P.op("gpsimd", lambda e: e.dma_start(out=bigB[:, 8192:16384], in_=wv_h[1]),
             inc=("wl", 16))
        vslot_ready[1] = P.val("wl")
        vslot_free = [0, 0]

        for ot in range(24):
            slot = ot % 3
            w = wb1[:, slot * 2048:(slot + 1) * 2048]
            wts = [("te", wslot_free[slot])] if wslot_free[slot] else []
            if ot == 0:
                wts.append(("te", te_ent))
            P.op("gpsimd", lambda e, ot=ot, w=w: e.dma_start(out=w, in_=wqk_h[ot]),
                 waits=wts, inc=("wl", 16))
            wslot_ready[slot] = P.val("wl")
            te_m = {}
            corr = 12 <= ot < 18
            for th in range(2):
                pbi = (ot % 2) * 2 + th
                for dt in range(ND):
                    wts = []
                    if dt == 0:
                        wts = [("wl", wslot_ready[slot])]
                        if ot == 0 and th == 0:
                            wts += [("ve", ev_xlnT), ("se", se_ln)]
                        if psum_free[pbi]:
                            wts.append(("se", psum_free[pbi]))
                        if ot == 12 and th == 0:
                            wts.append(("se", ev_c_done))
                    te_m[th] = P.op("tensor", lambda e, pbi=pbi, w=w, dt=dt, th=th,
                                    corr=corr: e.matmul(
                        pb[pbi][:, :], w[:, dt * 128:(dt + 1) * 128],
                        xlnT[:, dt * TLOC + th * 512:dt * TLOC + (th + 1) * 512],
                        start=(dt == 0), stop=(dt == ND - 1 and not corr)),
                        waits=wts,
                        inc=("te", 1) if (dt == ND - 1 and not corr) else None)
                if corr:
                    otp = ot - 12
                    te_m[th] = P.op("tensor", lambda e, pbi=pbi, otp=otp, th=th: e.matmul(
                        pb[pbi][:, :],
                        wb0[:, 1920 + otp * 128:1920 + (otp + 1) * 128],
                        cK[:, th * 512:(th + 1) * 512],
                        start=False, stop=True), inc=("te", 1))
            wslot_free[slot] = te_m[1]
            rs = ot % 6
            ev2 = None
            for th in range(2):
                pbi = (ot % 2) * 2 + th
                wts = [("te", te_m[th])]
                if th == 0 and ring_free[rs]:
                    wts.append(("st", ring_free[rs]))
                ev2 = P.op("scalar", lambda e, rs=rs, th=th, pbi=pbi, ot=ot: e.activation(
                    ring[rs][:, th * 512:(th + 1) * 512], pb[pbi][:, :], AF.Copy,
                    scale=(SC if ot < 12 else 1.0)),
                    waits=wts, inc=("se", 1))
                psum_free[pbi] = ev2
            d_qk = P.op("sync", lambda e, ot=ot, rs=rs: e.dma_start(
                out=qk_in.ap()[ot * 128:(ot + 1) * 128, :], in_=ring[rs][:, :]),
                waits=[("se", ev2)], inc=("st", 16))
            ring_free[rs] = d_qk

        d_v = 0
        vc_last_te = {}
        for vc in range(3):
            half = vc % 2
            if vc == 2:
                P.op("gpsimd", lambda e: e.dma_start(
                    out=bigB[:, 0:8192], in_=wv_h[2]),
                    waits=[("te", vc_last_te[0])], inc=("wl", 16))
                vslot_ready[0] = P.val("wl")
            wv = bigB[:, half * 8192:(half + 1) * 8192]
            for tt in range(NT):
                pbi = tt % 4
                wts0 = [("wl", vslot_ready[half])]
                if psum_free[pbi]:
                    wts0.append(("se", psum_free[pbi]))
                if vc == 0 and tt == 0:
                    wts0 += [("se", ev_c_done), ("wl", ev_tabs)]
                first_done = False
                if vc < 2:
                    P.op("tensor", lambda e, pbi=pbi, tt=tt, vc=vc: e.matmul(
                        pb[pbi][:, :], cV[:, tt * 128:(tt + 1) * 128],
                        wb0[:, 2688 + vc * 512:2688 + (vc + 1) * 512],
                        start=True, stop=False), waits=wts0)
                    first_done = True
                te_v = None
                for dt in range(ND):
                    wts = [] if (first_done or dt > 0) else wts0
                    te_v = P.op("tensor", lambda e, pbi=pbi, dt=dt, tt=tt, wv=wv,
                                vc=vc: e.matmul(
                        pb[pbi][:, :],
                        xlnT[:, dt * TLOC + tt * 128:dt * TLOC + (tt + 1) * 128],
                        wv[:, dt * 512:(dt + 1) * 512],
                        start=(dt == 0 and vc == 2), stop=(dt == ND - 1)),
                        waits=wts, inc=("te", 1) if dt == ND - 1 else None)
                vc_last_te[vc] = te_v
                rs = (vc * 8 + tt) % 6
                wts = [("te", te_v)]
                if ring_free[rs]:
                    wts.append(("st", ring_free[rs]))
                ev3 = P.op("scalar", lambda e, rs=rs, pbi=pbi: e.activation(
                    ring[rs][:, 0:512], pb[pbi][:, :], AF.Copy),
                    waits=wts, inc=("se", 1))
                psum_free[pbi] = ev3
                for hl in range(8):
                    d_v = P.op("sync", lambda e, vc=vc, tt=tt, hl=hl, rs=rs: e.dma_start(
                        out=v_in.ap()[(vc * 8 + hl) * 1024 + tt * 128:
                                      (vc * 8 + hl) * 1024 + (tt + 1) * 128, :],
                        in_=ring[rs][:, hl * 64:(hl + 1) * 64]),
                        waits=[("se", ev3)] if hl == 0 else (), inc=("st", 16))
                ring_free[rs] = d_v
            vslot_free[half] = vc_last_te[vc]
        ev_p4 = P.val("te")

        cc_qk = P.op("gpsimd", lambda e: e.collective_compute(
            "AllGather", ALU.bypass, replica_groups=[list(range(NCORES))],
            ins=[qk_in.ap().opt()], outs=[qk_out.ap().opt()]),
            waits=[("st", d_qk)], inc=("cc", 1))
        cc_v = P.op("gpsimd", lambda e: e.collective_compute(
            "AllGather", ALU.bypass, replica_groups=[list(range(NCORES))],
            ins=[v_in.ap().opt()], outs=[v_out.ap().opt()]),
            waits=[("st", d_v)], inc=("cc", 1))

        for ht in range(12):
            P.op("gpsimd", lambda e, ht=ht: e.dma_start(
                out=xlnT[:, ht * 1024:(ht + 1) * 1024], in_=wo_h[ht]),
                waits=[("te", ev_p4)] if ht == 0 else (), inc=("wl", 16))
        ev_wo0 = P.val("wl")

        for r in range(NCORES):
            P.op("gpsimd", lambda e, r=r: e.indirect_dma_start(
                out=bigA[:, r * TLOC:(r + 1) * TLOC], out_offset=None,
                in_=qk_out.ap()[0:128, :],
                in_offset=IndirectOffsetOnAxis(tgI[:, 256 + r:257 + r], 0)),
                waits=[("cc", cc_qk), ("xl", ev_idx)] if r == 0 else (),
                inc=("wl", 16))
        for r in range(NCORES):
            P.op("gpsimd", lambda e, r=r: e.indirect_dma_start(
                out=bigA[0:64, 8192 + r * TLOC:8192 + (r + 1) * TLOC],
                out_offset=None,
                in_=qk_out.ap()[0:64, :],
                in_offset=IndirectOffsetOnAxis(tgI[0:64, 264 + r:265 + r], 0)),
                inc=("wl", 16))
        for r in range(NCORES):
            P.op("gpsimd", lambda e, r=r: e.indirect_dma_start(
                out=bigB[:, r * TLOC:(r + 1) * TLOC], out_offset=None,
                in_=qk_out.ap()[0:128, :],
                in_offset=IndirectOffsetOnAxis(tgI[:, 272 + r:273 + r], 0)),
                waits=[("te", ev_p4)] if r == 0 else (), inc=("wl", 16))
        for r in range(NCORES):
            P.op("gpsimd", lambda e, r=r: e.indirect_dma_start(
                out=bigB[0:64, 8192 + r * TLOC:8192 + (r + 1) * TLOC],
                out_offset=None,
                in_=qk_out.ap()[0:64, :],
                in_offset=IndirectOffsetOnAxis(tgI[0:64, 280 + r:281 + r], 0)),
                inc=("wl", 16))
        for r in range(NCORES):
            P.op("gpsimd", lambda e, r=r: e.indirect_dma_start(
                out=bigA[64:128, 8192 + r * TLOC:8192 + (r + 1) * TLOC],
                out_offset=None,
                in_=qk_out.ap()[0:64, :],
                in_offset=IndirectOffsetOnAxis(tgI[0:64, 264 + r:265 + r], 0)),
                inc=("wl", 16))
        for r in range(NCORES):
            P.op("gpsimd", lambda e, r=r: e.indirect_dma_start(
                out=bigB[64:128, 8192 + r * TLOC:8192 + (r + 1) * TLOC],
                out_offset=None,
                in_=qk_out.ap()[0:64, :],
                in_offset=IndirectOffsetOnAxis(tgI[0:64, 280 + r:281 + r], 0)),
                inc=("wl", 16))
        def vbase(gt):
            if gt < 31:
                return self.wbuf[0][:, gt * 195:gt * 195 + 195]
            if gt < 62:
                return self.wbuf[1][:, (gt - 31) * 195:(gt - 31) * 195 + 195]
            return sbf[:, 1024 + (gt - 62) * 195:1024 + (gt - 62) * 195 + 195]

        P.op("gpsimd", lambda e: e.memset(wb0[:, :], 1.0),
             waits=[("te", ev_p4)], inc=("ge", 1))
        P.op("gpsimd", lambda e: e.memset(wb1[:, :], 1.0), inc=("ge", 1))
        P.op("gpsimd", lambda e: e.memset(sbf[:, 1024:1414], 1.0), inc=("ge", 1))
        ev_vones = P.val("ge")
        for r in range(NCORES):
            for hl in range(3):
                for tt in range(NT):
                    gt = r * NT + tt
                    wts = []
                    if r == 0 and hl == 0 and tt == 0:
                        wts = [("cc", cc_v), ("te", ev_p4), ("ge", ev_vones)]
                    P.op("gpsimd", lambda e, r=r, hl=hl, tt=tt, gt=gt: e.indirect_dma_start(
                        out=vbase(gt)[:, hl * 65:hl * 65 + 64],
                        out_offset=None,
                        in_=v_out.ap()[0:128, :],
                        in_offset=IndirectOffsetOnAxis(
                            tgI[:, (r * 3 + hl) * 8 + tt:(r * 3 + hl) * 8 + tt + 1], 0)),
                        waits=wts, inc=("wl", 16))
        ev_gath = P.val("wl")

        def qap(lh, qc):
            if lh < 2:
                return bigA[lh * 64:(lh + 1) * 64, qc * 512:(qc + 1) * 512]
            return bigA[0:64, 8192 + qc * 512:8192 + (qc + 1) * 512]

        def kap(lh, gt):
            if lh < 2:
                return bigB[lh * 64:(lh + 1) * 64, gt * 128:(gt + 1) * 128]
            return bigB[0:64, 8192 + gt * 128:8192 + (gt + 1) * 128]

        def vap(lh, gt):
            return vbase(gt)[:, lh * 65:(lh + 1) * 65]

        def qap2(half, qc):
            return bigA[half * 64:half * 64 + 64,
                        8192 + qc * 512:8192 + (qc + 1) * 512]

        def kap2(half, gt):
            return bigB[half * 64:half * 64 + 64,
                        8192 + gt * 128:8192 + (gt + 1) * 128]

        eslots = [sbf[:, 0:512], sbf[:, 512:1024],
                  sbf[:, 3072:3584], sbf[:, 3584:4096]]
        Oev = scr[1][:, 0:512]
        ys = [sbf[:, 4096 + s * 64:4096 + (s + 1) * 64] for s in range(8)]
        stf = dict(ysc=0, dy=0, oevac=0, trlast=0)
        tr_cons = {1: 0, 3: 0}
        cc_yh = {}

        def finalize(h, lh, qc, te_fin, tbank):
            oev = P.op("vector", lambda e, h=h: e.tensor_copy(
                Oev[0:65, :], pb[4 + h][0:65, :]),
                waits=[("te", te_fin)] +
                      ([("te", stf["trlast"])] if stf["trlast"] else []),
                inc=("ve", 1))
            stf["oevac"] = oev
            for j in range(4):
                wts = [("ve", oev)] if j == 0 else []
                if tr_cons[tbank]:
                    wts.append(("ve", tr_cons[tbank]))
                stf["trlast"] = P.op("tensor", lambda e, tbank=tbank, j=j: e.transpose(
                    pb[tbank][:, 0:65], Oev[0:65, j * 128:(j + 1) * 128],
                    self.identf[0:65, 0:65]),
                    waits=wts, inc=("te", 1))
                rec = P.op("vector", lambda e, tbank=tbank, j=j: e.reciprocal(
                    linv[:, j:j + 1], pb[tbank][:, 64:65]),
                    waits=[("te", stf["trlast"])], inc=("ve", 1))
                s = stf["ysc"] % 8
                stf["ysc"] += 1
                wts = [("ve", rec)]
                if ystage_free[s]:
                    wts.append(("st", ystage_free[s]))
                evm = P.op("vector", lambda e, tbank=tbank, j=j, s=s: e.tensor_scalar_mul(
                    ys[s][:, :], pb[tbank][:, 0:64], linv[:, j:j + 1]),
                    waits=wts, inc=("ve", 1))
                tr_cons[tbank] = evm
                stf["dy"] = P.op("sync", lambda e, lh=lh, qc=qc, j=j, s=s: e.dma_start(
                    out=yag_in[lh].ap()[qc * 512 + j * 128:qc * 512 + (j + 1) * 128, :],
                    in_=ys[s][:, :]),
                    waits=[("ve", evm)], inc=("st", 16))
                ystage_free[s] = stf["dy"]

        first_flash = [True]
        prev_exp_last = [0]
        for qc in range(NQC):
            nk = 4 * (qc + 1)
            exp_ev = {}
            msk_ev = {}
            pv_te = {}

            def emit_pv(h, gt, nk=nk):
                wts = [("se", exp_ev[(h, gt)])]
                if (h, gt) in msk_ev:
                    wts.append(("ge", msk_ev[(h, gt)]))
                if gt == 0 and stf["oevac"]:
                    wts.append(("ve", stf["oevac"]))
                sl = eslots[h * 2 + (gt % 2)]
                pv_te[(h, gt)] = P.op("tensor", lambda e, h=h, gt=gt, sl=sl,
                                      nk=nk: e.matmul(
                    pb[4 + h][0:65, :], vap(h, gt), sl,
                    start=(gt == 0), stop=(gt == nk - 1)),
                    waits=wts, inc=("te", 1))

            prev_batch = []
            for b in range(nk // 2):
                batch = []
                te_s = {}
                for j2 in range(2):
                    gt = 2 * b + j2
                    for h in range(2):
                        bi = h * 2 + (gt % 2)
                        bank = pb[bi]
                        wts = []
                        if first_flash[0]:
                            wts.append(("wl", ev_gath))
                            first_flash[0] = False
                        if gt >= 2:
                            wts.append(("se", exp_ev[(h, gt - 2)]))
                        elif prev_exp_last[0]:
                            wts.append(("se", prev_exp_last[0]))
                        if gt == 1 and tr_cons[bi]:
                            wts.append(("ve", tr_cons[bi]))
                        P.op("tensor", lambda e, h=h, gt=gt, qc=qc, bank=bank: e.matmul(
                            bank[:, :], kap(h, gt), qap(h, qc),
                            start=True, stop=True), waits=wts, inc=("te", 1))
                        te_s[(h, gt)] = P.val("te")
                        batch.append((h, gt))
                for (h, gt) in prev_batch:
                    emit_pv(h, gt)
                prev_batch = batch
                for (h, gt) in batch:
                    sl = eslots[h * 2 + (gt % 2)]
                    bank = pb[h * 2 + (gt % 2)]
                    ew = [("te", pv_te[(h, gt - 2)])] if (h, gt - 2) in pv_te \
                        else [("te", te_s[(h, gt)])]
                    exp_ev[(h, gt)] = P.op("scalar", lambda e, bank=bank,
                                           sl=sl: e.activation(
                        sl, bank[:, :], AF.Exp), waits=ew, inc=("se", 1))
                    if gt // 4 == qc:
                        msk_ev[(h, gt)] = P.op("gpsimd", lambda e, gt=gt,
                                               sl=sl: e.affine_select(
                            sl, sl, pattern=[[1, 512]], compare_op=ALU.is_ge,
                            fill=0.0, base=-((gt % 4) * 128),
                            channel_multiplier=-1),
                            waits=[("se", exp_ev[(h, gt)])], inc=("ge", 1))
            for (h, gt) in prev_batch:
                emit_pv(h, gt)
            prev_exp_last[0] = exp_ev[(1, nk - 1)]
            te_fin = P.val("te")
            finalize(0, 0, qc, te_fin, 1)
            finalize(1, 1, qc, te_fin, 3)
        for lh in range(2):
            cc_yh[lh] = P.op("gpsimd", lambda e, lh=lh: e.collective_compute(
                "AllGather", ALU.bypass, replica_groups=[list(range(NCORES))],
                ins=[yag_in[lh].ap().opt()], outs=[yag_out[lh].ap().opt()]),
                waits=[("st", stf["dy"])], inc=("cc", 1))

        bankmap = [0, 2, 1, 3]
        for qc in range(NQC):
            nk = 4 * (qc + 1)
            exp_ev = {}
            msk_ev = {}
            pv_te = {}

            def emit_pv2(gt, nk=nk):
                wts = [("se", exp_ev[gt])]
                if gt in msk_ev:
                    wts.append(("ge", msk_ev[gt]))
                if gt == 0 and stf["oevac"]:
                    wts.append(("ve", stf["oevac"]))
                sl = eslots[gt % 4]
                pv_te[gt] = P.op("tensor", lambda e, gt=gt, sl=sl, nk=nk: e.matmul(
                    pb[4][0:65, :], vap(2, gt), sl,
                    start=(gt == 0), stop=(gt == nk - 1)),
                    waits=wts, inc=("te", 1))

            prev_batch = []
            for b in range(nk // 4):
                batch = []
                te_s = {}
                for j4 in range(4):
                    gt = 4 * b + j4
                    bi = bankmap[gt % 4]
                    bank = pb[bi]
                    half = gt % 2 if not L1A_NODUP else 0
                    wts = []
                    if gt >= 4:
                        wts.append(("se", exp_ev[gt - 4]))
                    elif prev_exp_last[0]:
                        wts.append(("se", prev_exp_last[0]))
                    if bi in (1, 3) and gt <= 3 and tr_cons[bi]:
                        wts.append(("ve", tr_cons[bi]))
                    P.op("tensor", lambda e, gt=gt, qc=qc, bank=bank,
                         half=half: e.matmul(
                        bank[:, :], kap2(half, gt), qap2(half, qc),
                        start=True, stop=True), waits=wts, inc=("te", 1))
                    te_s[gt] = P.val("te")
                    batch.append(gt)
                for gt in prev_batch:
                    emit_pv2(gt)
                prev_batch = batch
                for gt in batch:
                    bank, sl = pb[bankmap[gt % 4]], eslots[gt % 4]
                    ew = [("te", pv_te[gt - 4])] if (gt - 4) in pv_te \
                        else [("te", te_s[gt])]
                    exp_ev[gt] = P.op("scalar", lambda e, bank=bank,
                                      sl=sl: e.activation(
                        sl, bank[:, :], AF.Exp), waits=ew, inc=("se", 1))
                    if gt // 4 == qc:
                        msk_ev[gt] = P.op("gpsimd", lambda e, gt=gt,
                                          sl=sl: e.affine_select(
                            sl, sl, pattern=[[1, 512]], compare_op=ALU.is_ge,
                            fill=0.0, base=-((gt % 4) * 128),
                            channel_multiplier=-1),
                            waits=[("se", exp_ev[gt])], inc=("ge", 1))
            for gt in prev_batch:
                emit_pv2(gt)
            prev_exp_last[0] = exp_ev[nk - 1]
            te_fin = P.val("te")
            finalize(0, 2, qc, te_fin, 1)
        cc_yh[2] = P.op("gpsimd", lambda e: e.collective_compute(
            "AllGather", ALU.bypass, replica_groups=[list(range(NCORES))],
            ins=[yag_in[2].ap().opt()], outs=[yag_out[2].ap().opt()]),
            waits=[("st", stf["dy"])], inc=("cc", 1))
        te_flash = P.val("te")
        se_flash = P.val("se")

        for ht in range(12):
            dst = wb0 if ht < 6 else wb1
            P.op("gpsimd", lambda e, ht=ht, dst=dst: e.dma_start(
                out=dst[:, (ht % 6) * 1024:(ht % 6 + 1) * 1024], in_=wo_h[12 + ht]),
                waits=[("te", te_flash)] if ht == 0 else (), inc=("wl", 16))
        ev_wo1 = P.val("wl")

        for hl in range(3):
            for tt in range(NT):
                for r in range(NCORES):
                    wts = []
                    if tt == 0 and r == 0:
                        wts = [("cc", cc_yh[hl])]
                        if hl == 0:
                            wts.append(("te", te_flash))
                    P.op("gpsimd", lambda e, hl=hl, tt=tt, r=r: e.indirect_dma_start(
                        out=bigB[:, tt * 1536 + r * 192 + hl * 64:
                                 tt * 1536 + r * 192 + (hl + 1) * 64],
                        out_offset=None,
                        in_=yag_out[hl].ap()[0:128, :],
                        in_offset=IndirectOffsetOnAxis(
                            tgI[:, 192 + tt * 8 + r:192 + tt * 8 + r + 1], 0)),
                        waits=wts, inc=("wl", 16))
        ev_ygath = P.val("wl")
        if DEBUG_L1A:
            dbg_yall = nc.dram_tensor("dbg_yall", [128, 12288], BF16,
                                      kind="ExternalOutput").ap()
            P.op("sync", lambda e: e.dma_start(out=dbg_yall, in_=bigB[:, 0:12288]),
                 waits=[("wl", ev_ygath)], inc=("st", 16))

        ev_ytr = {}
        prev_ycopy = {0: 0, 1: 0}
        for tt in range(NT):
            for ht in range(12):
                b = ht % 2
                wts = [("wl", ev_ygath)] if (tt == 0 and ht == 0) else []
                if prev_ycopy[b]:
                    wts.append(("ve", prev_ycopy[b]))
                te_y = P.op("tensor", lambda e, tt=tt, ht=ht, b=b: e.transpose(
                    ptr[b][:, 0:128],
                    bigB[:, tt * 1536 + ht * 128:tt * 1536 + (ht + 1) * 128],
                    self.identb[:, :]), waits=wts, inc=("te", 1))
                prev_ycopy[b] = P.op("vector", lambda e, tt=tt, ht=ht, b=b: e.tensor_copy(
                    bigA[:, tt * 1536 + ht * 128:tt * 1536 + (ht + 1) * 128],
                    ptr[b][:, 0:128]), waits=[("te", te_y)], inc=("ve", 1))
            ev_ytr[tt] = prev_ycopy[(11) % 2]

        wo_free = [0, 0]
        for tt in range(NT):
            for dc in range(4):
                pbi = (tt * 4 + dc) % 2
                te_o = None
                for ht in range(12):
                    wts = []
                    if ht == 0:
                        wts = [("ve", ev_ytr[tt])]
                        if tt == 0 and dc == 0:
                            wts.append(("se", se_flash))
                        wts.append(("wl", ev_wo0 if dc < 2 else ev_wo1))
                        if wo_free[pbi]:
                            wts.append(("ve", wo_free[pbi]))
                    if dc < 2:
                        rhs = xlnT[:, ht * 1024 + (dc % 2) * 512:
                                   ht * 1024 + (dc % 2) * 512 + 512]
                    else:
                        src = wb0 if ht < 6 else wb1
                        rhs = src[:, (ht % 6) * 1024 + (dc % 2) * 512:
                                  (ht % 6) * 1024 + (dc % 2) * 512 + 512]
                    te_o = P.op("tensor", lambda e, pbi=pbi, tt=tt, ht=ht,
                                rhs=rhs: e.matmul(
                        pb[pbi][:, :],
                        bigA[:, tt * 1536 + ht * 128:tt * 1536 + (ht + 1) * 128],
                        rhs, start=(ht == 0), stop=(ht == 11)),
                        waits=wts, inc=("te", 1) if ht == 11 else None)
                wo_free[pbi] = P.op("vector", lambda e, pbi=pbi, tt=tt, dc=dc: e.tensor_tensor(
                    res[:, tt * D + dc * 512:tt * D + (dc + 1) * 512],
                    pb[pbi][:, :],
                    res[:, tt * D + dc * 512:tt * D + (dc + 1) * 512],
                    op=ALU.add), waits=[("te", te_o)], inc=("ve", 1))

    def l1_ffn(self):
        P, nc = self.P, self.nc
        res, xlnT, bigA, bigB = self.res, self.xlnT, self.bigA, self.bigB
        scr, pb = self.scr, self.pb
        self.reset_cols()
        NI = 11
        NSI = 22

        rw_h = self.din("l1f_rw", [128, 128])
        gw_h = self.din("l1f_gw", [8, NI, 128, D])
        uw_h = self.din("l1f_uw", [8, NI, 128, D])
        dw_h = self.din("l1f_dw", [8, NI, 128, D])
        sgw_h = self.din("l1f_sgw", [NSI, 128, D])
        suw_h = self.din("l1f_suw", [NSI, 128, D])
        sdw_h = self.din("l1f_sdw", [NSI, 128, D])
        sgb_h = self.din("l1f_sgb", [128, NSI])
        sub_h = self.din("l1f_sub", [128, NSI])
        sdb_h = self.din("l1f_sdb", [1, D])

        rw_sb = self.col("rw_sb", 64).bitcast(BF16)
        ctab = self.col("ctab", 64)
        sgb_cm = self.col("sgb", NSI)
        sub_cm = self.col("sub", NSI)
        nm1 = self.col("nm1")
        nm2 = self.col("nm2")
        w1c = self.col("w1c")
        w2c = self.col("w2c")
        dcol = self.col("dcol")

        ent = [("ve", P.val("ve")), ("se", P.val("se")), ("te", P.val("te"))]
        self._l1f_te_ent = P.val("te")
        P.op("gpsimd", lambda e: e.dma_start(out=rw_sb, in_=rw_h), waits=ent,
             inc=("wl", 16))
        P.op("sync", lambda e: e.dma_start(out=sgb_cm, in_=sgb_h), waits=ent,
             inc=("xl", 16))
        P.op("sync", lambda e: e.dma_start(out=sub_cm, in_=sub_h), waits=ent,
             inc=("xl", 16))
        ev_rw = P.val("wl")
        ev_b = P.val("xl")
        if DEBUG_L0F:
            dbg_b0 = nc.dram_tensor("dbg_b0", [128, 124], F32, kind="ExternalOutput").ap()
            P.op("sync", lambda e: e.dma_start(out=dbg_b0, in_=scr[2][:, 900:1024]),
                 inc=("st", 16))

        ev_xln, ev_xlnT = self.emit_ln("l1f")

        c1s = scr[2][:, 0:8]
        lms = scr[2][:, 8:16]
        c2s = scr[2][:, 16:24]
        t1s = scr[2][:, 24:32]
        prev_comb = 0
        for tt in range(NT):
            te_r = None
            for dt in range(ND):
                wts = []
                if dt == 0:
                    wts = [("ve", ev_xlnT), ("wl", ev_rw)]
                    if prev_comb:
                        wts.append(("ve", prev_comb))
                te_r = P.op("tensor", lambda e, tt=tt, dt=dt: e.matmul(
                    pb[4][:, 0:8],
                    xlnT[:, dt * TLOC + tt * 128:dt * TLOC + (tt + 1) * 128],
                    rw_sb[:, dt * 8:(dt + 1) * 8],
                    start=(dt == 0), stop=(dt == ND - 1)),
                    waits=wts, inc=("te", 1) if dt == ND - 1 else None)
            v1 = P.op("vector", lambda e: e.reduce_max(
                nm1, pb[4][:, 0:8], axis=AX.X, negate=True),
                waits=[("te", te_r)], inc=("ve", 1))
            v2 = P.op("vector", lambda e: e.tensor_scalar(
                c1s, pb[4][:, 0:8], nm1, 0.0, op0=ALU.add, op1=ALU.is_ge),
                waits=[("ve", v1)], inc=("ve", 1))
            v3 = P.op("vector", lambda e: e.scalar_tensor_tensor(
                lms, c1s, -1e30, pb[4][:, 0:8], op0=ALU.mult, op1=ALU.add),
                waits=[("ve", v2)], inc=("ve", 1))
            v4 = P.op("vector", lambda e: e.reduce_max(
                nm2, lms, axis=AX.X, negate=True),
                waits=[("ve", v3)], inc=("ve", 1))
            v5 = P.op("vector", lambda e: e.tensor_scalar(
                c2s, lms, nm2, 0.0, op0=ALU.add, op1=ALU.is_ge),
                waits=[("ve", v4)], inc=("ve", 1))
            v6 = P.op("vector", lambda e: e.tensor_tensor(
                dcol, nm2, nm1, op=ALU.subtract),
                waits=[("ve", v5)], inc=("ve", 1))
            s1 = P.op("scalar", lambda e: e.activation(w1c, dcol, AF.Sigmoid),
                      waits=[("ve", v6)], inc=("se", 1))
            v7 = P.op("vector", lambda e: e.tensor_scalar(
                w2c, w1c, -1.0, 1.0, op0=ALU.mult, op1=ALU.add),
                waits=[("se", s1)], inc=("ve", 1))
            v8 = P.op("vector", lambda e: e.tensor_scalar_mul(t1s, c1s, w1c),
                      waits=[("ve", v7)], inc=("ve", 1))
            prev_comb = P.op("vector", lambda e, tt=tt: e.scalar_tensor_tensor(
                ctab[:, tt * 8:(tt + 1) * 8], c2s, w2c, t1s,
                op0=ALU.mult, op1=ALU.add),
                waits=[("ve", v8)], inc=("ve", 1))
        ev_ctab = prev_comb

        silu_bf = scr[0].bitcast(BF16)
        up_bf = scr[1].bitcast(BF16)
        self._l1f_evb = ev_b

        def ffn_pass(nit, gsrc, usrc, dsrc, gbias, ubias, comb_col, sdb_src,
                     first_wait_v, first_wait_s):
            wfree = [0, 0]
            prev_sil = 0
            prev_hmul = 0
            for it in range(nit):
                slot = it % 2
                w = self.wbuf[slot]
                wts = [("te", wfree[slot])] if wfree[slot] else []
                if it == 0:
                    wts += [("ve", first_wait_v), ("se", first_wait_s)]
                    if not wfree[slot] and self._l1f_te_ent:
                        wts.append(("te", self._l1f_te_ent))
                P.op("gpsimd", lambda e, it=it, w=w: e.dma_start(
                    out=w[:, 0:2048], in_=gsrc(it)), waits=wts, inc=("wl", 16))
                P.op("gpsimd", lambda e, it=it, w=w: e.dma_start(
                    out=w[:, 2048:4096], in_=usrc(it)), inc=("wl", 16))
                wready = P.val("wl")
                te_g = {}
                te_u = {}
                for th in range(2):
                    for dt in range(ND):
                        wts = []
                        if th == 0 and dt == 0:
                            wts = [("wl", wready), ("ve", ev_xlnT)]
                            if prev_hmul:
                                wts.append(("ve", prev_hmul))
                            if prev_sil:
                                wts.append(("se", prev_sil))
                        te_g[th] = P.op("tensor", lambda e, th=th, dt=dt, w=w: e.matmul(
                            pb[th][:, :], w[:, dt * 128:(dt + 1) * 128],
                            xlnT[:, dt * TLOC + th * 512:dt * TLOC + (th + 1) * 512],
                            start=(dt == 0), stop=(dt == ND - 1)),
                            waits=wts, inc=("te", 1) if dt == ND - 1 else None)
                for th in range(2):
                    for dt in range(ND):
                        te_u[th] = P.op("tensor", lambda e, th=th, dt=dt, w=w: e.matmul(
                            pb[2 + th][:, :], w[:, 2048 + dt * 128:2048 + (dt + 1) * 128],
                            xlnT[:, dt * TLOC + th * 512:dt * TLOC + (th + 1) * 512],
                            start=(dt == 0), stop=(dt == ND - 1)),
                            inc=("te", 1) if dt == ND - 1 else None)
                wfree[slot] = te_u[1]
                for th in range(2):
                    kw = {}
                    if gbias is not None:
                        kw["bias"] = gbias(it)
                    prev_sil = P.op("scalar", lambda e, th=th, it=it, kw=kw: e.activation(
                        silu_bf[:, th * 512:(th + 1) * 512], pb[th][:, :],
                        AF.Silu, **kw),
                        waits=[("te", te_g[th])] +
                              ([("xl", self._l1f_evb)] if kw else []),
                        inc=("se", 1))
                    if ubias is not None:
                        vub = P.op("vector", lambda e, th=th, it=it: e.tensor_scalar(
                            up_bf[:, th * 512:(th + 1) * 512], pb[2 + th][:, :],
                            ubias(it), None, op0=ALU.add),
                            waits=[("te", te_u[th]), ("xl", self._l1f_evb)],
                            inc=("ve", 1))
                        prev_hmul = P.op("vector", lambda e, th=th, it=it: e.tensor_tensor(
                            bigA[:, it * TLOC + th * 512:it * TLOC + (th + 1) * 512],
                            up_bf[:, th * 512:(th + 1) * 512],
                            silu_bf[:, th * 512:(th + 1) * 512], op=ALU.mult),
                            waits=[("se", prev_sil), ("ve", vub)], inc=("ve", 1))
                    else:
                        prev_hmul = P.op("vector", lambda e, th=th, it=it: e.tensor_tensor(
                            bigA[:, it * TLOC + th * 512:it * TLOC + (th + 1) * 512],
                            pb[2 + th][:, :],
                            silu_bf[:, th * 512:(th + 1) * 512], op=ALU.mult),
                            waits=[("se", prev_sil), ("te", te_u[th])], inc=("ve", 1))
            if DEBUG_L0F and not hasattr(self, "_l1f_d2"):
                self._l1f_d2 = True
                dbg_up2 = nc.dram_tensor("dbg_up2", [128, 2048], BF16, kind="ExternalOutput").ap()
                dbg_bias = nc.dram_tensor("dbg_bias", [128, 124], F32, kind="ExternalOutput").ap()
                P.op("sync", lambda e: e.dma_start(out=dbg_up2, in_=up_bf[:, :]),
                     waits=[("ve", prev_hmul)], inc=("st", 16))
                P.op("sync", lambda e: e.dma_start(out=dbg_bias, in_=scr[2][:, 900:1024]),
                     waits=[("ve", prev_hmul)], inc=("st", 16))
            ev_h = prev_hmul
            piece = 1024
            nphase = D // piece
            prev_ev = 0
            dmaD = None
            for ph in range(nphase):
                for it in range(nit):
                    wts = []
                    if it == 0:
                        wts = [("ve", ev_h)]
                        if prev_ev:
                            wts.append(("ve", prev_ev))
                        if dmaD:
                            wts.append(dmaD)
                    P.op("gpsimd", lambda e, ph=ph, it=it: e.dma_start(
                        out=bigB[:, it * piece:(it + 1) * piece],
                        in_=dsrc(it)[:, ph * piece:(ph + 1) * piece]),
                        waits=wts, inc=("wl", 16))
                dmaD = ("wl", P.val("wl"))
                if sdb_src is not None:
                    dsb = P.op("gpsimd", lambda e, ph=ph: e.dma_start(
                        out=scr[3][:, 0:piece],
                        in_=sdb_src[0:1, ph * piece:(ph + 1) * piece]
                        .to_broadcast([128, piece])),
                        waits=[("ve", prev_ev)] if prev_ev else [("ve", ev_h)],
                        inc=("wl", 16))
                    dmaD2 = ("wl", dsb)
                else:
                    dmaD2 = None
                nq = piece // 512
                for tt in range(NT):
                    te_d = {}
                    for it in range(nit):
                        for q2 in range(nq):
                            wts = []
                            if it == 0 and q2 == 0:
                                wts = [dmaD, ("ve", ev_h)] + ([dmaD2] if dmaD2 else [])
                                if tt == 0 and ph == 0:
                                    wts.append(("ve", ev_ctab))
                                if prev_ev:
                                    wts.append(("ve", prev_ev))
                            te_d[q2] = P.op("tensor", lambda e, it=it, q2=q2, tt=tt: e.matmul(
                                pb[4 + q2][:, :],
                                bigA[:, it * TLOC + tt * 128:it * TLOC + (tt + 1) * 128],
                                bigB[:, it * piece + q2 * 512:it * piece + (q2 + 1) * 512],
                                start=(it == 0), stop=(it == nit - 1)),
                                waits=wts,
                                inc=("te", 1) if (it == nit - 1 and q2 == nq - 1) else None)
                    if sdb_src is not None:
                        for q2 in range(nq):
                            P.op("vector", lambda e, q2=q2: e.tensor_tensor(
                                pb[4 + q2][:, :], pb[4 + q2][:, :],
                                scr[3][:, q2 * 512:(q2 + 1) * 512], op=ALU.add),
                                waits=([("te", te_d[nq - 1]), dmaD2]
                                       if q2 == 0 else []),
                                inc=("ve", 1))
                    for q2 in range(nq):
                        col0 = tt * D + ph * piece + q2 * 512
                        if comb_col is not None:
                            prev_ev = P.op("vector", lambda e, tt=tt, q2=q2, col0=col0: e.scalar_tensor_tensor(
                                res[:, col0:col0 + 512], pb[4 + q2][:, :],
                                comb_col(tt), res[:, col0:col0 + 512],
                                op0=ALU.mult, op1=ALU.add),
                                waits=[("te", te_d[nq - 1])] if q2 == 0 else (),
                                inc=("ve", 1))
                        else:
                            prev_ev = P.op("vector", lambda e, tt=tt, q2=q2, col0=col0: e.tensor_tensor(
                                res[:, col0:col0 + 512], pb[4 + q2][:, :],
                                res[:, col0:col0 + 512], op=ALU.add),
                                waits=[("te", te_d[nq - 1])] if q2 == 0 else (),
                                inc=("ve", 1))
            return prev_ev, P.val("se")

        fw_v, fw_s = ev_ctab, P.val("se")
        for e_i in range(L1F_NEXP):
            fw_v, fw_s = ffn_pass(
                NI,
                lambda it, e_i=e_i: gw_h[e_i, it],
                lambda it, e_i=e_i: uw_h[e_i, it],
                lambda it, e_i=e_i: dw_h[e_i, it],
                None, None,
                lambda tt, e_i=e_i: ctab[:, tt * 8 + e_i:tt * 8 + e_i + 1],
                None, fw_v, fw_s)
        if not L1F_SHARED:
            return
        fw_v, fw_s = ffn_pass(
            L1F_NIT or NI,
            lambda it: sgw_h[it], lambda it: suw_h[it], lambda it: sdw_h[it],
            lambda it: sgb_cm[:, it:it + 1], lambda it: sub_cm[:, it:it + 1],
            None, (sdb_h if not L1F_NOSDB else None),
            fw_v, fw_s)
        if not L1F_SH2:
            return
        ffn_pass(
            NSI - NI,
            lambda it: sgw_h[NI + it], lambda it: suw_h[NI + it],
            lambda it: sdw_h[NI + it],
            lambda it: sgb_cm[:, NI + it:NI + it + 1],
            lambda it: sub_cm[:, NI + it:NI + it + 1],
            None, None,
            fw_v, fw_s)

    def finish(self):
        P = self.P
        allv = P.val("ve")
        for i in range(NT):
            P.op("sync", lambda e, i=i: e.dma_start(
                out=self.y_out[i * 128:(i + 1) * 128, :],
                in_=self.res[:, i * D:(i + 1) * D]),
                waits=[("ve", allv)] if i == 0 else (), inc=("st", 16))
        final = P.val("st")
        P.op("sync", lambda e: e.nop(), waits=[("st", final)])


def prep_host(inp):
    H = {"ident_h": np.eye(128, dtype=np.float32)}
    f = lambda k: np.asarray(inp[k], dtype=np.float32)
    H["l0a_wr"] = _retile_dmajor(f("L0_attn_Wr"))
    H["l0a_wk"] = _retile_dmajor(f("L0_attn_Wk"))
    H["l0a_wv"] = _retile_dmajor(f("L0_attn_Wv"))
    H["l0a_wo"] = _rows_block(np.ascontiguousarray(f("L0_attn_Wo").T))
    H["l0a_decay"] = _cm_vec(f("L0_attn_decay"))
    H["l0a_gnw"] = _cm_vec(f("L0_attn_gn_weight"))
    H["l0a_gnb"] = _cm_vec(f("L0_attn_gn_bias"))
    wg = f("L0_ffn_W_gate")
    H["l0f_wg"] = np.ascontiguousarray(
        wg.reshape(16, 16, 128).transpose(2, 1, 0).reshape(128, 256))
    H["l0f_wqkv"] = _retile_dmajor(f("L0_ffn_W_qkv"))

    WQ, WK, WV, WO = (f("L1_attn_W_Q"), f("L1_attn_W_K"),
                      f("L1_attn_W_V"), f("L1_attn_W_O"))
    wq_t = WQ.reshape(16, 128, 12, 128).transpose(2, 1, 0, 3).reshape(12, 128, 2048)
    wk_t = WK.reshape(16, 128, 12, 128).transpose(2, 1, 0, 3).reshape(12, 128, 2048)
    H["l1a_wqk"] = np.ascontiguousarray(np.concatenate([wq_t, wk_t], axis=0))
    H["l1a_wv"] = np.ascontiguousarray(
        WV.reshape(16, 128, 3, 512).transpose(2, 1, 0, 3).reshape(3, 128, 8192))
    H["l1a_wo"] = np.ascontiguousarray(
        WO.reshape(12, 128, 2, 1024).transpose(2, 0, 1, 3).reshape(24, 128, 1024))
    fkv = np.zeros((128, 1920), dtype=np.float32)
    fkv[:, 0:448] = f("L1_attn_FK0").reshape(14, 128, 32).transpose(1, 0, 2).reshape(128, 448)
    fkv[:, 448:896] = f("L1_attn_FV0").reshape(14, 128, 32).transpose(1, 0, 2).reshape(128, 448)
    fkv[:, 896:1408] = f("L1_attn_FK1").reshape(8, 128, 64).transpose(1, 0, 2).reshape(128, 512)
    fkv[:, 1408:1920] = f("L1_attn_FV1").reshape(8, 128, 64).transpose(1, 0, 2).reshape(128, 512)
    H["l1a_fkv"] = fkv

    def _pblk(P0, P1, width):
        tab = np.zeros((128, width), dtype=np.float32)
        for h in range(4):
            for r in range(8):
                tab[h * 8 + r, h * 64:(h + 1) * 64] = P0[h, r]
        for hl in range(8):
            for r in range(8):
                tab[64 + hl * 8 + r, 256 + hl * 64:256 + (hl + 1) * 64] = P1[hl, r]
        return tab
    H["l1a_pkb"] = _pblk(f("L1_attn_PK0"), f("L1_attn_PK1"), 768)
    H["l1a_pvb"] = _pblk(f("L1_attn_PV0"), f("L1_attn_PV1"), 1024)

    rw = f("L1_ffn_router_w")
    H["l1f_rw"] = np.ascontiguousarray(
        rw.reshape(8, 16, 128).transpose(2, 1, 0).reshape(128, 128))
    gw = f("L1_ffn_gate_w"); uw = f("L1_ffn_up_w"); dw = f("L1_ffn_down_w")
    H["l1f_gw"] = np.stack([_retile_dmajor(np.ascontiguousarray(gw[e].T))
                            for e in range(8)])
    H["l1f_uw"] = np.stack([_retile_dmajor(np.ascontiguousarray(uw[e].T))
                            for e in range(8)])
    H["l1f_dw"] = np.stack([_rows_block(dw[e]) for e in range(8)])
    H["l1f_sgw"] = _retile_dmajor(np.ascontiguousarray(f("L1_ffn_s_gate_w").T))
    H["l1f_suw"] = _retile_dmajor(np.ascontiguousarray(f("L1_ffn_s_up_w").T))
    H["l1f_sdw"] = _rows_block(f("L1_ffn_s_down_w"))
    H["l1f_sgb"] = _cm_vec(f("L1_ffn_s_gate_b"))
    H["l1f_sub"] = _cm_vec(f("L1_ffn_s_up_b"))
    H["l1f_sdb"] = np.ascontiguousarray(f("L1_ffn_s_down_b").reshape(1, D))
    HP = [dict() for _ in range(NCORES)]
    for c in range(NCORES):
        tgv = np.arange(c * TLOC, (c + 1) * TLOC).astype(np.float32)
        HP[c]["tg_base"] = np.ascontiguousarray(
            np.broadcast_to(tgv[None, :], (128, TLOC)).copy())
        m = np.zeros((128, 16), dtype=np.float32)
        for r in range(NCORES):
            if r < c:
                for ct in range(16):
                    m[16 * r + ct, ct] = 1.0
        HP[c]["l0a_cmask"] = m
        p = np.arange(128)
        qidx = np.zeros((128, 16), dtype=np.int32)
        for lh in range(2):
            for r in range(NCORES):
                qidx[:, lh * 8 + r] = r * 2048 + (2 * c + lh) * 128 + p
        HP[c]["l0f_qidx"] = qidx
        yidx = np.zeros((128, 64), dtype=np.int32)
        for i in range(NT):
            for r in range(NCORES):
                yidx[:, i * 8 + r] = r * T + c * TLOC + i * 128 + p
        HP[c]["l0f_yidx"] = yidx
        lidx = np.zeros((128, 1), dtype=np.int32)
        for q in range(128):
            h, i = q // 8, q % 8
            lidx[q, 0] = 128 * (h // 2) + (h % 2) * 64 + 8 * c + i
        HP[c]["l0f_lidx"] = lidx
        aidx = np.zeros((128, 288), dtype=np.int32)
        for r in range(NCORES):
            for hl in range(3):
                h = 3 * c + hl
                for i in range(NT):
                    aidx[:, (r * 3 + hl) * 8 + i] = (r * 24576 + (h // 8) * 8192 +
                                                     (h % 8) * 1024 + i * 128 + p)
        for i in range(NT):
            for r in range(NCORES):
                aidx[:, 192 + i * 8 + r] = r * T + c * TLOC + i * 128 + p
        for r in range(NCORES):
            aidx[:, 256 + r] = r * 3072 + c * 192 + p
            aidx[0:64, 264 + r] = r * 3072 + c * 192 + 128 + p[0:64]
            aidx[:, 272 + r] = r * 3072 + 1536 + c * 192 + p
            aidx[0:64, 280 + r] = r * 3072 + 1536 + c * 192 + 128 + p[0:64]
        HP[c]["l1a_idx"] = aidx
    return H, HP


_BUILD_CACHE = {}


def _get_builder(blocks):
    key = tuple(blocks)
    if key not in _BUILD_CACHE:
        _BUILD_CACHE[key] = B(list(blocks)).build()
    return _BUILD_CACHE[key]


def run_group(blocks, x_cur_shards, H, HP):
    b = _get_builder(blocks)
    in_maps = []
    for c in range(NCORES):
        m = {}
        for n in b.shared_names:
            m[n] = H[n]
        for n in b.percore_names:
            m[n] = HP[c][n] if n != "x_in" else x_cur_shards[c]
        in_maps.append(m)
    res = run_bass_kernel_spmd(b.nc, in_maps, core_ids=list(range(NCORES)))
    global LAST_RESULTS
    LAST_RESULTS = res.results
    return [np.asarray(r["y_out"]) for r in res.results]


def kernel(**inputs):
    H, HP = prep_host(inputs)
    x = np.ascontiguousarray(np.asarray(inputs["L0_attn_x"], dtype=np.float32)[0])
    shards = shard_tokens(x)
    for group in PLAN:
        shards = run_group(group, shards, H, HP)
    y = unshard_tokens(shards)
    return y.reshape(1, T, D)
